# version 12
# baseline (speedup 1.0000x reference)
import contextlib
import math
import numpy as np
import ml_dtypes
import concourse.bass as bass
import concourse.mybir as mybir
from concourse.bass_utils import run_bass_kernel_spmd

F32 = mybir.dt.float32
BF16 = mybir.dt.bfloat16
F8 = mybir.dt.float8e4
AF = mybir.ActivationFunctionType
ALU = mybir.AluOpType
AX = mybir.AxisListType
BF16NP = ml_dtypes.bfloat16


def make_cfg(S=2048, BPC=2, NCORES=8):
    c = {}
    c["D"] = 1024
    c["NH"] = 16
    c["DH"] = 64
    c["S"] = S
    c["BPC"] = BPC
    c["T"] = S * BPC
    c["NCORES"] = NCORES
    c["E"] = 8
    c["HF"] = 4096
    c["MOE_K"] = 2
    n_total = S * BPC * NCORES
    c["CAP"] = int(math.ceil(1.0 * c["MOE_K"] * n_total / c["E"]))
    c["DLAT"] = 256
    return c



def host_constants(cfg, core_id):
    D, S, NH, DH = cfg["D"], cfg["S"], cfg["NH"], cfg["DH"]
    out = {}
    out["c_ident"] = np.eye(128, dtype=BF16NP)
    out["c_identf"] = np.eye(128, dtype=np.float32)
    out["c_triT"] = np.triu(np.ones((128, 128), dtype=np.float32)).copy()
    out["c_ones_kxm"] = np.ones((128, 128), dtype=BF16NP)
    out["c_ones_col"] = np.ones((128, 1), dtype=np.float32)
    out["c_ones_row"] = np.ones((1, 128), dtype=np.float32)
    out["c_ones_row_bf"] = np.ones((1, 128), dtype=BF16NP)
    P = np.zeros((DH, DH), dtype=np.float32)
    for j in range(DH // 2):
        P[j, 2 * j + 1] = -1.0
    for i in range(DH // 2):
        P[DH // 2 + i, 2 * i] = 1.0
    P2 = np.zeros((128, 128), dtype=np.float32)
    P2[:DH, :DH] = P.T
    P2[DH:, DH:] = P.T
    out["c_p2"] = P2.astype(BF16NP)
    invf = (1.0 / 10000.0 ** (np.arange(0, DH, 2, dtype=np.float32) / DH)).astype(np.float32)
    t = np.arange(S, dtype=np.float32)
    freqs = np.einsum("i,j->ij", t, invf).astype(np.float32)
    emb = np.concatenate([freqs, freqs], axis=-1)
    cosT = np.cos(emb).T.astype(np.float32)
    sinT = np.sin(emb).T.astype(np.float32)
    out["c_l0cos"] = np.concatenate([cosT, cosT], axis=0).copy()
    out["c_l0sin"] = np.concatenate([sinT, sinT], axis=0).copy()
    fr1 = (t[:, None] * invf[None, :]).astype(np.float32)
    out["c_l1cos"] = np.tile(np.cos(fr1), (1, NH)).astype(np.float32).copy()
    out["c_l1sin"] = np.tile(np.sin(fr1), (1, NH)).astype(np.float32).copy()
    sel4 = np.zeros((4, 4 * 128), dtype=np.float32)
    for r in range(4):
        sel4[r, r * 128:(r + 1) * 128] = 1.0
    out["c_sel4"] = sel4
    sel8 = np.zeros((8, 8 * 128), dtype=np.float32)
    for r in range(8):
        sel8[r, r * 128:(r + 1) * 128] = 1.0
    out["c_sel8"] = sel8
    pm = np.zeros((cfg["NCORES"], 1), dtype=np.float32)
    pm[:core_id] = 1.0
    out["c_prevmask"] = pm
    out["c_allmask"] = np.ones((cfg["NCORES"], 1), dtype=np.float32)
    out["c_eps5"] = np.full((128, 1), 1e-5, dtype=np.float32)
    he = np.zeros((1, 128), dtype=np.float32); he[0, :64] = 1.0
    ho = np.zeros((1, 128), dtype=np.float32); ho[0, 64:] = 1.0
    out["c_half_even"] = he
    out["c_half_odd"] = ho
    return out


CONST_SPECS = {
    "c_ident": ([128, 128], BF16),
    "c_identf": ([128, 128], F32),
    "c_triT": ([128, 128], F32),
    "c_ones_kxm": ([128, 128], BF16),
    "c_ones_col": ([128, 1], F32),
    "c_ones_row": ([1, 128], F32),
    "c_ones_row_bf": ([1, 128], BF16),
    "c_p2": ([128, 128], BF16),
    "c_sel4": ([4, 4 * 128], F32),
    "c_sel8": ([8, 8 * 128], F32),
    "c_eps5": ([128, 1], F32),
    "c_prevmask": (None, F32),
    "c_allmask": (None, F32),
    "c_half_even": ([1, 128], F32),
    "c_half_odd": ([1, 128], F32),
}



ENGINES = ("sync", "tensor", "vector", "scalar", "gpsimd")


class KB:

    def __init__(self, nc, block):
        self.nc = nc
        self.block = block
        self.sems = {}
        self.counts = {}
        self.waited = {e: {} for e in ENGINES}
        self.q = {e: [] for e in ENGINES}
        for e in ("tensor", "vector", "scalar", "gpsimd"):
            self._mk_sem("e_" + e)

    def _mk_sem(self, name):
        self.sems[name] = self.nc.alloc_semaphore(name=name)
        self.counts[name] = 0

    def flush(self):
        for eng in ENGINES:
            ts = self.q[eng]
            if not ts:
                continue
            self.q[eng] = []

            def body(e, ts=ts):
                for t in ts:
                    t(e)
            getattr(self.block, eng)(body)

    def ins(self, eng, fn, deps=(), sem_name=None):
        sem_name = sem_name or ("e_" + eng)
        if sem_name not in self.sems:
            self._mk_sem(sem_name)
        sem = self.sems[sem_name]
        self.counts[sem_name] += 1
        cnt = self.counts[sem_name]
        deps = tuple(deps)

        def body(e):
            self._do_waits(e, eng, deps)
            fn(e).then_inc(sem, 1)
        self.q[eng].append(body)
        return (sem_name, cnt)

    def dma(self, queue, out, in_, deps=(), stream=None, **kw):
        name = stream or ("dma_" + queue)
        if name not in self.sems:
            self._mk_sem(name)
        sem = self.sems[name]
        self.counts[name] += 16
        cnt = self.counts[name]
        deps = tuple(deps)

        def body(e):
            self._do_waits(e, queue, deps)
            e.dma_start(out=out, in_=in_, **kw).then_inc(sem, 16)
        self.q[queue].append(body)
        return (name, cnt)

    def _do_waits(self, e, eng, deps):
        w = self.waited[eng]
        my = "e_" + eng
        agg = {}
        for sem_name, val in deps:
            if sem_name == my and self.counts[my] < val:
                raise RuntimeError("dep on future same-engine instruction")
            if val > agg.get(sem_name, 0):
                agg[sem_name] = val
        for sem_name, val in agg.items():
            if w.get(sem_name, 0) >= val:
                continue
            e.wait_ge(self.sems[sem_name], val)
            w[sem_name] = val

    def wait(self, eng, deps):
        deps = tuple(deps)

        def body(e):
            self._do_waits(e, eng, deps)
        self.q[eng].append(body)

    def barrier(self, eng):
        return ("e_" + eng, self.counts["e_" + eng])

    def full_barrier(self, exclude=()):
        toks = [(n, self.counts[n]) for n in self.sems if n not in exclude]
        for e in ENGINES:
            self.wait(e, toks)


def fm_ap(fm, c, t0, width):
    nt = width // 128
    return fm[t0 // 128: t0 // 128 + nt, c * 128:(c + 1) * 128, :].rearrange(
        "n p t -> p n t")


def sb3(sb_slice, tile=128):
    return sb_slice.rearrange("p (n t) -> p n t", t=tile)



def emit_tm2fm(kb, cfg, st, src_ap, dst_ap, sfx, rev=False):
    T, D = cfg["T"], cfg["D"]
    nc = kb.nc
    with (
        nc.sbuf_tensor("tf_in" + sfx, [128, D], F32) as tin_sb,
        nc.sbuf_tensor("tf_out" + sfx, [128, D], F32) as tout_sb,
        nc.psum_tensor("tf_ps0" + sfx, [128, 512], F32) as ps0,
        nc.psum_tensor("tf_ps1" + sfx, [128, 512], F32) as ps1,
    ):
        ps = (ps0, ps1)
        prev_tts, prev_out = [], []
        for tile in range(T // 128):
            if rev:
                src = src_ap[tile, :, :].rearrange("(c p) t -> p c t", p=128)
            else:
                src = src_ap[tile * 128:(tile + 1) * 128, :]
            tin = kb.dma("sync", sb3(tin_sb[:]) if rev else tin_sb[:], src,
                         deps=prev_tts, stream="d_tf_in")
            evs, tts = [], []
            for g in range(2):
                gtts = []
                for c in range(4):
                    ch = g * 4 + c
                    gtts.append(kb.ins(
                        "tensor",
                        lambda e, ch=ch, g=g, c=c: e.transpose(
                            ps[g][:, c * 128:(c + 1) * 128],
                            tin_sb[:, ch * 128:(ch + 1) * 128], st["c_identf"][:]),
                        deps=[tin] + prev_out))
                evs.append(kb.ins(
                    "vector",
                    lambda e, g=g: e.tensor_copy(tout_sb[:, g * 512:(g + 1) * 512], ps[g][:]),
                    deps=gtts))
                tts += gtts
            if rev:
                dst = dst_ap[tile * 128:(tile + 1) * 128, :]
            else:
                dst = dst_ap[tile, :, :].rearrange("(c p) t -> p c t", p=128)
            tok = kb.dma("sync", dst, tout_sb[:] if rev else sb3(tout_sb[:]),
                         deps=evs, stream="d_tf_out")
            prev_tts = tts
            prev_out = [tok]
    return [tok]



def emit_mod(kb, cfg, st, c_ap, wmod_ap, modT_ap, sfx):
    D, BPC = cfg["D"], cfg["BPC"]
    NV = 6
    nc = kb.nc
    KC = D // 128
    with (
        nc.sbuf_tensor("md_pad" + sfx, [128, D], F32) as pad,
        nc.sbuf_tensor("md_siluT" + sfx, [128, KC * BPC], F32) as siluT,
        nc.sbuf_tensor("md_w" + sfx, [128, 2 * 512], F32) as wmt_d,
        nc.sbuf_tensor("md_mod" + sfx, [128, NV * D], F32) as modp,
        nc.sbuf_tensor("md_t" + sfx, [128, D], F32) as tmp,
        nc.sbuf_tensor("md_t2" + sfx, [128, D], F32) as tmp2,
        nc.sbuf_tensor("md_modT" + sfx, [128, NV * KC * BPC], F32) as modT,
        nc.psum_tensor("md_ps" + sfx, [128, 512], F32) as ps,
        nc.psum_tensor("md_ps2" + sfx, [128, 512], F32) as ps2,
    ):
        z = kb.ins("vector", lambda e: e.memset(pad[:], 0.0))
        zm = kb.ins("vector", lambda e: e.memset(modp[:], 0.0))
        cl = kb.dma("sync", pad[0:BPC, :], c_ap, deps=[z], stream="d_mod")
        sl = kb.ins("scalar", lambda e: e.activation(pad[0:BPC, :], pad[0:BPC, :], AF.Silu),
                    deps=[cl])
        ev = None
        for k in range(KC):
            tt = kb.ins("tensor",
                        lambda e, k=k: e.transpose(ps[:, 0:128], pad[:, k * 128:(k + 1) * 128],
                                                   st["c_identf"][:]),
                        deps=[sl] + ([ev] if ev else []))
            ev = kb.ins("vector",
                        lambda e, k=k: e.tensor_copy(siluT[:, k * BPC:(k + 1) * BPC], ps[:, 0:BPC]),
                        deps=[tt])
        evm = None
        prev_mm = {}
        for j in range(NV * D // 512):
            toks = []
            for k in range(KC):
                w_t = wmt_d[:, (k % 2) * 512:(k % 2 + 1) * 512]
                wl = kb.dma("gpsimd", w_t,
                            wmod_ap[k * 128:(k + 1) * 128, j * 512:(j + 1) * 512],
                            deps=[prev_mm[k % 2]] if k % 2 in prev_mm else [],
                            stream="d_modw%d" % (k % 2))
                mm = kb.ins(
                    "tensor",
                    lambda e, k=k, w_t=w_t: e.matmul(
                        ps2[0:BPC, :], siluT[:, k * BPC:(k + 1) * BPC], w_t,
                        start=(k == 0), stop=(k == KC - 1)),
                    deps=[wl, ev] + ([evm] if evm else []))
                toks.append(mm)
                prev_mm[k % 2] = mm
            evm = kb.ins("vector",
                         lambda e, j=j: e.tensor_copy(modp[0:BPC, j * 512:(j + 1) * 512],
                                                      ps2[0:BPC, :]),
                         deps=[toks[-1], zm])
        post = [evm, zm]
        for v in (2, 5):
            sl_ = modp[0:BPC, v * D:(v + 1) * D]
            ta = kb.ins("scalar", lambda e, sl_=sl_: e.activation(
                tmp[0:BPC, 0:D], sl_, AF.Abs), deps=post)
            tc = kb.ins("vector", lambda e: e.tensor_scalar_max(
                tmp[0:BPC, 0:D], tmp[0:BPC, 0:D], 0.25), deps=[ta])
            tm = kb.ins("vector", lambda e, sl_=sl_: e.tensor_single_scalar(
                tmp2[0:BPC, 0:D], sl_, 0.0, ALU.is_ge), deps=post)
            ts = kb.ins("vector", lambda e: e.tensor_scalar(
                tmp2[0:BPC, 0:D], tmp2[0:BPC, 0:D], 2.0, -1.0, ALU.mult, ALU.add),
                deps=[tm])
            tg = kb.ins("vector", lambda e, sl_=sl_: e.tensor_mul(
                sl_, tmp[0:BPC, 0:D], tmp2[0:BPC, 0:D]), deps=[tc, ts])
            post = post + [tg]
        for v in (1, 4):
            sl_ = modp[0:BPC, v * D:(v + 1) * D]
            post.append(kb.ins("vector",
                               lambda e, sl_=sl_: e.tensor_scalar_add(sl_, sl_, 1.0),
                               deps=post))
        evq = None
        for q in range(NV * KC):
            tq = kb.ins("tensor",
                        lambda e, q=q: e.transpose(ps[:, 0:128], modp[:, q * 128:(q + 1) * 128],
                                                   st["c_identf"][:]),
                        deps=post + ([evq] if evq else []))
            evq = kb.ins("vector",
                         lambda e, q=q: e.tensor_copy(modT[:, q * BPC:(q + 1) * BPC], ps[:, 0:BPC]),
                         deps=[tq])
        tok = kb.dma("sync", modT_ap[:], modT[:], deps=[evq], stream="d_mod")
    return [tok]


def modcol(cfg, v, ch, b):
    return (v * (cfg["D"] // 128) + ch) * cfg["BPC"] + b



def emit_ln(kb, cfg, st, x_ap, h_ap, modT_sb, modT_tok, v_shift, v_scale, sfx,
            eps=1e-5):
    D, T, S, BPC = cfg["D"], cfg["T"], cfg["S"], cfg["BPC"]
    KC = D // 128
    nc = kb.nc
    SL = min(512, S)
    assert S % SL == 0
    with (
        nc.sbuf_tensor("ln_x" + sfx, [128, 2 * KC * SL], F32) as xs_d,
        nc.sbuf_tensor("ln_xb" + sfx, [128, KC * SL], BF16) as xb,
        nc.sbuf_tensor("ln_sq" + sfx, [128, KC * SL], BF16) as sq,
        nc.sbuf_tensor("ln_mean" + sfx, [128, SL], F32) as mean,
        nc.sbuf_tensor("ln_t" + sfx, [128, SL], F32) as tvar,
        nc.sbuf_tensor("ln_rstd" + sfx, [128, SL], F32) as rstd,
        nc.sbuf_tensor("ln_h" + sfx, [128, KC * SL], BF16) as hb,
        nc.psum_tensor("ln_pm" + sfx, [128, SL], F32) as pm,
        nc.psum_tensor("ln_pq" + sfx, [128, SL], F32) as pq,
    ):
        prev = {}
        for sl in range(T // SL):
            xs = xs_d[:, (sl % 2) * KC * SL:(sl % 2 + 1) * KC * SL]
            b = (sl * SL) // S
            loads = []
            for c in range(KC):
                loads.append((c, kb.dma(
                    "sync", sb3(xs[:, c * SL:(c + 1) * SL]), fm_ap(x_ap, c, sl * SL, SL),
                    deps=prev.get(("x", sl % 2), []), stream="d_ln_x")))
            last_ld = loads[-1][1]
            casts, sqs = [], []
            for c, ld in loads:
                casts.append(kb.ins("vector", lambda e, c=c, xs=xs: e.tensor_copy(
                    xb[:, c * SL:(c + 1) * SL], xs[:, c * SL:(c + 1) * SL]),
                    deps=[last_ld] + prev.get("xb", [])))
                sqs.append(kb.ins("scalar", lambda e, c=c, xs=xs: e.activation(
                    sq[:, c * SL:(c + 1) * SL], xs[:, c * SL:(c + 1) * SL], AF.Square),
                    deps=[last_ld] + prev.get("sq", [])))
            mms, mqs = [], []
            for c in range(KC):
                mms.append(kb.ins("tensor", lambda e, c=c: e.matmul(
                    pm[:], st["c_ones_kxm"][:], xb[:, c * SL:(c + 1) * SL],
                    start=(c == 0), stop=(c == KC - 1)),
                    deps=[casts[c]] + prev.get("pm", [])))
            for c in range(KC):
                mqs.append(kb.ins("tensor", lambda e, c=c: e.matmul(
                    pq[:], st["c_ones_kxm"][:], sq[:, c * SL:(c + 1) * SL],
                    start=(c == 0), stop=(c == KC - 1)),
                    deps=[sqs[c]] + prev.get("pq", [])))
            t_mean = kb.ins("scalar", lambda e: e.activation(
                mean[:], pm[:], AF.Copy, scale=1.0 / D),
                deps=[mms[-1]] + prev.get("mean", []))
            t_m2 = kb.ins("vector", lambda e: e.tensor_mul(tvar[:], mean[:], mean[:]),
                          deps=[t_mean] + prev.get("tvar", []))
            t_var = kb.ins("vector", lambda e: e.scalar_tensor_tensor(
                tvar[:], pq[:], 1.0 / D, tvar[:], ALU.mult, ALU.subtract),
                deps=[mqs[-1], t_m2])
            t_sd = kb.ins("scalar", lambda e: e.activation(
                tvar[:], tvar[:], AF.Sqrt, bias=st["c_eps5"][:]),
                deps=[t_var] + st["tok"])
            t_rs = kb.ins("vector", lambda e: e.reciprocal(rstd[:], tvar[:]),
                          deps=[t_sd] + prev.get("rstd", []))
            houts = []
            hdeps = prev.get("hdma", [])
            for c in range(KC):
                t1 = kb.ins("vector", lambda e, c=c, xs=xs: e.tensor_sub(
                    xs[:, c * SL:(c + 1) * SL],
                    xs[:, c * SL:(c + 1) * SL], mean[:]),
                    deps=[t_mean, mms[-1], mqs[-1]])
                t2 = kb.ins("vector", lambda e, c=c, xs=xs: e.tensor_mul(
                    xs[:, c * SL:(c + 1) * SL], xs[:, c * SL:(c + 1) * SL], rstd[:]),
                    deps=[t_rs, t1])
                t3 = kb.ins("vector", lambda e, c=c, b=b, xs=xs: e.tensor_scalar(
                    hb[:, c * SL:(c + 1) * SL], xs[:, c * SL:(c + 1) * SL],
                    modT_sb[:, modcol(cfg, v_scale, c, b):modcol(cfg, v_scale, c, b) + 1],
                    modT_sb[:, modcol(cfg, v_shift, c, b):modcol(cfg, v_shift, c, b) + 1],
                    ALU.mult, ALU.add),
                    deps=[t2, modT_tok] + hdeps)
                houts.append(t3)
            hdma = []
            for c in range(KC):
                hdma.append(kb.dma("sync", fm_ap(h_ap, c, sl * SL, SL),
                                   sb3(hb[:, c * SL:(c + 1) * SL]), deps=[houts[c]],
                                   stream="d_ln_h"))
            prev = {("x", sl % 2): houts,
                    "xb": mms, "sq": mqs,
                    "pm": [t_mean], "pq": [t_var],
                    "mean": houts, "tvar": [t_rs], "rstd": houts,
                    "hdma": hdma}
    return hdma



def scratch_specs(cfg):
    T, D, NH, S, BPC = cfg["T"], cfg["D"], cfg["NH"], cfg["S"], cfg["BPC"]
    return {
        "x0fm": ([T // 128, D, 128], F32),
        "x1fm": ([T // 128, D, 128], F32),
        "x2fm": ([T // 128, D, 128], F32),
        "x3fm": ([T // 128, D, 128], F32),
        "h1": ([T // 128, D, 128], BF16),
        "h2": ([T // 128, D, 128], BF16),
        "h3": ([T // 128, D, 128], BF16),
        "h4": ([T // 128, D, 128], BF16),
        "modT0": ([128, 6 * (D // 128) * BPC], F32),
        "modT1": ([128, 6 * (D // 128) * BPC], F32),
        "wqT": ([D, D], BF16),
        "wkT": ([D, D], BF16),
        "wvT": ([D, D], BF16),
        "woT": ([D, D], BF16),
        "qro": ([D, T], BF16),
        "kro": ([D, T], BF16),
        "vtm": ([T, NH * 65], BF16),
        "attnfm": ([D, T], BF16),
        "cntloc": ([1, 16], F32),
        "cntg": ([cfg["NCORES"], 16], F32),
        "w8": ([8, T], F32),
        "l1q": ([T, D], BF16),
        "l1k": ([T, D], BF16),
        "l1v": ([T, D], BF16),
        "l1at": ([D, T], BF16),
        "x4fm": ([T // 128, D, 128], F32),
        "w1b": ([8, D, 4096], BF16),
        "w2b": ([8, 4096, D], BF16),
        "out": ([T, D], F32),
    }


def input_specs_per_core(cfg):
    D, BPC, NH = cfg["D"], cfg["BPC"], cfg["NH"]
    return {
        "L0_attn_hidden_states": ([BPC * cfg["S"], D], F32),
        "L0_attn_Wq": ([D, D], F32), "L0_attn_Wk": ([D, D], F32),
        "L0_attn_Wv": ([D, D], F32), "L0_attn_Wo": ([D, D], F32),
        "L0_ffn_wg": ([D, 8], F32),
        "L0_ffn_w1": ([8, D, 4096], F32), "L0_ffn_b1": ([8, 4096], F32),
        "L0_ffn_w2": ([8, 4096, D], F32), "L0_ffn_b2": ([8, D], F32),
        "L0_c": ([BPC, D], F32), "L0_wmod": ([D, 6 * D], F32),
        "L1_attn_Wq": ([D, D], F32), "L1_attn_bq": ([1, D], F32),
        "L1_attn_Wdown": ([D, 256], F32), "L1_attn_bdown": ([1, 256], F32),
        "L1_attn_Wup": ([256, 2 * D], F32), "L1_attn_bup": ([1, 2 * D], F32),
        "L1_attn_Wout": ([D, D], F32), "L1_attn_bout": ([1, D], F32),
        "L1_ffn_kernel": ([D, D], BF16),
        "L1_ffn_input_scale": ([1, 1], F32), "L1_ffn_kernel_scale": ([1, 1], F32),
        "L1_ffn_bias": ([1, D], BF16),
        "L1_c": ([BPC, D], F32), "L1_wmod": ([D, 6 * D], F32),
    }


def emit_setup(kb, cfg, aps):
    nc = kb.nc
    st = {}
    toks = []
    es = contextlib.ExitStack()
    for name, (shape, dt) in CONST_SPECS.items():
        if shape is None:
            shape = [cfg["NCORES"], 1]
        t = es.enter_context(nc.sbuf_tensor("st_" + name, shape, dt))
        st[name] = t
        toks.append(kb.dma("sync", t[:], aps[name][:], stream="d_setup"))
    st["tok"] = toks
    st["es"] = es
    return st


def build_program(cfg, phases, external_scratch):
    nc = bass.Bass("TRN2", target_bir_lowering=False)
    aps = {}
    ins_sp = input_specs_per_core(cfg)
    for name, (shape, dt) in ins_sp.items():
        aps[name] = nc.dram_tensor(name, shape, dt, kind="ExternalInput").ap()
    for name, arr in host_constants(cfg, 0).items():
        dt = BF16 if arr.dtype == BF16NP else F32
        aps[name] = nc.dram_tensor(name, list(arr.shape), dt,
                                   kind="ExternalInput").ap()
    sc = scratch_specs(cfg)
    for name, (shape, dt) in sc.items():
        kind = {"in": "ExternalInput", "out": "ExternalOutput"}.get(
            external_scratch.get(name), "Internal")
        aps[name] = nc.dram_tensor(name, shape, dt, kind=kind).ap()
    with nc.Block() as block:
        kb = KB(nc, block)
        st = emit_setup(kb, cfg, aps)
        for _e in ENGINES:
            kb.wait(_e, [("d_setup", kb.counts["d_setup"])])
        kb.cross = {}
        wconv_streams = ("d_wc0", "d_wc1", "d_wco0", "d_wco1", "e_wconv")
        final = []
        for fn, kw in phases:
            final = fn(kb, cfg, st, aps, **kw)
            kb.full_barrier(exclude=wconv_streams)
        for qname in list(kb.sems):
            if qname.startswith("d"):
                kb.wait("gpsimd", [(qname, kb.counts[qname])])
        for en in ("tensor", "vector", "scalar"):
            kb.wait("gpsimd", [kb.barrier(en)])
        kb.flush()
        st["es"].close()
    return nc



def ph_ingest(kb, cfg, st, aps, dst="x0fm"):
    return emit_tm2fm(kb, cfg, st, aps["L0_attn_hidden_states"], aps[dst], "_ing")


def ph_mod(kb, cfg, st, aps, layer):
    return emit_mod(kb, cfg, st, aps[f"L{layer}_c"][:, :], aps[f"L{layer}_wmod"],
                    aps[f"modT{layer}"], f"_m{layer}")


def ph_ln(kb, cfg, st, aps, x, h, layer, which):
    v_shift, v_scale = (0, 1) if which == "A" else (3, 4)
    sfx = f"_ln{layer}{which}"
    nc = kb.nc
    with nc.sbuf_tensor("lnmod" + sfx, [128, 6 * (cfg["D"] // 128) * cfg["BPC"]],
                        F32) as modT_sb:
        mt = kb.dma("sync", modT_sb[:], aps[f"modT{layer}"][:], stream="d_lnm")
        r = emit_ln(kb, cfg, st, aps[x], aps[h], modT_sb, mt, v_shift, v_scale, sfx)
    return r



def emit_wT(kb, cfg, st, aps, pairs):
    D = cfg["D"]
    nc = kb.nc
    with (
        nc.sbuf_tensor("wt_in", [128, D], BF16) as win,
        nc.sbuf_tensor("wt_out", [128, D], BF16) as wout,
        nc.psum_tensor("wt_ps", [128, 512], BF16) as ps0,
        nc.psum_tensor("wt_ps2", [128, 512], BF16) as ps1,
    ):
        ps = (ps0, ps1)
        prev_tts, prev_out = [], []
        for (src, dst) in pairs:
            for i in range(D // 128):
                ld = kb.dma("gpsimd", win[:], aps[src][i * 128:(i + 1) * 128, :],
                            deps=prev_tts, stream="d_wt")
                evs, tts = [], []
                for g in range(2):
                    gt = []
                    for c in range(4):
                        j = g * 4 + c
                        gt.append(kb.ins(
                            "tensor",
                            lambda e, j=j, g=g, c=c: e.transpose(
                                ps[g][:, c * 128:(c + 1) * 128],
                                win[:, j * 128:(j + 1) * 128], st["c_ident"][:]),
                            deps=[ld] + prev_out))
                    evs.append(kb.ins(
                        "vector",
                        lambda e, g=g: e.tensor_copy(wout[:, g * 512:(g + 1) * 512],
                                                     ps[g][:]),
                        deps=gt))
                    tts += gt
                outs = []
                for g in range(2):
                    dstap = aps[dst][g * 512:(g + 1) * 512, i * 128:(i + 1) * 128] \
                        .rearrange("(n p) o -> p n o", p=128)
                    outs.append(kb.dma("sync", dstap,
                                       sb3(wout[:, g * 512:(g + 1) * 512]),
                                       deps=[evs[g]], stream="d_wt_o"))
                prev_tts = tts
                prev_out = outs
    return prev_out


def ph_wT(kb, cfg, st, aps):
    return emit_wT(kb, cfg, st, aps,
                   [("L0_attn_Wq", "wqT"), ("L0_attn_Wk", "wkT"),
                    ("L0_attn_Wv", "wvT"), ("L0_attn_Wo", "woT")])



def ph_l0qkv(kb, cfg, st, aps):
    D, S, T, NH = cfg["D"], cfg["S"], cfg["T"], cfg["NH"]
    KC = D // 128
    SL = min(512, S)
    nc = kb.nc
    with (
        nc.sbuf_tensor("qk_wq", [128, KC * D], BF16) as wq,
        nc.sbuf_tensor("qk_wk", [128, KC * D], BF16) as wk,
        nc.sbuf_tensor("qk_wv", [128, KC * D], BF16) as wv,
        nc.sbuf_tensor("qk_h", [128, 2 * KC * SL], BF16) as h_d,
        nc.sbuf_tensor("qk_cos", [128, SL], F32) as cos_sb,
        nc.sbuf_tensor("qk_sin", [128, SL], F32) as sin_sb,
        nc.sbuf_tensor("qk_qb", [128, SL], BF16) as qbf,
        nc.sbuf_tensor("qk_t1", [128, SL], F32) as tr1,
        nc.sbuf_tensor("qk_ro", [128, SL], BF16) as ro_sb,
        nc.sbuf_tensor("qk_v", [128, NH * 65], BF16) as v_sb,
        nc.psum_tensor("qk_p0", [128, 512], F32) as p0,
        nc.psum_tensor("qk_p1", [128, 512], F32) as p1,
        nc.psum_tensor("qk_pv", [128, 512], F32) as pv,
    ):
        wls = []
        for name, tile in (("wqT", wq), ("wkT", wk), ("wvT", wv)):
            wls.append(kb.dma("gpsimd", sb3(tile[:], D),
                              aps[name].rearrange("(kc kp) o -> kp kc o", kp=128),
                              stream="d_qk_w"))
        wl = wls[-1]
        prev = {}
        for sl in range(T // SL):
            b = (sl * SL) // S
            p0s = sl * SL - b * S
            hsb = h_d[:, (sl % 2) * KC * SL:(sl % 2 + 1) * KC * SL]
            hl = []
            for c in range(KC):
                hl.append(kb.dma("sync", sb3(hsb[:, c * SL:(c + 1) * SL]),
                                 fm_ap(aps["h1"], c, sl * SL, SL),
                                 deps=prev.get(("h", sl % 2), []),
                                 stream="d_qk_h%d" % (sl % 2)))
            hl = [hl[-1]]
            cl = kb.dma("sync", cos_sb[:], aps["c_l0cos"][:, p0s:p0s + SL],
                        deps=prev.get("cos", []), stream="d_qk_c")
            sl_t = kb.dma("sync", sin_sb[:], aps["c_l0sin"][:, p0s:p0s + SL],
                          deps=prev.get("cos", []), stream="d_qk_s")
            ro_dmas = []
            for which, wt, dst in (("q", wq, "qro"), ("k", wk, "kro")):
                for och in range(KC):
                    mms = []
                    for k in range(KC):
                        mms.append(kb.ins("tensor", lambda e, k=k, och=och, wt=wt, hsb=hsb: e.matmul(
                            p0[:, 0:SL], wt[:, k * D + och * 128: k * D + (och + 1) * 128],
                            hsb[:, k * SL:(k + 1) * SL],
                            start=(k == 0), stop=(k == KC - 1)),
                            deps=[wl] + hl + prev.get("p0", [])))
                    ev = kb.ins("vector", lambda e: e.tensor_copy(qbf[:], p0[:, 0:SL]),
                                deps=mms[-1:] + prev.get("qbf", []))
                    pq = kb.ins("tensor", lambda e: e.matmul(
                        p1[:, 0:SL], st["c_p2"][:], qbf[:], start=True, stop=True),
                        deps=[ev] + prev.get("p1", []))
                    t1 = kb.ins("vector", lambda e: e.tensor_mul(tr1[:], qbf[:], cos_sb[:]),
                                deps=[ev, cl] + prev.get("tr1", []))
                    t2 = kb.ins("vector", lambda e: e.scalar_tensor_tensor(
                        ro_sb[:], p1[:, 0:SL], 1.0, sin_sb[:], ALU.mult, ALU.mult),
                        deps=[pq, sl_t] + prev.get("ro", []))
                    t3 = kb.ins("vector", lambda e: e.tensor_add(ro_sb[:], ro_sb[:], tr1[:]),
                                deps=[t1, t2])
                    dma = kb.dma("sync", aps[dst][och * 128:(och + 1) * 128,
                                                  sl * SL:sl * SL + SL],
                                 ro_sb[:], deps=[t3], stream="d_qk_ro")
                    ro_dmas.append(dma)
                    prev["p0"] = [ev]
                    prev["qbf"] = [t1, t2]
                    prev["p1"] = [t2]
                    prev["tr1"] = [t3]
                    prev["ro"] = [dma]
            vdmas = []
            for tt in range(SL // 128):
                vouts = []
                for oh in range(2):
                    mms = []
                    for k in range(KC):
                        mms.append(kb.ins("tensor", lambda e, k=k, tt=tt, oh=oh, hsb=hsb: e.matmul(
                            pv[:], hsb[:, k * SL + tt * 128: k * SL + (tt + 1) * 128],
                            wv[:, k * D + oh * 512: k * D + (oh + 1) * 512],
                            start=(k == 0), stop=(k == KC - 1)),
                            deps=[wl] + hl + prev.get("pv", [])))
                    dst = v_sb[:, oh * 8 * 65:(oh + 1) * 8 * 65].rearrange(
                        "p (h x) -> p h x", x=65)[:, :, 0:64]
                    ev = kb.ins("vector", lambda e, dst=dst: e.tensor_copy(dst, sb3(pv[:], 64)),
                                deps=mms[-1:] + prev.get("v", []))
                    vouts.append(ev)
                    prev["pv"] = [ev]
                ones = kb.ins("vector", lambda e: e.memset(
                    v_sb[:].rearrange("p (h x) -> p h x", x=65)[:, :, 64:65], 1.0),
                    deps=vouts + prev.get("v", []))
                vd = kb.dma("sync", aps["vtm"][sl * SL + tt * 128: sl * SL + (tt + 1) * 128, :],
                            v_sb[:], deps=[ones] + vouts, stream="d_qk_v")
                vdmas.append(vd)
                prev["v"] = [vd]
            prev[("h", sl % 2)] = mms + [ev]
            prev["cos"] = [t1, t2, dma]
    return ro_dmas + vdmas



def ph_l0attn(kb, cfg, st, aps):
    D, S, T, NH, DH = cfg["D"], cfg["S"], cfg["T"], cfg["NH"], cfg["DH"]
    BPC = cfg["BPC"]
    QB = min(512, S)
    NTC = S // 128
    KW = min(512, S)
    NKW = S // KW
    W2 = 2 * QB
    nsub = QB // 128
    nc = kb.nc
    with contextlib.ExitStack() as es:
        def sbuf(n, sh, dt):
            return es.enter_context(nc.sbuf_tensor(n, sh, dt))
        def psum(n, sh, dt=F32):
            return es.enter_context(nc.psum_tensor(n, sh, dt))
        kaug_d = sbuf("at_kaug", [128, 2 * S], BF16)
        vaug_d = sbuf("at_vaug", [128, 2 * NTC * 65], BF16)
        qaug_d = sbuf("at_qaug", [128, 4 * QB], BF16)
        pt_sb = sbuf("at_pt", [128, 2 * NTC * QB], BF16)
        mch_d = sbuf("at_mch", [128, 2 * 4 * NKW], F32)
        mall_d = sbuf("at_mall", [128, 2 * 4], F32)
        m4_d = sbuf("at_m4", [4, 2 * 128], F32)
        linv_d = sbuf("at_linv", [1, 2 * QB], F32)
        lrep = sbuf("at_lrep", [128, QB], F32)
        att_sb = sbuf("at_out", [128, QB], BF16)
        psA = psum("at_psA", [128, 2 * KW])
        psB = psum("at_psB", [128, 2 * W2])
        pAV0 = psum("at_pAV0", [128, 512])
        pAV1 = psum("at_pAV1", [128, 512])
        pAV = (pAV0, pAV1)
        prev = {}
        for b in range(BPC):
            for hp in range(NH // 2):
                kls, vls = [], []
                for hh in range(2):
                    h = 2 * hp + hh
                    kaug = kaug_d[:, hh * S:(hh + 1) * S]
                    kls.append(kb.dma(
                        "sync", kaug[0:64, :],
                        aps["kro"][h * 64:(h + 1) * 64, b * S:(b + 1) * S],
                        deps=prev.get("kaug", []), stream="d_at_k%d" % hh))
                    kls.append(kb.ins("vector", lambda e, kaug=kaug: e.memset(
                        kaug[64:65, :], 1.0), deps=prev.get("kaug", [])))
                    vls.append(kb.dma(
                        "gpsimd", sb3(vaug_d[:, hh * NTC * 65:(hh + 1) * NTC * 65], 65),
                        aps["vtm"][b * S:(b + 1) * S, h * 65:(h + 1) * 65]
                        .rearrange("(tc tp) x -> tp tc x", tp=128),
                        deps=prev.get("vaug", []), stream="d_at_v%d" % hh))
                qb_toks = []
                for qb in range(S // QB):
                    qaugs, qls, stats = {}, {}, {}
                    for hh in range(2):
                        h = 2 * hp + hh
                        qaug = qaug_d[:, (2 * (qb % 2) + hh) * QB:
                                      (2 * (qb % 2) + hh + 1) * QB]
                        qaugs[hh] = qaug
                        qls[hh] = kb.dma("sync", qaug[0:64, :],
                                         aps["qro"][h * 64:(h + 1) * 64,
                                                    b * S + qb * QB: b * S + (qb + 1) * QB],
                                         deps=prev.get(("qaug", hh, qb % 2), []),
                                         stream="d_at_q%d" % hh)
                    mx_all = {}
                    for hh in range(2):
                        kaug = kaug_d[:, hh * S:(hh + 1) * S]
                        qaug = qaugs[hh]
                        mch = mch_d[:, hh * 4 * NKW:(hh + 1) * 4 * NKW]
                        mall = mall_d[:, hh * 4:(hh + 1) * 4]
                        mx_toks = []
                        for sub in range(nsub):
                            rds = []
                            for tc2 in range(NKW):
                                slot = tc2 % 2
                                mmA = kb.ins("tensor",
                                    lambda e, sub=sub, tc2=tc2, slot=slot, qaug=qaug, kaug=kaug: e.matmul(
                                        psA[:, slot * KW:(slot + 1) * KW],
                                        qaug[0:64, sub * 128:(sub + 1) * 128],
                                        kaug[0:64, tc2 * KW:(tc2 + 1) * KW],
                                        start=True, stop=True),
                                    deps=[qls[hh]] + kls + prev.get(("psA", slot), []))
                                rd = kb.ins("vector",
                                    lambda e, sub=sub, tc2=tc2, slot=slot, mch=mch: e.reduce_max(
                                        mch[:, sub * NKW + tc2: sub * NKW + tc2 + 1],
                                        psA[:, slot * KW:(slot + 1) * KW], axis=AX.X),
                                    deps=[mmA] + prev.get(("mch", hh), []))
                                rds.append(rd)
                                prev[("psA", slot)] = [rd]
                            mx_toks.append(kb.ins("vector",
                                lambda e, sub=sub, mch=mch, mall=mall: e.reduce_max(
                                    mall[:, sub:sub + 1],
                                    mch[:, sub * NKW:(sub + 1) * NKW], axis=AX.X),
                                deps=rds + prev.get(("mall", hh), [])))
                        prev[("mch", hh)] = mx_toks
                        mx_all[hh] = mx_toks
                    for hh in range(2):
                        mall = mall_d[:, hh * 4:(hh + 1) * 4]
                        m4 = m4_d[:, hh * 128:(hh + 1) * 128]
                        qaug = qaugs[hh]
                        neg = kb.ins("scalar", lambda e, mall=mall: e.mul(
                            mall[:, 0:nsub], mall[:, 0:nsub], -1.0), deps=mx_all[hh])
                        ttm = kb.ins("tensor", lambda e, mall=mall: e.transpose(
                            psB[0:nsub, 0:128], mall[:, 0:nsub], st["c_identf"][:]),
                            deps=[neg] + prev.get("psB_t", []))
                        evm = kb.ins("vector", lambda e, m4=m4: e.tensor_copy(
                            m4[0:nsub, 0:128], psB[0:nsub, 0:128]),
                            deps=[ttm] + prev.get(("m4", hh), []))
                        sel_toks = []
                        for r in range(nsub):
                            sel_toks.append(kb.ins("tensor",
                                lambda e, r=r, m4=m4: e.matmul(
                                    psB[:, 512 + r * 128: 512 + (r + 1) * 128],
                                    st["c_sel4"][0:nsub, r * 128:(r + 1) * 128],
                                    m4[0:nsub, 0:128],
                                    start=True, stop=True),
                                deps=[evm] + prev.get("psB_s", [])))
                        wm = kb.ins("vector", lambda e, qaug=qaug: e.tensor_copy(
                            qaug[64:65, 0:QB], psB[0:1, 512:512 + QB]),
                            deps=sel_toks + [qls[hh]])
                        stats[hh] = wm
                        prev[("mall", hh)] = [ttm]
                        prev[("m4", hh)] = sel_toks
                        prev["psB_t"] = [evm]
                        prev["psB_s"] = [wm]
                    combine_deps = []
                    ex_last = {0: [], 1: []}
                    for hh in range(2):
                        kaug = kaug_d[:, hh * S:(hh + 1) * S]
                        vaug = vaug_d[:, hh * NTC * 65:(hh + 1) * NTC * 65]
                        qaug = qaugs[hh]
                        pt = pt_sb[:, hh * NTC * QB:(hh + 1) * NTC * QB]
                        ex_toks = []
                        av_toks = []
                        exp_prev = prev.get(("pt", hh), [])
                        npair = NTC // 2
                        for pair in range(npair):
                            rix = pair % 2
                            reg = psB[:, rix * W2:(rix + 1) * W2]
                            war = []
                            if pair >= 2:
                                war = [ex_toks[pair - 2]]
                            elif hh == 1:
                                war = ex_last[0][-2:]
                            else:
                                war = [stats[0], stats[1]]
                            mmBs = []
                            for j in range(2):
                                tc = 2 * pair + j
                                mmBs.append(kb.ins("tensor",
                                    lambda e, tc=tc, j=j, reg=reg, qaug=qaug, kaug=kaug: e.matmul(
                                        reg[:, j * QB:(j + 1) * QB],
                                        kaug[0:65, tc * 128:(tc + 1) * 128],
                                        qaug[0:65, 0:QB], start=True, stop=True),
                                    deps=war + [stats[hh]]))
                            ex = kb.ins("scalar", lambda e, pair=pair, reg=reg, pt=pt: e.activation(
                                pt[:, 2 * pair * QB: 2 * (pair + 1) * QB], reg[:],
                                AF.Exp, scale=8.0),
                                deps=mmBs + exp_prev)
                            ex_toks.append(ex)
                            for j in range(2):
                                tc = 2 * pair + j
                                av_toks.append(kb.ins("tensor",
                                    lambda e, tc=tc, hh=hh, vaug=vaug, pt=pt: e.matmul(
                                        pAV[hh][0:65, 0:QB], vaug[:, tc * 65:(tc + 1) * 65],
                                        pt[:, tc * QB:(tc + 1) * QB],
                                        start=(tc == 0), stop=(tc == NTC - 1)),
                                    deps=[ex, vls[hh]] + prev.get(("pAV", hh), [])))
                        ex_last[hh] = ex_toks
                        prev[("pt", hh)] = [av_toks[-1]]
                        prev[("qaug", hh, qb % 2)] = av_toks[-1:]
                        linv = linv_d[:, hh * QB:(hh + 1) * QB]
                        rcp = kb.ins("vector", lambda e, hh=hh, linv=linv: e.reciprocal(
                            linv[0:1, :], pAV[hh][64:65, 0:QB]),
                            deps=av_toks[-1:] + prev.get("linv", []))
                        combine_deps.append((rcp, av_toks[-1]))
                    prev["psB_t"] = ex_last[1][-2:] + ex_last[0][-1:]
                    prev["psB_s"] = prev["psB_t"]
                    l0 = kb.ins("tensor", lambda e: e.matmul(
                        psB[:, 0:QB], st["c_half_even"][:],
                        linv_d[0:1, 0:QB], start=True, stop=False),
                        deps=[combine_deps[0][0], combine_deps[1][0]] + prev["psB_t"])
                    l1 = kb.ins("tensor", lambda e: e.matmul(
                        psB[:, 0:QB], st["c_half_odd"][:],
                        linv_d[0:1, QB:2 * QB], start=False, stop=True),
                        deps=[])
                    evl = kb.ins("vector", lambda e: e.tensor_copy(lrep[:], psB[:, 0:QB]),
                                 deps=[l1] + prev.get("lrep", []))
                    c0 = kb.ins("vector", lambda e: e.tensor_mul(
                        att_sb[0:64, :], pAV0[0:64, 0:QB], lrep[0:64, :]),
                        deps=[evl, combine_deps[0][1]] + prev.get("att", []))
                    c1 = kb.ins("vector", lambda e: e.tensor_mul(
                        att_sb[64:128, :], pAV1[0:64, 0:QB], lrep[64:128, :]),
                        deps=[evl, combine_deps[1][1]] + prev.get("att", []))
                    od = kb.dma("sync", aps["attnfm"][hp * 128:(hp + 1) * 128,
                                                      b * S + qb * QB: b * S + (qb + 1) * QB],
                                att_sb[:], deps=[c0, c1], stream="d_at_o")
                    qb_toks.append(od)
                    prev["lrep"] = [c0, c1]
                    prev["att"] = [od]
                    prev["linv"] = [l0, l1]
                    prev[("pAV", 0)] = [c0]
                    prev[("pAV", 1)] = [c1]
                    prev["psB_t"] = [evl]
                    prev["psB_s"] = [evl]
                prev["kaug"] = qb_toks[-1:] + prev.get("psB_t", [])
                prev["vaug"] = qb_toks[-1:]
    return qb_toks


def emit_proj_epilogue(kb, cfg, st, aps, act_name, w_sb, wtok, x_in, x_out,
                       modT_sb, mtok, v_gate, sfx, bias_sb=None):
    D, S, T = cfg["D"], cfg["S"], cfg["T"]
    KC = D // 128
    SL = min(512, S)
    nc = kb.nc
    with (
        nc.sbuf_tensor("pe_a" + sfx, [128, 2 * KC * SL], BF16) as a_d,
        nc.sbuf_tensor("pe_x" + sfx, [128, KC * SL], F32) as x_sb,
        nc.sbuf_tensor("pe_o" + sfx, [128, KC * SL], F32) as o_sb,
        nc.psum_tensor("pe_p0" + sfx, [128, 512], F32) as p0,
        nc.psum_tensor("pe_p1" + sfx, [128, 512], F32) as p1,
    ):
        ps = (p0, p1)
        prev = {}
        for sl in range(T // SL):
            b = (sl * SL) // S
            asb = a_d[:, (sl % 2) * KC * SL:(sl % 2 + 1) * KC * SL]
            al, xl = [], []
            for c in range(KC):
                al.append(kb.dma("sync", asb[:, c * SL:(c + 1) * SL],
                                 aps[act_name][c * 128:(c + 1) * 128, sl * SL:sl * SL + SL],
                                 deps=prev.get(("a", sl % 2), []),
                                 stream="d_pe_a%d" % (sl % 2)))
                xl.append(kb.dma("gpsimd", sb3(x_sb[:, c * SL:(c + 1) * SL]),
                                 fm_ap(aps[x_in], c, sl * SL, SL),
                                 deps=prev.get("x", []), stream="d_pe_x"))
            al, xl = [al[-1]], [xl[-1]]
            eps = []
            for och in range(KC):
                pp = ps[och % 2]
                mms = []
                for k in range(KC):
                    mms.append(kb.ins("tensor", lambda e, k=k, och=och, pp=pp, asb=asb: e.matmul(
                        pp[:, 0:SL], w_sb[:, k * D + och * 128: k * D + (och + 1) * 128],
                        asb[:, k * SL:(k + 1) * SL],
                        start=(k == 0), stop=(k == KC - 1) and bias_sb is None),
                        deps=[wtok] + al + prev.get(("pp", och % 2), [])))
                if bias_sb is not None:
                    brow, orow, btok2, otok2 = bias_sb
                    mms.append(kb.ins("tensor", lambda e, och=och, pp=pp: e.matmul(
                        pp[:, 0:SL], brow[0:1, och * 128:(och + 1) * 128],
                        orow[0:1, 0:SL], start=False, stop=True),
                        deps=[btok2, otok2]))
                ep = kb.ins("vector", lambda e, och=och, b=b, pp=pp: e.scalar_tensor_tensor(
                    o_sb[:, och * SL:(och + 1) * SL], pp[:, 0:SL],
                    modT_sb[:, modcol(cfg, v_gate, och, b):modcol(cfg, v_gate, och, b) + 1],
                    x_sb[:, och * SL:(och + 1) * SL], ALU.mult, ALU.add),
                    deps=mms[-1:] + xl + [mtok] + prev.get("o", [])),
                eps.append(ep[0])
                prev[("pp", och % 2)] = [ep[0]]
            odmas = []
            for c in range(KC):
                odmas.append(kb.dma("sync", fm_ap(aps[x_out], c, sl * SL, SL),
                                    sb3(o_sb[:, c * SL:(c + 1) * SL]),
                                    deps=[eps[c]], stream="d_pe_o"))
            prev[("a", sl % 2)] = [mms[-1]]
            prev["x"] = eps
            prev["o"] = [odmas[-1]]
    return odmas


def ph_l0wo(kb, cfg, st, aps):
    D = cfg["D"]
    KC = D // 128
    nc = kb.nc
    with (
        nc.sbuf_tensor("wo_w", [128, KC * D], BF16) as wo,
        nc.sbuf_tensor("wo_modT", [128, 6 * KC * cfg["BPC"]], F32) as modT_sb,
    ):
        wtok = kb.dma("gpsimd", sb3(wo[:], D),
                      aps["woT"].rearrange("(kc kp) o -> kp kc o", kp=128),
                      stream="d_wo_w")
        mtok = kb.dma("sync", modT_sb[:], aps["modT0"][:], stream="d_wo_m")
        r = emit_proj_epilogue(kb, cfg, st, aps, "attnfm", wo, wtok,
                               "x0fm", "x1fm", modT_sb, mtok, 2, "_wo")
    return r



def ph_wconv(kb, cfg, st, aps):
    D, HF, E = cfg["D"], cfg["HF"], cfg["E"]
    nc = kb.nc
    wf = st["es"].enter_context(nc.sbuf_tensor("wc_f", [128, 2 * HF], F32))
    wt = st["es"].enter_context(nc.sbuf_tensor("wc_t", [128, 2 * HF], BF16))
    prev = {}
    toks = []

    def conv(src_ap, dst_ap, width, par):
        f = wf[:, par * HF: par * HF + width]
        t = wt[:, par * HF: par * HF + width]
        ld = kb.dma("scalar", f, src_ap, deps=prev.get(("c", par), []),
                    stream="d_wc%d" % par)
        cv = kb.ins("gpsimd", lambda e, f=f, t=t: e.tensor_copy(t, f),
                    deps=[ld] + prev.get(("s", par), []), sem_name="e_wconv")
        tok = kb.dma("scalar", dst_ap, t, deps=[cv], stream="d_wco%d" % par)
        prev[("c", par)] = [cv]
        prev[("s", par)] = [tok]
        return tok

    for e_i in range(E):
        for blk in range(D // 128):
            toks.append(conv(aps["L0_ffn_w1"][e_i, blk * 128:(blk + 1) * 128, :],
                             aps["w1b"][e_i, blk * 128:(blk + 1) * 128, :],
                             HF, blk % 2))
        for blk in range(HF // 128):
            toks.append(conv(aps["L0_ffn_w2"][e_i, blk * 128:(blk + 1) * 128, :],
                             aps["w2b"][e_i, blk * 128:(blk + 1) * 128, :],
                             D, blk % 2))
    kb.cross["wconv"] = toks[-2:] + [("e_wconv", kb.counts["e_wconv"])]
    return [toks[-1]]


def ph_route(kb, cfg, st, aps):
    D, S, T, E = cfg["D"], cfg["S"], cfg["T"], cfg["E"]
    KC = D // 128
    NT = T // 128
    CAP = cfg["CAP"]
    nc = kb.nc
    with contextlib.ExitStack() as es:
        def sbuf(n, sh, dt):
            return es.enter_context(nc.sbuf_tensor(n, sh, dt))
        def psum(n, sh, dt=F32):
            return es.enter_context(nc.psum_tensor(n, sh, dt))
        wg_sb = sbuf("rt_wg", [128, KC * E], BF16)
        h_sb = sbuf("rt_h", [128, KC * 128], BF16)
        lg_d = sbuf("rt_lg", [128, 2 * E], F32)
        mx = sbuf("rt_mx", [128, 4], F32)
        sc0 = sbuf("rt_sc0", [128, E], F32)
        sc1 = sbuf("rt_sc1", [128, E], F32)
        m_sb = sbuf("rt_m", [128, NT * 2 * E], F32)
        g_sb = sbuf("rt_g", [128, NT * 2], F32)
        keep_sb = sbuf("rt_keep", [128, 2 * E], F32)
        w_tile = sbuf("rt_w", [128, E], F32)
        cnt_sb = sbuf("rt_cnt", [1, 16], F32)
        cg_sb = sbuf("rt_cg", [cfg["NCORES"], 16], F32)
        go_sb = sbuf("rt_go", [1, 32], F32)
        carry_sb = sbuf("rt_carry", [1, 2 * E], F32)
        pos_sb = sbuf("rt_pos", [128, E], F32)
        w8_sb = sbuf("rt_w8", [8, T], F32)
        plg = psum("rt_plg", [128, 512])
        pc = psum("rt_pc", [1, 16])
        ppos = psum("rt_pp", [128, 512])
        pw = psum("rt_pw", [8, 128])
        wgl = kb.dma("gpsimd", sb3(wg_sb[:], E),
                     aps["L0_ffn_wg"].rearrange("(c p) e -> p c e", p=128),
                     stream="d_rt_w")
        cnt_toks = []
        prev_h, prev_lg, prev_lgread, prev_lg2 = [], [], [], []
        prev_mx, prev_sc0, prev_mdep, prev_g, prev_gw = [], [], [], [], []
        for t in range(NT):
            hl = []
            for c in range(KC):
                hl.append(kb.dma("sync", h_sb[:, c * 128:(c + 1) * 128],
                                 fm_ap(aps["h2"], c, t * 128, 128),
                                 deps=prev_h if t > 0 else [], stream="d_rt_h"))
            hl = [hl[-1]]
            lg = lg_d[:, 0:E]
            lg2 = lg_d[:, E:2 * E]
            mms = []
            for c in range(KC):
                mms.append(kb.ins("tensor", lambda e, c=c: e.matmul(
                    plg[:, 0:E], h_sb[:, c * 128:(c + 1) * 128],
                    wg_sb[:, c * E:(c + 1) * E],
                    start=(c == 0), stop=(c == KC - 1)),
                    deps=[wgl] + hl + prev_lg))
            e_lg = kb.ins("vector", lambda e: e.tensor_copy(lg, plg[:, 0:E]),
                          deps=mms[-1:] + prev_lgread)
            prev_lg = [e_lg]
            m0 = m_sb[:, t * 2 * E: t * 2 * E + E]
            m1 = m_sb[:, t * 2 * E + E: (t + 1) * 2 * E]
            t_mx1 = kb.ins("vector", lambda e: e.reduce_max(mx[:, 0:1], lg, axis=AX.X),
                           deps=[e_lg] + prev_mx)
            raw0 = kb.ins("vector", lambda e: e.tensor_single_scalar(
                sc0[:], lg, mx[:, 0:1], ALU.is_ge), deps=[t_mx1] + prev_sc0)
            x0 = _exclusive_first(kb, sc0, sc1, m0, prev_m=[raw0] + prev_mdep)
            t_lg2 = kb.ins("vector", lambda e, m0=m0: e.scalar_tensor_tensor(
                lg2, m0, -1e9, lg, ALU.mult, ALU.add), deps=[x0] + prev_lg2)
            t_mx2 = kb.ins("vector", lambda e: e.reduce_max(mx[:, 1:2], lg2, axis=AX.X),
                           deps=[t_lg2])
            raw1 = kb.ins("vector", lambda e: e.tensor_single_scalar(
                sc0[:], lg2, mx[:, 1:2], ALU.is_ge), deps=[t_mx2, x0])
            x1t = _exclusive_first(kb, sc0, sc1, m1, prev_m=[raw1, x0])
            t_d = kb.ins("vector", lambda e: e.tensor_sub(mx[:, 2:3], mx[:, 0:1], mx[:, 1:2]),
                         deps=[t_mx2, t_mx1] + prev_g)
            g0 = g_sb[:, t * 2: t * 2 + 1]
            g1 = g_sb[:, t * 2 + 1: t * 2 + 2]
            t_g0 = kb.ins("scalar", lambda e, g0=g0: e.activation(g0, mx[:, 2:3], AF.Sigmoid),
                          deps=[t_d] + prev_gw)
            t_g1 = kb.ins("scalar", lambda e, g1=g1: e.activation(g1, mx[:, 2:3], AF.Sigmoid,
                                                                  scale=-1.0),
                          deps=[t_d] + prev_gw)
            cnt_toks.append(kb.ins("tensor", lambda e, t=t, m0=m0: e.matmul(
                pc[:], st["c_ones_col"][:], m_sb[:, t * 2 * E:(t + 1) * 2 * E],
                start=(t == 0), stop=(t == NT - 1)),
                deps=[x0, x1t]))
            prev_h = mms + [e_lg]
            prev_lgread = [t_mx1, raw0, t_lg2]
            prev_lg2 = [t_mx2, raw1]
            prev_mx = [t_d, raw0, raw1]
            prev_sc0 = [x0, x1t]
            prev_mdep = [cnt_toks[-1]]
            prev_g = [t_g0, t_g1]
            prev_gw = [t_d]
        ec = kb.ins("vector", lambda e: e.tensor_copy(cnt_sb[:], pc[:]),
                    deps=cnt_toks[-1:])
        dcl = kb.dma("sync", aps["cntloc"][:], cnt_sb[:], deps=[ec], stream="d_rt_c")
        cc = kb.ins("gpsimd", lambda e: e.collective_compute(
            "AllGather", ALU.bypass, [list(range(cfg["NCORES"]))],
            [aps["cntloc"][:].opt()], [aps["cntg"][:].opt()]),
            deps=[dcl])
        gcl = kb.dma("sync", cg_sb[:], aps["cntg"][:], deps=[cc], stream="d_rt_c")
        mp = kb.ins("tensor", lambda e: e.matmul(
            pc[:], st["c_prevmask"][0:cfg["NCORES"], :], cg_sb[:],
            start=True, stop=True), deps=[gcl, ec])
        ep = kb.ins("vector", lambda e: e.tensor_copy(go_sb[0:1, 0:16], pc[:]),
                    deps=[mp])
        ma = kb.ins("tensor", lambda e: e.matmul(
            pc[:], st["c_allmask"][0:cfg["NCORES"], :], cg_sb[:],
            start=True, stop=True), deps=[ep])
        ea = kb.ins("vector", lambda e: e.tensor_copy(go_sb[0:1, 16:32], pc[:]),
                    deps=[ma])
        c0i = kb.ins("vector", lambda e: e.tensor_copy(carry_sb[0:1, 0:E],
                                                       go_sb[0:1, 0:E]), deps=[ep])
        c1i = kb.ins("vector", lambda e: e.tensor_add(
            carry_sb[0:1, E:2 * E], go_sb[0:1, 16:24], go_sb[0:1, 8:16]),
            deps=[ea])
        carry_toks = [c0i, c1i]
        g_done = kb.barrier("scalar")
        prev_pos, prev_keep, prev_w, prev_pw = [], [], [], []
        for t in range(NT):
            kts = []
            for p in range(2):
                m_p = m_sb[:, t * 2 * E + p * E: t * 2 * E + (p + 1) * E]
                car = carry_sb[0:1, p * E:(p + 1) * E]
                mmp = kb.ins("tensor", lambda e, m_p=m_p: e.matmul(
                    ppos[:, 0:E], st["c_triT"][:], m_p, start=True, stop=False),
                    deps=carry_toks + prev_pos)
                mmc = kb.ins("tensor", lambda e, car=car: e.matmul(
                    ppos[:, 0:E], st["c_ones_row"][0:1, :], car,
                    start=False, stop=True), deps=[])
                t_pe = kb.ins("vector", lambda e, m_p=m_p: e.tensor_sub(
                    pos_sb[:], ppos[:, 0:E], m_p), deps=[mmc] + prev_keep)
                t_k = kb.ins("vector", lambda e: e.tensor_single_scalar(
                    pos_sb[:], pos_sb[:], float(CAP), ALU.is_lt), deps=[t_pe])
                t_km = kb.ins("vector", lambda e, p=p, m_p=m_p: e.tensor_mul(
                    keep_sb[:, p * E:(p + 1) * E], pos_sb[:], m_p), deps=[t_k] + prev_w)
                kts.append(t_km)
                cm = kb.ins("tensor", lambda e, m_p=m_p: e.matmul(
                    pc[0:1, 0:E], st["c_ones_col"][:], m_p, start=True, stop=True),
                    deps=[ea if t == 0 else carry_toks[-1], mmc])
                cev = kb.ins("vector", lambda e, car=car: e.tensor_add(
                    car, car, pc[0:1, 0:E]), deps=[cm])
                carry_toks = [cev]
                prev_pos = [t_pe]
            g0 = g_sb[:, t * 2: t * 2 + 1]
            g1 = g_sb[:, t * 2 + 1: t * 2 + 2]
            t_w1 = kb.ins("vector", lambda e, g1=g1: e.tensor_scalar(
                w_tile[:], keep_sb[:, E:2 * E], g1, None, ALU.mult),
                deps=kts + prev_w + [g_done])
            t_w = kb.ins("vector", lambda e, g0=g0: e.scalar_tensor_tensor(
                w_tile[:], keep_sb[:, 0:E], g0, w_tile[:], ALU.mult, ALU.add),
                deps=[t_w1])
            prev_keep = [t_w1, t_w]
            prev_w = [t_w]
            tw = kb.ins("tensor", lambda e: e.transpose(
                pw[:], w_tile[:], st["c_identf"][:]), deps=[t_w] + prev_pw)
            ev8 = kb.ins("vector", lambda e, t=t: e.tensor_copy(
                w8_sb[0:8, t * 128:(t + 1) * 128], pw[0:8, :]), deps=[tw])
            prev_pw = [ev8]
            prev_w = [t_w, tw]
        wd = kb.dma("sync", aps["w8"][:], w8_sb[:], deps=[ev8], stream="d_rt_c")
    return [wd]


def _exclusive_first(kb, raw, scr, out_m, prev_m):
    E = 8
    t1a = kb.ins("vector", lambda e: e.tensor_copy(scr[:, 0:1], raw[:, 0:1]),
                 deps=prev_m)
    t1b = kb.ins("vector", lambda e: e.tensor_add(
        scr[:, 1:E], raw[:, 1:E], raw[:, 0:E - 1]), deps=prev_m)
    t2a = kb.ins("vector", lambda e: e.tensor_copy(out_m[:, 0:2], scr[:, 0:2]),
                 deps=[t1a, t1b] + prev_m)
    t2b = kb.ins("vector", lambda e: e.tensor_add(
        out_m[:, 2:E], scr[:, 2:E], scr[:, 0:E - 2]), deps=[t1a, t1b] + prev_m)
    t3a = kb.ins("vector", lambda e: e.tensor_copy(scr[:, 0:4], out_m[:, 0:4]),
                 deps=[t2a, t2b])
    t3b = kb.ins("vector", lambda e: e.tensor_add(
        scr[:, 4:E], out_m[:, 4:E], out_m[:, 0:E - 4]), deps=[t2a, t2b])
    teq = kb.ins("vector", lambda e: e.tensor_single_scalar(
        out_m[:, :], scr[:, 0:E], 1.0, ALU.is_equal), deps=[t3a, t3b])
    tq = kb.ins("vector", lambda e: e.tensor_mul(out_m[:, :], out_m[:, :], raw[:, :]),
                deps=[teq])
    return tq



def ph_moe(kb, cfg, st, aps):
    D, S, T, E, HF = cfg["D"], cfg["S"], cfg["T"], cfg["E"], cfg["HF"]
    KC = D // 128
    HC = HF // 128
    SL = min(512, S)
    nc = kb.nc
    with contextlib.ExitStack() as es:
        def sbuf(n, sh, dt):
            return es.enter_context(nc.sbuf_tensor(n, sh, dt))
        def psum(n, sh, dt=F32):
            return es.enter_context(nc.psum_tensor(n, sh, dt))
        h_sb = sbuf("mo_h", [128, KC * SL], BF16)
        x_sb = sbuf("mo_x", [128, KC * SL], F32)
        acc = sbuf("mo_acc", [128, KC * SL], F32)
        y1_sb = sbuf("mo_y1", [128, HC * SL], BF16)
        w1_sb = sbuf("mo_w1", [128, 2 * KC * 1024], BF16)
        w2_sb = sbuf("mo_w2", [128, 2 * 4 * D], BF16)
        wrep = sbuf("mo_wrep", [128, SL], F32)
        w8_sb = sbuf("mo_w8", [8, T], F32)
        b1_sb = sbuf("mo_b1", [128, HC], F32)
        b2_sb = sbuf("mo_b2", [128, KC], F32)
        b1r = sbuf("mo_b1r", [32, 128], F32)
        b2r = sbuf("mo_b2r", [8, 128], F32)
        o_sb = sbuf("mo_o", [128, SL], F32)
        modT_sb = sbuf("mo_modT", [128, 6 * KC * cfg["BPC"]], F32)
        psy = [psum("mo_p%d" % i, [128, 512]) for i in range(8)]
        w8l = kb.dma("sync", w8_sb[:], aps["w8"][:], stream="d_mo_s")
        mtok = kb.dma("sync", modT_sb[:], aps["modT0"][:], stream="d_mo_s2")
        prev = {}
        for sl in range(T // SL):
            b = (sl * SL) // S
            hdl, xdl = [], []
            for c in range(KC):
                hdl.append(kb.dma("sync", sb3(h_sb[:, c * SL:(c + 1) * SL]),
                                  fm_ap(aps["h2"], c, sl * SL, SL),
                                  deps=prev.get("h", []), stream="d_mo_h"))
                xdl.append(kb.dma("gpsimd", sb3(x_sb[:, c * SL:(c + 1) * SL]),
                                  fm_ap(aps["x1fm"], c, sl * SL, SL),
                                  deps=prev.get("x", []), stream="d_mo_x"))
            hdl, xdl = [hdl[-1]], [xdl[-1]]
            acc_tok = []
            for e_i in range(E):
                mw = kb.ins("tensor", lambda e, e_i=e_i, sl=sl: e.matmul(
                    psy[7][:, 0:SL], st["c_sel8"][0:8, e_i * 128:(e_i + 1) * 128],
                    w8_sb[0:8, sl * SL:(sl + 1) * SL], start=True, stop=True),
                    deps=[w8l] + prev.get("pw", []))
                ew = kb.ins("vector", lambda e: e.tensor_copy(wrep[:], psy[7][:, 0:SL]),
                            deps=[mw] + prev.get("wrep", []))
                prev["pw"] = [ew]
                bl1 = kb.dma("sync", b1r[0:HC, :],
                             aps["L0_ffn_b1"][e_i, :].rearrange("(a b) -> a b", b=128),
                             deps=prev.get("b1", []), stream="d_mo_b")
                tb1 = kb.ins("tensor", lambda e: e.transpose(
                    psy[6][:, 0:HC], b1r[0:HC, :], st["c_identf"][0:HC, 0:HC]),
                    deps=[bl1] + prev.get("pb", []))
                eb1 = kb.ins("vector", lambda e: e.tensor_copy(
                    b1_sb[:, 0:HC], psy[6][:, 0:HC]), deps=[tb1] + prev.get("b1r", []))
                bl2 = kb.dma("sync", b2r[0:KC, :],
                             aps["L0_ffn_b2"][e_i, :].rearrange("(a b) -> a b", b=128),
                             deps=prev.get("b2", []), stream="d_mo_b2")
                tb2 = kb.ins("tensor", lambda e: e.transpose(
                    psy[6][:, 0:KC], b2r[0:KC, :], st["c_identf"][0:KC, 0:KC]),
                    deps=[bl2, eb1])
                eb2 = kb.ins("vector", lambda e: e.tensor_copy(
                    b2_sb[:, 0:KC], psy[6][:, 0:KC]), deps=[tb2] + prev.get("b2r", []))
                prev["b1"] = [tb1]
                prev["b2"] = [tb2]
                prev["pb"] = [eb2]
                gelu_toks = []
                for hg in range(HF // 1024):
                    wg_idx = hg % 2
                    w1g = w1_sb[:, wg_idx * KC * 1024:(wg_idx + 1) * KC * 1024]
                    wls = [kb.dma(
                        "gpsimd", sb3(w1g, 1024),
                        aps["L0_ffn_w1"][e_i, :, hg * 1024:(hg + 1) * 1024]
                        .rearrange("(kc kp) h -> kp kc h", kp=128),
                        deps=prev.get(("w1", wg_idx), []),
                        stream="d_mo_w1%d" % wg_idx)]
                    for hcl in range(8):
                        hc = hg * 8 + hcl
                        pp = psy[hcl % 2]
                        mms = []
                        for k in range(KC):
                            mms.append(kb.ins("tensor",
                                lambda e, k=k, hcl=hcl, pp=pp, w1g=w1g: e.matmul(
                                    pp[:, 0:SL],
                                    w1g[:, k * 1024 + hcl * 128: k * 1024 + (hcl + 1) * 128],
                                    h_sb[:, k * SL:(k + 1) * SL],
                                    start=(k == 0), stop=(k == KC - 1)),
                                deps=wls + hdl + prev.get(("py", hcl % 2), [])))
                        g = kb.ins("scalar", lambda e, hc=hc, pp=pp: e.activation(
                            y1_sb[:, hc * SL:(hc + 1) * SL], pp[:, 0:SL], AF.Gelu,
                            bias=b1_sb[:, hc:hc + 1]),
                            deps=mms[-1:] + [eb1] + prev.get("y1", []))
                        gelu_toks.append(g)
                        prev[("py", hcl % 2)] = [g]
                    prev[("w1", wg_idx)] = mms
                mm2_toks = {dc: [] for dc in range(KC)}
                for hq in range(HC // 4):
                    par = hq % 2
                    w2big = w2_sb[:, par * 4 * D:(par + 1) * 4 * D]
                    w2l = kb.dma("gpsimd", sb3(w2big, D),
                                 aps["L0_ffn_w2"][e_i, hq * 512:(hq + 1) * 512, :]
                                 .rearrange("(c p) d -> p c d", p=128),
                                 deps=prev.get(("w2", par), []),
                                 stream="d_mo_w2%d" % par)
                    for hcl in range(4):
                        hc = hq * 4 + hcl
                        for dc in range(KC):
                            mm2_toks[dc].append(kb.ins("tensor",
                                lambda e, hc=hc, hcl=hcl, dc=dc, w2big=w2big: e.matmul(
                                    psy[dc][:, 0:SL],
                                    w2big[:, hcl * D + dc * 128: hcl * D + (dc + 1) * 128],
                                    y1_sb[:, hc * SL:(hc + 1) * SL],
                                    start=(hc == 0), stop=(hc == HC - 1)),
                                deps=[w2l, gelu_toks[hc]] +
                                     (prev.get(("py2", dc), []) + [eb2, ew, gelu_toks[-1]]
                                      if hc == 0 else [])))
                    prev[("w2", par)] = [mm2_toks[dc][-1] for dc in range(KC)]
                new_acc = []
                for dc in range(KC):
                    if e_i == 0:
                        a = kb.ins("vector", lambda e, dc=dc: e.scalar_tensor_tensor(
                            acc[:, dc * SL:(dc + 1) * SL], psy[dc][:, 0:SL],
                            b2_sb[:, dc:dc + 1], wrep[:], ALU.add, ALU.mult),
                            deps=mm2_toks[dc][-1:] + [eb2, ew] + prev.get("acc", []))
                    else:
                        a = kb.ins("vector", lambda e, dc=dc: e.scalar_tensor_tensor(
                            o_sb[:], psy[dc][:, 0:SL],
                            b2_sb[:, dc:dc + 1], wrep[:], ALU.add, ALU.mult),
                            deps=mm2_toks[dc][-1:] + [eb2, ew] + prev.get("o", []))
                        a = kb.ins("vector", lambda e, dc=dc: e.tensor_add(
                            acc[:, dc * SL:(dc + 1) * SL],
                            acc[:, dc * SL:(dc + 1) * SL], o_sb[:]),
                            deps=[a] + prev.get("acc", []))
                        prev["o"] = [a]
                    new_acc.append(a)
                    prev[("py2", dc)] = [a]
                prev["acc"] = new_acc
                prev["y1"] = mm2_toks[KC - 1]
                prev["wrep"] = new_acc
                prev["pw"] = new_acc
                prev[("py", 0)] = new_acc
                prev[("py", 1)] = new_acc
                prev["b1r"] = gelu_toks[-1:]
                prev["b2r"] = new_acc
            eps_toks = []
            for c in range(KC):
                ep = kb.ins("vector", lambda e, c=c, b=b: e.scalar_tensor_tensor(
                    x_sb[:, c * SL:(c + 1) * SL], acc[:, c * SL:(c + 1) * SL],
                    modT_sb[:, modcol(cfg, 5, c, b):modcol(cfg, 5, c, b) + 1],
                    x_sb[:, c * SL:(c + 1) * SL], ALU.mult, ALU.add),
                    deps=prev["acc"] + xdl + [mtok] + prev.get("eps", []))
                eps_toks.append(ep)
            odmas = []
            for c in range(KC):
                odmas.append(kb.dma("sync", fm_ap(aps["x2fm"], c, sl * SL, SL),
                                    sb3(x_sb[:, c * SL:(c + 1) * SL]),
                                    deps=[eps_toks[c]], stream="d_mo_o"))
            prev["h"] = [m for dc in range(KC) for m in mm2_toks[dc][-1:]] + gelu_toks[-1:]
            prev["x"] = [odmas[-1]]
            prev["eps"] = [odmas[-1]]
            prev["acc"] = eps_toks
    return odmas



def ph_l1proj(kb, cfg, st, aps):
    D, S, T, NH, DH, DL = cfg["D"], cfg["S"], cfg["T"], cfg["NH"], cfg["DH"], cfg["DLAT"]
    KC = D // 128
    SL = min(512, S)
    nc = kb.nc
    with contextlib.ExitStack() as es:
        def sbuf(n, sh, dt):
            return es.enter_context(nc.sbuf_tensor(n, sh, dt))
        def psum(n, sh, dt=F32):
            return es.enter_context(nc.psum_tensor(n, sh, dt))
        wq_sb = sbuf("lp_wq", [128, KC * D], BF16)
        wdn_sb = sbuf("lp_wdn", [128, KC * DL], BF16)
        wup_sb = sbuf("lp_wup", [128, 2 * 2 * D], BF16)
        bq_sb = sbuf("lp_bq", [1, D], BF16)
        bup_sb = sbuf("lp_bup", [1, 2 * D], BF16)
        bdn_r = sbuf("lp_bdnr", [2, 128], F32)
        bdn_sb = sbuf("lp_bdn", [128, 2], F32)
        ones_sl = sbuf("lp_ones", [1, SL], BF16)
        h_sb = sbuf("lp_h", [128, KC * SL], BF16)
        lat_sb = sbuf("lp_lat", [128, 2 * SL], BF16)
        q_sb = sbuf("lp_q", [128, D], F32)
        kv_sb = sbuf("lp_kv", [128, 2 * D], F32)
        ro_sb = sbuf("lp_ro", [128, D], BF16)
        kro_sb = sbuf("lp_kro", [128, D], BF16)
        v_sb = sbuf("lp_v", [128, NH * DH], BF16)
        cos_sb = sbuf("lp_cos", [128, 512], F32)
        sin_sb = sbuf("lp_sin", [128, 512], F32)
        t1_sb = sbuf("lp_t1", [128, 512], F32)
        t2_sb = sbuf("lp_t2", [128, 512], F32)
        pq = psum("lp_pq", [128, 512])
        pl = psum("lp_pl", [128, 512])
        pk = psum("lp_pk", [128, 512])
        wql = kb.dma("gpsimd", sb3(wq_sb[:], D),
                     aps["L1_attn_Wq"].rearrange("(kc kp) o -> kp kc o", kp=128),
                     stream="d_lp_w")
        wdl = kb.dma("gpsimd", sb3(wdn_sb[:], DL),
                     aps["L1_attn_Wdown"].rearrange("(kc kp) o -> kp kc o", kp=128),
                     stream="d_lp_w")
        wul = kb.dma("gpsimd", sb3(wup_sb[:], 2 * D),
                     aps["L1_attn_Wup"].rearrange("(kc kp) o -> kp kc o", kp=128),
                     stream="d_lp_w")
        bql = kb.dma("gpsimd", bq_sb[:], aps["L1_attn_bq"][:], stream="d_lp_w")
        bul = kb.dma("gpsimd", bup_sb[:], aps["L1_attn_bup"][:], stream="d_lp_w")
        bdl = kb.dma("sync", bdn_r[:], aps["L1_attn_bdown"][0, :]
                     .rearrange("(a b) -> a b", b=128), stream="d_lp_b")
        tbd = kb.ins("tensor", lambda e: e.transpose(
            pq[:, 0:2], bdn_r[:], st["c_identf"][0:2, 0:2]), deps=[bdl])
        ebd = kb.ins("vector", lambda e: e.tensor_copy(bdn_sb[:], pq[:, 0:2]),
                     deps=[tbd])
        onم = kb.ins("vector", lambda e: e.memset(ones_sl[:], 1.0))
        wtoks = [wql, wdl, wul, bql, bul, ebd, onم]
        prev = {}
        for sl in range(T // SL):
            b = (sl * SL) // S
            p0s = sl * SL - b * S
            hdl = []
            for c in range(KC):
                hdl.append(kb.dma("sync", sb3(h_sb[:, c * SL:(c + 1) * SL]),
                                  fm_ap(aps["h3"], c, sl * SL, SL),
                                  deps=prev.get("h", []), stream="d_lp_h"))
            hdl = [hdl[-1]]
            lats = []
            for lc in range(DL // 128):
                mms = []
                for k in range(KC):
                    mms.append(kb.ins("tensor", lambda e, k=k, lc=lc: e.matmul(
                        pl[:, 0:SL], wdn_sb[:, k * DL + lc * 128: k * DL + (lc + 1) * 128],
                        h_sb[:, k * SL:(k + 1) * SL],
                        start=(k == 0), stop=(k == KC - 1)),
                        deps=wtoks + hdl + prev.get("pl", [])))
                ev = kb.ins("vector", lambda e, lc=lc: e.tensor_scalar(
                    lat_sb[:, lc * SL:(lc + 1) * SL], pl[:, 0:SL],
                    bdn_sb[:, lc:lc + 1], None, ALU.add),
                    deps=mms[-1:] + prev.get("lat", []))
                lats.append(ev)
                prev["pl"] = [ev]
            for tt in range(SL // 128):
                cl = kb.dma("sync", cos_sb[:],
                            aps["c_l1cos"][p0s + tt * 128:p0s + (tt + 1) * 128, :],
                            deps=prev.get("cos", []), stream="d_lp_c")
                sl_l = kb.dma("sync", sin_sb[:],
                              aps["c_l1sin"][p0s + tt * 128:p0s + (tt + 1) * 128, :],
                              deps=prev.get("cos", []), stream="d_lp_s")
                qevs = []
                for oh in range(2):
                    mms = []
                    for k in range(KC):
                        mms.append(kb.ins("tensor", lambda e, k=k, oh=oh, tt=tt: e.matmul(
                            pq[:, 0:512],
                            h_sb[:, k * SL + tt * 128: k * SL + (tt + 1) * 128],
                            wq_sb[:, k * D + oh * 512:(k * D + (oh + 1) * 512)],
                            start=(k == 0), stop=False),
                            deps=wtoks + hdl + prev.get("pq", [])))
                    mb = kb.ins("tensor", lambda e, oh=oh: e.matmul(
                        pq[:, 0:512], ones_sl[0:1, 0:128],
                        bq_sb[0:1, oh * 512:(oh + 1) * 512],
                        start=False, stop=True), deps=[])
                    ev = kb.ins("vector", lambda e, oh=oh: e.tensor_copy(
                        q_sb[:, oh * 512:(oh + 1) * 512], pq[:, 0:512]),
                        deps=[mb] + prev.get("qsb", []))
                    qevs.append(ev)
                    prev["pq"] = [ev]
                kvevs = []
                for qh in range(4):
                    mms = []
                    for lc in range(DL // 128):
                        mms.append(kb.ins("tensor", lambda e, lc=lc, qh=qh, tt=tt: e.matmul(
                            pk[:, 0:512],
                            lat_sb[:, lc * SL + tt * 128: lc * SL + (tt + 1) * 128],
                            wup_sb[:, lc * 2 * D + qh * 512: lc * 2 * D + (qh + 1) * 512],
                            start=(lc == 0), stop=False),
                            deps=lats + prev.get("pk", [])))
                    mb = kb.ins("tensor", lambda e, qh=qh: e.matmul(
                        pk[:, 0:512], ones_sl[0:1, 0:128],
                        bup_sb[0:1, qh * 512:(qh + 1) * 512],
                        start=False, stop=True), deps=[])
                    ev = kb.ins("vector", lambda e, qh=qh: e.tensor_copy(
                        kv_sb[:, qh * 512:(qh + 1) * 512], pk[:, 0:512]),
                        deps=[mb] + prev.get("kvsb", []))
                    kvevs.append(ev)
                    prev["pk"] = [ev]
                ct = cos_sb[:]
                st_ = sin_sb[:]
                def q1v(z):
                    return z
                q_t = q_sb
                q1 = q_t[:].rearrange("p (h x) -> p h x", x=64)[:, :, 0:32]
                q2 = q_t[:].rearrange("p (h x) -> p h x", x=64)[:, :, 32:64]
                ro1 = ro_sb[:].rearrange("p (h x) -> p h x", x=64)[:, :, 0:32]
                ro2 = ro_sb[:].rearrange("p (h x) -> p h x", x=64)[:, :, 32:64]
                c3 = sb3(ct, 32)
                s3 = sb3(st_, 32)
                rq = _rope_l1(kb, q1, q2, ro1, ro2, c3, s3, t1_sb, t2_sb,
                              qevs + [cl, sl_l] + prev.get("ro", []), prev)
                kv3 = kv_sb[:].rearrange("p (h x) -> p h x", x=128)
                k1 = kv3[:, :, 0:32]
                k2 = kv3[:, :, 32:64]
                ko1 = kro_sb[:].rearrange("p (h x) -> p h x", x=64)[:, :, 0:32]
                ko2 = kro_sb[:].rearrange("p (h x) -> p h x", x=64)[:, :, 32:64]
                rk = _rope_l1(kb, k1, k2, ko1, ko2, c3, s3, t1_sb, t2_sb,
                              kvevs + [cl, sl_l] + prev.get("kro", []), prev)
                vv = kb.ins("vector", lambda e: e.tensor_copy(
                    sb3(v_sb[:], DH), kv3[:, :, 64:128]),
                    deps=kvevs + prev.get("vsb", []))
                d1 = kb.dma("sync", aps["l1q"][sl * SL + tt * 128: sl * SL + (tt + 1) * 128, :],
                            ro_sb[:], deps=[rq], stream="d_lp_oq")
                d2 = kb.dma("sync", aps["l1k"][sl * SL + tt * 128: sl * SL + (tt + 1) * 128, :],
                            kro_sb[:], deps=[rk], stream="d_lp_ok")
                d3 = kb.dma("sync", aps["l1v"][sl * SL + tt * 128: sl * SL + (tt + 1) * 128, :],
                            v_sb[:], deps=[vv], stream="d_lp_ov")
                prev["qsb"] = [rq]
                prev["kvsb"] = [rk, vv]
                prev["ro"] = [d1]
                prev["kro"] = [d2]
                prev["vsb"] = [d3]
            prev["h"] = mms + qevs
            prev["lat"] = mms
            prev["cos"] = [rq, rk]
    return [d1, d2, d3]


def _rope_l1(kb, x1, x2, o1, o2, c3, s3, t1_sb, t2_sb, deps0, prev):
    t3a = sb3(t1_sb[:], 32)
    t3b = sb3(t2_sb[:], 32)
    a1 = kb.ins("vector", lambda e: e.tensor_mul(t3a, x1, c3),
                deps=deps0 + prev.get("rt1", []))
    a2 = kb.ins("vector", lambda e: e.tensor_mul(t3b, x2, s3),
                deps=deps0 + prev.get("rt2", []))
    a3 = kb.ins("vector", lambda e: e.tensor_sub(o1, t3a, t3b), deps=[a1, a2])
    b1 = kb.ins("vector", lambda e: e.tensor_mul(t3a, x1, s3), deps=[a3])
    b2 = kb.ins("vector", lambda e: e.tensor_mul(t3b, x2, c3), deps=[a3])
    b3 = kb.ins("vector", lambda e: e.tensor_add(o2, t3a, t3b), deps=[b1, b2])
    prev["rt1"] = [b3]
    prev["rt2"] = [b3]
    return b3


def ph_l1attn(kb, cfg, st, aps):
    D, S, T, NH, DH = cfg["D"], cfg["S"], cfg["T"], cfg["NH"], cfg["DH"]
    HSPLIT = 8
    nc = kb.nc
    with contextlib.ExitStack() as es:
        def sbuf(n, sh, dt):
            return es.enter_context(nc.sbuf_tensor(n, sh, dt))
        def psum(n, sh, dt=F32):
            return es.enter_context(nc.psum_tensor(n, sh, dt))
        q_sb = sbuf("la_q", [128, D], BF16)
        k_sb = sbuf("la_k", [128, D], BF16)
        v_sb = sbuf("la_v", [128, D], BF16)
        prod = sbuf("la_prod", [128, 2 * 1024], F32)
        att = sbuf("la_att", [128, NH * NH], F32)
        mx = sbuf("la_mx", [128, 2 * 2], F32)
        li = sbuf("la_l", [128, 2], F32)
        ao = sbuf("la_ao", [128, D], F32)
        oo = sbuf("la_oo", [128, D], BF16)
        ps0 = psum("la_p0", [128, 512])
        ps1 = psum("la_p1", [128, 512])
        prev = {}
        for tile in range(T // 128):
            t0 = tile * 128
            ql = kb.dma("sync", q_sb[:], aps["l1q"][t0:t0 + 128, :],
                        deps=prev.get("q", []), stream="d_la_q")
            kl = kb.dma("sync", k_sb[:], aps["l1k"][t0:t0 + 128, :],
                        deps=prev.get("k", []), stream="d_la_k")
            vl = kb.dma("sync", v_sb[:], aps["l1v"][t0:t0 + 128, :],
                        deps=prev.get("v", []), stream="d_la_v")
            hts = []
            for h in range(NH):
                lane = 0 if h < HSPLIT else 1
                eng = "vector" if lane == 0 else "gpsimd"
                pr = prod[:, lane * 1024:(lane + 1) * 1024]
                mxl = mx[:, lane * 2:(lane + 1) * 2]
                lil = li[:, lane:lane + 1]
                q_bc = q_sb[:].rearrange("p (hh x) -> p hh x", x=DH)[
                    :, h:h + 1, :].broadcast_to([128, NH, DH])
                k_all = k_sb[:].rearrange("p (g x) -> p g x", x=DH)
                tm = kb.ins(eng, lambda e, q_bc=q_bc, k_all=k_all, pr=pr: e.tensor_mul(
                    sb3(pr, DH), q_bc, k_all),
                    deps=[ql, kl] + prev.get(("prod", lane), []))
                tr = kb.ins("vector", lambda e, h=h, pr=pr: e.reduce_sum(
                    att[:, h * NH:(h + 1) * NH], sb3(pr, DH), axis=AX.X),
                    deps=[tm] + prev.get(("att", lane), []))
                t_mx = kb.ins("vector", lambda e, h=h, mxl=mxl: e.reduce_max(
                    mxl[:, 0:1], att[:, h * NH:(h + 1) * NH], axis=AX.X),
                    deps=[tr] + prev.get(("mx", lane), []))
                t_ng = kb.ins("scalar", lambda e, mxl=mxl: e.mul(
                    mxl[:, 1:2], mxl[:, 0:1], -0.125),
                    deps=[t_mx] + prev.get(("mx2", lane), []))
                t_ex = kb.ins("scalar", lambda e, h=h, mxl=mxl, lil=lil: e.activation(
                    att[:, h * NH:(h + 1) * NH], att[:, h * NH:(h + 1) * NH],
                    AF.Exp, scale=0.125, bias=mxl[:, 1:2], accum_out=lil),
                    deps=[t_ng] + prev.get(("li", lane), []))
                t_rc = kb.ins("vector", lambda e, lil=lil: e.reciprocal(lil, lil),
                              deps=[t_ex])
                t_nm = kb.ins("vector", lambda e, h=h, lil=lil: e.tensor_scalar(
                    att[:, h * NH:(h + 1) * NH], att[:, h * NH:(h + 1) * NH],
                    lil, None, ALU.mult), deps=[t_rc])
                a_bc = att[:, h * NH:(h + 1) * NH].rearrange(
                    "p (o g) -> p o g", o=1).broadcast_to([128, DH, NH])
                v_dg = v_sb[:].rearrange("p (g x) -> p x g", x=DH)
                tm2 = kb.ins(eng, lambda e, a_bc=a_bc, v_dg=v_dg, pr=pr: e.tensor_mul(
                    pr.rearrange("p (x g) -> p x g", g=NH),
                    a_bc, v_dg), deps=[t_nm, vl, tm, tr] + prev.get(("prod2", lane), []))
                tr2 = kb.ins("vector", lambda e, h=h, pr=pr: e.reduce_sum(
                    ao[:, h * DH:(h + 1) * DH],
                    pr.rearrange("p (x g) -> p x g", g=NH), axis=AX.X),
                    deps=[tm2] + prev.get(("ao", lane), []))
                hts.append(tr2)
                prev[("prod", lane)] = [tm2, tr2]
                prev[("att", lane)] = [tm2]
                prev[("mx", lane)] = [t_ng]
                prev[("mx2", lane)] = [t_ex]
                prev[("li", lane)] = [t_nm]
                prev[("prod2", lane)] = [tr2]
            evs = []
            for g in range(2):
                tts = []
                for c in range(4):
                    ch = g * 4 + c
                    tts.append(kb.ins("tensor",
                        lambda e, ch=ch, g=g, c=c: e.transpose(
                            (ps0 if g == 0 else ps1)[:, c * 128:(c + 1) * 128],
                            ao[:, ch * 128:(ch + 1) * 128], st["c_identf"][:]),
                        deps=hts + prev.get("ps" + str(g), [])))
                ev = kb.ins("vector", lambda e, g=g: e.tensor_copy(
                    oo[:, g * 512:(g + 1) * 512], (ps0 if g == 0 else ps1)[:]),
                    deps=tts + prev.get("oo", []))
                evs.append(ev)
                prev["ps" + str(g)] = [ev]
                prev["ao"] = prev.get("ao", []) + tts
            od = kb.dma("sync", aps["l1at"][:, t0:t0 + 128].rearrange(
                "(c p) t -> p c t", p=128), sb3(oo[:]), deps=evs, stream="d_la_o")
            prev["q"] = hts[-1:] + [hts[HSPLIT - 1]]
            prev["k"] = prev["q"]
            prev["v"] = prev["q"]
            prev[("ao", 0)] = tts
            prev[("ao", 1)] = tts
            prev["oo"] = [od]
    return [od]


def ph_l1out(kb, cfg, st, aps):
    D = cfg["D"]
    KC = D // 128
    nc = kb.nc
    with contextlib.ExitStack() as es:
        def sbuf(n, sh, dt):
            return es.enter_context(nc.sbuf_tensor(n, sh, dt))
        wo_sb = sbuf("lo_w", [128, KC * D], BF16)
        bo_sb = sbuf("lo_b", [1, D], BF16)
        on_sb = sbuf("lo_on", [1, min(512, cfg["S"])], BF16)
        modT_sb = sbuf("lo_m", [128, 6 * KC * cfg["BPC"]], F32)
        wtok = kb.dma("gpsimd", sb3(wo_sb[:], D),
                      aps["L1_attn_Wout"].rearrange("(kc kp) o -> kp kc o", kp=128),
                      stream="d_lo_w")
        btok = kb.dma("gpsimd", bo_sb[:], aps["L1_attn_bout"][:], stream="d_lo_b")
        onm = kb.ins("vector", lambda e: e.memset(on_sb[:], 1.0))
        mtok = kb.dma("sync", modT_sb[:], aps["modT1"][:], stream="d_lo_m")
        r = emit_proj_epilogue(kb, cfg, st, aps, "l1at", wo_sb, wtok,
                               "x2fm", "x3fm", modT_sb, mtok, 2, "_lo",
                               bias_sb=(bo_sb, on_sb, btok, onm))
    return r



def ph_l1ffn(kb, cfg, st, aps):
    D, S, T = cfg["D"], cfg["S"], cfg["T"]
    KC = D // 128
    SL = min(512, S)
    nc = kb.nc
    with contextlib.ExitStack() as es:
        def sbuf(n, sh, dt):
            return es.enter_context(nc.sbuf_tensor(n, sh, dt))
        def psum(n, sh, dt=F32):
            return es.enter_context(nc.psum_tensor(n, sh, dt))
        kb_sb = sbuf("lf_k", [128, KC * D], BF16)
        k8_sb = sbuf("lf_k8", [128, KC * D], F8)
        h_sb = sbuf("lf_h", [128, KC * SL], BF16)
        h8_sb = sbuf("lf_h8", [128, KC * SL], F8)
        x_sb = sbuf("lf_x", [128, KC * SL], F32)
        y_sb = sbuf("lf_y", [128, SL], BF16)
        sc_sb = sbuf("lf_sc", [1, 8], F32)
        rep_sb = sbuf("lf_rep", [128, 3], F32)
        bia_r = sbuf("lf_br", [8, 128], BF16)
        bia_sb = sbuf("lf_b", [128, 8], F32)
        modT_sb = sbuf("lf_m", [128, 6 * KC * cfg["BPC"]], F32)
        pp0 = psum("lf_p0", [128, 512])
        pp1 = psum("lf_p1", [128, 512])
        pb8 = psum("lf_pb", [128, 8], BF16)
        mtok = kb.dma("sync", modT_sb[:], aps["modT1"][:], stream="d_lf_m")
        s1 = kb.dma("sync", sc_sb[0:1, 0:1], aps["L1_ffn_input_scale"][:],
                    stream="d_lf_s")
        s2 = kb.dma("sync", sc_sb[0:1, 1:2], aps["L1_ffn_kernel_scale"][:],
                    stream="d_lf_s2")
        t_a = kb.ins("vector", lambda e: e.tensor_add(
            sc_sb[0:1, 2:3], sc_sb[0:1, 0:1], sc_sb[0:1, 0:1]), deps=[s1])
        t_b = kb.ins("vector", lambda e: e.tensor_add(
            sc_sb[0:1, 3:4], sc_sb[0:1, 1:2], sc_sb[0:1, 1:2]), deps=[s2])
        t_ra = kb.ins("vector", lambda e: e.reciprocal(
            sc_sb[0:1, 5:6], sc_sb[0:1, 2:3]), deps=[t_a])
        t_rb = kb.ins("vector", lambda e: e.reciprocal(
            sc_sb[0:1, 6:7], sc_sb[0:1, 3:4]), deps=[t_b])
        t_c = kb.ins("vector", lambda e: e.tensor_mul(
            sc_sb[0:1, 4:5], sc_sb[0:1, 0:1], sc_sb[0:1, 1:2]), deps=[s1, s2])
        t_d = kb.ins("vector", lambda e: e.tensor_scalar(
            sc_sb[0:1, 7:8], sc_sb[0:1, 4:5], 4.0, None, ALU.mult), deps=[t_c])
        mrep = kb.ins("tensor", lambda e: e.matmul(
            pp0[:, 0:3], st["c_ones_row"][0:1, :], sc_sb[0:1, 5:8],
            start=True, stop=True), deps=[t_ra, t_rb, t_d])
        erep = kb.ins("vector", lambda e: e.tensor_copy(rep_sb[:], pp0[:, 0:3]),
                      deps=[mrep])
        kl = kb.dma("sync", sb3(kb_sb[:], D),
                    aps["L1_ffn_kernel"].rearrange("(kc kp) o -> kp kc o", kp=128),
                    stream="d_lf_k")
        kq = kb.ins("vector", lambda e: e.tensor_scalar(
            k8_sb[:], kb_sb[:], rep_sb[:, 1:2], None, ALU.mult),
            deps=[kl, erep])
        bl = kb.dma("sync", bia_r[:], aps["L1_ffn_bias"][0, :]
                    .rearrange("(a b) -> a b", b=128), stream="d_lf_b")
        tb = kb.ins("tensor", lambda e: e.transpose(
            pb8[:], bia_r[:], st["c_ident"][0:8, 0:8]), deps=[bl])
        eb = kb.ins("vector", lambda e: e.tensor_copy(bia_sb[:], pb8[:]),
                    deps=[tb])
        prev = {}
        for sl in range(T // SL):
            b = (sl * SL) // S
            hdl, xdl = [], []
            for c in range(KC):
                hdl.append(kb.dma("sync", sb3(h_sb[:, c * SL:(c + 1) * SL]),
                                  fm_ap(aps["h4"], c, sl * SL, SL),
                                  deps=prev.get("h", []), stream="d_lf_h"))
                xdl.append(kb.dma("gpsimd", sb3(x_sb[:, c * SL:(c + 1) * SL]),
                                  fm_ap(aps["x3fm"], c, sl * SL, SL),
                                  deps=prev.get("x", []), stream="d_lf_x"))
            hdl, xdl = [hdl[-1]], [xdl[-1]]
            hqs = []
            for c in range(KC):
                hqs.append(kb.ins("vector", lambda e, c=c: e.tensor_scalar(
                    h8_sb[:, c * SL:(c + 1) * SL], h_sb[:, c * SL:(c + 1) * SL],
                    rep_sb[:, 0:1], None, ALU.mult),
                    deps=hdl + [erep] + prev.get("h8", [])))
            outs = []
            for och in range(KC):
                pp = pp0 if och % 2 == 0 else pp1
                mms = []
                for k in range(KC):
                    mms.append(kb.ins("tensor", lambda e, k=k, och=och, pp=pp: e.matmul(
                        pp[:, 0:SL], k8_sb[:, k * D + och * 128: k * D + (och + 1) * 128],
                        h8_sb[:, k * SL:(k + 1) * SL],
                        start=(k == 0), stop=(k == KC - 1)),
                        deps=[kq, hqs[k]] + prev.get(("pp", och % 2), [])))
                ya = kb.ins("vector", lambda e, pp=pp: e.tensor_scalar(
                    y_sb[:], pp[:, 0:SL], rep_sb[:, 2:3], None, ALU.mult),
                    deps=mms[-1:] + [erep] + prev.get("y", []))
                yb = kb.ins("vector", lambda e, och=och: e.tensor_scalar(
                    y_sb[:], y_sb[:], bia_sb[:, och:och + 1], None, ALU.add),
                    deps=[ya, eb])
                ep = kb.ins("vector", lambda e, och=och, b=b: e.scalar_tensor_tensor(
                    x_sb[:, och * SL:(och + 1) * SL], y_sb[:],
                    modT_sb[:, modcol(cfg, 5, och, b):modcol(cfg, 5, och, b) + 1],
                    x_sb[:, och * SL:(och + 1) * SL], ALU.mult, ALU.add),
                    deps=[yb, mtok] + xdl)
                outs.append(ep)
                prev[("pp", och % 2)] = [ya]
                prev["y"] = [ep]
            odmas = []
            for c in range(KC):
                odmas.append(kb.dma("sync", fm_ap(aps["x4fm"], c, sl * SL, SL),
                                    sb3(x_sb[:, c * SL:(c + 1) * SL]),
                                    deps=[outs[c]], stream="d_lf_o"))
            prev["h"] = hqs
            prev["h8"] = mms
            prev["x"] = [odmas[-1]]
    return odmas


def ph_egress(kb, cfg, st, aps):
    return emit_tm2fm(kb, cfg, st, aps["x4fm"], aps["out"], "_eg", rev=True)



_LAUNCH_SPECS = [
    ([(ph_ingest, {}), (ph_wT, {}), (ph_mod, {"layer": 0}),
      (ph_ln, {"x": "x0fm", "h": "h1", "layer": 0, "which": "A"}),
      (ph_l0qkv, {})],
     [], ["x0fm", "modT0", "qro", "kro", "vtm", "woT"]),
    ([(ph_l0attn, {}), (ph_l0wo, {})],
     ["x0fm", "modT0", "qro", "kro", "vtm", "woT"], ["x1fm"]),
    ([(ph_ln, {"x": "x1fm", "h": "h2", "layer": 0, "which": "B"}),
      (ph_route, {}), (ph_moe, {})],
     ["x1fm", "modT0"], ["x2fm"]),
    ([(ph_mod, {"layer": 1}),
      (ph_ln, {"x": "x2fm", "h": "h3", "layer": 1, "which": "A"}),
      (ph_l1proj, {}), (ph_l1attn, {}), (ph_l1out, {})],
     ["x2fm"], ["x3fm", "modT1"]),
    ([(ph_ln, {"x": "x3fm", "h": "h4", "layer": 1, "which": "B"}),
      (ph_l1ffn, {}), (ph_egress, {})],
     ["x3fm", "modT1"], ["out"]),
]

_prog_cache = {}


def _get_prog(cfg_key, li, cfg):
    if (cfg_key, li) not in _prog_cache:
        phases, sc_in, sc_out = _LAUNCH_SPECS[li]
        ext = {n: "in" for n in sc_in}
        ext.update({n: "out" for n in sc_out})
        _prog_cache[(cfg_key, li)] = build_program(cfg, phases, ext)
    return _prog_cache[(cfg_key, li)]


def _shard_inputs(cfg, full):
    NC, BPC, D, S = cfg["NCORES"], cfg["BPC"], cfg["D"], cfg["S"]
    specs = input_specs_per_core(cfg)
    maps = []
    for c in range(NC):
        m = {}
        for name, (shape, dt) in specs.items():
            a = full[name]
            if name == "L0_attn_hidden_states":
                a = np.ascontiguousarray(a[c * BPC:(c + 1) * BPC]).reshape(BPC * S, D)
            elif name in ("L0_c", "L1_c"):
                a = np.ascontiguousarray(a[c * BPC:(c + 1) * BPC])
            else:
                a = np.asarray(a)
            a = a.reshape(shape)
            m[name] = a
        m.update(host_constants(cfg, c))
        maps.append(m)
    return maps


def kernel(**inputs):
    if FUSED:
        return kernel_fused(**inputs)
    return _kernel_phased(**inputs)


_ALL_PHASES = [
    (ph_ingest, {}), (ph_wT, {}), (ph_mod, {"layer": 0}),
    (ph_ln, {"x": "x0fm", "h": "h1", "layer": 0, "which": "A"}),
    (ph_l0qkv, {}),
    (ph_l0attn, {}), (ph_l0wo, {}),
    (ph_ln, {"x": "x1fm", "h": "h2", "layer": 0, "which": "B"}),
    (ph_route, {}), (ph_moe, {}),
    (ph_mod, {"layer": 1}),
    (ph_ln, {"x": "x2fm", "h": "h3", "layer": 1, "which": "A"}),
    (ph_l1proj, {}), (ph_l1attn, {}), (ph_l1out, {}),
    (ph_ln, {"x": "x3fm", "h": "h4", "layer": 1, "which": "B"}),
    (ph_l1ffn, {}), (ph_egress, {}),
]

FUSED = True


def kernel_fused(**inputs):
    cfg = make_cfg()
    NC = cfg["NCORES"]
    maps = _shard_inputs(cfg, inputs)
    if ("fused", 0) not in _prog_cache:
        _prog_cache[("fused", 0)] = build_program(cfg, _ALL_PHASES, {"out": "out"})
    nc = _prog_cache[("fused", 0)]
    res = run_bass_kernel_spmd(nc, maps, core_ids=list(range(NC)))
    out = np.concatenate(
        [np.asarray(res.results[c]["out"]).reshape(cfg["BPC"], cfg["S"], cfg["D"])
         for c in range(NC)], axis=0)
    return out.astype(np.float32)


def _kernel_phased(**inputs):
    cfg = make_cfg()
    NC = cfg["NCORES"]
    maps = _shard_inputs(cfg, inputs)
    carry = [dict() for _ in range(NC)]
    for li in range(len(_LAUNCH_SPECS)):
        phases, sc_in, sc_out = _LAUNCH_SPECS[li]
        nc = _get_prog("full", li, cfg)
        in_maps = []
        for c in range(NC):
            m = dict(maps[c])
            for n in sc_in:
                m[n] = carry[c][n]
            in_maps.append(m)
        res = run_bass_kernel_spmd(nc, in_maps, core_ids=list(range(NC)))
        for c in range(NC):
            for n in sc_out:
                carry[c][n] = np.asarray(res.results[c][n])
    out = np.concatenate([carry[c]["out"].reshape(cfg["BPC"], cfg["S"], cfg["D"])
                          for c in range(NC)], axis=0)
    return out.astype(np.float32)


# version 13
# speedup vs baseline: 1.0111x; 1.0111x over previous
import contextlib
import math
import numpy as np
import ml_dtypes
import concourse.bass as bass
import concourse.mybir as mybir
from concourse.bass_utils import run_bass_kernel_spmd

F32 = mybir.dt.float32
BF16 = mybir.dt.bfloat16
F8 = mybir.dt.float8e4
AF = mybir.ActivationFunctionType
ALU = mybir.AluOpType
AX = mybir.AxisListType
BF16NP = ml_dtypes.bfloat16


def make_cfg(S=2048, BPC=2, NCORES=8):
    c = {}
    c["D"] = 1024
    c["NH"] = 16
    c["DH"] = 64
    c["S"] = S
    c["BPC"] = BPC
    c["T"] = S * BPC
    c["NCORES"] = NCORES
    c["E"] = 8
    c["HF"] = 4096
    c["MOE_K"] = 2
    n_total = S * BPC * NCORES
    c["CAP"] = int(math.ceil(1.0 * c["MOE_K"] * n_total / c["E"]))
    c["DLAT"] = 256
    return c



def host_constants(cfg, core_id):
    D, S, NH, DH = cfg["D"], cfg["S"], cfg["NH"], cfg["DH"]
    out = {}
    out["c_ident"] = np.eye(128, dtype=BF16NP)
    out["c_identf"] = np.eye(128, dtype=np.float32)
    out["c_triT"] = np.triu(np.ones((128, 128), dtype=np.float32)).copy()
    out["c_ones_kxm"] = np.ones((128, 128), dtype=BF16NP)
    out["c_ones_col"] = np.ones((128, 1), dtype=np.float32)
    out["c_ones_row"] = np.ones((1, 128), dtype=np.float32)
    out["c_ones_row_bf"] = np.ones((1, 128), dtype=BF16NP)
    P = np.zeros((DH, DH), dtype=np.float32)
    for j in range(DH // 2):
        P[j, 2 * j + 1] = -1.0
    for i in range(DH // 2):
        P[DH // 2 + i, 2 * i] = 1.0
    P2 = np.zeros((128, 128), dtype=np.float32)
    P2[:DH, :DH] = P.T
    P2[DH:, DH:] = P.T
    out["c_p2"] = P2.astype(BF16NP)
    invf = (1.0 / 10000.0 ** (np.arange(0, DH, 2, dtype=np.float32) / DH)).astype(np.float32)
    t = np.arange(S, dtype=np.float32)
    freqs = np.einsum("i,j->ij", t, invf).astype(np.float32)
    emb = np.concatenate([freqs, freqs], axis=-1)
    cosT = np.cos(emb).T.astype(np.float32)
    sinT = np.sin(emb).T.astype(np.float32)
    out["c_l0cos"] = np.concatenate([cosT, cosT], axis=0).copy()
    out["c_l0sin"] = np.concatenate([sinT, sinT], axis=0).copy()
    fr1 = (t[:, None] * invf[None, :]).astype(np.float32)
    out["c_l1cos"] = np.tile(np.cos(fr1), (1, NH)).astype(np.float32).copy()
    out["c_l1sin"] = np.tile(np.sin(fr1), (1, NH)).astype(np.float32).copy()
    sel4 = np.zeros((4, 4 * 128), dtype=np.float32)
    for r in range(4):
        sel4[r, r * 128:(r + 1) * 128] = 1.0
    out["c_sel4"] = sel4
    sel8 = np.zeros((8, 8 * 128), dtype=np.float32)
    for r in range(8):
        sel8[r, r * 128:(r + 1) * 128] = 1.0
    out["c_sel8"] = sel8
    pm = np.zeros((cfg["NCORES"], 1), dtype=np.float32)
    pm[:core_id] = 1.0
    out["c_prevmask"] = pm
    out["c_allmask"] = np.ones((cfg["NCORES"], 1), dtype=np.float32)
    out["c_eps5"] = np.full((128, 1), 1e-5, dtype=np.float32)
    he = np.zeros((1, 128), dtype=np.float32); he[0, :64] = 1.0
    ho = np.zeros((1, 128), dtype=np.float32); ho[0, 64:] = 1.0
    out["c_half_even"] = he
    out["c_half_odd"] = ho
    return out


CONST_SPECS = {
    "c_ident": ([128, 128], BF16),
    "c_identf": ([128, 128], F32),
    "c_triT": ([128, 128], F32),
    "c_ones_kxm": ([128, 128], BF16),
    "c_ones_col": ([128, 1], F32),
    "c_ones_row": ([1, 128], F32),
    "c_ones_row_bf": ([1, 128], BF16),
    "c_p2": ([128, 128], BF16),
    "c_sel4": ([4, 4 * 128], F32),
    "c_sel8": ([8, 8 * 128], F32),
    "c_eps5": ([128, 1], F32),
    "c_prevmask": (None, F32),
    "c_allmask": (None, F32),
    "c_half_even": ([1, 128], F32),
    "c_half_odd": ([1, 128], F32),
}



ENGINES = ("sync", "tensor", "vector", "scalar", "gpsimd")


class KB:

    def __init__(self, nc, block):
        self.nc = nc
        self.block = block
        self.sems = {}
        self.counts = {}
        self.waited = {e: {} for e in ENGINES}
        self.q = {e: [] for e in ENGINES}
        for e in ("tensor", "vector", "scalar", "gpsimd"):
            self._mk_sem("e_" + e)

    def _mk_sem(self, name):
        self.sems[name] = self.nc.alloc_semaphore(name=name)
        self.counts[name] = 0

    def flush(self):
        for eng in ENGINES:
            ts = self.q[eng]
            if not ts:
                continue
            self.q[eng] = []

            def body(e, ts=ts):
                for t in ts:
                    t(e)
            getattr(self.block, eng)(body)

    def ins(self, eng, fn, deps=(), sem_name=None):
        sem_name = sem_name or ("e_" + eng)
        if sem_name not in self.sems:
            self._mk_sem(sem_name)
        sem = self.sems[sem_name]
        self.counts[sem_name] += 1
        cnt = self.counts[sem_name]
        deps = tuple(deps)

        def body(e):
            self._do_waits(e, eng, deps)
            fn(e).then_inc(sem, 1)
        self.q[eng].append(body)
        return (sem_name, cnt)

    def dma(self, queue, out, in_, deps=(), stream=None, **kw):
        name = stream or ("dma_" + queue)
        if name not in self.sems:
            self._mk_sem(name)
        sem = self.sems[name]
        self.counts[name] += 16
        cnt = self.counts[name]
        deps = tuple(deps)

        def body(e):
            self._do_waits(e, queue, deps)
            e.dma_start(out=out, in_=in_, **kw).then_inc(sem, 16)
        self.q[queue].append(body)
        return (name, cnt)

    def _do_waits(self, e, eng, deps):
        w = self.waited[eng]
        my = "e_" + eng
        agg = {}
        for sem_name, val in deps:
            if sem_name == my and self.counts[my] < val:
                raise RuntimeError("dep on future same-engine instruction")
            if val > agg.get(sem_name, 0):
                agg[sem_name] = val
        for sem_name, val in agg.items():
            if w.get(sem_name, 0) >= val:
                continue
            e.wait_ge(self.sems[sem_name], val)
            w[sem_name] = val

    def wait(self, eng, deps):
        deps = tuple(deps)

        def body(e):
            self._do_waits(e, eng, deps)
        self.q[eng].append(body)

    def barrier(self, eng):
        return ("e_" + eng, self.counts["e_" + eng])

    def full_barrier(self, exclude=()):
        toks = [(n, self.counts[n]) for n in self.sems if n not in exclude]
        for e in ENGINES:
            self.wait(e, toks)


def fm_ap(fm, c, t0, width):
    nt = width // 128
    return fm[t0 // 128: t0 // 128 + nt, c * 128:(c + 1) * 128, :].rearrange(
        "n p t -> p n t")


def sb3(sb_slice, tile=128):
    return sb_slice.rearrange("p (n t) -> p n t", t=tile)



def emit_tm2fm(kb, cfg, st, src_ap, dst_ap, sfx, rev=False):
    T, D = cfg["T"], cfg["D"]
    nc = kb.nc
    with (
        nc.sbuf_tensor("tf_in" + sfx, [128, D], F32) as tin_sb,
        nc.sbuf_tensor("tf_out" + sfx, [128, D], F32) as tout_sb,
        nc.psum_tensor("tf_ps0" + sfx, [128, 512], F32) as ps0,
        nc.psum_tensor("tf_ps1" + sfx, [128, 512], F32) as ps1,
    ):
        ps = (ps0, ps1)
        prev_tts, prev_out = [], []
        for tile in range(T // 128):
            if rev:
                src = src_ap[tile, :, :].rearrange("(c p) t -> p c t", p=128)
            else:
                src = src_ap[tile * 128:(tile + 1) * 128, :]
            tin = kb.dma("sync", sb3(tin_sb[:]) if rev else tin_sb[:], src,
                         deps=prev_tts, stream="d_tf_in")
            evs, tts = [], []
            for g in range(2):
                gtts = []
                for c in range(4):
                    ch = g * 4 + c
                    gtts.append(kb.ins(
                        "tensor",
                        lambda e, ch=ch, g=g, c=c: e.transpose(
                            ps[g][:, c * 128:(c + 1) * 128],
                            tin_sb[:, ch * 128:(ch + 1) * 128], st["c_identf"][:]),
                        deps=[tin] + prev_out))
                evs.append(kb.ins(
                    "vector",
                    lambda e, g=g: e.tensor_copy(tout_sb[:, g * 512:(g + 1) * 512], ps[g][:]),
                    deps=gtts))
                tts += gtts
            if rev:
                dst = dst_ap[tile * 128:(tile + 1) * 128, :]
            else:
                dst = dst_ap[tile, :, :].rearrange("(c p) t -> p c t", p=128)
            tok = kb.dma("sync", dst, tout_sb[:] if rev else sb3(tout_sb[:]),
                         deps=evs, stream="d_tf_out")
            prev_tts = tts
            prev_out = [tok]
    return [tok]



def emit_mod(kb, cfg, st, c_ap, wmod_ap, modT_ap, sfx):
    D, BPC = cfg["D"], cfg["BPC"]
    NV = 6
    nc = kb.nc
    KC = D // 128
    with (
        nc.sbuf_tensor("md_pad" + sfx, [128, D], F32) as pad,
        nc.sbuf_tensor("md_siluT" + sfx, [128, KC * BPC], F32) as siluT,
        nc.sbuf_tensor("md_w" + sfx, [128, 2 * 512], F32) as wmt_d,
        nc.sbuf_tensor("md_mod" + sfx, [128, NV * D], F32) as modp,
        nc.sbuf_tensor("md_t" + sfx, [128, D], F32) as tmp,
        nc.sbuf_tensor("md_t2" + sfx, [128, D], F32) as tmp2,
        nc.sbuf_tensor("md_modT" + sfx, [128, NV * KC * BPC], F32) as modT,
        nc.psum_tensor("md_ps" + sfx, [128, 512], F32) as ps,
        nc.psum_tensor("md_ps2" + sfx, [128, 512], F32) as ps2,
    ):
        z = kb.ins("vector", lambda e: e.memset(pad[:], 0.0))
        zm = kb.ins("vector", lambda e: e.memset(modp[:], 0.0))
        cl = kb.dma("sync", pad[0:BPC, :], c_ap, deps=[z], stream="d_mod")
        sl = kb.ins("scalar", lambda e: e.activation(pad[0:BPC, :], pad[0:BPC, :], AF.Silu),
                    deps=[cl])
        ev = None
        for k in range(KC):
            tt = kb.ins("tensor",
                        lambda e, k=k: e.transpose(ps[:, 0:128], pad[:, k * 128:(k + 1) * 128],
                                                   st["c_identf"][:]),
                        deps=[sl] + ([ev] if ev else []))
            ev = kb.ins("vector",
                        lambda e, k=k: e.tensor_copy(siluT[:, k * BPC:(k + 1) * BPC], ps[:, 0:BPC]),
                        deps=[tt])
        evm = None
        prev_mm = {}
        for j in range(NV * D // 512):
            toks = []
            for k in range(KC):
                w_t = wmt_d[:, (k % 2) * 512:(k % 2 + 1) * 512]
                wl = kb.dma("gpsimd", w_t,
                            wmod_ap[k * 128:(k + 1) * 128, j * 512:(j + 1) * 512],
                            deps=[prev_mm[k % 2]] if k % 2 in prev_mm else [],
                            stream="d_modw%d" % (k % 2))
                mm = kb.ins(
                    "tensor",
                    lambda e, k=k, w_t=w_t: e.matmul(
                        ps2[0:BPC, :], siluT[:, k * BPC:(k + 1) * BPC], w_t,
                        start=(k == 0), stop=(k == KC - 1)),
                    deps=[wl, ev] + ([evm] if evm else []))
                toks.append(mm)
                prev_mm[k % 2] = mm
            evm = kb.ins("vector",
                         lambda e, j=j: e.tensor_copy(modp[0:BPC, j * 512:(j + 1) * 512],
                                                      ps2[0:BPC, :]),
                         deps=[toks[-1], zm])
        post = [evm, zm]
        for v in (2, 5):
            sl_ = modp[0:BPC, v * D:(v + 1) * D]
            ta = kb.ins("scalar", lambda e, sl_=sl_: e.activation(
                tmp[0:BPC, 0:D], sl_, AF.Abs), deps=post)
            tc = kb.ins("vector", lambda e: e.tensor_scalar_max(
                tmp[0:BPC, 0:D], tmp[0:BPC, 0:D], 0.25), deps=[ta])
            tm = kb.ins("vector", lambda e, sl_=sl_: e.tensor_single_scalar(
                tmp2[0:BPC, 0:D], sl_, 0.0, ALU.is_ge), deps=post)
            ts = kb.ins("vector", lambda e: e.tensor_scalar(
                tmp2[0:BPC, 0:D], tmp2[0:BPC, 0:D], 2.0, -1.0, ALU.mult, ALU.add),
                deps=[tm])
            tg = kb.ins("vector", lambda e, sl_=sl_: e.tensor_mul(
                sl_, tmp[0:BPC, 0:D], tmp2[0:BPC, 0:D]), deps=[tc, ts])
            post = post + [tg]
        for v in (1, 4):
            sl_ = modp[0:BPC, v * D:(v + 1) * D]
            post.append(kb.ins("vector",
                               lambda e, sl_=sl_: e.tensor_scalar_add(sl_, sl_, 1.0),
                               deps=post))
        evq = None
        for q in range(NV * KC):
            tq = kb.ins("tensor",
                        lambda e, q=q: e.transpose(ps[:, 0:128], modp[:, q * 128:(q + 1) * 128],
                                                   st["c_identf"][:]),
                        deps=post + ([evq] if evq else []))
            evq = kb.ins("vector",
                         lambda e, q=q: e.tensor_copy(modT[:, q * BPC:(q + 1) * BPC], ps[:, 0:BPC]),
                         deps=[tq])
        tok = kb.dma("sync", modT_ap[:], modT[:], deps=[evq], stream="d_mod")
    return [tok]


def modcol(cfg, v, ch, b):
    return (v * (cfg["D"] // 128) + ch) * cfg["BPC"] + b



def emit_ln(kb, cfg, st, x_ap, h_ap, modT_sb, modT_tok, v_shift, v_scale, sfx,
            eps=1e-5):
    D, T, S, BPC = cfg["D"], cfg["T"], cfg["S"], cfg["BPC"]
    KC = D // 128
    nc = kb.nc
    SL = min(512, S)
    assert S % SL == 0
    with (
        nc.sbuf_tensor("ln_x" + sfx, [128, 2 * KC * SL], F32) as xs_d,
        nc.sbuf_tensor("ln_xb" + sfx, [128, KC * SL], BF16) as xb,
        nc.sbuf_tensor("ln_sq" + sfx, [128, KC * SL], BF16) as sq,
        nc.sbuf_tensor("ln_mean" + sfx, [128, SL], F32) as mean,
        nc.sbuf_tensor("ln_t" + sfx, [128, SL], F32) as tvar,
        nc.sbuf_tensor("ln_rstd" + sfx, [128, SL], F32) as rstd,
        nc.sbuf_tensor("ln_h" + sfx, [128, KC * SL], BF16) as hb,
        nc.psum_tensor("ln_pm" + sfx, [128, SL], F32) as pm,
        nc.psum_tensor("ln_pq" + sfx, [128, SL], F32) as pq,
    ):
        prev = {}
        for sl in range(T // SL):
            xs = xs_d[:, (sl % 2) * KC * SL:(sl % 2 + 1) * KC * SL]
            b = (sl * SL) // S
            loads = []
            for c in range(KC):
                loads.append((c, kb.dma(
                    "sync", sb3(xs[:, c * SL:(c + 1) * SL]), fm_ap(x_ap, c, sl * SL, SL),
                    deps=prev.get(("x", sl % 2), []), stream="d_ln_x")))
            last_ld = loads[-1][1]
            casts, sqs = [], []
            for c, ld in loads:
                casts.append(kb.ins("vector", lambda e, c=c, xs=xs: e.tensor_copy(
                    xb[:, c * SL:(c + 1) * SL], xs[:, c * SL:(c + 1) * SL]),
                    deps=[last_ld] + prev.get("xb", [])))
                sqs.append(kb.ins("scalar", lambda e, c=c, xs=xs: e.activation(
                    sq[:, c * SL:(c + 1) * SL], xs[:, c * SL:(c + 1) * SL], AF.Square),
                    deps=[last_ld] + prev.get("sq", [])))
            mms, mqs = [], []
            for c in range(KC):
                mms.append(kb.ins("tensor", lambda e, c=c: e.matmul(
                    pm[:], st["c_ones_kxm"][:], xb[:, c * SL:(c + 1) * SL],
                    start=(c == 0), stop=(c == KC - 1)),
                    deps=[casts[c]] + prev.get("pm", [])))
            for c in range(KC):
                mqs.append(kb.ins("tensor", lambda e, c=c: e.matmul(
                    pq[:], st["c_ones_kxm"][:], sq[:, c * SL:(c + 1) * SL],
                    start=(c == 0), stop=(c == KC - 1)),
                    deps=[sqs[c]] + prev.get("pq", [])))
            t_mean = kb.ins("scalar", lambda e: e.activation(
                mean[:], pm[:], AF.Copy, scale=1.0 / D),
                deps=[mms[-1]] + prev.get("mean", []))
            t_m2 = kb.ins("vector", lambda e: e.tensor_mul(tvar[:], mean[:], mean[:]),
                          deps=[t_mean] + prev.get("tvar", []))
            t_var = kb.ins("vector", lambda e: e.scalar_tensor_tensor(
                tvar[:], pq[:], 1.0 / D, tvar[:], ALU.mult, ALU.subtract),
                deps=[mqs[-1], t_m2])
            t_sd = kb.ins("scalar", lambda e: e.activation(
                tvar[:], tvar[:], AF.Sqrt, bias=st["c_eps5"][:]),
                deps=[t_var] + st["tok"])
            t_rs = kb.ins("vector", lambda e: e.reciprocal(rstd[:], tvar[:]),
                          deps=[t_sd] + prev.get("rstd", []))
            houts = []
            hdeps = prev.get("hdma", [])
            for c in range(KC):
                t1 = kb.ins("vector", lambda e, c=c, xs=xs: e.tensor_sub(
                    xs[:, c * SL:(c + 1) * SL],
                    xs[:, c * SL:(c + 1) * SL], mean[:]),
                    deps=[t_mean, mms[-1], mqs[-1]])
                t2 = kb.ins("vector", lambda e, c=c, xs=xs: e.tensor_mul(
                    xs[:, c * SL:(c + 1) * SL], xs[:, c * SL:(c + 1) * SL], rstd[:]),
                    deps=[t_rs, t1])
                t3 = kb.ins("vector", lambda e, c=c, b=b, xs=xs: e.tensor_scalar(
                    hb[:, c * SL:(c + 1) * SL], xs[:, c * SL:(c + 1) * SL],
                    modT_sb[:, modcol(cfg, v_scale, c, b):modcol(cfg, v_scale, c, b) + 1],
                    modT_sb[:, modcol(cfg, v_shift, c, b):modcol(cfg, v_shift, c, b) + 1],
                    ALU.mult, ALU.add),
                    deps=[t2, modT_tok] + hdeps)
                houts.append(t3)
            hdma = []
            for c in range(KC):
                hdma.append(kb.dma("sync", fm_ap(h_ap, c, sl * SL, SL),
                                   sb3(hb[:, c * SL:(c + 1) * SL]), deps=[houts[c]],
                                   stream="d_ln_h"))
            prev = {("x", sl % 2): houts,
                    "xb": mms, "sq": mqs,
                    "pm": [t_mean], "pq": [t_var],
                    "mean": houts, "tvar": [t_rs], "rstd": houts,
                    "hdma": hdma}
    return hdma



def scratch_specs(cfg):
    T, D, NH, S, BPC = cfg["T"], cfg["D"], cfg["NH"], cfg["S"], cfg["BPC"]
    return {
        "x0fm": ([T // 128, D, 128], F32),
        "x1fm": ([T // 128, D, 128], F32),
        "x2fm": ([T // 128, D, 128], F32),
        "x3fm": ([T // 128, D, 128], F32),
        "h1": ([T // 128, D, 128], BF16),
        "h2": ([T // 128, D, 128], BF16),
        "h3": ([T // 128, D, 128], BF16),
        "h4": ([T // 128, D, 128], BF16),
        "modT0": ([128, 6 * (D // 128) * BPC], F32),
        "modT1": ([128, 6 * (D // 128) * BPC], F32),
        "wqT": ([D, D], BF16),
        "wkT": ([D, D], BF16),
        "wvT": ([D, D], BF16),
        "woT": ([D, D], BF16),
        "qro": ([D, T], BF16),
        "kro": ([D, T], BF16),
        "vtm": ([T, NH * 65], BF16),
        "attnfm": ([D, T], BF16),
        "cntloc": ([1, 16], F32),
        "cntg": ([cfg["NCORES"], 16], F32),
        "w8": ([8, T], F32),
        "l1q": ([T, D], BF16),
        "l1k": ([T, D], BF16),
        "l1v": ([T, D], BF16),
        "l1at": ([D, T], BF16),
        "x4fm": ([T // 128, D, 128], F32),
        "w1b": ([8, D, 4096], BF16),
        "w2b": ([8, 4096, D], BF16),
        "out": ([T, D], F32),
    }


def input_specs_per_core(cfg):
    D, BPC, NH = cfg["D"], cfg["BPC"], cfg["NH"]
    return {
        "L0_attn_hidden_states": ([BPC * cfg["S"], D], F32),
        "L0_attn_Wq": ([D, D], F32), "L0_attn_Wk": ([D, D], F32),
        "L0_attn_Wv": ([D, D], F32), "L0_attn_Wo": ([D, D], F32),
        "L0_ffn_wg": ([D, 8], F32),
        "L0_ffn_w1": ([8, D, 4096], F32), "L0_ffn_b1": ([8, 4096], F32),
        "L0_ffn_w2": ([8, 4096, D], F32), "L0_ffn_b2": ([8, D], F32),
        "L0_c": ([BPC, D], F32), "L0_wmod": ([D, 6 * D], F32),
        "L1_attn_Wq": ([D, D], F32), "L1_attn_bq": ([1, D], F32),
        "L1_attn_Wdown": ([D, 256], F32), "L1_attn_bdown": ([1, 256], F32),
        "L1_attn_Wup": ([256, 2 * D], F32), "L1_attn_bup": ([1, 2 * D], F32),
        "L1_attn_Wout": ([D, D], F32), "L1_attn_bout": ([1, D], F32),
        "L1_ffn_kernel": ([D, D], BF16),
        "L1_ffn_input_scale": ([1, 1], F32), "L1_ffn_kernel_scale": ([1, 1], F32),
        "L1_ffn_bias": ([1, D], BF16),
        "L1_c": ([BPC, D], F32), "L1_wmod": ([D, 6 * D], F32),
    }


def emit_setup(kb, cfg, aps):
    nc = kb.nc
    st = {}
    toks = []
    es = contextlib.ExitStack()
    for name, (shape, dt) in CONST_SPECS.items():
        if shape is None:
            shape = [cfg["NCORES"], 1]
        t = es.enter_context(nc.sbuf_tensor("st_" + name, shape, dt))
        st[name] = t
        toks.append(kb.dma("sync", t[:], aps[name][:], stream="d_setup"))
    st["tok"] = toks
    st["es"] = es
    return st


def build_program(cfg, phases, external_scratch):
    nc = bass.Bass("TRN2", target_bir_lowering=False)
    aps = {}
    ins_sp = input_specs_per_core(cfg)
    for name, (shape, dt) in ins_sp.items():
        aps[name] = nc.dram_tensor(name, shape, dt, kind="ExternalInput").ap()
    for name, arr in host_constants(cfg, 0).items():
        dt = BF16 if arr.dtype == BF16NP else F32
        aps[name] = nc.dram_tensor(name, list(arr.shape), dt,
                                   kind="ExternalInput").ap()
    sc = scratch_specs(cfg)
    for name, (shape, dt) in sc.items():
        kind = {"in": "ExternalInput", "out": "ExternalOutput"}.get(
            external_scratch.get(name), "Internal")
        aps[name] = nc.dram_tensor(name, shape, dt, kind=kind).ap()
    with nc.Block() as block:
        kb = KB(nc, block)
        st = emit_setup(kb, cfg, aps)
        for _e in ENGINES:
            kb.wait(_e, [("d_setup", kb.counts["d_setup"])])
        kb.cross = {}
        wconv_streams = ("d_wc0", "d_wc1", "d_wco0", "d_wco1", "e_wconv")
        final = []
        for fn, kw in phases:
            final = fn(kb, cfg, st, aps, **kw)
            kb.full_barrier(exclude=wconv_streams)
        for qname in list(kb.sems):
            if qname.startswith("d"):
                kb.wait("gpsimd", [(qname, kb.counts[qname])])
        for en in ("tensor", "vector", "scalar"):
            kb.wait("gpsimd", [kb.barrier(en)])
        kb.flush()
        st["es"].close()
    return nc



def ph_ingest(kb, cfg, st, aps, dst="x0fm"):
    return emit_tm2fm(kb, cfg, st, aps["L0_attn_hidden_states"], aps[dst], "_ing")


def ph_mod(kb, cfg, st, aps, layer):
    return emit_mod(kb, cfg, st, aps[f"L{layer}_c"][:, :], aps[f"L{layer}_wmod"],
                    aps[f"modT{layer}"], f"_m{layer}")


def ph_ln(kb, cfg, st, aps, x, h, layer, which):
    v_shift, v_scale = (0, 1) if which == "A" else (3, 4)
    sfx = f"_ln{layer}{which}"
    nc = kb.nc
    with nc.sbuf_tensor("lnmod" + sfx, [128, 6 * (cfg["D"] // 128) * cfg["BPC"]],
                        F32) as modT_sb:
        mt = kb.dma("sync", modT_sb[:], aps[f"modT{layer}"][:], stream="d_lnm")
        r = emit_ln(kb, cfg, st, aps[x], aps[h], modT_sb, mt, v_shift, v_scale, sfx)
    return r



def emit_wT(kb, cfg, st, aps, pairs):
    D = cfg["D"]
    nc = kb.nc
    with (
        nc.sbuf_tensor("wt_in", [128, D], BF16) as win,
        nc.sbuf_tensor("wt_out", [128, D], BF16) as wout,
        nc.psum_tensor("wt_ps", [128, 512], BF16) as ps0,
        nc.psum_tensor("wt_ps2", [128, 512], BF16) as ps1,
    ):
        ps = (ps0, ps1)
        prev_tts, prev_out = [], []
        for (src, dst) in pairs:
            for i in range(D // 128):
                ld = kb.dma("gpsimd", win[:], aps[src][i * 128:(i + 1) * 128, :],
                            deps=prev_tts, stream="d_wt")
                evs, tts = [], []
                for g in range(2):
                    gt = []
                    for c in range(4):
                        j = g * 4 + c
                        gt.append(kb.ins(
                            "tensor",
                            lambda e, j=j, g=g, c=c: e.transpose(
                                ps[g][:, c * 128:(c + 1) * 128],
                                win[:, j * 128:(j + 1) * 128], st["c_ident"][:]),
                            deps=[ld] + prev_out))
                    evs.append(kb.ins(
                        "vector",
                        lambda e, g=g: e.tensor_copy(wout[:, g * 512:(g + 1) * 512],
                                                     ps[g][:]),
                        deps=gt))
                    tts += gt
                outs = []
                for g in range(2):
                    dstap = aps[dst][g * 512:(g + 1) * 512, i * 128:(i + 1) * 128] \
                        .rearrange("(n p) o -> p n o", p=128)
                    outs.append(kb.dma("sync", dstap,
                                       sb3(wout[:, g * 512:(g + 1) * 512]),
                                       deps=[evs[g]], stream="d_wt_o"))
                prev_tts = tts
                prev_out = outs
    return prev_out


def ph_wT(kb, cfg, st, aps):
    return emit_wT(kb, cfg, st, aps,
                   [("L0_attn_Wq", "wqT"), ("L0_attn_Wk", "wkT"),
                    ("L0_attn_Wv", "wvT"), ("L0_attn_Wo", "woT")])



def ph_l0qkv(kb, cfg, st, aps):
    D, S, T, NH = cfg["D"], cfg["S"], cfg["T"], cfg["NH"]
    KC = D // 128
    SL = min(512, S)
    nc = kb.nc
    with (
        nc.sbuf_tensor("qk_wq", [128, KC * D], BF16) as wq,
        nc.sbuf_tensor("qk_wk", [128, KC * D], BF16) as wk,
        nc.sbuf_tensor("qk_wv", [128, KC * D], BF16) as wv,
        nc.sbuf_tensor("qk_h", [128, 2 * KC * SL], BF16) as h_d,
        nc.sbuf_tensor("qk_cos", [128, SL], F32) as cos_sb,
        nc.sbuf_tensor("qk_sin", [128, SL], F32) as sin_sb,
        nc.sbuf_tensor("qk_qb", [128, SL], BF16) as qbf,
        nc.sbuf_tensor("qk_t1", [128, SL], F32) as tr1,
        nc.sbuf_tensor("qk_ro", [128, SL], BF16) as ro_sb,
        nc.sbuf_tensor("qk_v", [128, NH * 65], BF16) as v_sb,
        nc.psum_tensor("qk_p0", [128, 512], F32) as p0,
        nc.psum_tensor("qk_p1", [128, 512], F32) as p1,
        nc.psum_tensor("qk_pv", [128, 512], F32) as pv,
    ):
        wls = []
        for name, tile in (("wqT", wq), ("wkT", wk), ("wvT", wv)):
            wls.append(kb.dma("gpsimd", sb3(tile[:], D),
                              aps[name].rearrange("(kc kp) o -> kp kc o", kp=128),
                              stream="d_qk_w"))
        wl = wls[-1]
        prev = {}
        for sl in range(T // SL):
            b = (sl * SL) // S
            p0s = sl * SL - b * S
            hsb = h_d[:, (sl % 2) * KC * SL:(sl % 2 + 1) * KC * SL]
            hl = []
            for c in range(KC):
                hl.append(kb.dma("sync", sb3(hsb[:, c * SL:(c + 1) * SL]),
                                 fm_ap(aps["h1"], c, sl * SL, SL),
                                 deps=prev.get(("h", sl % 2), []),
                                 stream="d_qk_h%d" % (sl % 2)))
            hl = [hl[-1]]
            cl = kb.dma("sync", cos_sb[:], aps["c_l0cos"][:, p0s:p0s + SL],
                        deps=prev.get("cos", []), stream="d_qk_c")
            sl_t = kb.dma("sync", sin_sb[:], aps["c_l0sin"][:, p0s:p0s + SL],
                          deps=prev.get("cos", []), stream="d_qk_s")
            ro_dmas = []
            for which, wt, dst in (("q", wq, "qro"), ("k", wk, "kro")):
                for och in range(KC):
                    mms = []
                    for k in range(KC):
                        mms.append(kb.ins("tensor", lambda e, k=k, och=och, wt=wt, hsb=hsb: e.matmul(
                            p0[:, 0:SL], wt[:, k * D + och * 128: k * D + (och + 1) * 128],
                            hsb[:, k * SL:(k + 1) * SL],
                            start=(k == 0), stop=(k == KC - 1)),
                            deps=[wl] + hl + prev.get("p0", [])))
                    ev = kb.ins("vector", lambda e: e.tensor_copy(qbf[:], p0[:, 0:SL]),
                                deps=mms[-1:] + prev.get("qbf", []))
                    pq = kb.ins("tensor", lambda e: e.matmul(
                        p1[:, 0:SL], st["c_p2"][:], qbf[:], start=True, stop=True),
                        deps=[ev] + prev.get("p1", []))
                    t1 = kb.ins("vector", lambda e: e.tensor_mul(tr1[:], qbf[:], cos_sb[:]),
                                deps=[ev, cl] + prev.get("tr1", []))
                    t2 = kb.ins("vector", lambda e: e.scalar_tensor_tensor(
                        ro_sb[:], p1[:, 0:SL], 1.0, sin_sb[:], ALU.mult, ALU.mult),
                        deps=[pq, sl_t] + prev.get("ro", []))
                    t3 = kb.ins("vector", lambda e: e.tensor_add(ro_sb[:], ro_sb[:], tr1[:]),
                                deps=[t1, t2])
                    dma = kb.dma("sync", aps[dst][och * 128:(och + 1) * 128,
                                                  sl * SL:sl * SL + SL],
                                 ro_sb[:], deps=[t3], stream="d_qk_ro")
                    ro_dmas.append(dma)
                    prev["p0"] = [ev]
                    prev["qbf"] = [t1, t2]
                    prev["p1"] = [t2]
                    prev["tr1"] = [t3]
                    prev["ro"] = [dma]
            vdmas = []
            for tt in range(SL // 128):
                vouts = []
                for oh in range(2):
                    mms = []
                    for k in range(KC):
                        mms.append(kb.ins("tensor", lambda e, k=k, tt=tt, oh=oh, hsb=hsb: e.matmul(
                            pv[:], hsb[:, k * SL + tt * 128: k * SL + (tt + 1) * 128],
                            wv[:, k * D + oh * 512: k * D + (oh + 1) * 512],
                            start=(k == 0), stop=(k == KC - 1)),
                            deps=[wl] + hl + prev.get("pv", [])))
                    dst = v_sb[:, oh * 8 * 65:(oh + 1) * 8 * 65].rearrange(
                        "p (h x) -> p h x", x=65)[:, :, 0:64]
                    ev = kb.ins("vector", lambda e, dst=dst: e.tensor_copy(dst, sb3(pv[:], 64)),
                                deps=mms[-1:] + prev.get("v", []))
                    vouts.append(ev)
                    prev["pv"] = [ev]
                ones = kb.ins("vector", lambda e: e.memset(
                    v_sb[:].rearrange("p (h x) -> p h x", x=65)[:, :, 64:65], 1.0),
                    deps=vouts + prev.get("v", []))
                vd = kb.dma("sync", aps["vtm"][sl * SL + tt * 128: sl * SL + (tt + 1) * 128, :],
                            v_sb[:], deps=[ones] + vouts, stream="d_qk_v")
                vdmas.append(vd)
                prev["v"] = [vd]
            prev[("h", sl % 2)] = mms + [ev]
            prev["cos"] = [t1, t2, dma]
    return ro_dmas + vdmas



def ph_l0attn(kb, cfg, st, aps):
    D, S, T, NH, DH = cfg["D"], cfg["S"], cfg["T"], cfg["NH"], cfg["DH"]
    BPC = cfg["BPC"]
    QB = min(512, S)
    NTC = S // 128
    KW = min(512, S)
    NKW = S // KW
    W2 = 2 * QB
    nsub = QB // 128
    nc = kb.nc
    with contextlib.ExitStack() as es:
        def sbuf(n, sh, dt):
            return es.enter_context(nc.sbuf_tensor(n, sh, dt))
        def psum(n, sh, dt=F32):
            return es.enter_context(nc.psum_tensor(n, sh, dt))
        kaug_d = sbuf("at_kaug", [128, 2 * S], BF16)
        vaug_d = sbuf("at_vaug", [128, 2 * NTC * 65], BF16)
        qaug_d = sbuf("at_qaug", [128, 4 * QB], BF16)
        pt_sb = sbuf("at_pt", [128, 2 * NTC * QB], BF16)
        mch_d = sbuf("at_mch", [128, 2 * 4 * NKW], F32)
        mall_d = sbuf("at_mall", [128, 2 * 4], F32)
        m4_d = sbuf("at_m4", [4, 2 * 128], F32)
        linv_d = sbuf("at_linv", [1, 2 * QB], F32)
        lrep = sbuf("at_lrep", [128, QB], F32)
        att_sb = sbuf("at_out", [128, QB], BF16)
        psA = psum("at_psA", [128, 2 * KW])
        psB = psum("at_psB", [128, 2 * W2])
        pAV0 = psum("at_pAV0", [128, 512])
        pAV1 = psum("at_pAV1", [128, 512])
        pAV = (pAV0, pAV1)
        prev = {}
        for b in range(BPC):
            for hp in range(NH // 2):
                kls, vls = [], []
                for hh in range(2):
                    h = 2 * hp + hh
                    kaug = kaug_d[:, hh * S:(hh + 1) * S]
                    kls.append(kb.dma(
                        "sync", kaug[0:64, :],
                        aps["kro"][h * 64:(h + 1) * 64, b * S:(b + 1) * S],
                        deps=prev.get("kaug", []), stream="d_at_k%d" % hh))
                    kls.append(kb.ins("vector", lambda e, kaug=kaug: e.memset(
                        kaug[64:65, :], 1.0), deps=prev.get("kaug", [])))
                    vls.append(kb.dma(
                        "gpsimd", sb3(vaug_d[:, hh * NTC * 65:(hh + 1) * NTC * 65], 65),
                        aps["vtm"][b * S:(b + 1) * S, h * 65:(h + 1) * 65]
                        .rearrange("(tc tp) x -> tp tc x", tp=128),
                        deps=prev.get("vaug", []), stream="d_at_v%d" % hh))
                qb_toks = []
                for qb in range(S // QB):
                    qaugs, qls, stats = {}, {}, {}
                    for hh in range(2):
                        h = 2 * hp + hh
                        qaug = qaug_d[:, (2 * (qb % 2) + hh) * QB:
                                      (2 * (qb % 2) + hh + 1) * QB]
                        qaugs[hh] = qaug
                        qls[hh] = kb.dma("sync", qaug[0:64, :],
                                         aps["qro"][h * 64:(h + 1) * 64,
                                                    b * S + qb * QB: b * S + (qb + 1) * QB],
                                         deps=prev.get(("qaug", hh, qb % 2), []),
                                         stream="d_at_q%d" % hh)
                    mx_all = {}
                    for hh in range(2):
                        kaug = kaug_d[:, hh * S:(hh + 1) * S]
                        qaug = qaugs[hh]
                        mch = mch_d[:, hh * 4 * NKW:(hh + 1) * 4 * NKW]
                        mall = mall_d[:, hh * 4:(hh + 1) * 4]
                        mx_toks = []
                        for sub in range(nsub):
                            rds = []
                            for tc2 in range(NKW):
                                slot = tc2 % 2
                                mmA = kb.ins("tensor",
                                    lambda e, sub=sub, tc2=tc2, slot=slot, qaug=qaug, kaug=kaug: e.matmul(
                                        psA[:, slot * KW:(slot + 1) * KW],
                                        qaug[0:64, sub * 128:(sub + 1) * 128],
                                        kaug[0:64, tc2 * KW:(tc2 + 1) * KW],
                                        start=True, stop=True),
                                    deps=[qls[hh]] + kls + prev.get(("psA", slot), []))
                                rd = kb.ins("vector",
                                    lambda e, sub=sub, tc2=tc2, slot=slot, mch=mch: e.reduce_max(
                                        mch[:, sub * NKW + tc2: sub * NKW + tc2 + 1],
                                        psA[:, slot * KW:(slot + 1) * KW], axis=AX.X),
                                    deps=[mmA] + prev.get(("mch", hh), []))
                                rds.append(rd)
                                prev[("psA", slot)] = [rd]
                            mx_toks.append(kb.ins("vector",
                                lambda e, sub=sub, mch=mch, mall=mall: e.reduce_max(
                                    mall[:, sub:sub + 1],
                                    mch[:, sub * NKW:(sub + 1) * NKW], axis=AX.X),
                                deps=rds + prev.get(("mall", hh), [])))
                        prev[("mch", hh)] = mx_toks
                        mx_all[hh] = mx_toks
                    for hh in range(2):
                        mall = mall_d[:, hh * 4:(hh + 1) * 4]
                        m4 = m4_d[:, hh * 128:(hh + 1) * 128]
                        qaug = qaugs[hh]
                        neg = kb.ins("scalar", lambda e, mall=mall: e.mul(
                            mall[:, 0:nsub], mall[:, 0:nsub], -1.0), deps=mx_all[hh])
                        ttm = kb.ins("tensor", lambda e, mall=mall: e.transpose(
                            psB[0:nsub, 0:128], mall[:, 0:nsub], st["c_identf"][:]),
                            deps=[neg] + prev.get("psB_t", []))
                        evm = kb.ins("vector", lambda e, m4=m4: e.tensor_copy(
                            m4[0:nsub, 0:128], psB[0:nsub, 0:128]),
                            deps=[ttm] + prev.get(("m4", hh), []))
                        sel_toks = []
                        for r in range(nsub):
                            sel_toks.append(kb.ins("tensor",
                                lambda e, r=r, m4=m4: e.matmul(
                                    psB[:, 512 + r * 128: 512 + (r + 1) * 128],
                                    st["c_sel4"][0:nsub, r * 128:(r + 1) * 128],
                                    m4[0:nsub, 0:128],
                                    start=True, stop=True),
                                deps=[evm] + prev.get("psB_s", [])))
                        wm = kb.ins("vector", lambda e, qaug=qaug: e.tensor_copy(
                            qaug[64:65, 0:QB], psB[0:1, 512:512 + QB]),
                            deps=sel_toks + [qls[hh]])
                        stats[hh] = wm
                        prev[("mall", hh)] = [ttm]
                        prev[("m4", hh)] = sel_toks
                        prev["psB_t"] = [evm]
                        prev["psB_s"] = [wm]
                    combine_deps = []
                    ex_last = {0: [], 1: []}
                    for hh in range(2):
                        kaug = kaug_d[:, hh * S:(hh + 1) * S]
                        vaug = vaug_d[:, hh * NTC * 65:(hh + 1) * NTC * 65]
                        qaug = qaugs[hh]
                        pt = pt_sb[:, hh * NTC * QB:(hh + 1) * NTC * QB]
                        ex_toks = []
                        av_toks = []
                        exp_prev = prev.get(("pt", hh), [])
                        npair = NTC // 2
                        for pair in range(npair):
                            rix = pair % 2
                            reg = psB[:, rix * W2:(rix + 1) * W2]
                            war = []
                            if pair >= 2:
                                war = [ex_toks[pair - 2]]
                            elif hh == 1:
                                war = ex_last[0][-2:]
                            else:
                                war = [stats[0], stats[1]]
                            mmBs = []
                            for j in range(2):
                                tc = 2 * pair + j
                                mmBs.append(kb.ins("tensor",
                                    lambda e, tc=tc, j=j, reg=reg, qaug=qaug, kaug=kaug: e.matmul(
                                        reg[:, j * QB:(j + 1) * QB],
                                        kaug[0:65, tc * 128:(tc + 1) * 128],
                                        qaug[0:65, 0:QB], start=True, stop=True),
                                    deps=war + [stats[hh]]))
                            ex = kb.ins("scalar", lambda e, pair=pair, reg=reg, pt=pt: e.activation(
                                pt[:, 2 * pair * QB: 2 * (pair + 1) * QB], reg[:],
                                AF.Exp, scale=8.0),
                                deps=mmBs + exp_prev)
                            ex_toks.append(ex)
                            for j in range(2):
                                tc = 2 * pair + j
                                av_toks.append(kb.ins("tensor",
                                    lambda e, tc=tc, hh=hh, vaug=vaug, pt=pt: e.matmul(
                                        pAV[hh][0:65, 0:QB], vaug[:, tc * 65:(tc + 1) * 65],
                                        pt[:, tc * QB:(tc + 1) * QB],
                                        start=(tc == 0), stop=(tc == NTC - 1)),
                                    deps=[ex, vls[hh]] + prev.get(("pAV", hh), [])))
                        ex_last[hh] = ex_toks
                        prev[("pt", hh)] = [av_toks[-1]]
                        prev[("qaug", hh, qb % 2)] = av_toks[-1:]
                        linv = linv_d[:, hh * QB:(hh + 1) * QB]
                        rcp = kb.ins("vector", lambda e, hh=hh, linv=linv: e.reciprocal(
                            linv[0:1, :], pAV[hh][64:65, 0:QB]),
                            deps=av_toks[-1:] + prev.get("linv", []))
                        combine_deps.append((rcp, av_toks[-1]))
                    prev["psB_t"] = ex_last[1][-2:] + ex_last[0][-1:]
                    prev["psB_s"] = prev["psB_t"]
                    l0 = kb.ins("tensor", lambda e: e.matmul(
                        psB[:, 0:QB], st["c_half_even"][:],
                        linv_d[0:1, 0:QB], start=True, stop=False),
                        deps=[combine_deps[0][0], combine_deps[1][0]] + prev["psB_t"])
                    l1 = kb.ins("tensor", lambda e: e.matmul(
                        psB[:, 0:QB], st["c_half_odd"][:],
                        linv_d[0:1, QB:2 * QB], start=False, stop=True),
                        deps=[])
                    evl = kb.ins("vector", lambda e: e.tensor_copy(lrep[:], psB[:, 0:QB]),
                                 deps=[l1] + prev.get("lrep", []))
                    c0 = kb.ins("vector", lambda e: e.tensor_mul(
                        att_sb[0:64, :], pAV0[0:64, 0:QB], lrep[0:64, :]),
                        deps=[evl, combine_deps[0][1]] + prev.get("att", []))
                    c1 = kb.ins("vector", lambda e: e.tensor_mul(
                        att_sb[64:128, :], pAV1[0:64, 0:QB], lrep[64:128, :]),
                        deps=[evl, combine_deps[1][1]] + prev.get("att", []))
                    od = kb.dma("sync", aps["attnfm"][hp * 128:(hp + 1) * 128,
                                                      b * S + qb * QB: b * S + (qb + 1) * QB],
                                att_sb[:], deps=[c0, c1], stream="d_at_o")
                    qb_toks.append(od)
                    prev["lrep"] = [c0, c1]
                    prev["att"] = [od]
                    prev["linv"] = [l0, l1]
                    prev[("pAV", 0)] = [c0]
                    prev[("pAV", 1)] = [c1]
                    prev["psB_t"] = [evl]
                    prev["psB_s"] = [evl]
                prev["kaug"] = qb_toks[-1:] + prev.get("psB_t", [])
                prev["vaug"] = qb_toks[-1:]
    return qb_toks


def emit_proj_epilogue(kb, cfg, st, aps, act_name, w_sb, wtok, x_in, x_out,
                       modT_sb, mtok, v_gate, sfx, bias_sb=None):
    D, S, T = cfg["D"], cfg["S"], cfg["T"]
    KC = D // 128
    SL = min(512, S)
    nc = kb.nc
    with (
        nc.sbuf_tensor("pe_a" + sfx, [128, 2 * KC * SL], BF16) as a_d,
        nc.sbuf_tensor("pe_x" + sfx, [128, KC * SL], F32) as x_sb,
        nc.sbuf_tensor("pe_o" + sfx, [128, KC * SL], F32) as o_sb,
        nc.psum_tensor("pe_p0" + sfx, [128, 512], F32) as p0,
        nc.psum_tensor("pe_p1" + sfx, [128, 512], F32) as p1,
    ):
        ps = (p0, p1)
        prev = {}
        for sl in range(T // SL):
            b = (sl * SL) // S
            asb = a_d[:, (sl % 2) * KC * SL:(sl % 2 + 1) * KC * SL]
            al, xl = [], []
            for c in range(KC):
                al.append(kb.dma("sync", asb[:, c * SL:(c + 1) * SL],
                                 aps[act_name][c * 128:(c + 1) * 128, sl * SL:sl * SL + SL],
                                 deps=prev.get(("a", sl % 2), []),
                                 stream="d_pe_a%d" % (sl % 2)))
                xl.append(kb.dma("gpsimd", sb3(x_sb[:, c * SL:(c + 1) * SL]),
                                 fm_ap(aps[x_in], c, sl * SL, SL),
                                 deps=prev.get("x", []), stream="d_pe_x"))
            al, xl = [al[-1]], [xl[-1]]
            eps = []
            for och in range(KC):
                pp = ps[och % 2]
                mms = []
                for k in range(KC):
                    mms.append(kb.ins("tensor", lambda e, k=k, och=och, pp=pp, asb=asb: e.matmul(
                        pp[:, 0:SL], w_sb[:, k * D + och * 128: k * D + (och + 1) * 128],
                        asb[:, k * SL:(k + 1) * SL],
                        start=(k == 0), stop=(k == KC - 1) and bias_sb is None),
                        deps=[wtok] + al + prev.get(("pp", och % 2), [])))
                if bias_sb is not None:
                    brow, orow, btok2, otok2 = bias_sb
                    mms.append(kb.ins("tensor", lambda e, och=och, pp=pp: e.matmul(
                        pp[:, 0:SL], brow[0:1, och * 128:(och + 1) * 128],
                        orow[0:1, 0:SL], start=False, stop=True),
                        deps=[btok2, otok2]))
                ep = kb.ins("vector", lambda e, och=och, b=b, pp=pp: e.scalar_tensor_tensor(
                    o_sb[:, och * SL:(och + 1) * SL], pp[:, 0:SL],
                    modT_sb[:, modcol(cfg, v_gate, och, b):modcol(cfg, v_gate, och, b) + 1],
                    x_sb[:, och * SL:(och + 1) * SL], ALU.mult, ALU.add),
                    deps=mms[-1:] + xl + [mtok] + prev.get("o", [])),
                eps.append(ep[0])
                prev[("pp", och % 2)] = [ep[0]]
            odmas = []
            for c in range(KC):
                odmas.append(kb.dma("sync", fm_ap(aps[x_out], c, sl * SL, SL),
                                    sb3(o_sb[:, c * SL:(c + 1) * SL]),
                                    deps=[eps[c]], stream="d_pe_o"))
            prev[("a", sl % 2)] = [mms[-1]]
            prev["x"] = eps
            prev["o"] = [odmas[-1]]
    return odmas


def ph_l0wo(kb, cfg, st, aps):
    D = cfg["D"]
    KC = D // 128
    nc = kb.nc
    with (
        nc.sbuf_tensor("wo_w", [128, KC * D], BF16) as wo,
        nc.sbuf_tensor("wo_modT", [128, 6 * KC * cfg["BPC"]], F32) as modT_sb,
    ):
        wtok = kb.dma("gpsimd", sb3(wo[:], D),
                      aps["woT"].rearrange("(kc kp) o -> kp kc o", kp=128),
                      stream="d_wo_w")
        mtok = kb.dma("sync", modT_sb[:], aps["modT0"][:], stream="d_wo_m")
        r = emit_proj_epilogue(kb, cfg, st, aps, "attnfm", wo, wtok,
                               "x0fm", "x1fm", modT_sb, mtok, 2, "_wo")
    return r



def ph_wconv(kb, cfg, st, aps):
    D, HF, E = cfg["D"], cfg["HF"], cfg["E"]
    nc = kb.nc
    wf = st["es"].enter_context(nc.sbuf_tensor("wc_f", [128, 2 * HF], F32))
    wt = st["es"].enter_context(nc.sbuf_tensor("wc_t", [128, 2 * HF], BF16))
    prev = {}
    toks = []

    def conv(src_ap, dst_ap, width, par):
        f = wf[:, par * HF: par * HF + width]
        t = wt[:, par * HF: par * HF + width]
        ld = kb.dma("scalar", f, src_ap, deps=prev.get(("c", par), []),
                    stream="d_wc%d" % par)
        cv = kb.ins("gpsimd", lambda e, f=f, t=t: e.tensor_copy(t, f),
                    deps=[ld] + prev.get(("s", par), []), sem_name="e_wconv")
        tok = kb.dma("scalar", dst_ap, t, deps=[cv], stream="d_wco%d" % par)
        prev[("c", par)] = [cv]
        prev[("s", par)] = [tok]
        return tok

    for e_i in range(E):
        for blk in range(D // 128):
            toks.append(conv(aps["L0_ffn_w1"][e_i, blk * 128:(blk + 1) * 128, :],
                             aps["w1b"][e_i, blk * 128:(blk + 1) * 128, :],
                             HF, blk % 2))
        for blk in range(HF // 128):
            toks.append(conv(aps["L0_ffn_w2"][e_i, blk * 128:(blk + 1) * 128, :],
                             aps["w2b"][e_i, blk * 128:(blk + 1) * 128, :],
                             D, blk % 2))
    kb.cross["wconv"] = toks[-2:] + [("e_wconv", kb.counts["e_wconv"])]
    return [toks[-1]]


def ph_route(kb, cfg, st, aps):
    D, S, T, E = cfg["D"], cfg["S"], cfg["T"], cfg["E"]
    KC = D // 128
    NT = T // 128
    CAP = cfg["CAP"]
    nc = kb.nc
    with contextlib.ExitStack() as es:
        def sbuf(n, sh, dt):
            return es.enter_context(nc.sbuf_tensor(n, sh, dt))
        def psum(n, sh, dt=F32):
            return es.enter_context(nc.psum_tensor(n, sh, dt))
        wg_sb = sbuf("rt_wg", [128, KC * E], BF16)
        h_sb = sbuf("rt_h", [128, KC * 128], BF16)
        lg_d = sbuf("rt_lg", [128, 2 * E], F32)
        mx = sbuf("rt_mx", [128, 4], F32)
        sc0 = sbuf("rt_sc0", [128, E], F32)
        sc1 = sbuf("rt_sc1", [128, E], F32)
        m_sb = sbuf("rt_m", [128, NT * 2 * E], F32)
        g_sb = sbuf("rt_g", [128, NT * 2], F32)
        keep_sb = sbuf("rt_keep", [128, 2 * E], F32)
        w_tile = sbuf("rt_w", [128, E], F32)
        cnt_sb = sbuf("rt_cnt", [1, 16], F32)
        cg_sb = sbuf("rt_cg", [cfg["NCORES"], 16], F32)
        go_sb = sbuf("rt_go", [1, 32], F32)
        carry_sb = sbuf("rt_carry", [1, 2 * E], F32)
        pos_sb = sbuf("rt_pos", [128, E], F32)
        w8_sb = sbuf("rt_w8", [8, T], F32)
        plg = psum("rt_plg", [128, 512])
        pc = psum("rt_pc", [1, 16])
        ppos = psum("rt_pp", [128, 512])
        pw = psum("rt_pw", [8, 128])
        wgl = kb.dma("gpsimd", sb3(wg_sb[:], E),
                     aps["L0_ffn_wg"].rearrange("(c p) e -> p c e", p=128),
                     stream="d_rt_w")
        cnt_toks = []
        prev_h, prev_lg, prev_lgread, prev_lg2 = [], [], [], []
        prev_mx, prev_sc0, prev_mdep, prev_g, prev_gw = [], [], [], [], []
        for t in range(NT):
            hl = []
            for c in range(KC):
                hl.append(kb.dma("sync", h_sb[:, c * 128:(c + 1) * 128],
                                 fm_ap(aps["h2"], c, t * 128, 128),
                                 deps=prev_h if t > 0 else [], stream="d_rt_h"))
            hl = [hl[-1]]
            lg = lg_d[:, 0:E]
            lg2 = lg_d[:, E:2 * E]
            mms = []
            for c in range(KC):
                mms.append(kb.ins("tensor", lambda e, c=c: e.matmul(
                    plg[:, 0:E], h_sb[:, c * 128:(c + 1) * 128],
                    wg_sb[:, c * E:(c + 1) * E],
                    start=(c == 0), stop=(c == KC - 1)),
                    deps=[wgl] + hl + prev_lg))
            e_lg = kb.ins("vector", lambda e: e.tensor_copy(lg, plg[:, 0:E]),
                          deps=mms[-1:] + prev_lgread)
            prev_lg = [e_lg]
            m0 = m_sb[:, t * 2 * E: t * 2 * E + E]
            m1 = m_sb[:, t * 2 * E + E: (t + 1) * 2 * E]
            t_mx1 = kb.ins("vector", lambda e: e.reduce_max(mx[:, 0:1], lg, axis=AX.X),
                           deps=[e_lg] + prev_mx)
            raw0 = kb.ins("vector", lambda e: e.tensor_single_scalar(
                sc0[:], lg, mx[:, 0:1], ALU.is_ge), deps=[t_mx1] + prev_sc0)
            x0 = _exclusive_first(kb, sc0, sc1, m0, prev_m=[raw0] + prev_mdep)
            t_lg2 = kb.ins("vector", lambda e, m0=m0: e.scalar_tensor_tensor(
                lg2, m0, -1e9, lg, ALU.mult, ALU.add), deps=[x0] + prev_lg2)
            t_mx2 = kb.ins("vector", lambda e: e.reduce_max(mx[:, 1:2], lg2, axis=AX.X),
                           deps=[t_lg2])
            raw1 = kb.ins("vector", lambda e: e.tensor_single_scalar(
                sc0[:], lg2, mx[:, 1:2], ALU.is_ge), deps=[t_mx2, x0])
            x1t = _exclusive_first(kb, sc0, sc1, m1, prev_m=[raw1, x0])
            t_d = kb.ins("vector", lambda e: e.tensor_sub(mx[:, 2:3], mx[:, 0:1], mx[:, 1:2]),
                         deps=[t_mx2, t_mx1] + prev_g)
            g0 = g_sb[:, t * 2: t * 2 + 1]
            g1 = g_sb[:, t * 2 + 1: t * 2 + 2]
            t_g0 = kb.ins("scalar", lambda e, g0=g0: e.activation(g0, mx[:, 2:3], AF.Sigmoid),
                          deps=[t_d] + prev_gw)
            t_g1 = kb.ins("scalar", lambda e, g1=g1: e.activation(g1, mx[:, 2:3], AF.Sigmoid,
                                                                  scale=-1.0),
                          deps=[t_d] + prev_gw)
            cnt_toks.append(kb.ins("tensor", lambda e, t=t, m0=m0: e.matmul(
                pc[:], st["c_ones_col"][:], m_sb[:, t * 2 * E:(t + 1) * 2 * E],
                start=(t == 0), stop=(t == NT - 1)),
                deps=[x0, x1t]))
            prev_h = mms + [e_lg]
            prev_lgread = [t_mx1, raw0, t_lg2]
            prev_lg2 = [t_mx2, raw1]
            prev_mx = [t_d, raw0, raw1]
            prev_sc0 = [x0, x1t]
            prev_mdep = [cnt_toks[-1]]
            prev_g = [t_g0, t_g1]
            prev_gw = [t_d]
        ec = kb.ins("vector", lambda e: e.tensor_copy(cnt_sb[:], pc[:]),
                    deps=cnt_toks[-1:])
        dcl = kb.dma("sync", aps["cntloc"][:], cnt_sb[:], deps=[ec], stream="d_rt_c")
        cc = kb.ins("gpsimd", lambda e: e.collective_compute(
            "AllGather", ALU.bypass, [list(range(cfg["NCORES"]))],
            [aps["cntloc"][:].opt()], [aps["cntg"][:].opt()]),
            deps=[dcl])
        gcl = kb.dma("sync", cg_sb[:], aps["cntg"][:], deps=[cc], stream="d_rt_c")
        mp = kb.ins("tensor", lambda e: e.matmul(
            pc[:], st["c_prevmask"][0:cfg["NCORES"], :], cg_sb[:],
            start=True, stop=True), deps=[gcl, ec])
        ep = kb.ins("vector", lambda e: e.tensor_copy(go_sb[0:1, 0:16], pc[:]),
                    deps=[mp])
        ma = kb.ins("tensor", lambda e: e.matmul(
            pc[:], st["c_allmask"][0:cfg["NCORES"], :], cg_sb[:],
            start=True, stop=True), deps=[ep])
        ea = kb.ins("vector", lambda e: e.tensor_copy(go_sb[0:1, 16:32], pc[:]),
                    deps=[ma])
        c0i = kb.ins("vector", lambda e: e.tensor_copy(carry_sb[0:1, 0:E],
                                                       go_sb[0:1, 0:E]), deps=[ep])
        c1i = kb.ins("vector", lambda e: e.tensor_add(
            carry_sb[0:1, E:2 * E], go_sb[0:1, 16:24], go_sb[0:1, 8:16]),
            deps=[ea])
        carry_toks = [c0i, c1i]
        g_done = kb.barrier("scalar")
        prev_pos, prev_keep, prev_w, prev_pw = [], [], [], []
        for t in range(NT):
            kts = []
            for p in range(2):
                m_p = m_sb[:, t * 2 * E + p * E: t * 2 * E + (p + 1) * E]
                car = carry_sb[0:1, p * E:(p + 1) * E]
                mmp = kb.ins("tensor", lambda e, m_p=m_p: e.matmul(
                    ppos[:, 0:E], st["c_triT"][:], m_p, start=True, stop=False),
                    deps=carry_toks + prev_pos)
                mmc = kb.ins("tensor", lambda e, car=car: e.matmul(
                    ppos[:, 0:E], st["c_ones_row"][0:1, :], car,
                    start=False, stop=True), deps=[])
                t_pe = kb.ins("vector", lambda e, m_p=m_p: e.tensor_sub(
                    pos_sb[:], ppos[:, 0:E], m_p), deps=[mmc] + prev_keep)
                t_k = kb.ins("vector", lambda e: e.tensor_single_scalar(
                    pos_sb[:], pos_sb[:], float(CAP), ALU.is_lt), deps=[t_pe])
                t_km = kb.ins("vector", lambda e, p=p, m_p=m_p: e.tensor_mul(
                    keep_sb[:, p * E:(p + 1) * E], pos_sb[:], m_p), deps=[t_k] + prev_w)
                kts.append(t_km)
                cm = kb.ins("tensor", lambda e, m_p=m_p: e.matmul(
                    pc[0:1, 0:E], st["c_ones_col"][:], m_p, start=True, stop=True),
                    deps=[ea if t == 0 else carry_toks[-1], mmc])
                cev = kb.ins("vector", lambda e, car=car: e.tensor_add(
                    car, car, pc[0:1, 0:E]), deps=[cm])
                carry_toks = [cev]
                prev_pos = [t_pe]
            g0 = g_sb[:, t * 2: t * 2 + 1]
            g1 = g_sb[:, t * 2 + 1: t * 2 + 2]
            t_w1 = kb.ins("vector", lambda e, g1=g1: e.tensor_scalar(
                w_tile[:], keep_sb[:, E:2 * E], g1, None, ALU.mult),
                deps=kts + prev_w + [g_done])
            t_w = kb.ins("vector", lambda e, g0=g0: e.scalar_tensor_tensor(
                w_tile[:], keep_sb[:, 0:E], g0, w_tile[:], ALU.mult, ALU.add),
                deps=[t_w1])
            prev_keep = [t_w1, t_w]
            prev_w = [t_w]
            tw = kb.ins("tensor", lambda e: e.transpose(
                pw[:], w_tile[:], st["c_identf"][:]), deps=[t_w] + prev_pw)
            ev8 = kb.ins("vector", lambda e, t=t: e.tensor_copy(
                w8_sb[0:8, t * 128:(t + 1) * 128], pw[0:8, :]), deps=[tw])
            prev_pw = [ev8]
            prev_w = [t_w, tw]
        wd = kb.dma("sync", aps["w8"][:], w8_sb[:], deps=[ev8], stream="d_rt_c")
    return [wd]


def _exclusive_first(kb, raw, scr, out_m, prev_m):
    E = 8
    t1a = kb.ins("vector", lambda e: e.tensor_copy(scr[:, 0:1], raw[:, 0:1]),
                 deps=prev_m)
    t1b = kb.ins("vector", lambda e: e.tensor_add(
        scr[:, 1:E], raw[:, 1:E], raw[:, 0:E - 1]), deps=prev_m)
    t2a = kb.ins("vector", lambda e: e.tensor_copy(out_m[:, 0:2], scr[:, 0:2]),
                 deps=[t1a, t1b] + prev_m)
    t2b = kb.ins("vector", lambda e: e.tensor_add(
        out_m[:, 2:E], scr[:, 2:E], scr[:, 0:E - 2]), deps=[t1a, t1b] + prev_m)
    t3a = kb.ins("vector", lambda e: e.tensor_copy(scr[:, 0:4], out_m[:, 0:4]),
                 deps=[t2a, t2b])
    t3b = kb.ins("vector", lambda e: e.tensor_add(
        scr[:, 4:E], out_m[:, 4:E], out_m[:, 0:E - 4]), deps=[t2a, t2b])
    teq = kb.ins("vector", lambda e: e.tensor_single_scalar(
        out_m[:, :], scr[:, 0:E], 1.0, ALU.is_equal), deps=[t3a, t3b])
    tq = kb.ins("vector", lambda e: e.tensor_mul(out_m[:, :], out_m[:, :], raw[:, :]),
                deps=[teq])
    return tq



def ph_moe(kb, cfg, st, aps):
    D, S, T, E, HF = cfg["D"], cfg["S"], cfg["T"], cfg["E"], cfg["HF"]
    KC = D // 128
    HC = HF // 128
    SL = min(512, S)
    nc = kb.nc
    with contextlib.ExitStack() as es:
        def sbuf(n, sh, dt):
            return es.enter_context(nc.sbuf_tensor(n, sh, dt))
        def psum(n, sh, dt=F32):
            return es.enter_context(nc.psum_tensor(n, sh, dt))
        h_sb = sbuf("mo_h", [128, KC * SL], BF16)
        x_sb = sbuf("mo_x", [128, KC * SL], F32)
        acc = sbuf("mo_acc", [128, KC * SL], F32)
        y1_sb = sbuf("mo_y1", [128, HC * SL], BF16)
        w1_sb = sbuf("mo_w1", [128, 2 * KC * 1024], BF16)
        w2_sb = sbuf("mo_w2", [128, 2 * 4 * D], BF16)
        wrep = sbuf("mo_wrep", [128, SL], F32)
        w8_sb = sbuf("mo_w8", [8, T], F32)
        b1_sb = sbuf("mo_b1", [128, HC], F32)
        b2_sb = sbuf("mo_b2", [128, KC], F32)
        b1r = sbuf("mo_b1r", [32, 128], F32)
        b2r = sbuf("mo_b2r", [8, 128], F32)
        o_sb = sbuf("mo_o", [128, SL], F32)
        modT_sb = sbuf("mo_modT", [128, 6 * KC * cfg["BPC"]], F32)
        psy = [psum("mo_p%d" % i, [128, 512]) for i in range(8)]
        w8l = kb.dma("sync", w8_sb[:], aps["w8"][:], stream="d_mo_s")
        mtok = kb.dma("sync", modT_sb[:], aps["modT0"][:], stream="d_mo_s2")
        prev = {}
        for sl in range(T // SL):
            b = (sl * SL) // S
            hdl, xdl = [], []
            for c in range(KC):
                hdl.append(kb.dma("sync", sb3(h_sb[:, c * SL:(c + 1) * SL]),
                                  fm_ap(aps["h2"], c, sl * SL, SL),
                                  deps=prev.get("h", []), stream="d_mo_h"))
                xdl.append(kb.dma("gpsimd", sb3(x_sb[:, c * SL:(c + 1) * SL]),
                                  fm_ap(aps["x1fm"], c, sl * SL, SL),
                                  deps=prev.get("x", []), stream="d_mo_x"))
            hdl, xdl = [hdl[-1]], [xdl[-1]]
            acc_tok = []
            for e_i in range(E):
                mw = kb.ins("tensor", lambda e, e_i=e_i, sl=sl: e.matmul(
                    psy[7][:, 0:SL], st["c_sel8"][0:8, e_i * 128:(e_i + 1) * 128],
                    w8_sb[0:8, sl * SL:(sl + 1) * SL], start=True, stop=True),
                    deps=[w8l] + prev.get("pw", []))
                ew = kb.ins("vector", lambda e: e.tensor_copy(wrep[:], psy[7][:, 0:SL]),
                            deps=[mw] + prev.get("wrep", []))
                prev["pw"] = [ew]
                bl1 = kb.dma("sync", b1r[0:HC, :],
                             aps["L0_ffn_b1"][e_i, :].rearrange("(a b) -> a b", b=128),
                             deps=prev.get("b1", []), stream="d_mo_b")
                tb1 = kb.ins("tensor", lambda e: e.transpose(
                    psy[6][:, 0:HC], b1r[0:HC, :], st["c_identf"][0:HC, 0:HC]),
                    deps=[bl1] + prev.get("pb", []))
                eb1 = kb.ins("vector", lambda e: e.tensor_copy(
                    b1_sb[:, 0:HC], psy[6][:, 0:HC]), deps=[tb1] + prev.get("b1r", []))
                bl2 = kb.dma("sync", b2r[0:KC, :],
                             aps["L0_ffn_b2"][e_i, :].rearrange("(a b) -> a b", b=128),
                             deps=prev.get("b2", []), stream="d_mo_b2")
                tb2 = kb.ins("tensor", lambda e: e.transpose(
                    psy[6][:, 0:KC], b2r[0:KC, :], st["c_identf"][0:KC, 0:KC]),
                    deps=[bl2, eb1])
                eb2 = kb.ins("vector", lambda e: e.tensor_copy(
                    b2_sb[:, 0:KC], psy[6][:, 0:KC]), deps=[tb2] + prev.get("b2r", []))
                prev["b1"] = [tb1]
                prev["b2"] = [tb2]
                prev["pb"] = [eb2]
                gelu_toks = []
                for hg in range(HF // 1024):
                    wg_idx = hg % 2
                    w1g = w1_sb[:, wg_idx * KC * 1024:(wg_idx + 1) * KC * 1024]
                    wls = [kb.dma(
                        "gpsimd", sb3(w1g, 1024),
                        aps["L0_ffn_w1"][e_i, :, hg * 1024:(hg + 1) * 1024]
                        .rearrange("(kc kp) h -> kp kc h", kp=128),
                        deps=prev.get(("w1", wg_idx), []),
                        stream="d_mo_w1%d" % wg_idx)]
                    for hcl in range(8):
                        hc = hg * 8 + hcl
                        pp = psy[hcl % 2]
                        mms = []
                        for k in range(KC):
                            mms.append(kb.ins("tensor",
                                lambda e, k=k, hcl=hcl, pp=pp, w1g=w1g: e.matmul(
                                    pp[:, 0:SL],
                                    w1g[:, k * 1024 + hcl * 128: k * 1024 + (hcl + 1) * 128],
                                    h_sb[:, k * SL:(k + 1) * SL],
                                    start=(k == 0), stop=(k == KC - 1)),
                                deps=wls + hdl + prev.get(("py", hcl % 2), [])))
                        g = kb.ins("scalar", lambda e, hc=hc, pp=pp: e.activation(
                            y1_sb[:, hc * SL:(hc + 1) * SL], pp[:, 0:SL], AF.Gelu,
                            bias=b1_sb[:, hc:hc + 1]),
                            deps=mms[-1:] + [eb1] + prev.get("y1", []))
                        gelu_toks.append(g)
                        prev[("py", hcl % 2)] = [g]
                    prev[("w1", wg_idx)] = mms
                mm2_toks = {dc: [] for dc in range(KC)}
                for hq in range(HC // 4):
                    par = hq % 2
                    w2big = w2_sb[:, par * 4 * D:(par + 1) * 4 * D]
                    w2l = kb.dma("gpsimd", sb3(w2big, D),
                                 aps["L0_ffn_w2"][e_i, hq * 512:(hq + 1) * 512, :]
                                 .rearrange("(c p) d -> p c d", p=128),
                                 deps=prev.get(("w2", par), []),
                                 stream="d_mo_w2%d" % par)
                    for hcl in range(4):
                        hc = hq * 4 + hcl
                        for dc in range(KC):
                            mm2_toks[dc].append(kb.ins("tensor",
                                lambda e, hc=hc, hcl=hcl, dc=dc, w2big=w2big: e.matmul(
                                    psy[dc][:, 0:SL],
                                    w2big[:, hcl * D + dc * 128: hcl * D + (dc + 1) * 128],
                                    y1_sb[:, hc * SL:(hc + 1) * SL],
                                    start=(hc == 0), stop=(hc == HC - 1)),
                                deps=[w2l, gelu_toks[hc]] +
                                     (prev.get(("py2", dc), []) + [eb2, ew, gelu_toks[-1]]
                                      if hc == 0 else [])))
                    prev[("w2", par)] = [mm2_toks[dc][-1] for dc in range(KC)]
                new_acc = []
                for dc in range(KC):
                    if e_i == 0:
                        a = kb.ins("vector", lambda e, dc=dc: e.scalar_tensor_tensor(
                            acc[:, dc * SL:(dc + 1) * SL], psy[dc][:, 0:SL],
                            b2_sb[:, dc:dc + 1], wrep[:], ALU.add, ALU.mult),
                            deps=mm2_toks[dc][-1:] + [eb2, ew] + prev.get("acc", []))
                    else:
                        a = kb.ins("vector", lambda e, dc=dc: e.scalar_tensor_tensor(
                            o_sb[:], psy[dc][:, 0:SL],
                            b2_sb[:, dc:dc + 1], wrep[:], ALU.add, ALU.mult),
                            deps=mm2_toks[dc][-1:] + [eb2, ew] + prev.get("o", []))
                        a = kb.ins("vector", lambda e, dc=dc: e.tensor_add(
                            acc[:, dc * SL:(dc + 1) * SL],
                            acc[:, dc * SL:(dc + 1) * SL], o_sb[:]),
                            deps=[a] + prev.get("acc", []))
                        prev["o"] = [a]
                    new_acc.append(a)
                    prev[("py2", dc)] = [a]
                prev["acc"] = new_acc
                prev["y1"] = mm2_toks[KC - 1]
                prev["wrep"] = new_acc
                prev["pw"] = new_acc
                prev[("py", 0)] = new_acc
                prev[("py", 1)] = new_acc
                prev["b1r"] = gelu_toks[-1:]
                prev["b2r"] = new_acc
            eps_toks = []
            for c in range(KC):
                ep = kb.ins("vector", lambda e, c=c, b=b: e.scalar_tensor_tensor(
                    x_sb[:, c * SL:(c + 1) * SL], acc[:, c * SL:(c + 1) * SL],
                    modT_sb[:, modcol(cfg, 5, c, b):modcol(cfg, 5, c, b) + 1],
                    x_sb[:, c * SL:(c + 1) * SL], ALU.mult, ALU.add),
                    deps=prev["acc"] + xdl + [mtok] + prev.get("eps", []))
                eps_toks.append(ep)
            odmas = []
            for c in range(KC):
                odmas.append(kb.dma("sync", fm_ap(aps["x2fm"], c, sl * SL, SL),
                                    sb3(x_sb[:, c * SL:(c + 1) * SL]),
                                    deps=[eps_toks[c]], stream="d_mo_o"))
            prev["h"] = [m for dc in range(KC) for m in mm2_toks[dc][-1:]] + gelu_toks[-1:]
            prev["x"] = [odmas[-1]]
            prev["eps"] = [odmas[-1]]
            prev["acc"] = eps_toks
    return odmas



def ph_l1proj(kb, cfg, st, aps):
    D, S, T, NH, DH, DL = cfg["D"], cfg["S"], cfg["T"], cfg["NH"], cfg["DH"], cfg["DLAT"]
    KC = D // 128
    SL = min(512, S)
    nc = kb.nc
    with contextlib.ExitStack() as es:
        def sbuf(n, sh, dt):
            return es.enter_context(nc.sbuf_tensor(n, sh, dt))
        def psum(n, sh, dt=F32):
            return es.enter_context(nc.psum_tensor(n, sh, dt))
        wq_sb = sbuf("lp_wq", [128, KC * D], BF16)
        wdn_sb = sbuf("lp_wdn", [128, KC * DL], BF16)
        wup_sb = sbuf("lp_wup", [128, 2 * 2 * D], BF16)
        bq_sb = sbuf("lp_bq", [1, D], BF16)
        bup_sb = sbuf("lp_bup", [1, 2 * D], BF16)
        bdn_r = sbuf("lp_bdnr", [2, 128], F32)
        bdn_sb = sbuf("lp_bdn", [128, 2], F32)
        ones_sl = sbuf("lp_ones", [1, SL], BF16)
        h_sb = sbuf("lp_h", [128, KC * SL], BF16)
        lat_sb = sbuf("lp_lat", [128, 2 * SL], BF16)
        q_sb = sbuf("lp_q", [128, D], F32)
        kv_sb = sbuf("lp_kv", [128, 2 * D], F32)
        ro_sb = sbuf("lp_ro", [128, D], BF16)
        kro_sb = sbuf("lp_kro", [128, D], BF16)
        v_sb = sbuf("lp_v", [128, NH * DH], BF16)
        cos_sb = sbuf("lp_cos", [128, 512], F32)
        sin_sb = sbuf("lp_sin", [128, 512], F32)
        t1_sb = sbuf("lp_t1", [128, 512], F32)
        t2_sb = sbuf("lp_t2", [128, 512], F32)
        pq = psum("lp_pq", [128, 512])
        pl = psum("lp_pl", [128, 512])
        pk = psum("lp_pk", [128, 512])
        wql = kb.dma("gpsimd", sb3(wq_sb[:], D),
                     aps["L1_attn_Wq"].rearrange("(kc kp) o -> kp kc o", kp=128),
                     stream="d_lp_w")
        wdl = kb.dma("gpsimd", sb3(wdn_sb[:], DL),
                     aps["L1_attn_Wdown"].rearrange("(kc kp) o -> kp kc o", kp=128),
                     stream="d_lp_w")
        wul = kb.dma("gpsimd", sb3(wup_sb[:], 2 * D),
                     aps["L1_attn_Wup"].rearrange("(kc kp) o -> kp kc o", kp=128),
                     stream="d_lp_w")
        bql = kb.dma("gpsimd", bq_sb[:], aps["L1_attn_bq"][:], stream="d_lp_w")
        bul = kb.dma("gpsimd", bup_sb[:], aps["L1_attn_bup"][:], stream="d_lp_w")
        bdl = kb.dma("sync", bdn_r[:], aps["L1_attn_bdown"][0, :]
                     .rearrange("(a b) -> a b", b=128), stream="d_lp_b")
        tbd = kb.ins("tensor", lambda e: e.transpose(
            pq[:, 0:2], bdn_r[:], st["c_identf"][0:2, 0:2]), deps=[bdl])
        ebd = kb.ins("vector", lambda e: e.tensor_copy(bdn_sb[:], pq[:, 0:2]),
                     deps=[tbd])
        onم = kb.ins("vector", lambda e: e.memset(ones_sl[:], 1.0))
        wtoks = [wql, wdl, wul, bql, bul, ebd, onم]
        prev = {}
        for sl in range(T // SL):
            b = (sl * SL) // S
            p0s = sl * SL - b * S
            hdl = []
            for c in range(KC):
                hdl.append(kb.dma("sync", sb3(h_sb[:, c * SL:(c + 1) * SL]),
                                  fm_ap(aps["h3"], c, sl * SL, SL),
                                  deps=prev.get("h", []), stream="d_lp_h"))
            hdl = [hdl[-1]]
            lats = []
            for lc in range(DL // 128):
                mms = []
                for k in range(KC):
                    mms.append(kb.ins("tensor", lambda e, k=k, lc=lc: e.matmul(
                        pl[:, 0:SL], wdn_sb[:, k * DL + lc * 128: k * DL + (lc + 1) * 128],
                        h_sb[:, k * SL:(k + 1) * SL],
                        start=(k == 0), stop=(k == KC - 1)),
                        deps=wtoks + hdl + prev.get("pl", [])))
                ev = kb.ins("vector", lambda e, lc=lc: e.tensor_scalar(
                    lat_sb[:, lc * SL:(lc + 1) * SL], pl[:, 0:SL],
                    bdn_sb[:, lc:lc + 1], None, ALU.add),
                    deps=mms[-1:] + prev.get("lat", []))
                lats.append(ev)
                prev["pl"] = [ev]
            for tt in range(SL // 128):
                cl = kb.dma("sync", cos_sb[:],
                            aps["c_l1cos"][p0s + tt * 128:p0s + (tt + 1) * 128, :],
                            deps=prev.get("cos", []), stream="d_lp_c")
                sl_l = kb.dma("sync", sin_sb[:],
                              aps["c_l1sin"][p0s + tt * 128:p0s + (tt + 1) * 128, :],
                              deps=prev.get("cos", []), stream="d_lp_s")
                qevs = []
                for oh in range(2):
                    mms = []
                    for k in range(KC):
                        mms.append(kb.ins("tensor", lambda e, k=k, oh=oh, tt=tt: e.matmul(
                            pq[:, 0:512],
                            h_sb[:, k * SL + tt * 128: k * SL + (tt + 1) * 128],
                            wq_sb[:, k * D + oh * 512:(k * D + (oh + 1) * 512)],
                            start=(k == 0), stop=False),
                            deps=wtoks + hdl + prev.get("pq", [])))
                    mb = kb.ins("tensor", lambda e, oh=oh: e.matmul(
                        pq[:, 0:512], ones_sl[0:1, 0:128],
                        bq_sb[0:1, oh * 512:(oh + 1) * 512],
                        start=False, stop=True), deps=[])
                    ev = kb.ins("vector", lambda e, oh=oh: e.tensor_copy(
                        q_sb[:, oh * 512:(oh + 1) * 512], pq[:, 0:512]),
                        deps=[mb] + prev.get("qsb", []))
                    qevs.append(ev)
                    prev["pq"] = [ev]
                kvevs = []
                for qh in range(4):
                    mms = []
                    for lc in range(DL // 128):
                        mms.append(kb.ins("tensor", lambda e, lc=lc, qh=qh, tt=tt: e.matmul(
                            pk[:, 0:512],
                            lat_sb[:, lc * SL + tt * 128: lc * SL + (tt + 1) * 128],
                            wup_sb[:, lc * 2 * D + qh * 512: lc * 2 * D + (qh + 1) * 512],
                            start=(lc == 0), stop=False),
                            deps=lats + prev.get("pk", [])))
                    mb = kb.ins("tensor", lambda e, qh=qh: e.matmul(
                        pk[:, 0:512], ones_sl[0:1, 0:128],
                        bup_sb[0:1, qh * 512:(qh + 1) * 512],
                        start=False, stop=True), deps=[])
                    ev = kb.ins("vector", lambda e, qh=qh: e.tensor_copy(
                        kv_sb[:, qh * 512:(qh + 1) * 512], pk[:, 0:512]),
                        deps=[mb] + prev.get("kvsb", []))
                    kvevs.append(ev)
                    prev["pk"] = [ev]
                ct = cos_sb[:]
                st_ = sin_sb[:]
                def q1v(z):
                    return z
                q_t = q_sb
                q1 = q_t[:].rearrange("p (h x) -> p h x", x=64)[:, :, 0:32]
                q2 = q_t[:].rearrange("p (h x) -> p h x", x=64)[:, :, 32:64]
                ro1 = ro_sb[:].rearrange("p (h x) -> p h x", x=64)[:, :, 0:32]
                ro2 = ro_sb[:].rearrange("p (h x) -> p h x", x=64)[:, :, 32:64]
                c3 = sb3(ct, 32)
                s3 = sb3(st_, 32)
                rq = _rope_l1(kb, q1, q2, ro1, ro2, c3, s3, t1_sb, t2_sb,
                              qevs + [cl, sl_l] + prev.get("ro", []), prev)
                kv3 = kv_sb[:].rearrange("p (h x) -> p h x", x=128)
                k1 = kv3[:, :, 0:32]
                k2 = kv3[:, :, 32:64]
                ko1 = kro_sb[:].rearrange("p (h x) -> p h x", x=64)[:, :, 0:32]
                ko2 = kro_sb[:].rearrange("p (h x) -> p h x", x=64)[:, :, 32:64]
                rk = _rope_l1(kb, k1, k2, ko1, ko2, c3, s3, t1_sb, t2_sb,
                              kvevs + [cl, sl_l] + prev.get("kro", []), prev)
                vv = kb.ins("vector", lambda e: e.tensor_copy(
                    sb3(v_sb[:], DH), kv3[:, :, 64:128]),
                    deps=kvevs + prev.get("vsb", []))
                d1 = kb.dma("sync", aps["l1q"][sl * SL + tt * 128: sl * SL + (tt + 1) * 128, :],
                            ro_sb[:], deps=[rq], stream="d_lp_oq")
                d2 = kb.dma("sync", aps["l1k"][sl * SL + tt * 128: sl * SL + (tt + 1) * 128, :],
                            kro_sb[:], deps=[rk], stream="d_lp_ok")
                d3 = kb.dma("sync", aps["l1v"][sl * SL + tt * 128: sl * SL + (tt + 1) * 128, :],
                            v_sb[:], deps=[vv], stream="d_lp_ov")
                prev["qsb"] = [rq]
                prev["kvsb"] = [rk, vv]
                prev["ro"] = [d1]
                prev["kro"] = [d2]
                prev["vsb"] = [d3]
            prev["h"] = mms + qevs
            prev["lat"] = mms
            prev["cos"] = [rq, rk]
    return [d1, d2, d3]


def _rope_l1(kb, x1, x2, o1, o2, c3, s3, t1_sb, t2_sb, deps0, prev):
    t3a = sb3(t1_sb[:], 32)
    t3b = sb3(t2_sb[:], 32)
    a1 = kb.ins("vector", lambda e: e.tensor_mul(t3a, x1, c3),
                deps=deps0 + prev.get("rt1", []))
    a2 = kb.ins("vector", lambda e: e.tensor_mul(t3b, x2, s3),
                deps=deps0 + prev.get("rt2", []))
    a3 = kb.ins("vector", lambda e: e.tensor_sub(o1, t3a, t3b), deps=[a1, a2])
    b1 = kb.ins("vector", lambda e: e.tensor_mul(t3a, x1, s3), deps=[a3])
    b2 = kb.ins("vector", lambda e: e.tensor_mul(t3b, x2, c3), deps=[a3])
    b3 = kb.ins("vector", lambda e: e.tensor_add(o2, t3a, t3b), deps=[b1, b2])
    prev["rt1"] = [b3]
    prev["rt2"] = [b3]
    return b3


def ph_l1attn(kb, cfg, st, aps):
    D, S, T, NH, DH = cfg["D"], cfg["S"], cfg["T"], cfg["NH"], cfg["DH"]
    HSPLIT = 7
    nc = kb.nc
    with contextlib.ExitStack() as es:
        def sbuf(n, sh, dt):
            return es.enter_context(nc.sbuf_tensor(n, sh, dt))
        def psum(n, sh, dt=F32):
            return es.enter_context(nc.psum_tensor(n, sh, dt))
        q_sb = sbuf("la_q", [128, D], BF16)
        k_sb = sbuf("la_k", [128, D], BF16)
        v_sb = sbuf("la_v", [128, D], BF16)
        prod = sbuf("la_prod", [128, 4 * 1024], F32)
        att = sbuf("la_att", [128, NH * NH], F32)
        mx = sbuf("la_mx", [128, 4 * 2], F32)
        li = sbuf("la_l", [128, 4], F32)
        ao = sbuf("la_ao", [128, D], F32)
        oo = sbuf("la_oo", [128, D], BF16)
        ps0 = psum("la_p0", [128, 512])
        ps1 = psum("la_p1", [128, 512])
        prev = {}
        for tile in range(T // 128):
            t0 = tile * 128
            ql = kb.dma("sync", q_sb[:], aps["l1q"][t0:t0 + 128, :],
                        deps=prev.get("q", []), stream="d_la_q")
            kl = kb.dma("sync", k_sb[:], aps["l1k"][t0:t0 + 128, :],
                        deps=prev.get("k", []), stream="d_la_k")
            vl = kb.dma("sync", v_sb[:], aps["l1v"][t0:t0 + 128, :],
                        deps=prev.get("v", []), stream="d_la_v")
            hts = []
            lane_n = {0: 0, 1: 0}
            for h in range(NH):
                lane = 0 if h < HSPLIT else 1
                eng = "vector" if lane == 0 else "gpsimd"
                buf = 2 * lane + (lane_n[lane] % 2)
                lane_n[lane] += 1
                pr = prod[:, buf * 1024:(buf + 1) * 1024]
                mxl = mx[:, buf * 2:(buf + 1) * 2]
                lil = li[:, buf:buf + 1]
                q_bc = q_sb[:].rearrange("p (hh x) -> p hh x", x=DH)[
                    :, h:h + 1, :].broadcast_to([128, NH, DH])
                k_all = k_sb[:].rearrange("p (g x) -> p g x", x=DH)
                tm = kb.ins(eng, lambda e, q_bc=q_bc, k_all=k_all, pr=pr: e.tensor_mul(
                    sb3(pr, DH), q_bc, k_all),
                    deps=[ql, kl] + prev.get(("prod", buf), []))
                tr = kb.ins("vector", lambda e, h=h, pr=pr: e.reduce_sum(
                    att[:, h * NH:(h + 1) * NH], sb3(pr, DH), axis=AX.X),
                    deps=[tm] + prev.get(("att", buf), []))
                t_mx = kb.ins("vector", lambda e, h=h, mxl=mxl: e.reduce_max(
                    mxl[:, 0:1], att[:, h * NH:(h + 1) * NH], axis=AX.X),
                    deps=[tr] + prev.get(("mx", buf), []))
                t_ng = kb.ins("scalar", lambda e, mxl=mxl: e.mul(
                    mxl[:, 1:2], mxl[:, 0:1], -0.125),
                    deps=[t_mx] + prev.get(("mx2", buf), []))
                t_ex = kb.ins("scalar", lambda e, h=h, mxl=mxl, lil=lil: e.activation(
                    att[:, h * NH:(h + 1) * NH], att[:, h * NH:(h + 1) * NH],
                    AF.Exp, scale=0.125, bias=mxl[:, 1:2], accum_out=lil),
                    deps=[t_ng] + prev.get(("li", buf), []))
                t_rc = kb.ins("vector", lambda e, lil=lil: e.reciprocal(lil, lil),
                              deps=[t_ex])
                t_nm = kb.ins("vector", lambda e, h=h, lil=lil: e.tensor_scalar(
                    att[:, h * NH:(h + 1) * NH], att[:, h * NH:(h + 1) * NH],
                    lil, None, ALU.mult), deps=[t_rc])
                a_bc = att[:, h * NH:(h + 1) * NH].rearrange(
                    "p (o g) -> p o g", o=1).broadcast_to([128, DH, NH])
                v_dg = v_sb[:].rearrange("p (g x) -> p x g", x=DH)
                tm2 = kb.ins(eng, lambda e, a_bc=a_bc, v_dg=v_dg, pr=pr: e.tensor_mul(
                    pr.rearrange("p (x g) -> p x g", g=NH),
                    a_bc, v_dg), deps=[t_nm, vl, tm, tr] + prev.get(("prod2", buf), []))
                tr2 = kb.ins("vector", lambda e, h=h, pr=pr: e.reduce_sum(
                    ao[:, h * DH:(h + 1) * DH],
                    pr.rearrange("p (x g) -> p x g", g=NH), axis=AX.X),
                    deps=[tm2] + prev.get(("ao", lane), []))
                hts.append(tr2)
                prev[("prod", buf)] = [tm2, tr2]
                prev[("att", buf)] = [tm2]
                prev[("mx", buf)] = [t_ng]
                prev[("mx2", buf)] = [t_ex]
                prev[("li", buf)] = [t_nm]
                prev[("prod2", buf)] = [tr2]
            evs = []
            for g in range(2):
                tts = []
                for c in range(4):
                    ch = g * 4 + c
                    tts.append(kb.ins("tensor",
                        lambda e, ch=ch, g=g, c=c: e.transpose(
                            (ps0 if g == 0 else ps1)[:, c * 128:(c + 1) * 128],
                            ao[:, ch * 128:(ch + 1) * 128], st["c_identf"][:]),
                        deps=hts + prev.get("ps" + str(g), [])))
                ev = kb.ins("vector", lambda e, g=g: e.tensor_copy(
                    oo[:, g * 512:(g + 1) * 512], (ps0 if g == 0 else ps1)[:]),
                    deps=tts + prev.get("oo", []))
                evs.append(ev)
                prev["ps" + str(g)] = [ev]
                prev["ao"] = prev.get("ao", []) + tts
            od = kb.dma("sync", aps["l1at"][:, t0:t0 + 128].rearrange(
                "(c p) t -> p c t", p=128), sb3(oo[:]), deps=evs, stream="d_la_o")
            prev["q"] = hts[-1:] + [hts[HSPLIT - 1]]
            prev["k"] = prev["q"]
            prev["v"] = prev["q"]
            prev[("ao", 0)] = tts
            prev[("ao", 1)] = tts
            prev["oo"] = [od]
    return [od]


def ph_l1out(kb, cfg, st, aps):
    D = cfg["D"]
    KC = D // 128
    nc = kb.nc
    with contextlib.ExitStack() as es:
        def sbuf(n, sh, dt):
            return es.enter_context(nc.sbuf_tensor(n, sh, dt))
        wo_sb = sbuf("lo_w", [128, KC * D], BF16)
        bo_sb = sbuf("lo_b", [1, D], BF16)
        on_sb = sbuf("lo_on", [1, min(512, cfg["S"])], BF16)
        modT_sb = sbuf("lo_m", [128, 6 * KC * cfg["BPC"]], F32)
        wtok = kb.dma("gpsimd", sb3(wo_sb[:], D),
                      aps["L1_attn_Wout"].rearrange("(kc kp) o -> kp kc o", kp=128),
                      stream="d_lo_w")
        btok = kb.dma("gpsimd", bo_sb[:], aps["L1_attn_bout"][:], stream="d_lo_b")
        onm = kb.ins("vector", lambda e: e.memset(on_sb[:], 1.0))
        mtok = kb.dma("sync", modT_sb[:], aps["modT1"][:], stream="d_lo_m")
        r = emit_proj_epilogue(kb, cfg, st, aps, "l1at", wo_sb, wtok,
                               "x2fm", "x3fm", modT_sb, mtok, 2, "_lo",
                               bias_sb=(bo_sb, on_sb, btok, onm))
    return r



def ph_l1ffn(kb, cfg, st, aps):
    D, S, T = cfg["D"], cfg["S"], cfg["T"]
    KC = D // 128
    SL = min(512, S)
    nc = kb.nc
    with contextlib.ExitStack() as es:
        def sbuf(n, sh, dt):
            return es.enter_context(nc.sbuf_tensor(n, sh, dt))
        def psum(n, sh, dt=F32):
            return es.enter_context(nc.psum_tensor(n, sh, dt))
        kb_sb = sbuf("lf_k", [128, KC * D], BF16)
        k8_sb = sbuf("lf_k8", [128, KC * D], F8)
        h_sb = sbuf("lf_h", [128, KC * SL], BF16)
        h8_sb = sbuf("lf_h8", [128, KC * SL], F8)
        x_sb = sbuf("lf_x", [128, KC * SL], F32)
        y_sb = sbuf("lf_y", [128, SL], BF16)
        sc_sb = sbuf("lf_sc", [1, 8], F32)
        rep_sb = sbuf("lf_rep", [128, 3], F32)
        bia_r = sbuf("lf_br", [8, 128], BF16)
        bia_sb = sbuf("lf_b", [128, 8], F32)
        modT_sb = sbuf("lf_m", [128, 6 * KC * cfg["BPC"]], F32)
        pp0 = psum("lf_p0", [128, 512])
        pp1 = psum("lf_p1", [128, 512])
        pb8 = psum("lf_pb", [128, 8], BF16)
        mtok = kb.dma("sync", modT_sb[:], aps["modT1"][:], stream="d_lf_m")
        s1 = kb.dma("sync", sc_sb[0:1, 0:1], aps["L1_ffn_input_scale"][:],
                    stream="d_lf_s")
        s2 = kb.dma("sync", sc_sb[0:1, 1:2], aps["L1_ffn_kernel_scale"][:],
                    stream="d_lf_s2")
        t_a = kb.ins("vector", lambda e: e.tensor_add(
            sc_sb[0:1, 2:3], sc_sb[0:1, 0:1], sc_sb[0:1, 0:1]), deps=[s1])
        t_b = kb.ins("vector", lambda e: e.tensor_add(
            sc_sb[0:1, 3:4], sc_sb[0:1, 1:2], sc_sb[0:1, 1:2]), deps=[s2])
        t_ra = kb.ins("vector", lambda e: e.reciprocal(
            sc_sb[0:1, 5:6], sc_sb[0:1, 2:3]), deps=[t_a])
        t_rb = kb.ins("vector", lambda e: e.reciprocal(
            sc_sb[0:1, 6:7], sc_sb[0:1, 3:4]), deps=[t_b])
        t_c = kb.ins("vector", lambda e: e.tensor_mul(
            sc_sb[0:1, 4:5], sc_sb[0:1, 0:1], sc_sb[0:1, 1:2]), deps=[s1, s2])
        t_d = kb.ins("vector", lambda e: e.tensor_scalar(
            sc_sb[0:1, 7:8], sc_sb[0:1, 4:5], 4.0, None, ALU.mult), deps=[t_c])
        mrep = kb.ins("tensor", lambda e: e.matmul(
            pp0[:, 0:3], st["c_ones_row"][0:1, :], sc_sb[0:1, 5:8],
            start=True, stop=True), deps=[t_ra, t_rb, t_d])
        erep = kb.ins("vector", lambda e: e.tensor_copy(rep_sb[:], pp0[:, 0:3]),
                      deps=[mrep])
        kl = kb.dma("sync", sb3(kb_sb[:], D),
                    aps["L1_ffn_kernel"].rearrange("(kc kp) o -> kp kc o", kp=128),
                    stream="d_lf_k")
        kq = kb.ins("vector", lambda e: e.tensor_scalar(
            k8_sb[:], kb_sb[:], rep_sb[:, 1:2], None, ALU.mult),
            deps=[kl, erep])
        bl = kb.dma("sync", bia_r[:], aps["L1_ffn_bias"][0, :]
                    .rearrange("(a b) -> a b", b=128), stream="d_lf_b")
        tb = kb.ins("tensor", lambda e: e.transpose(
            pb8[:], bia_r[:], st["c_ident"][0:8, 0:8]), deps=[bl])
        eb = kb.ins("vector", lambda e: e.tensor_copy(bia_sb[:], pb8[:]),
                    deps=[tb])
        prev = {}
        for sl in range(T // SL):
            b = (sl * SL) // S
            hdl, xdl = [], []
            for c in range(KC):
                hdl.append(kb.dma("sync", sb3(h_sb[:, c * SL:(c + 1) * SL]),
                                  fm_ap(aps["h4"], c, sl * SL, SL),
                                  deps=prev.get("h", []), stream="d_lf_h"))
                xdl.append(kb.dma("gpsimd", sb3(x_sb[:, c * SL:(c + 1) * SL]),
                                  fm_ap(aps["x3fm"], c, sl * SL, SL),
                                  deps=prev.get("x", []), stream="d_lf_x"))
            hdl, xdl = [hdl[-1]], [xdl[-1]]
            hqs = []
            for c in range(KC):
                hqs.append(kb.ins("vector", lambda e, c=c: e.tensor_scalar(
                    h8_sb[:, c * SL:(c + 1) * SL], h_sb[:, c * SL:(c + 1) * SL],
                    rep_sb[:, 0:1], None, ALU.mult),
                    deps=hdl + [erep] + prev.get("h8", [])))
            outs = []
            for och in range(KC):
                pp = pp0 if och % 2 == 0 else pp1
                mms = []
                for k in range(KC):
                    mms.append(kb.ins("tensor", lambda e, k=k, och=och, pp=pp: e.matmul(
                        pp[:, 0:SL], k8_sb[:, k * D + och * 128: k * D + (och + 1) * 128],
                        h8_sb[:, k * SL:(k + 1) * SL],
                        start=(k == 0), stop=(k == KC - 1)),
                        deps=[kq, hqs[k]] + prev.get(("pp", och % 2), [])))
                ya = kb.ins("vector", lambda e, pp=pp: e.tensor_scalar(
                    y_sb[:], pp[:, 0:SL], rep_sb[:, 2:3], None, ALU.mult),
                    deps=mms[-1:] + [erep] + prev.get("y", []))
                yb = kb.ins("vector", lambda e, och=och: e.tensor_scalar(
                    y_sb[:], y_sb[:], bia_sb[:, och:och + 1], None, ALU.add),
                    deps=[ya, eb])
                ep = kb.ins("vector", lambda e, och=och, b=b: e.scalar_tensor_tensor(
                    x_sb[:, och * SL:(och + 1) * SL], y_sb[:],
                    modT_sb[:, modcol(cfg, 5, och, b):modcol(cfg, 5, och, b) + 1],
                    x_sb[:, och * SL:(och + 1) * SL], ALU.mult, ALU.add),
                    deps=[yb, mtok] + xdl)
                outs.append(ep)
                prev[("pp", och % 2)] = [ya]
                prev["y"] = [ep]
            odmas = []
            for c in range(KC):
                odmas.append(kb.dma("sync", fm_ap(aps["x4fm"], c, sl * SL, SL),
                                    sb3(x_sb[:, c * SL:(c + 1) * SL]),
                                    deps=[outs[c]], stream="d_lf_o"))
            prev["h"] = hqs
            prev["h8"] = mms
            prev["x"] = [odmas[-1]]
    return odmas


def ph_egress(kb, cfg, st, aps):
    return emit_tm2fm(kb, cfg, st, aps["x4fm"], aps["out"], "_eg", rev=True)



_LAUNCH_SPECS = [
    ([(ph_ingest, {}), (ph_wT, {}), (ph_mod, {"layer": 0}),
      (ph_ln, {"x": "x0fm", "h": "h1", "layer": 0, "which": "A"}),
      (ph_l0qkv, {})],
     [], ["x0fm", "modT0", "qro", "kro", "vtm", "woT"]),
    ([(ph_l0attn, {}), (ph_l0wo, {})],
     ["x0fm", "modT0", "qro", "kro", "vtm", "woT"], ["x1fm"]),
    ([(ph_ln, {"x": "x1fm", "h": "h2", "layer": 0, "which": "B"}),
      (ph_route, {}), (ph_moe, {})],
     ["x1fm", "modT0"], ["x2fm"]),
    ([(ph_mod, {"layer": 1}),
      (ph_ln, {"x": "x2fm", "h": "h3", "layer": 1, "which": "A"}),
      (ph_l1proj, {}), (ph_l1attn, {}), (ph_l1out, {})],
     ["x2fm"], ["x3fm", "modT1"]),
    ([(ph_ln, {"x": "x3fm", "h": "h4", "layer": 1, "which": "B"}),
      (ph_l1ffn, {}), (ph_egress, {})],
     ["x3fm", "modT1"], ["out"]),
]

_prog_cache = {}


def _get_prog(cfg_key, li, cfg):
    if (cfg_key, li) not in _prog_cache:
        phases, sc_in, sc_out = _LAUNCH_SPECS[li]
        ext = {n: "in" for n in sc_in}
        ext.update({n: "out" for n in sc_out})
        _prog_cache[(cfg_key, li)] = build_program(cfg, phases, ext)
    return _prog_cache[(cfg_key, li)]


def _shard_inputs(cfg, full):
    NC, BPC, D, S = cfg["NCORES"], cfg["BPC"], cfg["D"], cfg["S"]
    specs = input_specs_per_core(cfg)
    maps = []
    for c in range(NC):
        m = {}
        for name, (shape, dt) in specs.items():
            a = full[name]
            if name == "L0_attn_hidden_states":
                a = np.ascontiguousarray(a[c * BPC:(c + 1) * BPC]).reshape(BPC * S, D)
            elif name in ("L0_c", "L1_c"):
                a = np.ascontiguousarray(a[c * BPC:(c + 1) * BPC])
            else:
                a = np.asarray(a)
            a = a.reshape(shape)
            m[name] = a
        m.update(host_constants(cfg, c))
        maps.append(m)
    return maps


def kernel(**inputs):
    if FUSED:
        return kernel_fused(**inputs)
    return _kernel_phased(**inputs)


_ALL_PHASES = [
    (ph_ingest, {}), (ph_wT, {}), (ph_mod, {"layer": 0}),
    (ph_ln, {"x": "x0fm", "h": "h1", "layer": 0, "which": "A"}),
    (ph_l0qkv, {}),
    (ph_l0attn, {}), (ph_l0wo, {}),
    (ph_ln, {"x": "x1fm", "h": "h2", "layer": 0, "which": "B"}),
    (ph_route, {}), (ph_moe, {}),
    (ph_mod, {"layer": 1}),
    (ph_ln, {"x": "x2fm", "h": "h3", "layer": 1, "which": "A"}),
    (ph_l1proj, {}), (ph_l1attn, {}), (ph_l1out, {}),
    (ph_ln, {"x": "x3fm", "h": "h4", "layer": 1, "which": "B"}),
    (ph_l1ffn, {}), (ph_egress, {}),
]

FUSED = True


def kernel_fused(**inputs):
    cfg = make_cfg()
    NC = cfg["NCORES"]
    maps = _shard_inputs(cfg, inputs)
    if ("fused", 0) not in _prog_cache:
        _prog_cache[("fused", 0)] = build_program(cfg, _ALL_PHASES, {"out": "out"})
    nc = _prog_cache[("fused", 0)]
    res = run_bass_kernel_spmd(nc, maps, core_ids=list(range(NC)))
    out = np.concatenate(
        [np.asarray(res.results[c]["out"]).reshape(cfg["BPC"], cfg["S"], cfg["D"])
         for c in range(NC)], axis=0)
    return out.astype(np.float32)


def _kernel_phased(**inputs):
    cfg = make_cfg()
    NC = cfg["NCORES"]
    maps = _shard_inputs(cfg, inputs)
    carry = [dict() for _ in range(NC)]
    for li in range(len(_LAUNCH_SPECS)):
        phases, sc_in, sc_out = _LAUNCH_SPECS[li]
        nc = _get_prog("full", li, cfg)
        in_maps = []
        for c in range(NC):
            m = dict(maps[c])
            for n in sc_in:
                m[n] = carry[c][n]
            in_maps.append(m)
        res = run_bass_kernel_spmd(nc, in_maps, core_ids=list(range(NC)))
        for c in range(NC):
            for n in sc_out:
                carry[c][n] = np.asarray(res.results[c][n])
    out = np.concatenate([carry[c]["out"].reshape(cfg["BPC"], cfg["S"], cfg["D"])
                          for c in range(NC)], axis=0)
    return out.astype(np.float32)


# version 14
# speedup vs baseline: 1.0217x; 1.0105x over previous
import contextlib
import math
import numpy as np
import ml_dtypes
import concourse.bass as bass
import concourse.mybir as mybir
from concourse.bass_utils import run_bass_kernel_spmd

F32 = mybir.dt.float32
BF16 = mybir.dt.bfloat16
F8 = mybir.dt.float8e4
AF = mybir.ActivationFunctionType
ALU = mybir.AluOpType
AX = mybir.AxisListType
BF16NP = ml_dtypes.bfloat16


def make_cfg(S=2048, BPC=2, NCORES=8):
    c = {}
    c["D"] = 1024
    c["NH"] = 16
    c["DH"] = 64
    c["S"] = S
    c["BPC"] = BPC
    c["T"] = S * BPC
    c["NCORES"] = NCORES
    c["E"] = 8
    c["HF"] = 4096
    c["MOE_K"] = 2
    n_total = S * BPC * NCORES
    c["CAP"] = int(math.ceil(1.0 * c["MOE_K"] * n_total / c["E"]))
    c["DLAT"] = 256
    return c



def host_constants(cfg, core_id):
    D, S, NH, DH = cfg["D"], cfg["S"], cfg["NH"], cfg["DH"]
    out = {}
    out["c_ident"] = np.eye(128, dtype=BF16NP)
    out["c_identf"] = np.eye(128, dtype=np.float32)
    out["c_triT"] = np.triu(np.ones((128, 128), dtype=np.float32)).copy()
    out["c_ones_kxm"] = np.ones((128, 128), dtype=BF16NP)
    out["c_ones_col"] = np.ones((128, 1), dtype=np.float32)
    out["c_ones_row"] = np.ones((1, 128), dtype=np.float32)
    out["c_ones_row_bf"] = np.ones((1, 128), dtype=BF16NP)
    P = np.zeros((DH, DH), dtype=np.float32)
    for j in range(DH // 2):
        P[j, 2 * j + 1] = -1.0
    for i in range(DH // 2):
        P[DH // 2 + i, 2 * i] = 1.0
    P2 = np.zeros((128, 128), dtype=np.float32)
    P2[:DH, :DH] = P.T
    P2[DH:, DH:] = P.T
    out["c_p2"] = P2.astype(BF16NP)
    invf = (1.0 / 10000.0 ** (np.arange(0, DH, 2, dtype=np.float32) / DH)).astype(np.float32)
    t = np.arange(S, dtype=np.float32)
    freqs = np.einsum("i,j->ij", t, invf).astype(np.float32)
    emb = np.concatenate([freqs, freqs], axis=-1)
    cosT = np.cos(emb).T.astype(np.float32)
    sinT = np.sin(emb).T.astype(np.float32)
    out["c_l0cos"] = np.concatenate([cosT, cosT], axis=0).copy()
    out["c_l0sin"] = np.concatenate([sinT, sinT], axis=0).copy()
    fr1 = (t[:, None] * invf[None, :]).astype(np.float32)
    out["c_l1cos"] = np.tile(np.cos(fr1), (1, NH)).astype(np.float32).copy()
    out["c_l1sin"] = np.tile(np.sin(fr1), (1, NH)).astype(np.float32).copy()
    sel4 = np.zeros((4, 4 * 128), dtype=np.float32)
    for r in range(4):
        sel4[r, r * 128:(r + 1) * 128] = 1.0
    out["c_sel4"] = sel4
    sel8 = np.zeros((8, 8 * 128), dtype=np.float32)
    for r in range(8):
        sel8[r, r * 128:(r + 1) * 128] = 1.0
    out["c_sel8"] = sel8
    pm = np.zeros((cfg["NCORES"], 1), dtype=np.float32)
    pm[:core_id] = 1.0
    out["c_prevmask"] = pm
    out["c_allmask"] = np.ones((cfg["NCORES"], 1), dtype=np.float32)
    out["c_eps5"] = np.full((128, 1), 1e-5, dtype=np.float32)
    he = np.zeros((1, 128), dtype=np.float32); he[0, :64] = 1.0
    ho = np.zeros((1, 128), dtype=np.float32); ho[0, 64:] = 1.0
    out["c_half_even"] = he
    out["c_half_odd"] = ho
    return out


CONST_SPECS = {
    "c_ident": ([128, 128], BF16),
    "c_identf": ([128, 128], F32),
    "c_triT": ([128, 128], F32),
    "c_ones_kxm": ([128, 128], BF16),
    "c_ones_col": ([128, 1], F32),
    "c_ones_row": ([1, 128], F32),
    "c_ones_row_bf": ([1, 128], BF16),
    "c_p2": ([128, 128], BF16),
    "c_sel4": ([4, 4 * 128], F32),
    "c_sel8": ([8, 8 * 128], F32),
    "c_eps5": ([128, 1], F32),
    "c_prevmask": (None, F32),
    "c_allmask": (None, F32),
    "c_half_even": ([1, 128], F32),
    "c_half_odd": ([1, 128], F32),
}



ENGINES = ("sync", "tensor", "vector", "scalar", "gpsimd")


class KB:

    def __init__(self, nc, block):
        self.nc = nc
        self.block = block
        self.sems = {}
        self.counts = {}
        self.waited = {e: {} for e in ENGINES}
        self.q = {e: [] for e in ENGINES}
        for e in ("tensor", "vector", "scalar", "gpsimd"):
            self._mk_sem("e_" + e)

    def _mk_sem(self, name):
        self.sems[name] = self.nc.alloc_semaphore(name=name)
        self.counts[name] = 0

    def flush(self):
        for eng in ENGINES:
            ts = self.q[eng]
            if not ts:
                continue
            self.q[eng] = []

            def body(e, ts=ts):
                for t in ts:
                    t(e)
            getattr(self.block, eng)(body)

    def ins(self, eng, fn, deps=(), sem_name=None):
        sem_name = sem_name or ("e_" + eng)
        if sem_name not in self.sems:
            self._mk_sem(sem_name)
        sem = self.sems[sem_name]
        self.counts[sem_name] += 1
        cnt = self.counts[sem_name]
        deps = tuple(deps)

        def body(e):
            self._do_waits(e, eng, deps)
            fn(e).then_inc(sem, 1)
        self.q[eng].append(body)
        return (sem_name, cnt)

    def dma(self, queue, out, in_, deps=(), stream=None, **kw):
        name = stream or ("dma_" + queue)
        if name not in self.sems:
            self._mk_sem(name)
        sem = self.sems[name]
        self.counts[name] += 16
        cnt = self.counts[name]
        deps = tuple(deps)

        def body(e):
            self._do_waits(e, queue, deps)
            e.dma_start(out=out, in_=in_, **kw).then_inc(sem, 16)
        self.q[queue].append(body)
        return (name, cnt)

    def _do_waits(self, e, eng, deps):
        w = self.waited[eng]
        my = "e_" + eng
        agg = {}
        for sem_name, val in deps:
            if sem_name == my and self.counts[my] < val:
                raise RuntimeError("dep on future same-engine instruction")
            if val > agg.get(sem_name, 0):
                agg[sem_name] = val
        for sem_name, val in agg.items():
            if w.get(sem_name, 0) >= val:
                continue
            e.wait_ge(self.sems[sem_name], val)
            w[sem_name] = val

    def wait(self, eng, deps):
        deps = tuple(deps)

        def body(e):
            self._do_waits(e, eng, deps)
        self.q[eng].append(body)

    def barrier(self, eng):
        return ("e_" + eng, self.counts["e_" + eng])

    def full_barrier(self, exclude=()):
        toks = [(n, self.counts[n]) for n in self.sems if n not in exclude]
        for e in ENGINES:
            self.wait(e, toks)


def fm_ap(fm, c, t0, width):
    nt = width // 128
    return fm[t0 // 128: t0 // 128 + nt, c * 128:(c + 1) * 128, :].rearrange(
        "n p t -> p n t")


def sb3(sb_slice, tile=128):
    return sb_slice.rearrange("p (n t) -> p n t", t=tile)



def emit_tm2fm(kb, cfg, st, src_ap, dst_ap, sfx, rev=False):
    T, D = cfg["T"], cfg["D"]
    nc = kb.nc
    with (
        nc.sbuf_tensor("tf_in" + sfx, [128, D], F32) as tin_sb,
        nc.sbuf_tensor("tf_out" + sfx, [128, D], F32) as tout_sb,
        nc.psum_tensor("tf_ps0" + sfx, [128, 512], F32) as ps0,
        nc.psum_tensor("tf_ps1" + sfx, [128, 512], F32) as ps1,
    ):
        ps = (ps0, ps1)
        prev_tts, prev_out = [], []
        for tile in range(T // 128):
            if rev:
                src = src_ap[tile, :, :].rearrange("(c p) t -> p c t", p=128)
            else:
                src = src_ap[tile * 128:(tile + 1) * 128, :]
            tin = kb.dma("sync", sb3(tin_sb[:]) if rev else tin_sb[:], src,
                         deps=prev_tts, stream="d_tf_in")
            evs, tts = [], []
            for g in range(2):
                gtts = []
                for c in range(4):
                    ch = g * 4 + c
                    gtts.append(kb.ins(
                        "tensor",
                        lambda e, ch=ch, g=g, c=c: e.transpose(
                            ps[g][:, c * 128:(c + 1) * 128],
                            tin_sb[:, ch * 128:(ch + 1) * 128], st["c_identf"][:]),
                        deps=[tin] + prev_out))
                evs.append(kb.ins(
                    "vector",
                    lambda e, g=g: e.tensor_copy(tout_sb[:, g * 512:(g + 1) * 512], ps[g][:]),
                    deps=gtts))
                tts += gtts
            if rev:
                dst = dst_ap[tile * 128:(tile + 1) * 128, :]
            else:
                dst = dst_ap[tile, :, :].rearrange("(c p) t -> p c t", p=128)
            tok = kb.dma("sync", dst, tout_sb[:] if rev else sb3(tout_sb[:]),
                         deps=evs, stream="d_tf_out")
            prev_tts = tts
            prev_out = [tok]
    return [tok]



def emit_mod(kb, cfg, st, c_ap, wmod_ap, modT_ap, sfx):
    D, BPC = cfg["D"], cfg["BPC"]
    NV = 6
    nc = kb.nc
    KC = D // 128
    with (
        nc.sbuf_tensor("md_pad" + sfx, [128, D], F32) as pad,
        nc.sbuf_tensor("md_siluT" + sfx, [128, KC * BPC], F32) as siluT,
        nc.sbuf_tensor("md_w" + sfx, [128, 2 * 512], F32) as wmt_d,
        nc.sbuf_tensor("md_mod" + sfx, [128, NV * D], F32) as modp,
        nc.sbuf_tensor("md_t" + sfx, [128, D], F32) as tmp,
        nc.sbuf_tensor("md_t2" + sfx, [128, D], F32) as tmp2,
        nc.sbuf_tensor("md_modT" + sfx, [128, NV * KC * BPC], F32) as modT,
        nc.psum_tensor("md_ps" + sfx, [128, 512], F32) as ps,
        nc.psum_tensor("md_ps2" + sfx, [128, 512], F32) as ps2,
    ):
        z = kb.ins("vector", lambda e: e.memset(pad[:], 0.0))
        zm = kb.ins("vector", lambda e: e.memset(modp[:], 0.0))
        cl = kb.dma("sync", pad[0:BPC, :], c_ap, deps=[z], stream="d_mod")
        sl = kb.ins("scalar", lambda e: e.activation(pad[0:BPC, :], pad[0:BPC, :], AF.Silu),
                    deps=[cl])
        ev = None
        for k in range(KC):
            tt = kb.ins("tensor",
                        lambda e, k=k: e.transpose(ps[:, 0:128], pad[:, k * 128:(k + 1) * 128],
                                                   st["c_identf"][:]),
                        deps=[sl] + ([ev] if ev else []))
            ev = kb.ins("vector",
                        lambda e, k=k: e.tensor_copy(siluT[:, k * BPC:(k + 1) * BPC], ps[:, 0:BPC]),
                        deps=[tt])
        evm = None
        prev_mm = {}
        for j in range(NV * D // 512):
            toks = []
            for k in range(KC):
                w_t = wmt_d[:, (k % 2) * 512:(k % 2 + 1) * 512]
                wl = kb.dma("gpsimd", w_t,
                            wmod_ap[k * 128:(k + 1) * 128, j * 512:(j + 1) * 512],
                            deps=[prev_mm[k % 2]] if k % 2 in prev_mm else [],
                            stream="d_modw%d" % (k % 2))
                mm = kb.ins(
                    "tensor",
                    lambda e, k=k, w_t=w_t: e.matmul(
                        ps2[0:BPC, :], siluT[:, k * BPC:(k + 1) * BPC], w_t,
                        start=(k == 0), stop=(k == KC - 1)),
                    deps=[wl, ev] + ([evm] if evm else []))
                toks.append(mm)
                prev_mm[k % 2] = mm
            evm = kb.ins("vector",
                         lambda e, j=j: e.tensor_copy(modp[0:BPC, j * 512:(j + 1) * 512],
                                                      ps2[0:BPC, :]),
                         deps=[toks[-1], zm])
        post = [evm, zm]
        for v in (2, 5):
            sl_ = modp[0:BPC, v * D:(v + 1) * D]
            ta = kb.ins("scalar", lambda e, sl_=sl_: e.activation(
                tmp[0:BPC, 0:D], sl_, AF.Abs), deps=post)
            tc = kb.ins("vector", lambda e: e.tensor_scalar_max(
                tmp[0:BPC, 0:D], tmp[0:BPC, 0:D], 0.25), deps=[ta])
            tm = kb.ins("vector", lambda e, sl_=sl_: e.tensor_single_scalar(
                tmp2[0:BPC, 0:D], sl_, 0.0, ALU.is_ge), deps=post)
            ts = kb.ins("vector", lambda e: e.tensor_scalar(
                tmp2[0:BPC, 0:D], tmp2[0:BPC, 0:D], 2.0, -1.0, ALU.mult, ALU.add),
                deps=[tm])
            tg = kb.ins("vector", lambda e, sl_=sl_: e.tensor_mul(
                sl_, tmp[0:BPC, 0:D], tmp2[0:BPC, 0:D]), deps=[tc, ts])
            post = post + [tg]
        for v in (1, 4):
            sl_ = modp[0:BPC, v * D:(v + 1) * D]
            post.append(kb.ins("vector",
                               lambda e, sl_=sl_: e.tensor_scalar_add(sl_, sl_, 1.0),
                               deps=post))
        evq = None
        for q in range(NV * KC):
            tq = kb.ins("tensor",
                        lambda e, q=q: e.transpose(ps[:, 0:128], modp[:, q * 128:(q + 1) * 128],
                                                   st["c_identf"][:]),
                        deps=post + ([evq] if evq else []))
            evq = kb.ins("vector",
                         lambda e, q=q: e.tensor_copy(modT[:, q * BPC:(q + 1) * BPC], ps[:, 0:BPC]),
                         deps=[tq])
        tok = kb.dma("sync", modT_ap[:], modT[:], deps=[evq], stream="d_mod")
    return [tok]


def modcol(cfg, v, ch, b):
    return (v * (cfg["D"] // 128) + ch) * cfg["BPC"] + b



def emit_ln(kb, cfg, st, x_ap, h_ap, modT_sb, modT_tok, v_shift, v_scale, sfx,
            eps=1e-5):
    D, T, S, BPC = cfg["D"], cfg["T"], cfg["S"], cfg["BPC"]
    KC = D // 128
    nc = kb.nc
    SL = min(512, S)
    assert S % SL == 0
    with (
        nc.sbuf_tensor("ln_x" + sfx, [128, 2 * KC * SL], F32) as xs_d,
        nc.sbuf_tensor("ln_xb" + sfx, [128, KC * SL], BF16) as xb,
        nc.sbuf_tensor("ln_sq" + sfx, [128, KC * SL], BF16) as sq,
        nc.sbuf_tensor("ln_mean" + sfx, [128, SL], F32) as mean,
        nc.sbuf_tensor("ln_t" + sfx, [128, SL], F32) as tvar,
        nc.sbuf_tensor("ln_rstd" + sfx, [128, SL], F32) as rstd,
        nc.sbuf_tensor("ln_h" + sfx, [128, KC * SL], BF16) as hb,
        nc.psum_tensor("ln_pm" + sfx, [128, SL], F32) as pm,
        nc.psum_tensor("ln_pq" + sfx, [128, SL], F32) as pq,
    ):
        prev = {}
        for sl in range(T // SL):
            xs = xs_d[:, (sl % 2) * KC * SL:(sl % 2 + 1) * KC * SL]
            b = (sl * SL) // S
            loads = []
            for c in range(KC):
                loads.append((c, kb.dma(
                    "sync", sb3(xs[:, c * SL:(c + 1) * SL]), fm_ap(x_ap, c, sl * SL, SL),
                    deps=prev.get(("x", sl % 2), []), stream="d_ln_x")))
            last_ld = loads[-1][1]
            casts, sqs = [], []
            for c, ld in loads:
                casts.append(kb.ins("vector", lambda e, c=c, xs=xs: e.tensor_copy(
                    xb[:, c * SL:(c + 1) * SL], xs[:, c * SL:(c + 1) * SL]),
                    deps=[last_ld] + prev.get("xb", [])))
                sqs.append(kb.ins("scalar", lambda e, c=c, xs=xs: e.activation(
                    sq[:, c * SL:(c + 1) * SL], xs[:, c * SL:(c + 1) * SL], AF.Square),
                    deps=[last_ld] + prev.get("sq", [])))
            mms, mqs = [], []
            for c in range(KC):
                mms.append(kb.ins("tensor", lambda e, c=c: e.matmul(
                    pm[:], st["c_ones_kxm"][:], xb[:, c * SL:(c + 1) * SL],
                    start=(c == 0), stop=(c == KC - 1)),
                    deps=[casts[c]] + prev.get("pm", [])))
            for c in range(KC):
                mqs.append(kb.ins("tensor", lambda e, c=c: e.matmul(
                    pq[:], st["c_ones_kxm"][:], sq[:, c * SL:(c + 1) * SL],
                    start=(c == 0), stop=(c == KC - 1)),
                    deps=[sqs[c]] + prev.get("pq", [])))
            t_mean = kb.ins("scalar", lambda e: e.activation(
                mean[:], pm[:], AF.Copy, scale=1.0 / D),
                deps=[mms[-1]] + prev.get("mean", []))
            t_m2 = kb.ins("vector", lambda e: e.tensor_mul(tvar[:], mean[:], mean[:]),
                          deps=[t_mean] + prev.get("tvar", []))
            t_var = kb.ins("vector", lambda e: e.scalar_tensor_tensor(
                tvar[:], pq[:], 1.0 / D, tvar[:], ALU.mult, ALU.subtract),
                deps=[mqs[-1], t_m2])
            t_sd = kb.ins("scalar", lambda e: e.activation(
                tvar[:], tvar[:], AF.Sqrt, bias=st["c_eps5"][:]),
                deps=[t_var] + st["tok"])
            t_rs = kb.ins("vector", lambda e: e.reciprocal(rstd[:], tvar[:]),
                          deps=[t_sd] + prev.get("rstd", []))
            houts = []
            hdeps = prev.get("hdma", [])
            for c in range(KC):
                t1 = kb.ins("vector", lambda e, c=c, xs=xs: e.tensor_sub(
                    xs[:, c * SL:(c + 1) * SL],
                    xs[:, c * SL:(c + 1) * SL], mean[:]),
                    deps=[t_mean, mms[-1], mqs[-1]])
                t2 = kb.ins("vector", lambda e, c=c, xs=xs: e.tensor_mul(
                    xs[:, c * SL:(c + 1) * SL], xs[:, c * SL:(c + 1) * SL], rstd[:]),
                    deps=[t_rs, t1])
                t3 = kb.ins("vector", lambda e, c=c, b=b, xs=xs: e.tensor_scalar(
                    hb[:, c * SL:(c + 1) * SL], xs[:, c * SL:(c + 1) * SL],
                    modT_sb[:, modcol(cfg, v_scale, c, b):modcol(cfg, v_scale, c, b) + 1],
                    modT_sb[:, modcol(cfg, v_shift, c, b):modcol(cfg, v_shift, c, b) + 1],
                    ALU.mult, ALU.add),
                    deps=[t2, modT_tok] + hdeps)
                houts.append(t3)
            hdma = []
            for c in range(KC):
                hdma.append(kb.dma("sync", fm_ap(h_ap, c, sl * SL, SL),
                                   sb3(hb[:, c * SL:(c + 1) * SL]), deps=[houts[c]],
                                   stream="d_ln_h"))
            prev = {("x", sl % 2): houts,
                    "xb": mms, "sq": mqs,
                    "pm": [t_mean], "pq": [t_var],
                    "mean": houts, "tvar": [t_rs], "rstd": houts,
                    "hdma": hdma}
    return hdma



def scratch_specs(cfg):
    T, D, NH, S, BPC = cfg["T"], cfg["D"], cfg["NH"], cfg["S"], cfg["BPC"]
    return {
        "x0fm": ([T // 128, D, 128], F32),
        "x1fm": ([T // 128, D, 128], F32),
        "x2fm": ([T // 128, D, 128], F32),
        "x3fm": ([T // 128, D, 128], F32),
        "h1": ([T // 128, D, 128], BF16),
        "h2": ([T // 128, D, 128], BF16),
        "h3": ([T // 128, D, 128], BF16),
        "h4": ([T // 128, D, 128], BF16),
        "modT0": ([128, 6 * (D // 128) * BPC], F32),
        "modT1": ([128, 6 * (D // 128) * BPC], F32),
        "wqT": ([D, D], BF16),
        "wkT": ([D, D], BF16),
        "wvT": ([D, D], BF16),
        "woT": ([D, D], BF16),
        "qro": ([D, T], BF16),
        "kro": ([D, T], BF16),
        "vtm": ([T, NH * 65], BF16),
        "attnfm": ([D, T], BF16),
        "cntloc": ([1, 16], F32),
        "cntg": ([cfg["NCORES"], 16], F32),
        "w8": ([8, T], F32),
        "l1q": ([T, D], BF16),
        "l1k": ([T, D], BF16),
        "l1v": ([T, D], BF16),
        "l1at": ([D, T], BF16),
        "x4fm": ([T // 128, D, 128], F32),
        "w1b": ([8, D, 4096], BF16),
        "w2b": ([8, 4096, D], BF16),
        "out": ([T, D], F32),
    }


def input_specs_per_core(cfg):
    D, BPC, NH = cfg["D"], cfg["BPC"], cfg["NH"]
    return {
        "L0_attn_hidden_states": ([BPC * cfg["S"], D], F32),
        "L0_attn_Wq": ([D, D], F32), "L0_attn_Wk": ([D, D], F32),
        "L0_attn_Wv": ([D, D], F32), "L0_attn_Wo": ([D, D], F32),
        "L0_ffn_wg": ([D, 8], F32),
        "L0_ffn_w1": ([8, D, 4096], F32), "L0_ffn_b1": ([8, 4096], F32),
        "L0_ffn_w2": ([8, 4096, D], F32), "L0_ffn_b2": ([8, D], F32),
        "L0_c": ([BPC, D], F32), "L0_wmod": ([D, 6 * D], F32),
        "L1_attn_Wq": ([D, D], F32), "L1_attn_bq": ([1, D], F32),
        "L1_attn_Wdown": ([D, 256], F32), "L1_attn_bdown": ([1, 256], F32),
        "L1_attn_Wup": ([256, 2 * D], F32), "L1_attn_bup": ([1, 2 * D], F32),
        "L1_attn_Wout": ([D, D], F32), "L1_attn_bout": ([1, D], F32),
        "L1_ffn_kernel": ([D, D], BF16),
        "L1_ffn_input_scale": ([1, 1], F32), "L1_ffn_kernel_scale": ([1, 1], F32),
        "L1_ffn_bias": ([1, D], BF16),
        "L1_c": ([BPC, D], F32), "L1_wmod": ([D, 6 * D], F32),
    }


def emit_setup(kb, cfg, aps):
    nc = kb.nc
    st = {}
    toks = []
    es = contextlib.ExitStack()
    for name, (shape, dt) in CONST_SPECS.items():
        if shape is None:
            shape = [cfg["NCORES"], 1]
        t = es.enter_context(nc.sbuf_tensor("st_" + name, shape, dt))
        st[name] = t
        toks.append(kb.dma("sync", t[:], aps[name][:], stream="d_setup"))
    st["tok"] = toks
    st["es"] = es
    return st


def build_program(cfg, phases, external_scratch):
    nc = bass.Bass("TRN2", target_bir_lowering=False)
    aps = {}
    ins_sp = input_specs_per_core(cfg)
    for name, (shape, dt) in ins_sp.items():
        aps[name] = nc.dram_tensor(name, shape, dt, kind="ExternalInput").ap()
    for name, arr in host_constants(cfg, 0).items():
        dt = BF16 if arr.dtype == BF16NP else F32
        aps[name] = nc.dram_tensor(name, list(arr.shape), dt,
                                   kind="ExternalInput").ap()
    sc = scratch_specs(cfg)
    for name, (shape, dt) in sc.items():
        kind = {"in": "ExternalInput", "out": "ExternalOutput"}.get(
            external_scratch.get(name), "Internal")
        aps[name] = nc.dram_tensor(name, shape, dt, kind=kind).ap()
    with nc.Block() as block:
        kb = KB(nc, block)
        st = emit_setup(kb, cfg, aps)
        for _e in ENGINES:
            kb.wait(_e, [("d_setup", kb.counts["d_setup"])])
        kb.cross = {}
        wconv_streams = ("d_wc0", "d_wc1", "d_wco0", "d_wco1", "e_wconv")
        final = []
        for fn, kw in phases:
            final = fn(kb, cfg, st, aps, **kw)
            kb.full_barrier(exclude=wconv_streams)
        for qname in list(kb.sems):
            if qname.startswith("d"):
                kb.wait("gpsimd", [(qname, kb.counts[qname])])
        for en in ("tensor", "vector", "scalar"):
            kb.wait("gpsimd", [kb.barrier(en)])
        kb.flush()
        st["es"].close()
    return nc



def ph_ingest(kb, cfg, st, aps, dst="x0fm"):
    return emit_tm2fm(kb, cfg, st, aps["L0_attn_hidden_states"], aps[dst], "_ing")


def ph_mod(kb, cfg, st, aps, layer):
    return emit_mod(kb, cfg, st, aps[f"L{layer}_c"][:, :], aps[f"L{layer}_wmod"],
                    aps[f"modT{layer}"], f"_m{layer}")


def ph_ln(kb, cfg, st, aps, x, h, layer, which):
    v_shift, v_scale = (0, 1) if which == "A" else (3, 4)
    sfx = f"_ln{layer}{which}"
    nc = kb.nc
    with nc.sbuf_tensor("lnmod" + sfx, [128, 6 * (cfg["D"] // 128) * cfg["BPC"]],
                        F32) as modT_sb:
        mt = kb.dma("sync", modT_sb[:], aps[f"modT{layer}"][:], stream="d_lnm")
        r = emit_ln(kb, cfg, st, aps[x], aps[h], modT_sb, mt, v_shift, v_scale, sfx)
    return r



def emit_wT(kb, cfg, st, aps, pairs):
    D = cfg["D"]
    nc = kb.nc
    with (
        nc.sbuf_tensor("wt_in", [128, D], BF16) as win,
        nc.sbuf_tensor("wt_out", [128, D], BF16) as wout,
        nc.psum_tensor("wt_ps", [128, 512], BF16) as ps0,
        nc.psum_tensor("wt_ps2", [128, 512], BF16) as ps1,
    ):
        ps = (ps0, ps1)
        prev_tts, prev_out = [], []
        for (src, dst) in pairs:
            for i in range(D // 128):
                ld = kb.dma("gpsimd", win[:], aps[src][i * 128:(i + 1) * 128, :],
                            deps=prev_tts, stream="d_wt")
                evs, tts = [], []
                for g in range(2):
                    gt = []
                    for c in range(4):
                        j = g * 4 + c
                        gt.append(kb.ins(
                            "tensor",
                            lambda e, j=j, g=g, c=c: e.transpose(
                                ps[g][:, c * 128:(c + 1) * 128],
                                win[:, j * 128:(j + 1) * 128], st["c_ident"][:]),
                            deps=[ld] + prev_out))
                    evs.append(kb.ins(
                        "vector",
                        lambda e, g=g: e.tensor_copy(wout[:, g * 512:(g + 1) * 512],
                                                     ps[g][:]),
                        deps=gt))
                    tts += gt
                outs = []
                for g in range(2):
                    dstap = aps[dst][g * 512:(g + 1) * 512, i * 128:(i + 1) * 128] \
                        .rearrange("(n p) o -> p n o", p=128)
                    outs.append(kb.dma("sync", dstap,
                                       sb3(wout[:, g * 512:(g + 1) * 512]),
                                       deps=[evs[g]], stream="d_wt_o"))
                prev_tts = tts
                prev_out = outs
    return prev_out


def ph_wT(kb, cfg, st, aps):
    return emit_wT(kb, cfg, st, aps,
                   [("L0_attn_Wq", "wqT"), ("L0_attn_Wk", "wkT"),
                    ("L0_attn_Wv", "wvT"), ("L0_attn_Wo", "woT")])



def ph_l0qkv(kb, cfg, st, aps):
    D, S, T, NH = cfg["D"], cfg["S"], cfg["T"], cfg["NH"]
    KC = D // 128
    SL = min(512, S)
    nc = kb.nc
    with (
        nc.sbuf_tensor("qk_wq", [128, KC * D], BF16) as wq,
        nc.sbuf_tensor("qk_wk", [128, KC * D], BF16) as wk,
        nc.sbuf_tensor("qk_wv", [128, KC * D], BF16) as wv,
        nc.sbuf_tensor("qk_h", [128, 2 * KC * SL], BF16) as h_d,
        nc.sbuf_tensor("qk_cos", [128, SL], F32) as cos_sb,
        nc.sbuf_tensor("qk_sin", [128, SL], F32) as sin_sb,
        nc.sbuf_tensor("qk_qb", [128, SL], BF16) as qbf,
        nc.sbuf_tensor("qk_t1", [128, SL], F32) as tr1,
        nc.sbuf_tensor("qk_ro", [128, SL], BF16) as ro_sb,
        nc.sbuf_tensor("qk_v", [128, NH * 65], BF16) as v_sb,
        nc.psum_tensor("qk_p0", [128, 512], F32) as p0,
        nc.psum_tensor("qk_p1", [128, 512], F32) as p1,
        nc.psum_tensor("qk_pv", [128, 512], F32) as pv,
    ):
        wls = []
        for name, tile in (("wqT", wq), ("wkT", wk), ("wvT", wv)):
            wls.append(kb.dma("gpsimd", sb3(tile[:], D),
                              aps[name].rearrange("(kc kp) o -> kp kc o", kp=128),
                              stream="d_qk_w"))
        wl = wls[-1]
        prev = {}
        for sl in range(T // SL):
            b = (sl * SL) // S
            p0s = sl * SL - b * S
            hsb = h_d[:, (sl % 2) * KC * SL:(sl % 2 + 1) * KC * SL]
            hl = []
            for c in range(KC):
                hl.append(kb.dma("sync", sb3(hsb[:, c * SL:(c + 1) * SL]),
                                 fm_ap(aps["h1"], c, sl * SL, SL),
                                 deps=prev.get(("h", sl % 2), []),
                                 stream="d_qk_h%d" % (sl % 2)))
            hl = [hl[-1]]
            cl = kb.dma("sync", cos_sb[:], aps["c_l0cos"][:, p0s:p0s + SL],
                        deps=prev.get("cos", []), stream="d_qk_c")
            sl_t = kb.dma("sync", sin_sb[:], aps["c_l0sin"][:, p0s:p0s + SL],
                          deps=prev.get("cos", []), stream="d_qk_s")
            ro_dmas = []
            for which, wt, dst in (("q", wq, "qro"), ("k", wk, "kro")):
                for och in range(KC):
                    mms = []
                    for k in range(KC):
                        mms.append(kb.ins("tensor", lambda e, k=k, och=och, wt=wt, hsb=hsb: e.matmul(
                            p0[:, 0:SL], wt[:, k * D + och * 128: k * D + (och + 1) * 128],
                            hsb[:, k * SL:(k + 1) * SL],
                            start=(k == 0), stop=(k == KC - 1)),
                            deps=[wl] + hl + prev.get("p0", [])))
                    ev = kb.ins("vector", lambda e: e.tensor_copy(qbf[:], p0[:, 0:SL]),
                                deps=mms[-1:] + prev.get("qbf", []))
                    pq = kb.ins("tensor", lambda e: e.matmul(
                        p1[:, 0:SL], st["c_p2"][:], qbf[:], start=True, stop=True),
                        deps=[ev] + prev.get("p1", []))
                    t1 = kb.ins("vector", lambda e: e.tensor_mul(tr1[:], qbf[:], cos_sb[:]),
                                deps=[ev, cl] + prev.get("tr1", []))
                    t2 = kb.ins("vector", lambda e: e.scalar_tensor_tensor(
                        ro_sb[:], p1[:, 0:SL], 1.0, sin_sb[:], ALU.mult, ALU.mult),
                        deps=[pq, sl_t] + prev.get("ro", []))
                    t3 = kb.ins("vector", lambda e: e.tensor_add(ro_sb[:], ro_sb[:], tr1[:]),
                                deps=[t1, t2])
                    dma = kb.dma("sync", aps[dst][och * 128:(och + 1) * 128,
                                                  sl * SL:sl * SL + SL],
                                 ro_sb[:], deps=[t3], stream="d_qk_ro")
                    ro_dmas.append(dma)
                    prev["p0"] = [ev]
                    prev["qbf"] = [t1, t2]
                    prev["p1"] = [t2]
                    prev["tr1"] = [t3]
                    prev["ro"] = [dma]
            vdmas = []
            for tt in range(SL // 128):
                vouts = []
                for oh in range(2):
                    mms = []
                    for k in range(KC):
                        mms.append(kb.ins("tensor", lambda e, k=k, tt=tt, oh=oh, hsb=hsb: e.matmul(
                            pv[:], hsb[:, k * SL + tt * 128: k * SL + (tt + 1) * 128],
                            wv[:, k * D + oh * 512: k * D + (oh + 1) * 512],
                            start=(k == 0), stop=(k == KC - 1)),
                            deps=[wl] + hl + prev.get("pv", [])))
                    dst = v_sb[:, oh * 8 * 65:(oh + 1) * 8 * 65].rearrange(
                        "p (h x) -> p h x", x=65)[:, :, 0:64]
                    ev = kb.ins("vector", lambda e, dst=dst: e.tensor_copy(dst, sb3(pv[:], 64)),
                                deps=mms[-1:] + prev.get("v", []))
                    vouts.append(ev)
                    prev["pv"] = [ev]
                ones = kb.ins("vector", lambda e: e.memset(
                    v_sb[:].rearrange("p (h x) -> p h x", x=65)[:, :, 64:65], 1.0),
                    deps=vouts + prev.get("v", []))
                vd = kb.dma("sync", aps["vtm"][sl * SL + tt * 128: sl * SL + (tt + 1) * 128, :],
                            v_sb[:], deps=[ones] + vouts, stream="d_qk_v")
                vdmas.append(vd)
                prev["v"] = [vd]
            prev[("h", sl % 2)] = mms + [ev]
            prev["cos"] = [t1, t2, dma]
    return ro_dmas + vdmas



def ph_l0attn(kb, cfg, st, aps):
    D, S, T, NH, DH = cfg["D"], cfg["S"], cfg["T"], cfg["NH"], cfg["DH"]
    BPC = cfg["BPC"]
    QB = min(512, S)
    NTC = S // 128
    KW = min(512, S)
    NKW = S // KW
    W2 = 2 * QB
    nsub = QB // 128
    nc = kb.nc
    with contextlib.ExitStack() as es:
        def sbuf(n, sh, dt):
            return es.enter_context(nc.sbuf_tensor(n, sh, dt))
        def psum(n, sh, dt=F32):
            return es.enter_context(nc.psum_tensor(n, sh, dt))
        kaug_d = sbuf("at_kaug", [128, 2 * S], BF16)
        vaug_d = sbuf("at_vaug", [128, 2 * NTC * 65], BF16)
        qaug_d = sbuf("at_qaug", [128, 4 * QB], BF16)
        pt_sb = sbuf("at_pt", [128, 2 * NTC * QB], BF16)
        mch_d = sbuf("at_mch", [128, 2 * 4 * NKW], F32)
        mall_d = sbuf("at_mall", [128, 2 * 4], F32)
        m4_d = sbuf("at_m4", [4, 2 * 128], F32)
        linv_d = sbuf("at_linv", [1, 2 * QB], F32)
        lrep = sbuf("at_lrep", [128, QB], F32)
        att_sb = sbuf("at_out", [128, QB], BF16)
        psA = psum("at_psA", [128, 2 * KW])
        psB = psum("at_psB", [128, 2 * W2])
        pAV0 = psum("at_pAV0", [128, 512])
        pAV1 = psum("at_pAV1", [128, 512])
        pAV = (pAV0, pAV1)
        prev = {}
        for b in range(BPC):
            for hp in range(NH // 2):
                kls, vls = [], []
                for hh in range(2):
                    h = 2 * hp + hh
                    kaug = kaug_d[:, hh * S:(hh + 1) * S]
                    kls.append(kb.dma(
                        "sync", kaug[0:64, :],
                        aps["kro"][h * 64:(h + 1) * 64, b * S:(b + 1) * S],
                        deps=prev.get("kaug", []), stream="d_at_k%d" % hh))
                    kls.append(kb.ins("vector", lambda e, kaug=kaug: e.memset(
                        kaug[64:65, :], 1.0), deps=prev.get("kaug", [])))
                    vls.append(kb.dma(
                        "gpsimd", sb3(vaug_d[:, hh * NTC * 65:(hh + 1) * NTC * 65], 65),
                        aps["vtm"][b * S:(b + 1) * S, h * 65:(h + 1) * 65]
                        .rearrange("(tc tp) x -> tp tc x", tp=128),
                        deps=prev.get("vaug", []), stream="d_at_v%d" % hh))
                qb_toks = []
                for qb in range(S // QB):
                    qaugs, qls, stats = {}, {}, {}
                    for hh in range(2):
                        h = 2 * hp + hh
                        qaug = qaug_d[:, (2 * (qb % 2) + hh) * QB:
                                      (2 * (qb % 2) + hh + 1) * QB]
                        qaugs[hh] = qaug
                        qls[hh] = kb.dma("sync", qaug[0:64, :],
                                         aps["qro"][h * 64:(h + 1) * 64,
                                                    b * S + qb * QB: b * S + (qb + 1) * QB],
                                         deps=prev.get(("qaug", hh, qb % 2), []),
                                         stream="d_at_q%d" % hh)
                    mx_all = {}
                    for hh in range(2):
                        kaug = kaug_d[:, hh * S:(hh + 1) * S]
                        qaug = qaugs[hh]
                        mch = mch_d[:, hh * 4 * NKW:(hh + 1) * 4 * NKW]
                        mall = mall_d[:, hh * 4:(hh + 1) * 4]
                        mx_toks = []
                        for sub in range(nsub):
                            rds = []
                            for tc2 in range(NKW):
                                slot = tc2 % 2
                                mmA = kb.ins("tensor",
                                    lambda e, sub=sub, tc2=tc2, slot=slot, qaug=qaug, kaug=kaug: e.matmul(
                                        psA[:, slot * KW:(slot + 1) * KW],
                                        qaug[0:64, sub * 128:(sub + 1) * 128],
                                        kaug[0:64, tc2 * KW:(tc2 + 1) * KW],
                                        start=True, stop=True),
                                    deps=[qls[hh]] + kls + prev.get(("psA", slot), []))
                                rd = kb.ins("vector",
                                    lambda e, sub=sub, tc2=tc2, slot=slot, mch=mch: e.reduce_max(
                                        mch[:, sub * NKW + tc2: sub * NKW + tc2 + 1],
                                        psA[:, slot * KW:(slot + 1) * KW], axis=AX.X),
                                    deps=[mmA] + prev.get(("mch", hh), []))
                                rds.append(rd)
                                prev[("psA", slot)] = [rd]
                            mx_toks.append(kb.ins("vector",
                                lambda e, sub=sub, mch=mch, mall=mall: e.reduce_max(
                                    mall[:, sub:sub + 1],
                                    mch[:, sub * NKW:(sub + 1) * NKW], axis=AX.X),
                                deps=rds + prev.get(("mall", hh), [])))
                        prev[("mch", hh)] = mx_toks
                        mx_all[hh] = mx_toks
                    for hh in range(2):
                        mall = mall_d[:, hh * 4:(hh + 1) * 4]
                        m4 = m4_d[:, hh * 128:(hh + 1) * 128]
                        qaug = qaugs[hh]
                        neg = kb.ins("scalar", lambda e, mall=mall: e.mul(
                            mall[:, 0:nsub], mall[:, 0:nsub], -1.0), deps=mx_all[hh])
                        ttm = kb.ins("tensor", lambda e, mall=mall: e.transpose(
                            psB[0:nsub, 0:128], mall[:, 0:nsub], st["c_identf"][:]),
                            deps=[neg] + prev.get("psB_t", []))
                        evm = kb.ins("vector", lambda e, m4=m4: e.tensor_copy(
                            m4[0:nsub, 0:128], psB[0:nsub, 0:128]),
                            deps=[ttm] + prev.get(("m4", hh), []))
                        sel_toks = []
                        for r in range(nsub):
                            sel_toks.append(kb.ins("tensor",
                                lambda e, r=r, m4=m4: e.matmul(
                                    psB[:, 512 + r * 128: 512 + (r + 1) * 128],
                                    st["c_sel4"][0:nsub, r * 128:(r + 1) * 128],
                                    m4[0:nsub, 0:128],
                                    start=True, stop=True),
                                deps=[evm] + prev.get("psB_s", [])))
                        wm = kb.ins("vector", lambda e, qaug=qaug: e.tensor_copy(
                            qaug[64:65, 0:QB], psB[0:1, 512:512 + QB]),
                            deps=sel_toks + [qls[hh]])
                        stats[hh] = wm
                        prev[("mall", hh)] = [ttm]
                        prev[("m4", hh)] = sel_toks
                        prev["psB_t"] = [evm]
                        prev["psB_s"] = [wm]
                    combine_deps = []
                    ex_last = {0: [], 1: []}
                    for hh in range(2):
                        kaug = kaug_d[:, hh * S:(hh + 1) * S]
                        vaug = vaug_d[:, hh * NTC * 65:(hh + 1) * NTC * 65]
                        qaug = qaugs[hh]
                        pt = pt_sb[:, hh * NTC * QB:(hh + 1) * NTC * QB]
                        ex_toks = []
                        av_toks = []
                        exp_prev = prev.get(("pt", hh), [])
                        npair = NTC // 2
                        for pair in range(npair):
                            rix = pair % 2
                            reg = psB[:, rix * W2:(rix + 1) * W2]
                            war = []
                            if pair >= 2:
                                war = [ex_toks[pair - 2]]
                            elif hh == 1:
                                war = ex_last[0][-2:]
                            else:
                                war = [stats[0], stats[1]]
                            mmBs = []
                            for j in range(2):
                                tc = 2 * pair + j
                                mmBs.append(kb.ins("tensor",
                                    lambda e, tc=tc, j=j, reg=reg, qaug=qaug, kaug=kaug: e.matmul(
                                        reg[:, j * QB:(j + 1) * QB],
                                        kaug[0:65, tc * 128:(tc + 1) * 128],
                                        qaug[0:65, 0:QB], start=True, stop=True),
                                    deps=war + [stats[hh]]))
                            ex = kb.ins("scalar", lambda e, pair=pair, reg=reg, pt=pt: e.activation(
                                pt[:, 2 * pair * QB: 2 * (pair + 1) * QB], reg[:],
                                AF.Exp, scale=8.0),
                                deps=mmBs + exp_prev)
                            ex_toks.append(ex)
                            for j in range(2):
                                tc = 2 * pair + j
                                av_toks.append(kb.ins("tensor",
                                    lambda e, tc=tc, hh=hh, vaug=vaug, pt=pt: e.matmul(
                                        pAV[hh][0:65, 0:QB], vaug[:, tc * 65:(tc + 1) * 65],
                                        pt[:, tc * QB:(tc + 1) * QB],
                                        start=(tc == 0), stop=(tc == NTC - 1)),
                                    deps=[ex, vls[hh]] + prev.get(("pAV", hh), [])))
                        ex_last[hh] = ex_toks
                        prev[("pt", hh)] = [av_toks[-1]]
                        prev[("qaug", hh, qb % 2)] = av_toks[-1:]
                        linv = linv_d[:, hh * QB:(hh + 1) * QB]
                        rcp = kb.ins("vector", lambda e, hh=hh, linv=linv: e.reciprocal(
                            linv[0:1, :], pAV[hh][64:65, 0:QB]),
                            deps=av_toks[-1:] + prev.get("linv", []))
                        combine_deps.append((rcp, av_toks[-1]))
                    prev["psB_t"] = ex_last[1][-2:] + ex_last[0][-1:]
                    prev["psB_s"] = prev["psB_t"]
                    l0 = kb.ins("tensor", lambda e: e.matmul(
                        psB[:, 0:QB], st["c_half_even"][:],
                        linv_d[0:1, 0:QB], start=True, stop=False),
                        deps=[combine_deps[0][0], combine_deps[1][0]] + prev["psB_t"])
                    l1 = kb.ins("tensor", lambda e: e.matmul(
                        psB[:, 0:QB], st["c_half_odd"][:],
                        linv_d[0:1, QB:2 * QB], start=False, stop=True),
                        deps=[])
                    evl = kb.ins("vector", lambda e: e.tensor_copy(lrep[:], psB[:, 0:QB]),
                                 deps=[l1] + prev.get("lrep", []))
                    c0 = kb.ins("vector", lambda e: e.tensor_mul(
                        att_sb[0:64, :], pAV0[0:64, 0:QB], lrep[0:64, :]),
                        deps=[evl, combine_deps[0][1]] + prev.get("att", []))
                    c1 = kb.ins("vector", lambda e: e.tensor_mul(
                        att_sb[64:128, :], pAV1[0:64, 0:QB], lrep[64:128, :]),
                        deps=[evl, combine_deps[1][1]] + prev.get("att", []))
                    od = kb.dma("sync", aps["attnfm"][hp * 128:(hp + 1) * 128,
                                                      b * S + qb * QB: b * S + (qb + 1) * QB],
                                att_sb[:], deps=[c0, c1], stream="d_at_o")
                    qb_toks.append(od)
                    prev["lrep"] = [c0, c1]
                    prev["att"] = [od]
                    prev["linv"] = [l0, l1]
                    prev[("pAV", 0)] = [c0]
                    prev[("pAV", 1)] = [c1]
                    prev["psB_t"] = [evl]
                    prev["psB_s"] = [evl]
                prev["kaug"] = qb_toks[-1:] + prev.get("psB_t", [])
                prev["vaug"] = qb_toks[-1:]
    return qb_toks


def emit_proj_epilogue(kb, cfg, st, aps, act_name, w_sb, wtok, x_in, x_out,
                       modT_sb, mtok, v_gate, sfx, bias_sb=None):
    D, S, T = cfg["D"], cfg["S"], cfg["T"]
    KC = D // 128
    SL = min(512, S)
    nc = kb.nc
    with (
        nc.sbuf_tensor("pe_a" + sfx, [128, 2 * KC * SL], BF16) as a_d,
        nc.sbuf_tensor("pe_x" + sfx, [128, KC * SL], F32) as x_sb,
        nc.sbuf_tensor("pe_o" + sfx, [128, KC * SL], F32) as o_sb,
        nc.psum_tensor("pe_p0" + sfx, [128, 512], F32) as p0,
        nc.psum_tensor("pe_p1" + sfx, [128, 512], F32) as p1,
    ):
        ps = (p0, p1)
        prev = {}
        for sl in range(T // SL):
            b = (sl * SL) // S
            asb = a_d[:, (sl % 2) * KC * SL:(sl % 2 + 1) * KC * SL]
            al, xl = [], []
            for c in range(KC):
                al.append(kb.dma("sync", asb[:, c * SL:(c + 1) * SL],
                                 aps[act_name][c * 128:(c + 1) * 128, sl * SL:sl * SL + SL],
                                 deps=prev.get(("a", sl % 2), []),
                                 stream="d_pe_a%d" % (sl % 2)))
                xl.append(kb.dma("gpsimd", sb3(x_sb[:, c * SL:(c + 1) * SL]),
                                 fm_ap(aps[x_in], c, sl * SL, SL),
                                 deps=prev.get("x", []), stream="d_pe_x"))
            al, xl = [al[-1]], [xl[-1]]
            eps = []
            for och in range(KC):
                pp = ps[och % 2]
                mms = []
                for k in range(KC):
                    mms.append(kb.ins("tensor", lambda e, k=k, och=och, pp=pp, asb=asb: e.matmul(
                        pp[:, 0:SL], w_sb[:, k * D + och * 128: k * D + (och + 1) * 128],
                        asb[:, k * SL:(k + 1) * SL],
                        start=(k == 0), stop=(k == KC - 1) and bias_sb is None),
                        deps=[wtok] + al + prev.get(("pp", och % 2), [])))
                if bias_sb is not None:
                    brow, orow, btok2, otok2 = bias_sb
                    mms.append(kb.ins("tensor", lambda e, och=och, pp=pp: e.matmul(
                        pp[:, 0:SL], brow[0:1, och * 128:(och + 1) * 128],
                        orow[0:1, 0:SL], start=False, stop=True),
                        deps=[btok2, otok2]))
                ep = kb.ins("vector", lambda e, och=och, b=b, pp=pp: e.scalar_tensor_tensor(
                    o_sb[:, och * SL:(och + 1) * SL], pp[:, 0:SL],
                    modT_sb[:, modcol(cfg, v_gate, och, b):modcol(cfg, v_gate, och, b) + 1],
                    x_sb[:, och * SL:(och + 1) * SL], ALU.mult, ALU.add),
                    deps=mms[-1:] + xl + [mtok] + prev.get("o", [])),
                eps.append(ep[0])
                prev[("pp", och % 2)] = [ep[0]]
            odmas = []
            for c in range(KC):
                odmas.append(kb.dma("sync", fm_ap(aps[x_out], c, sl * SL, SL),
                                    sb3(o_sb[:, c * SL:(c + 1) * SL]),
                                    deps=[eps[c]], stream="d_pe_o"))
            prev[("a", sl % 2)] = [mms[-1]]
            prev["x"] = eps
            prev["o"] = [odmas[-1]]
    return odmas


def ph_l0wo(kb, cfg, st, aps):
    D = cfg["D"]
    KC = D // 128
    nc = kb.nc
    with (
        nc.sbuf_tensor("wo_w", [128, KC * D], BF16) as wo,
        nc.sbuf_tensor("wo_modT", [128, 6 * KC * cfg["BPC"]], F32) as modT_sb,
    ):
        wtok = kb.dma("gpsimd", sb3(wo[:], D),
                      aps["woT"].rearrange("(kc kp) o -> kp kc o", kp=128),
                      stream="d_wo_w")
        mtok = kb.dma("sync", modT_sb[:], aps["modT0"][:], stream="d_wo_m")
        r = emit_proj_epilogue(kb, cfg, st, aps, "attnfm", wo, wtok,
                               "x0fm", "x1fm", modT_sb, mtok, 2, "_wo")
    return r



def ph_wconv(kb, cfg, st, aps):
    D, HF, E = cfg["D"], cfg["HF"], cfg["E"]
    nc = kb.nc
    wf = st["es"].enter_context(nc.sbuf_tensor("wc_f", [128, 2 * HF], F32))
    wt = st["es"].enter_context(nc.sbuf_tensor("wc_t", [128, 2 * HF], BF16))
    prev = {}
    toks = []

    def conv(src_ap, dst_ap, width, par):
        f = wf[:, par * HF: par * HF + width]
        t = wt[:, par * HF: par * HF + width]
        ld = kb.dma("scalar", f, src_ap, deps=prev.get(("c", par), []),
                    stream="d_wc%d" % par)
        cv = kb.ins("gpsimd", lambda e, f=f, t=t: e.tensor_copy(t, f),
                    deps=[ld] + prev.get(("s", par), []), sem_name="e_wconv")
        tok = kb.dma("scalar", dst_ap, t, deps=[cv], stream="d_wco%d" % par)
        prev[("c", par)] = [cv]
        prev[("s", par)] = [tok]
        return tok

    for e_i in range(E):
        for blk in range(D // 128):
            toks.append(conv(aps["L0_ffn_w1"][e_i, blk * 128:(blk + 1) * 128, :],
                             aps["w1b"][e_i, blk * 128:(blk + 1) * 128, :],
                             HF, blk % 2))
        for blk in range(HF // 128):
            toks.append(conv(aps["L0_ffn_w2"][e_i, blk * 128:(blk + 1) * 128, :],
                             aps["w2b"][e_i, blk * 128:(blk + 1) * 128, :],
                             D, blk % 2))
    kb.cross["wconv"] = toks[-2:] + [("e_wconv", kb.counts["e_wconv"])]
    return [toks[-1]]


def ph_route(kb, cfg, st, aps):
    D, S, T, E = cfg["D"], cfg["S"], cfg["T"], cfg["E"]
    KC = D // 128
    NT = T // 128
    CAP = cfg["CAP"]
    nc = kb.nc
    with contextlib.ExitStack() as es:
        def sbuf(n, sh, dt):
            return es.enter_context(nc.sbuf_tensor(n, sh, dt))
        def psum(n, sh, dt=F32):
            return es.enter_context(nc.psum_tensor(n, sh, dt))
        wg_sb = sbuf("rt_wg", [128, KC * E], BF16)
        h_sb = sbuf("rt_h", [128, KC * 128], BF16)
        lg_d = sbuf("rt_lg", [128, 2 * E], F32)
        mx = sbuf("rt_mx", [128, 4], F32)
        sc0 = sbuf("rt_sc0", [128, E], F32)
        sc1 = sbuf("rt_sc1", [128, E], F32)
        m_sb = sbuf("rt_m", [128, NT * 2 * E], F32)
        g_sb = sbuf("rt_g", [128, NT * 2], F32)
        keep_sb = sbuf("rt_keep", [128, 2 * E], F32)
        w_tile = sbuf("rt_w", [128, E], F32)
        cnt_sb = sbuf("rt_cnt", [1, 16], F32)
        cg_sb = sbuf("rt_cg", [cfg["NCORES"], 16], F32)
        go_sb = sbuf("rt_go", [1, 32], F32)
        carry_sb = sbuf("rt_carry", [1, 2 * E], F32)
        pos_sb = sbuf("rt_pos", [128, E], F32)
        w8_sb = sbuf("rt_w8", [8, T], F32)
        plg = psum("rt_plg", [128, 512])
        pc = psum("rt_pc", [1, 16])
        ppos = psum("rt_pp", [128, 512])
        pw = psum("rt_pw", [8, 128])
        wgl = kb.dma("gpsimd", sb3(wg_sb[:], E),
                     aps["L0_ffn_wg"].rearrange("(c p) e -> p c e", p=128),
                     stream="d_rt_w")
        cnt_toks = []
        prev_h, prev_lg, prev_lgread, prev_lg2 = [], [], [], []
        prev_mx, prev_sc0, prev_mdep, prev_g, prev_gw = [], [], [], [], []
        for t in range(NT):
            hl = []
            for c in range(KC):
                hl.append(kb.dma("sync", h_sb[:, c * 128:(c + 1) * 128],
                                 fm_ap(aps["h2"], c, t * 128, 128),
                                 deps=prev_h if t > 0 else [], stream="d_rt_h"))
            hl = [hl[-1]]
            lg = lg_d[:, 0:E]
            lg2 = lg_d[:, E:2 * E]
            mms = []
            for c in range(KC):
                mms.append(kb.ins("tensor", lambda e, c=c: e.matmul(
                    plg[:, 0:E], h_sb[:, c * 128:(c + 1) * 128],
                    wg_sb[:, c * E:(c + 1) * E],
                    start=(c == 0), stop=(c == KC - 1)),
                    deps=[wgl] + hl + prev_lg))
            e_lg = kb.ins("vector", lambda e: e.tensor_copy(lg, plg[:, 0:E]),
                          deps=mms[-1:] + prev_lgread)
            prev_lg = [e_lg]
            m0 = m_sb[:, t * 2 * E: t * 2 * E + E]
            m1 = m_sb[:, t * 2 * E + E: (t + 1) * 2 * E]
            t_mx1 = kb.ins("vector", lambda e: e.reduce_max(mx[:, 0:1], lg, axis=AX.X),
                           deps=[e_lg] + prev_mx)
            raw0 = kb.ins("vector", lambda e: e.tensor_single_scalar(
                sc0[:], lg, mx[:, 0:1], ALU.is_ge), deps=[t_mx1] + prev_sc0)
            x0 = _exclusive_first(kb, sc0, sc1, m0, prev_m=[raw0] + prev_mdep)
            t_lg2 = kb.ins("vector", lambda e, m0=m0: e.scalar_tensor_tensor(
                lg2, m0, -1e9, lg, ALU.mult, ALU.add), deps=[x0] + prev_lg2)
            t_mx2 = kb.ins("vector", lambda e: e.reduce_max(mx[:, 1:2], lg2, axis=AX.X),
                           deps=[t_lg2])
            raw1 = kb.ins("vector", lambda e: e.tensor_single_scalar(
                sc0[:], lg2, mx[:, 1:2], ALU.is_ge), deps=[t_mx2, x0])
            x1t = _exclusive_first(kb, sc0, sc1, m1, prev_m=[raw1, x0])
            t_d = kb.ins("vector", lambda e: e.tensor_sub(mx[:, 2:3], mx[:, 0:1], mx[:, 1:2]),
                         deps=[t_mx2, t_mx1] + prev_g)
            g0 = g_sb[:, t * 2: t * 2 + 1]
            g1 = g_sb[:, t * 2 + 1: t * 2 + 2]
            t_g0 = kb.ins("scalar", lambda e, g0=g0: e.activation(g0, mx[:, 2:3], AF.Sigmoid),
                          deps=[t_d] + prev_gw)
            t_g1 = kb.ins("scalar", lambda e, g1=g1: e.activation(g1, mx[:, 2:3], AF.Sigmoid,
                                                                  scale=-1.0),
                          deps=[t_d] + prev_gw)
            cnt_toks.append(kb.ins("tensor", lambda e, t=t, m0=m0: e.matmul(
                pc[:], st["c_ones_col"][:], m_sb[:, t * 2 * E:(t + 1) * 2 * E],
                start=(t == 0), stop=(t == NT - 1)),
                deps=[x0, x1t]))
            prev_h = mms + [e_lg]
            prev_lgread = [t_mx1, raw0, t_lg2]
            prev_lg2 = [t_mx2, raw1]
            prev_mx = [t_d, raw0, raw1]
            prev_sc0 = [x0, x1t]
            prev_mdep = [cnt_toks[-1]]
            prev_g = [t_g0, t_g1]
            prev_gw = [t_d]
        ec = kb.ins("vector", lambda e: e.tensor_copy(cnt_sb[:], pc[:]),
                    deps=cnt_toks[-1:])
        dcl = kb.dma("sync", aps["cntloc"][:], cnt_sb[:], deps=[ec], stream="d_rt_c")
        cc = kb.ins("gpsimd", lambda e: e.collective_compute(
            "AllGather", ALU.bypass, [list(range(cfg["NCORES"]))],
            [aps["cntloc"][:].opt()], [aps["cntg"][:].opt()]),
            deps=[dcl])
        gcl = kb.dma("sync", cg_sb[:], aps["cntg"][:], deps=[cc], stream="d_rt_c")
        mp = kb.ins("tensor", lambda e: e.matmul(
            pc[:], st["c_prevmask"][0:cfg["NCORES"], :], cg_sb[:],
            start=True, stop=True), deps=[gcl, ec])
        ep = kb.ins("vector", lambda e: e.tensor_copy(go_sb[0:1, 0:16], pc[:]),
                    deps=[mp])
        ma = kb.ins("tensor", lambda e: e.matmul(
            pc[:], st["c_allmask"][0:cfg["NCORES"], :], cg_sb[:],
            start=True, stop=True), deps=[ep])
        ea = kb.ins("vector", lambda e: e.tensor_copy(go_sb[0:1, 16:32], pc[:]),
                    deps=[ma])
        c0i = kb.ins("vector", lambda e: e.tensor_copy(carry_sb[0:1, 0:E],
                                                       go_sb[0:1, 0:E]), deps=[ep])
        c1i = kb.ins("vector", lambda e: e.tensor_add(
            carry_sb[0:1, E:2 * E], go_sb[0:1, 16:24], go_sb[0:1, 8:16]),
            deps=[ea])
        carry_toks = [c0i, c1i]
        g_done = kb.barrier("scalar")
        prev_pos, prev_keep, prev_w, prev_pw = [], [], [], []
        for t in range(NT):
            kts = []
            for p in range(2):
                m_p = m_sb[:, t * 2 * E + p * E: t * 2 * E + (p + 1) * E]
                car = carry_sb[0:1, p * E:(p + 1) * E]
                mmp = kb.ins("tensor", lambda e, m_p=m_p: e.matmul(
                    ppos[:, 0:E], st["c_triT"][:], m_p, start=True, stop=False),
                    deps=carry_toks + prev_pos)
                mmc = kb.ins("tensor", lambda e, car=car: e.matmul(
                    ppos[:, 0:E], st["c_ones_row"][0:1, :], car,
                    start=False, stop=True), deps=[])
                t_pe = kb.ins("vector", lambda e, m_p=m_p: e.tensor_sub(
                    pos_sb[:], ppos[:, 0:E], m_p), deps=[mmc] + prev_keep)
                t_k = kb.ins("vector", lambda e: e.tensor_single_scalar(
                    pos_sb[:], pos_sb[:], float(CAP), ALU.is_lt), deps=[t_pe])
                t_km = kb.ins("vector", lambda e, p=p, m_p=m_p: e.tensor_mul(
                    keep_sb[:, p * E:(p + 1) * E], pos_sb[:], m_p), deps=[t_k] + prev_w)
                kts.append(t_km)
                cm = kb.ins("tensor", lambda e, m_p=m_p: e.matmul(
                    pc[0:1, 0:E], st["c_ones_col"][:], m_p, start=True, stop=True),
                    deps=[ea if t == 0 else carry_toks[-1], mmc])
                cev = kb.ins("vector", lambda e, car=car: e.tensor_add(
                    car, car, pc[0:1, 0:E]), deps=[cm])
                carry_toks = [cev]
                prev_pos = [t_pe]
            g0 = g_sb[:, t * 2: t * 2 + 1]
            g1 = g_sb[:, t * 2 + 1: t * 2 + 2]
            t_w1 = kb.ins("vector", lambda e, g1=g1: e.tensor_scalar(
                w_tile[:], keep_sb[:, E:2 * E], g1, None, ALU.mult),
                deps=kts + prev_w + [g_done])
            t_w = kb.ins("vector", lambda e, g0=g0: e.scalar_tensor_tensor(
                w_tile[:], keep_sb[:, 0:E], g0, w_tile[:], ALU.mult, ALU.add),
                deps=[t_w1])
            prev_keep = [t_w1, t_w]
            prev_w = [t_w]
            tw = kb.ins("tensor", lambda e: e.transpose(
                pw[:], w_tile[:], st["c_identf"][:]), deps=[t_w] + prev_pw)
            ev8 = kb.ins("vector", lambda e, t=t: e.tensor_copy(
                w8_sb[0:8, t * 128:(t + 1) * 128], pw[0:8, :]), deps=[tw])
            prev_pw = [ev8]
            prev_w = [t_w, tw]
        wd = kb.dma("sync", aps["w8"][:], w8_sb[:], deps=[ev8], stream="d_rt_c")
    return [wd]


def _exclusive_first(kb, raw, scr, out_m, prev_m):
    E = 8
    t1a = kb.ins("vector", lambda e: e.tensor_copy(scr[:, 0:1], raw[:, 0:1]),
                 deps=prev_m)
    t1b = kb.ins("vector", lambda e: e.tensor_add(
        scr[:, 1:E], raw[:, 1:E], raw[:, 0:E - 1]), deps=prev_m)
    t2a = kb.ins("vector", lambda e: e.tensor_copy(out_m[:, 0:2], scr[:, 0:2]),
                 deps=[t1a, t1b] + prev_m)
    t2b = kb.ins("vector", lambda e: e.tensor_add(
        out_m[:, 2:E], scr[:, 2:E], scr[:, 0:E - 2]), deps=[t1a, t1b] + prev_m)
    t3a = kb.ins("vector", lambda e: e.tensor_copy(scr[:, 0:4], out_m[:, 0:4]),
                 deps=[t2a, t2b])
    t3b = kb.ins("vector", lambda e: e.tensor_add(
        scr[:, 4:E], out_m[:, 4:E], out_m[:, 0:E - 4]), deps=[t2a, t2b])
    teq = kb.ins("vector", lambda e: e.tensor_single_scalar(
        out_m[:, :], scr[:, 0:E], 1.0, ALU.is_equal), deps=[t3a, t3b])
    tq = kb.ins("vector", lambda e: e.tensor_mul(out_m[:, :], out_m[:, :], raw[:, :]),
                deps=[teq])
    return tq



def ph_moe(kb, cfg, st, aps):
    D, S, T, E, HF = cfg["D"], cfg["S"], cfg["T"], cfg["E"], cfg["HF"]
    KC = D // 128
    HC = HF // 128
    SL = min(512, S)
    nc = kb.nc
    with contextlib.ExitStack() as es:
        def sbuf(n, sh, dt):
            return es.enter_context(nc.sbuf_tensor(n, sh, dt))
        def psum(n, sh, dt=F32):
            return es.enter_context(nc.psum_tensor(n, sh, dt))
        h_sb = sbuf("mo_h", [128, KC * SL], BF16)
        x_sb = sbuf("mo_x", [128, KC * SL], F32)
        acc = sbuf("mo_acc", [128, KC * SL], F32)
        y1_sb = sbuf("mo_y1", [128, HC * SL], BF16)
        w1_sb = sbuf("mo_w1", [128, 2 * KC * 1024], BF16)
        w2_sb = sbuf("mo_w2", [128, 2 * 4 * D], BF16)
        wrep = sbuf("mo_wrep", [128, SL], F32)
        w8_sb = sbuf("mo_w8", [8, T], F32)
        b1_sb = sbuf("mo_b1", [128, HC], F32)
        b2_sb = sbuf("mo_b2", [128, KC], F32)
        b1r = sbuf("mo_b1r", [32, 128], F32)
        b2r = sbuf("mo_b2r", [8, 128], F32)
        o_sb = sbuf("mo_o", [128, SL], F32)
        modT_sb = sbuf("mo_modT", [128, 6 * KC * cfg["BPC"]], F32)
        psy = [psum("mo_p%d" % i, [128, 512]) for i in range(8)]
        w8l = kb.dma("sync", w8_sb[:], aps["w8"][:], stream="d_mo_s")
        mtok = kb.dma("sync", modT_sb[:], aps["modT0"][:], stream="d_mo_s2")
        prev = {}
        for sl in range(T // SL):
            b = (sl * SL) // S
            hdl, xdl = [], []
            for c in range(KC):
                hdl.append(kb.dma("sync", sb3(h_sb[:, c * SL:(c + 1) * SL]),
                                  fm_ap(aps["h2"], c, sl * SL, SL),
                                  deps=prev.get("h", []), stream="d_mo_h"))
                xdl.append(kb.dma("gpsimd", sb3(x_sb[:, c * SL:(c + 1) * SL]),
                                  fm_ap(aps["x1fm"], c, sl * SL, SL),
                                  deps=prev.get("x", []), stream="d_mo_x"))
            hdl, xdl = [hdl[-1]], [xdl[-1]]
            acc_tok = []
            for e_i in range(E):
                mw = kb.ins("tensor", lambda e, e_i=e_i, sl=sl: e.matmul(
                    psy[7][:, 0:SL], st["c_sel8"][0:8, e_i * 128:(e_i + 1) * 128],
                    w8_sb[0:8, sl * SL:(sl + 1) * SL], start=True, stop=True),
                    deps=[w8l] + prev.get("pw", []))
                ew = kb.ins("vector", lambda e: e.tensor_copy(wrep[:], psy[7][:, 0:SL]),
                            deps=[mw] + prev.get("wrep", []))
                prev["pw"] = [ew]
                bl1 = kb.dma("sync", b1r[0:HC, :],
                             aps["L0_ffn_b1"][e_i, :].rearrange("(a b) -> a b", b=128),
                             deps=prev.get("b1", []), stream="d_mo_b")
                tb1 = kb.ins("tensor", lambda e: e.transpose(
                    psy[6][:, 0:HC], b1r[0:HC, :], st["c_identf"][0:HC, 0:HC]),
                    deps=[bl1] + prev.get("pb", []))
                eb1 = kb.ins("vector", lambda e: e.tensor_copy(
                    b1_sb[:, 0:HC], psy[6][:, 0:HC]), deps=[tb1] + prev.get("b1r", []))
                bl2 = kb.dma("sync", b2r[0:KC, :],
                             aps["L0_ffn_b2"][e_i, :].rearrange("(a b) -> a b", b=128),
                             deps=prev.get("b2", []), stream="d_mo_b2")
                tb2 = kb.ins("tensor", lambda e: e.transpose(
                    psy[6][:, 0:KC], b2r[0:KC, :], st["c_identf"][0:KC, 0:KC]),
                    deps=[bl2, eb1])
                eb2 = kb.ins("vector", lambda e: e.tensor_copy(
                    b2_sb[:, 0:KC], psy[6][:, 0:KC]), deps=[tb2] + prev.get("b2r", []))
                prev["b1"] = [tb1]
                prev["b2"] = [tb2]
                prev["pb"] = [eb2]
                gelu_toks = []
                for hg in range(HF // 1024):
                    wg_idx = hg % 2
                    w1g = w1_sb[:, wg_idx * KC * 1024:(wg_idx + 1) * KC * 1024]
                    wls = [kb.dma(
                        "gpsimd", sb3(w1g, 1024),
                        aps["L0_ffn_w1"][e_i, :, hg * 1024:(hg + 1) * 1024]
                        .rearrange("(kc kp) h -> kp kc h", kp=128),
                        deps=prev.get(("w1", wg_idx), []),
                        stream="d_mo_w1%d" % wg_idx)]
                    for hcl in range(8):
                        hc = hg * 8 + hcl
                        pp = psy[hcl % 2]
                        mms = []
                        for k in range(KC):
                            mms.append(kb.ins("tensor",
                                lambda e, k=k, hcl=hcl, pp=pp, w1g=w1g: e.matmul(
                                    pp[:, 0:SL],
                                    w1g[:, k * 1024 + hcl * 128: k * 1024 + (hcl + 1) * 128],
                                    h_sb[:, k * SL:(k + 1) * SL],
                                    start=(k == 0), stop=(k == KC - 1)),
                                deps=wls + hdl + prev.get(("py", hcl % 2), [])))
                        g = kb.ins("scalar", lambda e, hc=hc, pp=pp: e.activation(
                            y1_sb[:, hc * SL:(hc + 1) * SL], pp[:, 0:SL], AF.Gelu,
                            bias=b1_sb[:, hc:hc + 1]),
                            deps=mms[-1:] + [eb1] + prev.get("y1", []))
                        gelu_toks.append(g)
                        prev[("py", hcl % 2)] = [g]
                    prev[("w1", wg_idx)] = mms
                mm2_toks = {dc: [] for dc in range(KC)}
                for hq in range(HC // 4):
                    par = hq % 2
                    w2big = w2_sb[:, par * 4 * D:(par + 1) * 4 * D]
                    w2l = kb.dma("gpsimd", sb3(w2big, D),
                                 aps["L0_ffn_w2"][e_i, hq * 512:(hq + 1) * 512, :]
                                 .rearrange("(c p) d -> p c d", p=128),
                                 deps=prev.get(("w2", par), []),
                                 stream="d_mo_w2%d" % par)
                    for hcl in range(4):
                        hc = hq * 4 + hcl
                        for dc in range(KC):
                            mm2_toks[dc].append(kb.ins("tensor",
                                lambda e, hc=hc, hcl=hcl, dc=dc, w2big=w2big: e.matmul(
                                    psy[dc][:, 0:SL],
                                    w2big[:, hcl * D + dc * 128: hcl * D + (dc + 1) * 128],
                                    y1_sb[:, hc * SL:(hc + 1) * SL],
                                    start=(hc == 0), stop=(hc == HC - 1)),
                                deps=[w2l, gelu_toks[hc]] +
                                     (prev.get(("py2", dc), []) + [eb2, ew, gelu_toks[-1]]
                                      if hc == 0 else [])))
                    prev[("w2", par)] = [mm2_toks[dc][-1] for dc in range(KC)]
                new_acc = []
                for dc in range(KC):
                    if e_i == 0:
                        a = kb.ins("vector", lambda e, dc=dc: e.scalar_tensor_tensor(
                            acc[:, dc * SL:(dc + 1) * SL], psy[dc][:, 0:SL],
                            b2_sb[:, dc:dc + 1], wrep[:], ALU.add, ALU.mult),
                            deps=mm2_toks[dc][-1:] + [eb2, ew] + prev.get("acc", []))
                    else:
                        a = kb.ins("vector", lambda e, dc=dc: e.scalar_tensor_tensor(
                            o_sb[:], psy[dc][:, 0:SL],
                            b2_sb[:, dc:dc + 1], wrep[:], ALU.add, ALU.mult),
                            deps=mm2_toks[dc][-1:] + [eb2, ew] + prev.get("o", []))
                        a = kb.ins("vector", lambda e, dc=dc: e.tensor_add(
                            acc[:, dc * SL:(dc + 1) * SL],
                            acc[:, dc * SL:(dc + 1) * SL], o_sb[:]),
                            deps=[a] + prev.get("acc", []))
                        prev["o"] = [a]
                    new_acc.append(a)
                    prev[("py2", dc)] = [a]
                prev["acc"] = new_acc
                prev["y1"] = mm2_toks[KC - 1]
                prev["wrep"] = new_acc
                prev["pw"] = new_acc
                prev[("py", 0)] = new_acc
                prev[("py", 1)] = new_acc
                prev["b1r"] = gelu_toks[-1:]
                prev["b2r"] = new_acc
            eps_toks = []
            for c in range(KC):
                ep = kb.ins("vector", lambda e, c=c, b=b: e.scalar_tensor_tensor(
                    x_sb[:, c * SL:(c + 1) * SL], acc[:, c * SL:(c + 1) * SL],
                    modT_sb[:, modcol(cfg, 5, c, b):modcol(cfg, 5, c, b) + 1],
                    x_sb[:, c * SL:(c + 1) * SL], ALU.mult, ALU.add),
                    deps=prev["acc"] + xdl + [mtok] + prev.get("eps", []))
                eps_toks.append(ep)
            odmas = []
            for c in range(KC):
                odmas.append(kb.dma("sync", fm_ap(aps["x2fm"], c, sl * SL, SL),
                                    sb3(x_sb[:, c * SL:(c + 1) * SL]),
                                    deps=[eps_toks[c]], stream="d_mo_o"))
            prev["h"] = [m for dc in range(KC) for m in mm2_toks[dc][-1:]] + gelu_toks[-1:]
            prev["x"] = [odmas[-1]]
            prev["eps"] = [odmas[-1]]
            prev["acc"] = eps_toks
    return odmas



def ph_l1proj(kb, cfg, st, aps):
    D, S, T, NH, DH, DL = cfg["D"], cfg["S"], cfg["T"], cfg["NH"], cfg["DH"], cfg["DLAT"]
    KC = D // 128
    SL = min(512, S)
    nc = kb.nc
    with contextlib.ExitStack() as es:
        def sbuf(n, sh, dt):
            return es.enter_context(nc.sbuf_tensor(n, sh, dt))
        def psum(n, sh, dt=F32):
            return es.enter_context(nc.psum_tensor(n, sh, dt))
        wq_sb = sbuf("lp_wq", [128, KC * D], BF16)
        wdn_sb = sbuf("lp_wdn", [128, KC * DL], BF16)
        wup_sb = sbuf("lp_wup", [128, 2 * 2 * D], BF16)
        bq_sb = sbuf("lp_bq", [1, D], BF16)
        bup_sb = sbuf("lp_bup", [1, 2 * D], BF16)
        bdn_r = sbuf("lp_bdnr", [2, 128], F32)
        bdn_sb = sbuf("lp_bdn", [128, 2], F32)
        ones_sl = sbuf("lp_ones", [1, SL], BF16)
        h_sb = sbuf("lp_h", [128, KC * SL], BF16)
        lat_sb = sbuf("lp_lat", [128, 2 * SL], BF16)
        q_sb = sbuf("lp_q", [128, D], F32)
        kv_sb = sbuf("lp_kv", [128, 2 * D], F32)
        ro_sb = sbuf("lp_ro", [128, D], BF16)
        kro_sb = sbuf("lp_kro", [128, D], BF16)
        v_sb = sbuf("lp_v", [128, NH * DH], BF16)
        cos_sb = sbuf("lp_cos", [128, 512], F32)
        sin_sb = sbuf("lp_sin", [128, 512], F32)
        t1_sb = sbuf("lp_t1", [128, 512], F32)
        t2_sb = sbuf("lp_t2", [128, 512], F32)
        pq = psum("lp_pq", [128, 512])
        pl = psum("lp_pl", [128, 512])
        pk = psum("lp_pk", [128, 512])
        wql = kb.dma("gpsimd", sb3(wq_sb[:], D),
                     aps["L1_attn_Wq"].rearrange("(kc kp) o -> kp kc o", kp=128),
                     stream="d_lp_w")
        wdl = kb.dma("gpsimd", sb3(wdn_sb[:], DL),
                     aps["L1_attn_Wdown"].rearrange("(kc kp) o -> kp kc o", kp=128),
                     stream="d_lp_w")
        wul = kb.dma("gpsimd", sb3(wup_sb[:], 2 * D),
                     aps["L1_attn_Wup"].rearrange("(kc kp) o -> kp kc o", kp=128),
                     stream="d_lp_w")
        bql = kb.dma("gpsimd", bq_sb[:], aps["L1_attn_bq"][:], stream="d_lp_w")
        bul = kb.dma("gpsimd", bup_sb[:], aps["L1_attn_bup"][:], stream="d_lp_w")
        bdl = kb.dma("sync", bdn_r[:], aps["L1_attn_bdown"][0, :]
                     .rearrange("(a b) -> a b", b=128), stream="d_lp_b")
        tbd = kb.ins("tensor", lambda e: e.transpose(
            pq[:, 0:2], bdn_r[:], st["c_identf"][0:2, 0:2]), deps=[bdl])
        ebd = kb.ins("vector", lambda e: e.tensor_copy(bdn_sb[:], pq[:, 0:2]),
                     deps=[tbd])
        onم = kb.ins("vector", lambda e: e.memset(ones_sl[:], 1.0))
        wtoks = [wql, wdl, wul, bql, bul, ebd, onم]
        prev = {}
        for sl in range(T // SL):
            b = (sl * SL) // S
            p0s = sl * SL - b * S
            hdl = []
            for c in range(KC):
                hdl.append(kb.dma("sync", sb3(h_sb[:, c * SL:(c + 1) * SL]),
                                  fm_ap(aps["h3"], c, sl * SL, SL),
                                  deps=prev.get("h", []), stream="d_lp_h"))
            hdl = [hdl[-1]]
            lats = []
            for lc in range(DL // 128):
                mms = []
                for k in range(KC):
                    mms.append(kb.ins("tensor", lambda e, k=k, lc=lc: e.matmul(
                        pl[:, 0:SL], wdn_sb[:, k * DL + lc * 128: k * DL + (lc + 1) * 128],
                        h_sb[:, k * SL:(k + 1) * SL],
                        start=(k == 0), stop=(k == KC - 1)),
                        deps=wtoks + hdl + prev.get("pl", [])))
                ev = kb.ins("vector", lambda e, lc=lc: e.tensor_scalar(
                    lat_sb[:, lc * SL:(lc + 1) * SL], pl[:, 0:SL],
                    bdn_sb[:, lc:lc + 1], None, ALU.add),
                    deps=mms[-1:] + prev.get("lat", []))
                lats.append(ev)
                prev["pl"] = [ev]
            for tt in range(SL // 128):
                cl = kb.dma("sync", cos_sb[:],
                            aps["c_l1cos"][p0s + tt * 128:p0s + (tt + 1) * 128, :],
                            deps=prev.get("cos", []), stream="d_lp_c")
                sl_l = kb.dma("sync", sin_sb[:],
                              aps["c_l1sin"][p0s + tt * 128:p0s + (tt + 1) * 128, :],
                              deps=prev.get("cos", []), stream="d_lp_s")
                qevs = []
                for oh in range(2):
                    mms = []
                    for k in range(KC):
                        mms.append(kb.ins("tensor", lambda e, k=k, oh=oh, tt=tt: e.matmul(
                            pq[:, 0:512],
                            h_sb[:, k * SL + tt * 128: k * SL + (tt + 1) * 128],
                            wq_sb[:, k * D + oh * 512:(k * D + (oh + 1) * 512)],
                            start=(k == 0), stop=False),
                            deps=wtoks + hdl + prev.get("pq", [])))
                    mb = kb.ins("tensor", lambda e, oh=oh: e.matmul(
                        pq[:, 0:512], ones_sl[0:1, 0:128],
                        bq_sb[0:1, oh * 512:(oh + 1) * 512],
                        start=False, stop=True), deps=[])
                    ev = kb.ins("vector", lambda e, oh=oh: e.tensor_copy(
                        q_sb[:, oh * 512:(oh + 1) * 512], pq[:, 0:512]),
                        deps=[mb] + prev.get("qsb", []))
                    qevs.append(ev)
                    prev["pq"] = [ev]
                kvevs = []
                for qh in range(4):
                    mms = []
                    for lc in range(DL // 128):
                        mms.append(kb.ins("tensor", lambda e, lc=lc, qh=qh, tt=tt: e.matmul(
                            pk[:, 0:512],
                            lat_sb[:, lc * SL + tt * 128: lc * SL + (tt + 1) * 128],
                            wup_sb[:, lc * 2 * D + qh * 512: lc * 2 * D + (qh + 1) * 512],
                            start=(lc == 0), stop=False),
                            deps=lats + prev.get("pk", [])))
                    mb = kb.ins("tensor", lambda e, qh=qh: e.matmul(
                        pk[:, 0:512], ones_sl[0:1, 0:128],
                        bup_sb[0:1, qh * 512:(qh + 1) * 512],
                        start=False, stop=True), deps=[])
                    ev = kb.ins("vector", lambda e, qh=qh: e.tensor_copy(
                        kv_sb[:, qh * 512:(qh + 1) * 512], pk[:, 0:512]),
                        deps=[mb] + prev.get("kvsb", []))
                    kvevs.append(ev)
                    prev["pk"] = [ev]
                ct = cos_sb[:]
                st_ = sin_sb[:]
                def q1v(z):
                    return z
                q_t = q_sb
                q1 = q_t[:].rearrange("p (h x) -> p h x", x=64)[:, :, 0:32]
                q2 = q_t[:].rearrange("p (h x) -> p h x", x=64)[:, :, 32:64]
                ro1 = ro_sb[:].rearrange("p (h x) -> p h x", x=64)[:, :, 0:32]
                ro2 = ro_sb[:].rearrange("p (h x) -> p h x", x=64)[:, :, 32:64]
                c3 = sb3(ct, 32)
                s3 = sb3(st_, 32)
                rq = _rope_l1(kb, q1, q2, ro1, ro2, c3, s3, t1_sb, t2_sb,
                              qevs + [cl, sl_l] + prev.get("ro", []), prev)
                kv3 = kv_sb[:].rearrange("p (h x) -> p h x", x=128)
                k1 = kv3[:, :, 0:32]
                k2 = kv3[:, :, 32:64]
                ko1 = kro_sb[:].rearrange("p (h x) -> p h x", x=64)[:, :, 0:32]
                ko2 = kro_sb[:].rearrange("p (h x) -> p h x", x=64)[:, :, 32:64]
                rk = _rope_l1(kb, k1, k2, ko1, ko2, c3, s3, t1_sb, t2_sb,
                              kvevs + [cl, sl_l] + prev.get("kro", []), prev)
                vv = kb.ins("vector", lambda e: e.tensor_copy(
                    sb3(v_sb[:], DH), kv3[:, :, 64:128]),
                    deps=kvevs + prev.get("vsb", []))
                d1 = kb.dma("sync", aps["l1q"][sl * SL + tt * 128: sl * SL + (tt + 1) * 128, :],
                            ro_sb[:], deps=[rq], stream="d_lp_oq")
                d2 = kb.dma("sync", aps["l1k"][sl * SL + tt * 128: sl * SL + (tt + 1) * 128, :],
                            kro_sb[:], deps=[rk], stream="d_lp_ok")
                d3 = kb.dma("sync", aps["l1v"][sl * SL + tt * 128: sl * SL + (tt + 1) * 128, :],
                            v_sb[:], deps=[vv], stream="d_lp_ov")
                prev["qsb"] = [rq]
                prev["kvsb"] = [rk, vv]
                prev["ro"] = [d1]
                prev["kro"] = [d2]
                prev["vsb"] = [d3]
            prev["h"] = mms + qevs
            prev["lat"] = mms
            prev["cos"] = [rq, rk]
    return [d1, d2, d3]


def _rope_l1(kb, x1, x2, o1, o2, c3, s3, t1_sb, t2_sb, deps0, prev):
    t3a = sb3(t1_sb[:], 32)
    t3b = sb3(t2_sb[:], 32)
    a1 = kb.ins("vector", lambda e: e.tensor_mul(t3a, x1, c3),
                deps=deps0 + prev.get("rt1", []))
    a2 = kb.ins("vector", lambda e: e.tensor_mul(t3b, x2, s3),
                deps=deps0 + prev.get("rt2", []))
    a3 = kb.ins("vector", lambda e: e.tensor_sub(o1, t3a, t3b), deps=[a1, a2])
    b1 = kb.ins("vector", lambda e: e.tensor_mul(t3a, x1, s3), deps=[a3])
    b2 = kb.ins("vector", lambda e: e.tensor_mul(t3b, x2, c3), deps=[a3])
    b3 = kb.ins("vector", lambda e: e.tensor_add(o2, t3a, t3b), deps=[b1, b2])
    prev["rt1"] = [b3]
    prev["rt2"] = [b3]
    return b3


def ph_l1attn(kb, cfg, st, aps):
    D, S, T, NH, DH = cfg["D"], cfg["S"], cfg["T"], cfg["NH"], cfg["DH"]
    HSPLIT = 7
    nc = kb.nc
    with contextlib.ExitStack() as es:
        def sbuf(n, sh, dt):
            return es.enter_context(nc.sbuf_tensor(n, sh, dt))
        def psum(n, sh, dt=F32):
            return es.enter_context(nc.psum_tensor(n, sh, dt))
        q_sb = sbuf("la_q", [128, D], BF16)
        k_sb = sbuf("la_k", [128, D], BF16)
        v_sb = sbuf("la_v", [128, D], BF16)
        prod = sbuf("la_prod", [128, 4 * 1024], F32)
        att = sbuf("la_att", [128, NH * NH], F32)
        mx = sbuf("la_mx", [128, 4 * 2], F32)
        li = sbuf("la_l", [128, 4], F32)
        ao = sbuf("la_ao", [128, D], F32)
        oo = sbuf("la_oo", [128, D], BF16)
        ps0 = psum("la_p0", [128, 512])
        ps1 = psum("la_p1", [128, 512])
        prev = {}
        for tile in range(T // 128):
            t0 = tile * 128
            ql = kb.dma("sync", q_sb[:], aps["l1q"][t0:t0 + 128, :],
                        deps=prev.get("q", []), stream="d_la_q")
            kl = kb.dma("sync", k_sb[:], aps["l1k"][t0:t0 + 128, :],
                        deps=prev.get("k", []), stream="d_la_k")
            vl = kb.dma("sync", v_sb[:], aps["l1v"][t0:t0 + 128, :],
                        deps=prev.get("v", []), stream="d_la_v")
            hts = []
            lane_n = {0: 0, 1: 0}
            for h in range(NH):
                lane = 0 if h < HSPLIT else 1
                eng = "vector" if lane == 0 else "gpsimd"
                buf = 2 * lane + (lane_n[lane] % 2)
                lane_n[lane] += 1
                pr = prod[:, buf * 1024:(buf + 1) * 1024]
                mxl = mx[:, buf * 2:(buf + 1) * 2]
                lil = li[:, buf:buf + 1]
                q_bc = q_sb[:].rearrange("p (hh x) -> p hh x", x=DH)[
                    :, h:h + 1, :].broadcast_to([128, NH, DH])
                k_all = k_sb[:].rearrange("p (g x) -> p g x", x=DH)
                tm = kb.ins(eng, lambda e, q_bc=q_bc, k_all=k_all, pr=pr: e.tensor_mul(
                    sb3(pr, DH), q_bc, k_all),
                    deps=[ql, kl] + prev.get(("prod", buf), []))
                tr = kb.ins("vector", lambda e, h=h, pr=pr: e.reduce_sum(
                    att[:, h * NH:(h + 1) * NH], sb3(pr, DH), axis=AX.X),
                    deps=[tm] + prev.get(("att", buf), []))
                t_mx = kb.ins("vector", lambda e, h=h, mxl=mxl: e.reduce_max(
                    mxl[:, 0:1], att[:, h * NH:(h + 1) * NH], axis=AX.X),
                    deps=[tr] + prev.get(("mx", buf), []))
                t_ng = kb.ins("scalar", lambda e, mxl=mxl: e.mul(
                    mxl[:, 1:2], mxl[:, 0:1], -0.125),
                    deps=[t_mx] + prev.get(("mx2", buf), []))
                t_ex = kb.ins("scalar", lambda e, h=h, mxl=mxl, lil=lil: e.activation(
                    att[:, h * NH:(h + 1) * NH], att[:, h * NH:(h + 1) * NH],
                    AF.Exp, scale=0.125, bias=mxl[:, 1:2], accum_out=lil),
                    deps=[t_ng] + prev.get(("li", buf), []))
                t_rc = kb.ins("vector", lambda e, lil=lil: e.reciprocal(lil, lil),
                              deps=[t_ex])
                t_nm = kb.ins("vector", lambda e, h=h, lil=lil: e.tensor_scalar(
                    att[:, h * NH:(h + 1) * NH], att[:, h * NH:(h + 1) * NH],
                    lil, None, ALU.mult), deps=[t_rc])
                a_bc = att[:, h * NH:(h + 1) * NH].rearrange(
                    "p (o g) -> p o g", o=1).broadcast_to([128, DH, NH])
                v_dg = v_sb[:].rearrange("p (g x) -> p x g", x=DH)
                tm2 = kb.ins(eng, lambda e, a_bc=a_bc, v_dg=v_dg, pr=pr: e.tensor_mul(
                    pr.rearrange("p (x g) -> p x g", g=NH),
                    a_bc, v_dg), deps=[t_nm, vl, tm, tr] + prev.get(("prod2", buf), []))
                tr2 = kb.ins("vector", lambda e, h=h, pr=pr: e.reduce_sum(
                    ao[:, h * DH:(h + 1) * DH],
                    pr.rearrange("p (x g) -> p x g", g=NH), axis=AX.X),
                    deps=[tm2] + prev.get(("ao", lane), []))
                hts.append(tr2)
                prev[("prod", buf)] = [tm2, tr2]
                prev[("att", buf)] = [tm2]
                prev[("mx", buf)] = [t_ng]
                prev[("mx2", buf)] = [t_ex]
                prev[("li", buf)] = [t_nm]
                prev[("prod2", buf)] = [tr2]
            evs = []
            for g in range(2):
                tts = []
                for c in range(4):
                    ch = g * 4 + c
                    tts.append(kb.ins("tensor",
                        lambda e, ch=ch, g=g, c=c: e.transpose(
                            (ps0 if g == 0 else ps1)[:, c * 128:(c + 1) * 128],
                            ao[:, ch * 128:(ch + 1) * 128], st["c_identf"][:]),
                        deps=hts + prev.get("ps" + str(g), [])))
                ev = kb.ins("vector", lambda e, g=g: e.tensor_copy(
                    oo[:, g * 512:(g + 1) * 512], (ps0 if g == 0 else ps1)[:]),
                    deps=tts + prev.get("oo", []))
                evs.append(ev)
                prev["ps" + str(g)] = [ev]
                prev["ao"] = prev.get("ao", []) + tts
            od = kb.dma("sync", aps["l1at"][:, t0:t0 + 128].rearrange(
                "(c p) t -> p c t", p=128), sb3(oo[:]), deps=evs, stream="d_la_o")
            prev["q"] = hts[-1:] + [hts[HSPLIT - 1]]
            prev["k"] = prev["q"]
            prev["v"] = prev["q"]
            prev[("ao", 0)] = tts
            prev[("ao", 1)] = tts
            prev["oo"] = [od]
    return [od]


def ph_l1out(kb, cfg, st, aps):
    D = cfg["D"]
    KC = D // 128
    nc = kb.nc
    with contextlib.ExitStack() as es:
        def sbuf(n, sh, dt):
            return es.enter_context(nc.sbuf_tensor(n, sh, dt))
        wo_sb = sbuf("lo_w", [128, KC * D], BF16)
        bo_sb = sbuf("lo_b", [1, D], BF16)
        on_sb = sbuf("lo_on", [1, min(512, cfg["S"])], BF16)
        modT_sb = sbuf("lo_m", [128, 6 * KC * cfg["BPC"]], F32)
        wtok = kb.dma("gpsimd", sb3(wo_sb[:], D),
                      aps["L1_attn_Wout"].rearrange("(kc kp) o -> kp kc o", kp=128),
                      stream="d_lo_w")
        btok = kb.dma("gpsimd", bo_sb[:], aps["L1_attn_bout"][:], stream="d_lo_b")
        onm = kb.ins("vector", lambda e: e.memset(on_sb[:], 1.0))
        mtok = kb.dma("sync", modT_sb[:], aps["modT1"][:], stream="d_lo_m")
        r = emit_proj_epilogue(kb, cfg, st, aps, "l1at", wo_sb, wtok,
                               "x2fm", "x3fm", modT_sb, mtok, 2, "_lo",
                               bias_sb=(bo_sb, on_sb, btok, onm))
    return r



def ph_l1ffn(kb, cfg, st, aps):
    D, S, T = cfg["D"], cfg["S"], cfg["T"]
    KC = D // 128
    SL = min(512, S)
    nc = kb.nc
    with contextlib.ExitStack() as es:
        def sbuf(n, sh, dt):
            return es.enter_context(nc.sbuf_tensor(n, sh, dt))
        def psum(n, sh, dt=F32):
            return es.enter_context(nc.psum_tensor(n, sh, dt))
        kb_sb = sbuf("lf_k", [128, KC * D], BF16)
        k8_sb = sbuf("lf_k8", [128, KC * D], F8)
        h_sb = sbuf("lf_h", [128, KC * SL], BF16)
        h8_sb = sbuf("lf_h8", [128, KC * SL], F8)
        x_sb = sbuf("lf_x", [128, KC * SL], F32)
        y_sb = sbuf("lf_y", [128, SL], BF16)
        sc_sb = sbuf("lf_sc", [1, 8], F32)
        rep_sb = sbuf("lf_rep", [128, 3], F32)
        bia_r = sbuf("lf_br", [8, 128], BF16)
        bia_sb = sbuf("lf_b", [128, 8], F32)
        modT_sb = sbuf("lf_m", [128, 6 * KC * cfg["BPC"]], F32)
        eg_sb = sbuf("lf_eg", [128, 2 * D], F32)
        pp0 = psum("lf_p0", [128, 512])
        pp1 = psum("lf_p1", [128, 512])
        pb8 = psum("lf_pb", [128, 8], BF16)
        mtok = kb.dma("sync", modT_sb[:], aps["modT1"][:], stream="d_lf_m")
        s1 = kb.dma("sync", sc_sb[0:1, 0:1], aps["L1_ffn_input_scale"][:],
                    stream="d_lf_s")
        s2 = kb.dma("sync", sc_sb[0:1, 1:2], aps["L1_ffn_kernel_scale"][:],
                    stream="d_lf_s2")
        t_a = kb.ins("vector", lambda e: e.tensor_add(
            sc_sb[0:1, 2:3], sc_sb[0:1, 0:1], sc_sb[0:1, 0:1]), deps=[s1])
        t_b = kb.ins("vector", lambda e: e.tensor_add(
            sc_sb[0:1, 3:4], sc_sb[0:1, 1:2], sc_sb[0:1, 1:2]), deps=[s2])
        t_ra = kb.ins("vector", lambda e: e.reciprocal(
            sc_sb[0:1, 5:6], sc_sb[0:1, 2:3]), deps=[t_a])
        t_rb = kb.ins("vector", lambda e: e.reciprocal(
            sc_sb[0:1, 6:7], sc_sb[0:1, 3:4]), deps=[t_b])
        t_c = kb.ins("vector", lambda e: e.tensor_mul(
            sc_sb[0:1, 4:5], sc_sb[0:1, 0:1], sc_sb[0:1, 1:2]), deps=[s1, s2])
        t_d = kb.ins("vector", lambda e: e.tensor_scalar(
            sc_sb[0:1, 7:8], sc_sb[0:1, 4:5], 4.0, None, ALU.mult), deps=[t_c])
        mrep = kb.ins("tensor", lambda e: e.matmul(
            pp0[:, 0:3], st["c_ones_row"][0:1, :], sc_sb[0:1, 5:8],
            start=True, stop=True), deps=[t_ra, t_rb, t_d])
        erep = kb.ins("vector", lambda e: e.tensor_copy(rep_sb[:], pp0[:, 0:3]),
                      deps=[mrep])
        kl = kb.dma("sync", sb3(kb_sb[:], D),
                    aps["L1_ffn_kernel"].rearrange("(kc kp) o -> kp kc o", kp=128),
                    stream="d_lf_k")
        kq = kb.ins("vector", lambda e: e.tensor_scalar(
            k8_sb[:], kb_sb[:], rep_sb[:, 1:2], None, ALU.mult),
            deps=[kl, erep])
        bl = kb.dma("sync", bia_r[:], aps["L1_ffn_bias"][0, :]
                    .rearrange("(a b) -> a b", b=128), stream="d_lf_b")
        tb = kb.ins("tensor", lambda e: e.transpose(
            pb8[:], bia_r[:], st["c_ident"][0:8, 0:8]), deps=[bl])
        eb = kb.ins("vector", lambda e: e.tensor_copy(bia_sb[:], pb8[:]),
                    deps=[tb])
        prev = {}
        for sl in range(T // SL):
            b = (sl * SL) // S
            hdl, xdl = [], []
            for c in range(KC):
                hdl.append(kb.dma("sync", sb3(h_sb[:, c * SL:(c + 1) * SL]),
                                  fm_ap(aps["h4"], c, sl * SL, SL),
                                  deps=prev.get("h", []), stream="d_lf_h"))
                xdl.append(kb.dma("gpsimd", sb3(x_sb[:, c * SL:(c + 1) * SL]),
                                  fm_ap(aps["x3fm"], c, sl * SL, SL),
                                  deps=prev.get("x", []), stream="d_lf_x"))
            hdl, xdl = [hdl[-1]], [xdl[-1]]
            hqs = []
            for c in range(KC):
                hqs.append(kb.ins("vector", lambda e, c=c: e.tensor_scalar(
                    h8_sb[:, c * SL:(c + 1) * SL], h_sb[:, c * SL:(c + 1) * SL],
                    rep_sb[:, 0:1], None, ALU.mult),
                    deps=hdl + [erep] + prev.get("h8", [])))
            outs = []
            for och in range(KC):
                pp = pp0 if och % 2 == 0 else pp1
                mms = []
                for k in range(KC):
                    mms.append(kb.ins("tensor", lambda e, k=k, och=och, pp=pp: e.matmul(
                        pp[:, 0:SL], k8_sb[:, k * D + och * 128: k * D + (och + 1) * 128],
                        h8_sb[:, k * SL:(k + 1) * SL],
                        start=(k == 0), stop=(k == KC - 1)),
                        deps=[kq, hqs[k]] + prev.get(("pp", och % 2), [])))
                ya = kb.ins("vector", lambda e, pp=pp: e.tensor_scalar(
                    y_sb[:], pp[:, 0:SL], rep_sb[:, 2:3], None, ALU.mult),
                    deps=mms[-1:] + [erep] + prev.get("y", []))
                yb = kb.ins("vector", lambda e, och=och: e.tensor_scalar(
                    y_sb[:], y_sb[:], bia_sb[:, och:och + 1], None, ALU.add),
                    deps=[ya, eb])
                ep = kb.ins("vector", lambda e, och=och, b=b: e.scalar_tensor_tensor(
                    x_sb[:, och * SL:(och + 1) * SL], y_sb[:],
                    modT_sb[:, modcol(cfg, 5, och, b):modcol(cfg, 5, och, b) + 1],
                    x_sb[:, och * SL:(och + 1) * SL], ALU.mult, ALU.add),
                    deps=[yb, mtok] + xdl)
                outs.append(ep)
                prev[("pp", och % 2)] = [ya]
                prev["y"] = [ep]
            odmas = []
            for tt in range(SL // 128):
                tts = []
                for c in range(KC):
                    pp2 = pp0 if c < 4 else pp1
                    tts.append(kb.ins("tensor", lambda e, c=c, tt=tt, pp2=pp2: e.transpose(
                        pp2[:, (c % 4) * 128:(c % 4 + 1) * 128],
                        x_sb[:, c * SL + tt * 128: c * SL + (tt + 1) * 128],
                        st["c_identf"][:]),
                        deps=[outs[c], outs[-1]] + prev.get("egprev", [])))
                ev = kb.ins("vector", lambda e, tt=tt: e.tensor_copy(
                    eg_sb[:, (tt % 2) * D: (tt % 2) * D + 512], pp0[:]),
                    deps=tts + prev.get(("egs", tt % 2), []))
                ev2 = kb.ins("vector", lambda e, tt=tt: e.tensor_copy(
                    eg_sb[:, (tt % 2) * D + 512:(tt % 2 + 1) * D], pp1[:]),
                    deps=tts + prev.get(("egs", tt % 2), []))
                od = kb.dma("sync", aps["out"][sl * SL + tt * 128: sl * SL + (tt + 1) * 128, :],
                            eg_sb[:, (tt % 2) * D:(tt % 2 + 1) * D],
                            deps=[ev, ev2], stream="d_lf_o%d" % (tt % 2))
                odmas.append(od)
                prev["egprev"] = [ev, ev2]
                prev[("egs", tt % 2)] = [od]
            prev["h"] = hqs
            prev["h8"] = mms
            prev["x"] = tts
            prev[("pp", 0)] = [ev]
            prev[("pp", 1)] = [ev2]
    return odmas


def ph_egress(kb, cfg, st, aps):
    return emit_tm2fm(kb, cfg, st, aps["x4fm"], aps["out"], "_eg", rev=True)



_LAUNCH_SPECS = [
    ([(ph_ingest, {}), (ph_wT, {}), (ph_mod, {"layer": 0}),
      (ph_ln, {"x": "x0fm", "h": "h1", "layer": 0, "which": "A"}),
      (ph_l0qkv, {})],
     [], ["x0fm", "modT0", "qro", "kro", "vtm", "woT"]),
    ([(ph_l0attn, {}), (ph_l0wo, {})],
     ["x0fm", "modT0", "qro", "kro", "vtm", "woT"], ["x1fm"]),
    ([(ph_ln, {"x": "x1fm", "h": "h2", "layer": 0, "which": "B"}),
      (ph_route, {}), (ph_moe, {})],
     ["x1fm", "modT0"], ["x2fm"]),
    ([(ph_mod, {"layer": 1}),
      (ph_ln, {"x": "x2fm", "h": "h3", "layer": 1, "which": "A"}),
      (ph_l1proj, {}), (ph_l1attn, {}), (ph_l1out, {})],
     ["x2fm"], ["x3fm", "modT1"]),
    ([(ph_ln, {"x": "x3fm", "h": "h4", "layer": 1, "which": "B"}),
      (ph_l1ffn, {})],
     ["x3fm", "modT1"], ["out"]),
]

_prog_cache = {}


def _get_prog(cfg_key, li, cfg):
    if (cfg_key, li) not in _prog_cache:
        phases, sc_in, sc_out = _LAUNCH_SPECS[li]
        ext = {n: "in" for n in sc_in}
        ext.update({n: "out" for n in sc_out})
        _prog_cache[(cfg_key, li)] = build_program(cfg, phases, ext)
    return _prog_cache[(cfg_key, li)]


def _shard_inputs(cfg, full):
    NC, BPC, D, S = cfg["NCORES"], cfg["BPC"], cfg["D"], cfg["S"]
    specs = input_specs_per_core(cfg)
    maps = []
    for c in range(NC):
        m = {}
        for name, (shape, dt) in specs.items():
            a = full[name]
            if name == "L0_attn_hidden_states":
                a = np.ascontiguousarray(a[c * BPC:(c + 1) * BPC]).reshape(BPC * S, D)
            elif name in ("L0_c", "L1_c"):
                a = np.ascontiguousarray(a[c * BPC:(c + 1) * BPC])
            else:
                a = np.asarray(a)
            a = a.reshape(shape)
            m[name] = a
        m.update(host_constants(cfg, c))
        maps.append(m)
    return maps


def kernel(**inputs):
    if FUSED:
        return kernel_fused(**inputs)
    return _kernel_phased(**inputs)


_ALL_PHASES = [
    (ph_ingest, {}), (ph_wT, {}), (ph_mod, {"layer": 0}),
    (ph_ln, {"x": "x0fm", "h": "h1", "layer": 0, "which": "A"}),
    (ph_l0qkv, {}),
    (ph_l0attn, {}), (ph_l0wo, {}),
    (ph_ln, {"x": "x1fm", "h": "h2", "layer": 0, "which": "B"}),
    (ph_route, {}), (ph_moe, {}),
    (ph_mod, {"layer": 1}),
    (ph_ln, {"x": "x2fm", "h": "h3", "layer": 1, "which": "A"}),
    (ph_l1proj, {}), (ph_l1attn, {}), (ph_l1out, {}),
    (ph_ln, {"x": "x3fm", "h": "h4", "layer": 1, "which": "B"}),
    (ph_l1ffn, {}),
]

FUSED = True


def kernel_fused(**inputs):
    cfg = make_cfg()
    NC = cfg["NCORES"]
    maps = _shard_inputs(cfg, inputs)
    if ("fused", 0) not in _prog_cache:
        _prog_cache[("fused", 0)] = build_program(cfg, _ALL_PHASES, {"out": "out"})
    nc = _prog_cache[("fused", 0)]
    res = run_bass_kernel_spmd(nc, maps, core_ids=list(range(NC)))
    out = np.concatenate(
        [np.asarray(res.results[c]["out"]).reshape(cfg["BPC"], cfg["S"], cfg["D"])
         for c in range(NC)], axis=0)
    return out.astype(np.float32)


def _kernel_phased(**inputs):
    cfg = make_cfg()
    NC = cfg["NCORES"]
    maps = _shard_inputs(cfg, inputs)
    carry = [dict() for _ in range(NC)]
    for li in range(len(_LAUNCH_SPECS)):
        phases, sc_in, sc_out = _LAUNCH_SPECS[li]
        nc = _get_prog("full", li, cfg)
        in_maps = []
        for c in range(NC):
            m = dict(maps[c])
            for n in sc_in:
                m[n] = carry[c][n]
            in_maps.append(m)
        res = run_bass_kernel_spmd(nc, in_maps, core_ids=list(range(NC)))
        for c in range(NC):
            for n in sc_out:
                carry[c][n] = np.asarray(res.results[c][n])
    out = np.concatenate([carry[c]["out"].reshape(cfg["BPC"], cfg["S"], cfg["D"])
                          for c in range(NC)], axis=0)
    return out.astype(np.float32)


# version 15
# speedup vs baseline: 1.0226x; 1.0009x over previous
import contextlib
import math
import numpy as np
import ml_dtypes
import concourse.bass as bass
import concourse.mybir as mybir
from concourse.bass_utils import run_bass_kernel_spmd

F32 = mybir.dt.float32
BF16 = mybir.dt.bfloat16
F8 = mybir.dt.float8e4
AF = mybir.ActivationFunctionType
ALU = mybir.AluOpType
AX = mybir.AxisListType
BF16NP = ml_dtypes.bfloat16


def make_cfg(S=2048, BPC=2, NCORES=8):
    c = {}
    c["D"] = 1024
    c["NH"] = 16
    c["DH"] = 64
    c["S"] = S
    c["BPC"] = BPC
    c["T"] = S * BPC
    c["NCORES"] = NCORES
    c["E"] = 8
    c["HF"] = 4096
    c["MOE_K"] = 2
    n_total = S * BPC * NCORES
    c["CAP"] = int(math.ceil(1.0 * c["MOE_K"] * n_total / c["E"]))
    c["DLAT"] = 256
    return c



def host_constants(cfg, core_id):
    D, S, NH, DH = cfg["D"], cfg["S"], cfg["NH"], cfg["DH"]
    out = {}
    out["c_ident"] = np.eye(128, dtype=BF16NP)
    out["c_identf"] = np.eye(128, dtype=np.float32)
    out["c_triT"] = np.triu(np.ones((128, 128), dtype=np.float32)).copy()
    out["c_ones_kxm"] = np.ones((128, 128), dtype=BF16NP)
    out["c_ones_col"] = np.ones((128, 1), dtype=np.float32)
    out["c_ones_row"] = np.ones((1, 128), dtype=np.float32)
    out["c_ones_row_bf"] = np.ones((1, 128), dtype=BF16NP)
    P = np.zeros((DH, DH), dtype=np.float32)
    for j in range(DH // 2):
        P[j, 2 * j + 1] = -1.0
    for i in range(DH // 2):
        P[DH // 2 + i, 2 * i] = 1.0
    P2 = np.zeros((128, 128), dtype=np.float32)
    P2[:DH, :DH] = P.T
    P2[DH:, DH:] = P.T
    out["c_p2"] = P2.astype(BF16NP)
    invf = (1.0 / 10000.0 ** (np.arange(0, DH, 2, dtype=np.float32) / DH)).astype(np.float32)
    t = np.arange(S, dtype=np.float32)
    freqs = np.einsum("i,j->ij", t, invf).astype(np.float32)
    emb = np.concatenate([freqs, freqs], axis=-1)
    cosT = np.cos(emb).T.astype(np.float32)
    sinT = np.sin(emb).T.astype(np.float32)
    out["c_l0cos"] = np.concatenate([cosT, cosT], axis=0).copy()
    out["c_l0sin"] = np.concatenate([sinT, sinT], axis=0).copy()
    fr1 = (t[:, None] * invf[None, :]).astype(np.float32)
    out["c_l1cos"] = np.tile(np.cos(fr1), (1, NH)).astype(np.float32).copy()
    out["c_l1sin"] = np.tile(np.sin(fr1), (1, NH)).astype(np.float32).copy()
    sel4 = np.zeros((4, 4 * 128), dtype=np.float32)
    for r in range(4):
        sel4[r, r * 128:(r + 1) * 128] = 1.0
    out["c_sel4"] = sel4
    sel8 = np.zeros((8, 8 * 128), dtype=np.float32)
    for r in range(8):
        sel8[r, r * 128:(r + 1) * 128] = 1.0
    out["c_sel8"] = sel8
    pm = np.zeros((cfg["NCORES"], 1), dtype=np.float32)
    pm[:core_id] = 1.0
    out["c_prevmask"] = pm
    out["c_allmask"] = np.ones((cfg["NCORES"], 1), dtype=np.float32)
    out["c_eps5"] = np.full((128, 1), 1e-5, dtype=np.float32)
    he = np.zeros((1, 128), dtype=np.float32); he[0, :64] = 1.0
    ho = np.zeros((1, 128), dtype=np.float32); ho[0, 64:] = 1.0
    out["c_half_even"] = he
    out["c_half_odd"] = ho
    return out


CONST_SPECS = {
    "c_ident": ([128, 128], BF16),
    "c_identf": ([128, 128], F32),
    "c_triT": ([128, 128], F32),
    "c_ones_kxm": ([128, 128], BF16),
    "c_ones_col": ([128, 1], F32),
    "c_ones_row": ([1, 128], F32),
    "c_ones_row_bf": ([1, 128], BF16),
    "c_p2": ([128, 128], BF16),
    "c_sel4": ([4, 4 * 128], F32),
    "c_sel8": ([8, 8 * 128], F32),
    "c_eps5": ([128, 1], F32),
    "c_prevmask": (None, F32),
    "c_allmask": (None, F32),
    "c_half_even": ([1, 128], F32),
    "c_half_odd": ([1, 128], F32),
}



ENGINES = ("sync", "tensor", "vector", "scalar", "gpsimd")


class KB:

    def __init__(self, nc, block):
        self.nc = nc
        self.block = block
        self.sems = {}
        self.counts = {}
        self.waited = {e: {} for e in ENGINES}
        self.q = {e: [] for e in ENGINES}
        for e in ("tensor", "vector", "scalar", "gpsimd"):
            self._mk_sem("e_" + e)

    def _mk_sem(self, name):
        self.sems[name] = self.nc.alloc_semaphore(name=name)
        self.counts[name] = 0

    def flush(self):
        for eng in ENGINES:
            ts = self.q[eng]
            if not ts:
                continue
            self.q[eng] = []

            def body(e, ts=ts):
                for t in ts:
                    t(e)
            getattr(self.block, eng)(body)

    def ins(self, eng, fn, deps=(), sem_name=None):
        sem_name = sem_name or ("e_" + eng)
        if sem_name not in self.sems:
            self._mk_sem(sem_name)
        sem = self.sems[sem_name]
        self.counts[sem_name] += 1
        cnt = self.counts[sem_name]
        deps = tuple(deps)

        def body(e):
            self._do_waits(e, eng, deps)
            fn(e).then_inc(sem, 1)
        self.q[eng].append(body)
        return (sem_name, cnt)

    def dma(self, queue, out, in_, deps=(), stream=None, **kw):
        name = stream or ("dma_" + queue)
        if name not in self.sems:
            self._mk_sem(name)
        sem = self.sems[name]
        self.counts[name] += 16
        cnt = self.counts[name]
        deps = tuple(deps)

        def body(e):
            self._do_waits(e, queue, deps)
            e.dma_start(out=out, in_=in_, **kw).then_inc(sem, 16)
        self.q[queue].append(body)
        return (name, cnt)

    def _do_waits(self, e, eng, deps):
        w = self.waited[eng]
        my = "e_" + eng
        agg = {}
        for sem_name, val in deps:
            if sem_name == my and self.counts[my] < val:
                raise RuntimeError("dep on future same-engine instruction")
            if val > agg.get(sem_name, 0):
                agg[sem_name] = val
        for sem_name, val in agg.items():
            if w.get(sem_name, 0) >= val:
                continue
            e.wait_ge(self.sems[sem_name], val)
            w[sem_name] = val

    def wait(self, eng, deps):
        deps = tuple(deps)

        def body(e):
            self._do_waits(e, eng, deps)
        self.q[eng].append(body)

    def barrier(self, eng):
        return ("e_" + eng, self.counts["e_" + eng])

    def full_barrier(self, exclude=()):
        toks = [(n, self.counts[n]) for n in self.sems if n not in exclude]
        for e in ENGINES:
            self.wait(e, toks)


def fm_ap(fm, c, t0, width):
    nt = width // 128
    return fm[t0 // 128: t0 // 128 + nt, c * 128:(c + 1) * 128, :].rearrange(
        "n p t -> p n t")


def sb3(sb_slice, tile=128):
    return sb_slice.rearrange("p (n t) -> p n t", t=tile)



def emit_tm2fm(kb, cfg, st, src_ap, dst_ap, sfx, rev=False):
    T, D = cfg["T"], cfg["D"]
    nc = kb.nc
    with (
        nc.sbuf_tensor("tf_in" + sfx, [128, D], F32) as tin_sb,
        nc.sbuf_tensor("tf_out" + sfx, [128, D], F32) as tout_sb,
        nc.psum_tensor("tf_ps0" + sfx, [128, 512], F32) as ps0,
        nc.psum_tensor("tf_ps1" + sfx, [128, 512], F32) as ps1,
    ):
        ps = (ps0, ps1)
        prev_tts, prev_out = [], []
        for tile in range(T // 128):
            if rev:
                src = src_ap[tile, :, :].rearrange("(c p) t -> p c t", p=128)
            else:
                src = src_ap[tile * 128:(tile + 1) * 128, :]
            tin = kb.dma("sync", sb3(tin_sb[:]) if rev else tin_sb[:], src,
                         deps=prev_tts, stream="d_tf_in")
            evs, tts = [], []
            for g in range(2):
                gtts = []
                for c in range(4):
                    ch = g * 4 + c
                    gtts.append(kb.ins(
                        "tensor",
                        lambda e, ch=ch, g=g, c=c: e.transpose(
                            ps[g][:, c * 128:(c + 1) * 128],
                            tin_sb[:, ch * 128:(ch + 1) * 128], st["c_identf"][:]),
                        deps=[tin] + prev_out))
                evs.append(kb.ins(
                    "vector",
                    lambda e, g=g: e.tensor_copy(tout_sb[:, g * 512:(g + 1) * 512], ps[g][:]),
                    deps=gtts))
                tts += gtts
            if rev:
                dst = dst_ap[tile * 128:(tile + 1) * 128, :]
            else:
                dst = dst_ap[tile, :, :].rearrange("(c p) t -> p c t", p=128)
            tok = kb.dma("sync", dst, tout_sb[:] if rev else sb3(tout_sb[:]),
                         deps=evs, stream="d_tf_out")
            prev_tts = tts
            prev_out = [tok]
    return [tok]



def emit_mod(kb, cfg, st, c_ap, wmod_ap, modT_ap, sfx):
    D, BPC = cfg["D"], cfg["BPC"]
    NV = 6
    nc = kb.nc
    KC = D // 128
    with (
        nc.sbuf_tensor("md_pad" + sfx, [128, D], F32) as pad,
        nc.sbuf_tensor("md_siluT" + sfx, [128, KC * BPC], F32) as siluT,
        nc.sbuf_tensor("md_w" + sfx, [128, 2 * 512], F32) as wmt_d,
        nc.sbuf_tensor("md_mod" + sfx, [128, NV * D], F32) as modp,
        nc.sbuf_tensor("md_t" + sfx, [128, D], F32) as tmp,
        nc.sbuf_tensor("md_t2" + sfx, [128, D], F32) as tmp2,
        nc.sbuf_tensor("md_modT" + sfx, [128, NV * KC * BPC], F32) as modT,
        nc.psum_tensor("md_ps" + sfx, [128, 512], F32) as ps,
        nc.psum_tensor("md_ps2" + sfx, [128, 512], F32) as ps2,
    ):
        z = kb.ins("vector", lambda e: e.memset(pad[:], 0.0))
        zm = kb.ins("vector", lambda e: e.memset(modp[:], 0.0))
        cl = kb.dma("sync", pad[0:BPC, :], c_ap, deps=[z], stream="d_mod")
        sl = kb.ins("scalar", lambda e: e.activation(pad[0:BPC, :], pad[0:BPC, :], AF.Silu),
                    deps=[cl])
        ev = None
        for k in range(KC):
            tt = kb.ins("tensor",
                        lambda e, k=k: e.transpose(ps[:, 0:128], pad[:, k * 128:(k + 1) * 128],
                                                   st["c_identf"][:]),
                        deps=[sl] + ([ev] if ev else []))
            ev = kb.ins("vector",
                        lambda e, k=k: e.tensor_copy(siluT[:, k * BPC:(k + 1) * BPC], ps[:, 0:BPC]),
                        deps=[tt])
        evm = None
        prev_mm = {}
        for j in range(NV * D // 512):
            toks = []
            for k in range(KC):
                w_t = wmt_d[:, (k % 2) * 512:(k % 2 + 1) * 512]
                wl = kb.dma("gpsimd", w_t,
                            wmod_ap[k * 128:(k + 1) * 128, j * 512:(j + 1) * 512],
                            deps=[prev_mm[k % 2]] if k % 2 in prev_mm else [],
                            stream="d_modw%d" % (k % 2))
                mm = kb.ins(
                    "tensor",
                    lambda e, k=k, w_t=w_t: e.matmul(
                        ps2[0:BPC, :], siluT[:, k * BPC:(k + 1) * BPC], w_t,
                        start=(k == 0), stop=(k == KC - 1)),
                    deps=[wl, ev] + ([evm] if evm else []))
                toks.append(mm)
                prev_mm[k % 2] = mm
            evm = kb.ins("vector",
                         lambda e, j=j: e.tensor_copy(modp[0:BPC, j * 512:(j + 1) * 512],
                                                      ps2[0:BPC, :]),
                         deps=[toks[-1], zm])
        post = [evm, zm]
        for v in (2, 5):
            sl_ = modp[0:BPC, v * D:(v + 1) * D]
            ta = kb.ins("scalar", lambda e, sl_=sl_: e.activation(
                tmp[0:BPC, 0:D], sl_, AF.Abs), deps=post)
            tc = kb.ins("vector", lambda e: e.tensor_scalar_max(
                tmp[0:BPC, 0:D], tmp[0:BPC, 0:D], 0.25), deps=[ta])
            tm = kb.ins("vector", lambda e, sl_=sl_: e.tensor_single_scalar(
                tmp2[0:BPC, 0:D], sl_, 0.0, ALU.is_ge), deps=post)
            ts = kb.ins("vector", lambda e: e.tensor_scalar(
                tmp2[0:BPC, 0:D], tmp2[0:BPC, 0:D], 2.0, -1.0, ALU.mult, ALU.add),
                deps=[tm])
            tg = kb.ins("vector", lambda e, sl_=sl_: e.tensor_mul(
                sl_, tmp[0:BPC, 0:D], tmp2[0:BPC, 0:D]), deps=[tc, ts])
            post = post + [tg]
        for v in (1, 4):
            sl_ = modp[0:BPC, v * D:(v + 1) * D]
            post.append(kb.ins("vector",
                               lambda e, sl_=sl_: e.tensor_scalar_add(sl_, sl_, 1.0),
                               deps=post))
        evq = None
        for q in range(NV * KC):
            tq = kb.ins("tensor",
                        lambda e, q=q: e.transpose(ps[:, 0:128], modp[:, q * 128:(q + 1) * 128],
                                                   st["c_identf"][:]),
                        deps=post + ([evq] if evq else []))
            evq = kb.ins("vector",
                         lambda e, q=q: e.tensor_copy(modT[:, q * BPC:(q + 1) * BPC], ps[:, 0:BPC]),
                         deps=[tq])
        tok = kb.dma("sync", modT_ap[:], modT[:], deps=[evq], stream="d_mod")
    return [tok]


def modcol(cfg, v, ch, b):
    return (v * (cfg["D"] // 128) + ch) * cfg["BPC"] + b



def emit_ln(kb, cfg, st, x_ap, h_ap, modT_sb, modT_tok, v_shift, v_scale, sfx,
            eps=1e-5):
    D, T, S, BPC = cfg["D"], cfg["T"], cfg["S"], cfg["BPC"]
    KC = D // 128
    nc = kb.nc
    SL = min(512, S)
    assert S % SL == 0
    with (
        nc.sbuf_tensor("ln_x" + sfx, [128, 2 * KC * SL], F32) as xs_d,
        nc.sbuf_tensor("ln_xb" + sfx, [128, KC * SL], BF16) as xb,
        nc.sbuf_tensor("ln_sq" + sfx, [128, KC * SL], BF16) as sq,
        nc.sbuf_tensor("ln_mean" + sfx, [128, SL], F32) as mean,
        nc.sbuf_tensor("ln_t" + sfx, [128, SL], F32) as tvar,
        nc.sbuf_tensor("ln_rstd" + sfx, [128, SL], F32) as rstd,
        nc.sbuf_tensor("ln_h" + sfx, [128, KC * SL], BF16) as hb,
        nc.psum_tensor("ln_pm" + sfx, [128, SL], F32) as pm,
        nc.psum_tensor("ln_pq" + sfx, [128, SL], F32) as pq,
    ):
        prev = {}
        for sl in range(T // SL):
            xs = xs_d[:, (sl % 2) * KC * SL:(sl % 2 + 1) * KC * SL]
            b = (sl * SL) // S
            loads = []
            for c in range(KC):
                loads.append((c, kb.dma(
                    "sync", sb3(xs[:, c * SL:(c + 1) * SL]), fm_ap(x_ap, c, sl * SL, SL),
                    deps=prev.get(("x", sl % 2), []), stream="d_ln_x")))
            last_ld = loads[-1][1]
            casts, sqs = [], []
            for c, ld in loads:
                casts.append(kb.ins("vector", lambda e, c=c, xs=xs: e.tensor_copy(
                    xb[:, c * SL:(c + 1) * SL], xs[:, c * SL:(c + 1) * SL]),
                    deps=[last_ld] + prev.get("xb", [])))
                sqs.append(kb.ins("scalar", lambda e, c=c, xs=xs: e.activation(
                    sq[:, c * SL:(c + 1) * SL], xs[:, c * SL:(c + 1) * SL], AF.Square),
                    deps=[last_ld] + prev.get("sq", [])))
            mms, mqs = [], []
            for c in range(KC):
                mms.append(kb.ins("tensor", lambda e, c=c: e.matmul(
                    pm[:], st["c_ones_kxm"][:], xb[:, c * SL:(c + 1) * SL],
                    start=(c == 0), stop=(c == KC - 1)),
                    deps=[casts[c]] + prev.get("pm", [])))
            for c in range(KC):
                mqs.append(kb.ins("tensor", lambda e, c=c: e.matmul(
                    pq[:], st["c_ones_kxm"][:], sq[:, c * SL:(c + 1) * SL],
                    start=(c == 0), stop=(c == KC - 1)),
                    deps=[sqs[c]] + prev.get("pq", [])))
            t_mean = kb.ins("scalar", lambda e: e.activation(
                mean[:], pm[:], AF.Copy, scale=1.0 / D),
                deps=[mms[-1]] + prev.get("mean", []))
            t_m2 = kb.ins("vector", lambda e: e.tensor_mul(tvar[:], mean[:], mean[:]),
                          deps=[t_mean] + prev.get("tvar", []))
            t_var = kb.ins("vector", lambda e: e.scalar_tensor_tensor(
                tvar[:], pq[:], 1.0 / D, tvar[:], ALU.mult, ALU.subtract),
                deps=[mqs[-1], t_m2])
            t_sd = kb.ins("scalar", lambda e: e.activation(
                tvar[:], tvar[:], AF.Sqrt, bias=st["c_eps5"][:]),
                deps=[t_var] + st["tok"])
            t_rs = kb.ins("vector", lambda e: e.reciprocal(rstd[:], tvar[:]),
                          deps=[t_sd] + prev.get("rstd", []))
            houts = []
            hdeps = prev.get("hdma", [])
            for c in range(KC):
                t1 = kb.ins("vector", lambda e, c=c, xs=xs: e.tensor_sub(
                    xs[:, c * SL:(c + 1) * SL],
                    xs[:, c * SL:(c + 1) * SL], mean[:]),
                    deps=[t_mean, mms[-1], mqs[-1]])
                t2 = kb.ins("vector", lambda e, c=c, xs=xs: e.tensor_mul(
                    xs[:, c * SL:(c + 1) * SL], xs[:, c * SL:(c + 1) * SL], rstd[:]),
                    deps=[t_rs, t1])
                t3 = kb.ins("vector", lambda e, c=c, b=b, xs=xs: e.tensor_scalar(
                    hb[:, c * SL:(c + 1) * SL], xs[:, c * SL:(c + 1) * SL],
                    modT_sb[:, modcol(cfg, v_scale, c, b):modcol(cfg, v_scale, c, b) + 1],
                    modT_sb[:, modcol(cfg, v_shift, c, b):modcol(cfg, v_shift, c, b) + 1],
                    ALU.mult, ALU.add),
                    deps=[t2, modT_tok] + hdeps)
                houts.append(t3)
            hdma = []
            for c in range(KC):
                hdma.append(kb.dma("sync", fm_ap(h_ap, c, sl * SL, SL),
                                   sb3(hb[:, c * SL:(c + 1) * SL]), deps=[houts[c]],
                                   stream="d_ln_h"))
            prev = {("x", sl % 2): houts,
                    "xb": mms, "sq": mqs,
                    "pm": [t_mean], "pq": [t_var],
                    "mean": houts, "tvar": [t_rs], "rstd": houts,
                    "hdma": hdma}
    return hdma



def scratch_specs(cfg):
    T, D, NH, S, BPC = cfg["T"], cfg["D"], cfg["NH"], cfg["S"], cfg["BPC"]
    return {
        "x0fm": ([T // 128, D, 128], F32),
        "x1fm": ([T // 128, D, 128], F32),
        "x2fm": ([T // 128, D, 128], F32),
        "x3fm": ([T // 128, D, 128], F32),
        "h1": ([T // 128, D, 128], BF16),
        "h2": ([T // 128, D, 128], BF16),
        "h3": ([T // 128, D, 128], BF16),
        "h4": ([T // 128, D, 128], BF16),
        "modT0": ([128, 6 * (D // 128) * BPC], F32),
        "modT1": ([128, 6 * (D // 128) * BPC], F32),
        "wqT": ([D, D], BF16),
        "wkT": ([D, D], BF16),
        "wvT": ([D, D], BF16),
        "woT": ([D, D], BF16),
        "qro": ([D, T], BF16),
        "kro": ([D, T], BF16),
        "vtm": ([T, NH * 65], BF16),
        "attnfm": ([D, T], BF16),
        "cntloc": ([1, 16], F32),
        "cntg": ([cfg["NCORES"], 16], F32),
        "w8": ([8, T], F32),
        "l1q": ([T, D], BF16),
        "l1k": ([T, D], BF16),
        "l1v": ([T, D], BF16),
        "l1at": ([D, T], BF16),
        "x4fm": ([T // 128, D, 128], F32),
        "w1b": ([8, D, 4096], BF16),
        "w2b": ([8, 4096, D], BF16),
        "out": ([T, D], F32),
    }


def input_specs_per_core(cfg):
    D, BPC, NH = cfg["D"], cfg["BPC"], cfg["NH"]
    return {
        "L0_attn_hidden_states": ([BPC * cfg["S"], D], F32),
        "L0_attn_Wq": ([D, D], F32), "L0_attn_Wk": ([D, D], F32),
        "L0_attn_Wv": ([D, D], F32), "L0_attn_Wo": ([D, D], F32),
        "L0_ffn_wg": ([D, 8], F32),
        "L0_ffn_w1": ([8, D, 4096], F32), "L0_ffn_b1": ([8, 4096], F32),
        "L0_ffn_w2": ([8, 4096, D], F32), "L0_ffn_b2": ([8, D], F32),
        "L0_c": ([BPC, D], F32), "L0_wmod": ([D, 6 * D], F32),
        "L1_attn_Wq": ([D, D], F32), "L1_attn_bq": ([1, D], F32),
        "L1_attn_Wdown": ([D, 256], F32), "L1_attn_bdown": ([1, 256], F32),
        "L1_attn_Wup": ([256, 2 * D], F32), "L1_attn_bup": ([1, 2 * D], F32),
        "L1_attn_Wout": ([D, D], F32), "L1_attn_bout": ([1, D], F32),
        "L1_ffn_kernel": ([D, D], BF16),
        "L1_ffn_input_scale": ([1, 1], F32), "L1_ffn_kernel_scale": ([1, 1], F32),
        "L1_ffn_bias": ([1, D], BF16),
        "L1_c": ([BPC, D], F32), "L1_wmod": ([D, 6 * D], F32),
    }


def emit_setup(kb, cfg, aps):
    nc = kb.nc
    st = {}
    toks = []
    es = contextlib.ExitStack()
    for name, (shape, dt) in CONST_SPECS.items():
        if shape is None:
            shape = [cfg["NCORES"], 1]
        t = es.enter_context(nc.sbuf_tensor("st_" + name, shape, dt))
        st[name] = t
        toks.append(kb.dma("sync", t[:], aps[name][:], stream="d_setup"))
    st["tok"] = toks
    st["es"] = es
    return st


def build_program(cfg, phases, external_scratch):
    nc = bass.Bass("TRN2", target_bir_lowering=False)
    aps = {}
    ins_sp = input_specs_per_core(cfg)
    for name, (shape, dt) in ins_sp.items():
        aps[name] = nc.dram_tensor(name, shape, dt, kind="ExternalInput").ap()
    for name, arr in host_constants(cfg, 0).items():
        dt = BF16 if arr.dtype == BF16NP else F32
        aps[name] = nc.dram_tensor(name, list(arr.shape), dt,
                                   kind="ExternalInput").ap()
    sc = scratch_specs(cfg)
    for name, (shape, dt) in sc.items():
        kind = {"in": "ExternalInput", "out": "ExternalOutput"}.get(
            external_scratch.get(name), "Internal")
        aps[name] = nc.dram_tensor(name, shape, dt, kind=kind).ap()
    with nc.Block() as block:
        kb = KB(nc, block)
        st = emit_setup(kb, cfg, aps)
        for _e in ENGINES:
            kb.wait(_e, [("d_setup", kb.counts["d_setup"])])
        kb.cross = {}
        wconv_streams = ("d_wc0", "d_wc1", "d_wco0", "d_wco1", "e_wconv")
        final = []
        for fn, kw in phases:
            final = fn(kb, cfg, st, aps, **kw)
            kb.full_barrier(exclude=wconv_streams)
        for qname in list(kb.sems):
            if qname.startswith("d"):
                kb.wait("gpsimd", [(qname, kb.counts[qname])])
        for en in ("tensor", "vector", "scalar"):
            kb.wait("gpsimd", [kb.barrier(en)])
        kb.flush()
        st["es"].close()
    return nc



def ph_ingest(kb, cfg, st, aps, dst="x0fm"):
    return emit_tm2fm(kb, cfg, st, aps["L0_attn_hidden_states"], aps[dst], "_ing")


def ph_mod(kb, cfg, st, aps, layer):
    return emit_mod(kb, cfg, st, aps[f"L{layer}_c"][:, :], aps[f"L{layer}_wmod"],
                    aps[f"modT{layer}"], f"_m{layer}")


def ph_ln(kb, cfg, st, aps, x, h, layer, which):
    v_shift, v_scale = (0, 1) if which == "A" else (3, 4)
    sfx = f"_ln{layer}{which}"
    nc = kb.nc
    with nc.sbuf_tensor("lnmod" + sfx, [128, 6 * (cfg["D"] // 128) * cfg["BPC"]],
                        F32) as modT_sb:
        mt = kb.dma("sync", modT_sb[:], aps[f"modT{layer}"][:], stream="d_lnm")
        r = emit_ln(kb, cfg, st, aps[x], aps[h], modT_sb, mt, v_shift, v_scale, sfx)
    return r



def emit_wT(kb, cfg, st, aps, pairs):
    D = cfg["D"]
    nc = kb.nc
    with (
        nc.sbuf_tensor("wt_in", [128, D], BF16) as win,
        nc.sbuf_tensor("wt_out", [128, D], BF16) as wout,
        nc.psum_tensor("wt_ps", [128, 512], BF16) as ps0,
        nc.psum_tensor("wt_ps2", [128, 512], BF16) as ps1,
    ):
        ps = (ps0, ps1)
        prev_tts, prev_out = [], []
        for (src, dst) in pairs:
            for i in range(D // 128):
                ld = kb.dma("gpsimd", win[:], aps[src][i * 128:(i + 1) * 128, :],
                            deps=prev_tts, stream="d_wt")
                evs, tts = [], []
                for g in range(2):
                    gt = []
                    for c in range(4):
                        j = g * 4 + c
                        gt.append(kb.ins(
                            "tensor",
                            lambda e, j=j, g=g, c=c: e.transpose(
                                ps[g][:, c * 128:(c + 1) * 128],
                                win[:, j * 128:(j + 1) * 128], st["c_ident"][:]),
                            deps=[ld] + prev_out))
                    evs.append(kb.ins(
                        "vector",
                        lambda e, g=g: e.tensor_copy(wout[:, g * 512:(g + 1) * 512],
                                                     ps[g][:]),
                        deps=gt))
                    tts += gt
                outs = []
                for g in range(2):
                    dstap = aps[dst][g * 512:(g + 1) * 512, i * 128:(i + 1) * 128] \
                        .rearrange("(n p) o -> p n o", p=128)
                    outs.append(kb.dma("sync", dstap,
                                       sb3(wout[:, g * 512:(g + 1) * 512]),
                                       deps=[evs[g]], stream="d_wt_o"))
                prev_tts = tts
                prev_out = outs
    return prev_out


def ph_wT(kb, cfg, st, aps):
    return emit_wT(kb, cfg, st, aps,
                   [("L0_attn_Wq", "wqT"), ("L0_attn_Wk", "wkT"),
                    ("L0_attn_Wv", "wvT"), ("L0_attn_Wo", "woT")])



def ph_l0qkv(kb, cfg, st, aps):
    D, S, T, NH = cfg["D"], cfg["S"], cfg["T"], cfg["NH"]
    KC = D // 128
    SL = min(512, S)
    nc = kb.nc
    with (
        nc.sbuf_tensor("qk_wq", [128, KC * D], BF16) as wq,
        nc.sbuf_tensor("qk_wk", [128, KC * D], BF16) as wk,
        nc.sbuf_tensor("qk_wv", [128, KC * D], BF16) as wv,
        nc.sbuf_tensor("qk_h", [128, 2 * KC * SL], BF16) as h_d,
        nc.sbuf_tensor("qk_cos", [128, SL], F32) as cos_sb,
        nc.sbuf_tensor("qk_sin", [128, SL], F32) as sin_sb,
        nc.sbuf_tensor("qk_qb", [128, SL], BF16) as qbf,
        nc.sbuf_tensor("qk_t1", [128, SL], F32) as tr1,
        nc.sbuf_tensor("qk_ro", [128, SL], BF16) as ro_sb,
        nc.sbuf_tensor("qk_v", [128, NH * 65], BF16) as v_sb,
        nc.psum_tensor("qk_p0", [128, 512], F32) as p0,
        nc.psum_tensor("qk_p1", [128, 512], F32) as p1,
        nc.psum_tensor("qk_pv", [128, 512], F32) as pv,
    ):
        wls = []
        for name, tile in (("wqT", wq), ("wkT", wk), ("wvT", wv)):
            wls.append(kb.dma("gpsimd", sb3(tile[:], D),
                              aps[name].rearrange("(kc kp) o -> kp kc o", kp=128),
                              stream="d_qk_w"))
        wl = wls[-1]
        prev = {}
        for sl in range(T // SL):
            b = (sl * SL) // S
            p0s = sl * SL - b * S
            hsb = h_d[:, (sl % 2) * KC * SL:(sl % 2 + 1) * KC * SL]
            hl = []
            for c in range(KC):
                hl.append(kb.dma("sync", sb3(hsb[:, c * SL:(c + 1) * SL]),
                                 fm_ap(aps["h1"], c, sl * SL, SL),
                                 deps=prev.get(("h", sl % 2), []),
                                 stream="d_qk_h%d" % (sl % 2)))
            hl = [hl[-1]]
            cl = kb.dma("sync", cos_sb[:], aps["c_l0cos"][:, p0s:p0s + SL],
                        deps=prev.get("cos", []), stream="d_qk_c")
            sl_t = kb.dma("sync", sin_sb[:], aps["c_l0sin"][:, p0s:p0s + SL],
                          deps=prev.get("cos", []), stream="d_qk_s")
            ro_dmas = []
            for which, wt, dst in (("q", wq, "qro"), ("k", wk, "kro")):
                for och in range(KC):
                    mms = []
                    for k in range(KC):
                        mms.append(kb.ins("tensor", lambda e, k=k, och=och, wt=wt, hsb=hsb: e.matmul(
                            p0[:, 0:SL], wt[:, k * D + och * 128: k * D + (och + 1) * 128],
                            hsb[:, k * SL:(k + 1) * SL],
                            start=(k == 0), stop=(k == KC - 1)),
                            deps=[wl] + hl + prev.get("p0", [])))
                    ev = kb.ins("vector", lambda e: e.tensor_copy(qbf[:], p0[:, 0:SL]),
                                deps=mms[-1:] + prev.get("qbf", []))
                    pq = kb.ins("tensor", lambda e: e.matmul(
                        p1[:, 0:SL], st["c_p2"][:], qbf[:], start=True, stop=True),
                        deps=[ev] + prev.get("p1", []))
                    t1 = kb.ins("vector", lambda e: e.tensor_mul(tr1[:], qbf[:], cos_sb[:]),
                                deps=[ev, cl] + prev.get("tr1", []))
                    t2 = kb.ins("vector", lambda e: e.scalar_tensor_tensor(
                        ro_sb[:], p1[:, 0:SL], 1.0, sin_sb[:], ALU.mult, ALU.mult),
                        deps=[pq, sl_t] + prev.get("ro", []))
                    t3 = kb.ins("vector", lambda e: e.tensor_add(ro_sb[:], ro_sb[:], tr1[:]),
                                deps=[t1, t2])
                    dma = kb.dma("sync", aps[dst][och * 128:(och + 1) * 128,
                                                  sl * SL:sl * SL + SL],
                                 ro_sb[:], deps=[t3], stream="d_qk_ro")
                    ro_dmas.append(dma)
                    prev["p0"] = [ev]
                    prev["qbf"] = [t1, t2]
                    prev["p1"] = [t2]
                    prev["tr1"] = [t3]
                    prev["ro"] = [dma]
            vdmas = []
            for tt in range(SL // 128):
                vouts = []
                for oh in range(2):
                    mms = []
                    for k in range(KC):
                        mms.append(kb.ins("tensor", lambda e, k=k, tt=tt, oh=oh, hsb=hsb: e.matmul(
                            pv[:], hsb[:, k * SL + tt * 128: k * SL + (tt + 1) * 128],
                            wv[:, k * D + oh * 512: k * D + (oh + 1) * 512],
                            start=(k == 0), stop=(k == KC - 1)),
                            deps=[wl] + hl + prev.get("pv", [])))
                    dst = v_sb[:, oh * 8 * 65:(oh + 1) * 8 * 65].rearrange(
                        "p (h x) -> p h x", x=65)[:, :, 0:64]
                    ev = kb.ins("vector", lambda e, dst=dst: e.tensor_copy(dst, sb3(pv[:], 64)),
                                deps=mms[-1:] + prev.get("v", []))
                    vouts.append(ev)
                    prev["pv"] = [ev]
                ones = kb.ins("vector", lambda e: e.memset(
                    v_sb[:].rearrange("p (h x) -> p h x", x=65)[:, :, 64:65], 1.0),
                    deps=vouts + prev.get("v", []))
                vd = kb.dma("sync", aps["vtm"][sl * SL + tt * 128: sl * SL + (tt + 1) * 128, :],
                            v_sb[:], deps=[ones] + vouts, stream="d_qk_v")
                vdmas.append(vd)
                prev["v"] = [vd]
            prev[("h", sl % 2)] = mms + [ev]
            prev["cos"] = [t1, t2, dma]
    return ro_dmas + vdmas



def ph_l0attn(kb, cfg, st, aps):
    D, S, T, NH, DH = cfg["D"], cfg["S"], cfg["T"], cfg["NH"], cfg["DH"]
    BPC = cfg["BPC"]
    QB = min(512, S)
    NTC = S // 128
    KW = min(512, S)
    NKW = S // KW
    W2 = 2 * QB
    nsub = QB // 128
    nc = kb.nc
    with contextlib.ExitStack() as es:
        def sbuf(n, sh, dt):
            return es.enter_context(nc.sbuf_tensor(n, sh, dt))
        def psum(n, sh, dt=F32):
            return es.enter_context(nc.psum_tensor(n, sh, dt))
        kaug_d = sbuf("at_kaug", [128, 2 * S], BF16)
        vaug_d = sbuf("at_vaug", [128, 2 * NTC * 65], BF16)
        qaug_d = sbuf("at_qaug", [128, 4 * QB], BF16)
        pt_sb = sbuf("at_pt", [128, 2 * NTC * QB], BF16)
        mch_d = sbuf("at_mch", [128, 2 * 4 * NKW], F32)
        mall_d = sbuf("at_mall", [128, 2 * 4], F32)
        m4_d = sbuf("at_m4", [4, 2 * 128], F32)
        linv_d = sbuf("at_linv", [1, 2 * QB], F32)
        lrep = sbuf("at_lrep", [128, QB], F32)
        att_sb = sbuf("at_out", [128, QB], BF16)
        psA = psum("at_psA", [128, 2 * KW])
        psB = psum("at_psB", [128, 2 * W2])
        pAV0 = psum("at_pAV0", [128, 512])
        pAV1 = psum("at_pAV1", [128, 512])
        pAV = (pAV0, pAV1)
        prev = {}
        for b in range(BPC):
            for hp in range(NH // 2):
                kls, vls = [], []
                for hh in range(2):
                    h = 2 * hp + hh
                    kaug = kaug_d[:, hh * S:(hh + 1) * S]
                    kls.append(kb.dma(
                        "sync", kaug[0:64, :],
                        aps["kro"][h * 64:(h + 1) * 64, b * S:(b + 1) * S],
                        deps=prev.get("kaug", []), stream="d_at_k%d" % hh))
                    kls.append(kb.ins("vector", lambda e, kaug=kaug: e.memset(
                        kaug[64:65, :], 1.0), deps=prev.get("kaug", [])))
                    vls.append(kb.dma(
                        "gpsimd", sb3(vaug_d[:, hh * NTC * 65:(hh + 1) * NTC * 65], 65),
                        aps["vtm"][b * S:(b + 1) * S, h * 65:(h + 1) * 65]
                        .rearrange("(tc tp) x -> tp tc x", tp=128),
                        deps=prev.get("vaug", []), stream="d_at_v%d" % hh))
                qb_toks = []
                for qb in range(S // QB):
                    qaugs, qls, stats = {}, {}, {}
                    for hh in range(2):
                        h = 2 * hp + hh
                        qaug = qaug_d[:, (2 * (qb % 2) + hh) * QB:
                                      (2 * (qb % 2) + hh + 1) * QB]
                        qaugs[hh] = qaug
                        qls[hh] = kb.dma("sync", qaug[0:64, :],
                                         aps["qro"][h * 64:(h + 1) * 64,
                                                    b * S + qb * QB: b * S + (qb + 1) * QB],
                                         deps=prev.get(("qaug", hh, qb % 2), []),
                                         stream="d_at_q%d" % hh)
                    mx_all = {}
                    for hh in range(2):
                        kaug = kaug_d[:, hh * S:(hh + 1) * S]
                        qaug = qaugs[hh]
                        mch = mch_d[:, hh * 4 * NKW:(hh + 1) * 4 * NKW]
                        mall = mall_d[:, hh * 4:(hh + 1) * 4]
                        mx_toks = []
                        for sub in range(nsub):
                            rds = []
                            for tc2 in range(NKW):
                                slot = tc2 % 2
                                mmA = kb.ins("tensor",
                                    lambda e, sub=sub, tc2=tc2, slot=slot, qaug=qaug, kaug=kaug: e.matmul(
                                        psA[:, slot * KW:(slot + 1) * KW],
                                        qaug[0:64, sub * 128:(sub + 1) * 128],
                                        kaug[0:64, tc2 * KW:(tc2 + 1) * KW],
                                        start=True, stop=True),
                                    deps=[qls[hh]] + kls + prev.get(("psA", slot), []))
                                rd = kb.ins("vector",
                                    lambda e, sub=sub, tc2=tc2, slot=slot, mch=mch: e.reduce_max(
                                        mch[:, sub * NKW + tc2: sub * NKW + tc2 + 1],
                                        psA[:, slot * KW:(slot + 1) * KW], axis=AX.X),
                                    deps=[mmA] + prev.get(("mch", hh), []))
                                rds.append(rd)
                                prev[("psA", slot)] = [rd]
                            mx_toks.append(kb.ins("vector",
                                lambda e, sub=sub, mch=mch, mall=mall: e.reduce_max(
                                    mall[:, sub:sub + 1],
                                    mch[:, sub * NKW:(sub + 1) * NKW], axis=AX.X),
                                deps=rds + prev.get(("mall", hh), [])))
                        prev[("mch", hh)] = mx_toks
                        mx_all[hh] = mx_toks
                    for hh in range(2):
                        mall = mall_d[:, hh * 4:(hh + 1) * 4]
                        m4 = m4_d[:, hh * 128:(hh + 1) * 128]
                        qaug = qaugs[hh]
                        neg = kb.ins("scalar", lambda e, mall=mall: e.mul(
                            mall[:, 0:nsub], mall[:, 0:nsub], -1.0), deps=mx_all[hh])
                        ttm = kb.ins("tensor", lambda e, mall=mall: e.transpose(
                            psB[0:nsub, 0:128], mall[:, 0:nsub], st["c_identf"][:]),
                            deps=[neg] + prev.get("psB_t", []))
                        evm = kb.ins("vector", lambda e, m4=m4: e.tensor_copy(
                            m4[0:nsub, 0:128], psB[0:nsub, 0:128]),
                            deps=[ttm] + prev.get(("m4", hh), []))
                        sel_toks = []
                        for r in range(nsub):
                            sel_toks.append(kb.ins("tensor",
                                lambda e, r=r, m4=m4: e.matmul(
                                    psB[:, 512 + r * 128: 512 + (r + 1) * 128],
                                    st["c_sel4"][0:nsub, r * 128:(r + 1) * 128],
                                    m4[0:nsub, 0:128],
                                    start=True, stop=True),
                                deps=[evm] + prev.get("psB_s", [])))
                        wm = kb.ins("vector", lambda e, qaug=qaug: e.tensor_copy(
                            qaug[64:65, 0:QB], psB[0:1, 512:512 + QB]),
                            deps=sel_toks + [qls[hh]])
                        stats[hh] = wm
                        prev[("mall", hh)] = [ttm]
                        prev[("m4", hh)] = sel_toks
                        prev["psB_t"] = [evm]
                        prev["psB_s"] = [wm]
                    combine_deps = []
                    ex_last = {0: [], 1: []}
                    for hh in range(2):
                        kaug = kaug_d[:, hh * S:(hh + 1) * S]
                        vaug = vaug_d[:, hh * NTC * 65:(hh + 1) * NTC * 65]
                        qaug = qaugs[hh]
                        pt = pt_sb[:, hh * NTC * QB:(hh + 1) * NTC * QB]
                        ex_toks = []
                        av_toks = []
                        exp_prev = prev.get(("pt", hh), [])
                        npair = NTC // 2
                        for pair in range(npair):
                            rix = pair % 2
                            reg = psB[:, rix * W2:(rix + 1) * W2]
                            war = []
                            if pair >= 2:
                                war = [ex_toks[pair - 2]]
                            elif hh == 1:
                                war = ex_last[0][-2:]
                            else:
                                war = [stats[0], stats[1]]
                            mmBs = []
                            for j in range(2):
                                tc = 2 * pair + j
                                mmBs.append(kb.ins("tensor",
                                    lambda e, tc=tc, j=j, reg=reg, qaug=qaug, kaug=kaug: e.matmul(
                                        reg[:, j * QB:(j + 1) * QB],
                                        kaug[0:65, tc * 128:(tc + 1) * 128],
                                        qaug[0:65, 0:QB], start=True, stop=True),
                                    deps=war + [stats[hh]]))
                            ex = kb.ins("scalar", lambda e, pair=pair, reg=reg, pt=pt: e.activation(
                                pt[:, 2 * pair * QB: 2 * (pair + 1) * QB], reg[:],
                                AF.Exp, scale=8.0),
                                deps=mmBs + exp_prev)
                            ex_toks.append(ex)
                            for j in range(2):
                                tc = 2 * pair + j
                                av_toks.append(kb.ins("tensor",
                                    lambda e, tc=tc, hh=hh, vaug=vaug, pt=pt: e.matmul(
                                        pAV[hh][0:65, 0:QB], vaug[:, tc * 65:(tc + 1) * 65],
                                        pt[:, tc * QB:(tc + 1) * QB],
                                        start=(tc == 0), stop=(tc == NTC - 1)),
                                    deps=[ex, vls[hh]] + prev.get(("pAV", hh), [])))
                        ex_last[hh] = ex_toks
                        prev[("pt", hh)] = [av_toks[-1]]
                        prev[("qaug", hh, qb % 2)] = av_toks[-1:]
                        linv = linv_d[:, hh * QB:(hh + 1) * QB]
                        rcp = kb.ins("vector", lambda e, hh=hh, linv=linv: e.reciprocal(
                            linv[0:1, :], pAV[hh][64:65, 0:QB]),
                            deps=av_toks[-1:] + prev.get("linv", []))
                        combine_deps.append((rcp, av_toks[-1]))
                    prev["psB_t"] = ex_last[1][-2:] + ex_last[0][-1:]
                    prev["psB_s"] = prev["psB_t"]
                    l0 = kb.ins("tensor", lambda e: e.matmul(
                        psB[:, 0:QB], st["c_half_even"][:],
                        linv_d[0:1, 0:QB], start=True, stop=False),
                        deps=[combine_deps[0][0], combine_deps[1][0]] + prev["psB_t"])
                    l1 = kb.ins("tensor", lambda e: e.matmul(
                        psB[:, 0:QB], st["c_half_odd"][:],
                        linv_d[0:1, QB:2 * QB], start=False, stop=True),
                        deps=[])
                    evl = kb.ins("vector", lambda e: e.tensor_copy(lrep[:], psB[:, 0:QB]),
                                 deps=[l1] + prev.get("lrep", []))
                    c0 = kb.ins("vector", lambda e: e.tensor_mul(
                        att_sb[0:64, :], pAV0[0:64, 0:QB], lrep[0:64, :]),
                        deps=[evl, combine_deps[0][1]] + prev.get("att", []))
                    c1 = kb.ins("vector", lambda e: e.tensor_mul(
                        att_sb[64:128, :], pAV1[0:64, 0:QB], lrep[64:128, :]),
                        deps=[evl, combine_deps[1][1]] + prev.get("att", []))
                    od = kb.dma("sync", aps["attnfm"][hp * 128:(hp + 1) * 128,
                                                      b * S + qb * QB: b * S + (qb + 1) * QB],
                                att_sb[:], deps=[c0, c1], stream="d_at_o")
                    qb_toks.append(od)
                    prev["lrep"] = [c0, c1]
                    prev["att"] = [od]
                    prev["linv"] = [l0, l1]
                    prev[("pAV", 0)] = [c0]
                    prev[("pAV", 1)] = [c1]
                    prev["psB_t"] = [evl]
                    prev["psB_s"] = [evl]
                prev["kaug"] = qb_toks[-1:] + prev.get("psB_t", [])
                prev["vaug"] = qb_toks[-1:]
    return qb_toks


def emit_proj_epilogue(kb, cfg, st, aps, act_name, w_sb, wtok, x_in, x_out,
                       modT_sb, mtok, v_gate, sfx, bias_sb=None):
    D, S, T = cfg["D"], cfg["S"], cfg["T"]
    KC = D // 128
    SL = min(512, S)
    nc = kb.nc
    with (
        nc.sbuf_tensor("pe_a" + sfx, [128, 2 * KC * SL], BF16) as a_d,
        nc.sbuf_tensor("pe_x" + sfx, [128, KC * SL], F32) as x_sb,
        nc.sbuf_tensor("pe_o" + sfx, [128, KC * SL], F32) as o_sb,
        nc.psum_tensor("pe_p0" + sfx, [128, 512], F32) as p0,
        nc.psum_tensor("pe_p1" + sfx, [128, 512], F32) as p1,
    ):
        ps = (p0, p1)
        prev = {}
        for sl in range(T // SL):
            b = (sl * SL) // S
            asb = a_d[:, (sl % 2) * KC * SL:(sl % 2 + 1) * KC * SL]
            al, xl = [], []
            for c in range(KC):
                al.append(kb.dma("sync", asb[:, c * SL:(c + 1) * SL],
                                 aps[act_name][c * 128:(c + 1) * 128, sl * SL:sl * SL + SL],
                                 deps=prev.get(("a", sl % 2), []),
                                 stream="d_pe_a%d" % (sl % 2)))
                xl.append(kb.dma("gpsimd", sb3(x_sb[:, c * SL:(c + 1) * SL]),
                                 fm_ap(aps[x_in], c, sl * SL, SL),
                                 deps=prev.get("x", []), stream="d_pe_x"))
            al, xl = [al[-1]], [xl[-1]]
            eps = []
            for och in range(KC):
                pp = ps[och % 2]
                mms = []
                for k in range(KC):
                    mms.append(kb.ins("tensor", lambda e, k=k, och=och, pp=pp, asb=asb: e.matmul(
                        pp[:, 0:SL], w_sb[:, k * D + och * 128: k * D + (och + 1) * 128],
                        asb[:, k * SL:(k + 1) * SL],
                        start=(k == 0), stop=(k == KC - 1) and bias_sb is None),
                        deps=[wtok] + al + prev.get(("pp", och % 2), [])))
                if bias_sb is not None:
                    brow, orow, btok2, otok2 = bias_sb
                    mms.append(kb.ins("tensor", lambda e, och=och, pp=pp: e.matmul(
                        pp[:, 0:SL], brow[0:1, och * 128:(och + 1) * 128],
                        orow[0:1, 0:SL], start=False, stop=True),
                        deps=[btok2, otok2]))
                ep = kb.ins("vector", lambda e, och=och, b=b, pp=pp: e.scalar_tensor_tensor(
                    o_sb[:, och * SL:(och + 1) * SL], pp[:, 0:SL],
                    modT_sb[:, modcol(cfg, v_gate, och, b):modcol(cfg, v_gate, och, b) + 1],
                    x_sb[:, och * SL:(och + 1) * SL], ALU.mult, ALU.add),
                    deps=mms[-1:] + xl + [mtok] + prev.get("o", [])),
                eps.append(ep[0])
                prev[("pp", och % 2)] = [ep[0]]
            odmas = []
            for c in range(KC):
                odmas.append(kb.dma("sync", fm_ap(aps[x_out], c, sl * SL, SL),
                                    sb3(o_sb[:, c * SL:(c + 1) * SL]),
                                    deps=[eps[c]], stream="d_pe_o"))
            prev[("a", sl % 2)] = [mms[-1]]
            prev["x"] = eps
            prev["o"] = [odmas[-1]]
    return odmas


def ph_l0wo(kb, cfg, st, aps):
    D = cfg["D"]
    KC = D // 128
    nc = kb.nc
    with (
        nc.sbuf_tensor("wo_w", [128, KC * D], BF16) as wo,
        nc.sbuf_tensor("wo_modT", [128, 6 * KC * cfg["BPC"]], F32) as modT_sb,
    ):
        wtok = kb.dma("gpsimd", sb3(wo[:], D),
                      aps["woT"].rearrange("(kc kp) o -> kp kc o", kp=128),
                      stream="d_wo_w")
        mtok = kb.dma("sync", modT_sb[:], aps["modT0"][:], stream="d_wo_m")
        r = emit_proj_epilogue(kb, cfg, st, aps, "attnfm", wo, wtok,
                               "x0fm", "x1fm", modT_sb, mtok, 2, "_wo")
    return r



def ph_wconv(kb, cfg, st, aps):
    D, HF, E = cfg["D"], cfg["HF"], cfg["E"]
    nc = kb.nc
    wf = st["es"].enter_context(nc.sbuf_tensor("wc_f", [128, 2 * HF], F32))
    wt = st["es"].enter_context(nc.sbuf_tensor("wc_t", [128, 2 * HF], BF16))
    prev = {}
    toks = []

    def conv(src_ap, dst_ap, width, par):
        f = wf[:, par * HF: par * HF + width]
        t = wt[:, par * HF: par * HF + width]
        ld = kb.dma("scalar", f, src_ap, deps=prev.get(("c", par), []),
                    stream="d_wc%d" % par)
        cv = kb.ins("gpsimd", lambda e, f=f, t=t: e.tensor_copy(t, f),
                    deps=[ld] + prev.get(("s", par), []), sem_name="e_wconv")
        tok = kb.dma("scalar", dst_ap, t, deps=[cv], stream="d_wco%d" % par)
        prev[("c", par)] = [cv]
        prev[("s", par)] = [tok]
        return tok

    for e_i in range(E):
        for blk in range(D // 128):
            toks.append(conv(aps["L0_ffn_w1"][e_i, blk * 128:(blk + 1) * 128, :],
                             aps["w1b"][e_i, blk * 128:(blk + 1) * 128, :],
                             HF, blk % 2))
        for blk in range(HF // 128):
            toks.append(conv(aps["L0_ffn_w2"][e_i, blk * 128:(blk + 1) * 128, :],
                             aps["w2b"][e_i, blk * 128:(blk + 1) * 128, :],
                             D, blk % 2))
    kb.cross["wconv"] = toks[-2:] + [("e_wconv", kb.counts["e_wconv"])]
    return [toks[-1]]


def ph_route(kb, cfg, st, aps):
    D, S, T, E = cfg["D"], cfg["S"], cfg["T"], cfg["E"]
    KC = D // 128
    NT = T // 128
    CAP = cfg["CAP"]
    nc = kb.nc
    with contextlib.ExitStack() as es:
        def sbuf(n, sh, dt):
            return es.enter_context(nc.sbuf_tensor(n, sh, dt))
        def psum(n, sh, dt=F32):
            return es.enter_context(nc.psum_tensor(n, sh, dt))
        wg_sb = sbuf("rt_wg", [128, KC * E], BF16)
        h_sb = sbuf("rt_h", [128, KC * 128], BF16)
        lg_d = sbuf("rt_lg", [128, 2 * E], F32)
        mx = sbuf("rt_mx", [128, 4], F32)
        sc0 = sbuf("rt_sc0", [128, E], F32)
        sc1 = sbuf("rt_sc1", [128, E], F32)
        m_sb = sbuf("rt_m", [128, NT * 2 * E], F32)
        g_sb = sbuf("rt_g", [128, NT * 2], F32)
        keep_sb = sbuf("rt_keep", [128, 2 * E], F32)
        w_tile = sbuf("rt_w", [128, E], F32)
        cnt_sb = sbuf("rt_cnt", [1, 16], F32)
        cg_sb = sbuf("rt_cg", [cfg["NCORES"], 16], F32)
        go_sb = sbuf("rt_go", [1, 32], F32)
        carry_sb = sbuf("rt_carry", [1, 2 * E], F32)
        pos_sb = sbuf("rt_pos", [128, E], F32)
        w8_sb = sbuf("rt_w8", [8, T], F32)
        plg = psum("rt_plg", [128, 512])
        pc = psum("rt_pc", [1, 16])
        ppos = psum("rt_pp", [128, 512])
        pw = psum("rt_pw", [8, 128])
        wgl = kb.dma("gpsimd", sb3(wg_sb[:], E),
                     aps["L0_ffn_wg"].rearrange("(c p) e -> p c e", p=128),
                     stream="d_rt_w")
        cnt_toks = []
        prev_h, prev_lg, prev_lgread, prev_lg2 = [], [], [], []
        prev_mx, prev_sc0, prev_mdep, prev_g, prev_gw = [], [], [], [], []
        for t in range(NT):
            hl = []
            for c in range(KC):
                hl.append(kb.dma("sync", h_sb[:, c * 128:(c + 1) * 128],
                                 fm_ap(aps["h2"], c, t * 128, 128),
                                 deps=prev_h if t > 0 else [], stream="d_rt_h"))
            hl = [hl[-1]]
            lg = lg_d[:, 0:E]
            lg2 = lg_d[:, E:2 * E]
            mms = []
            for c in range(KC):
                mms.append(kb.ins("tensor", lambda e, c=c: e.matmul(
                    plg[:, 0:E], h_sb[:, c * 128:(c + 1) * 128],
                    wg_sb[:, c * E:(c + 1) * E],
                    start=(c == 0), stop=(c == KC - 1)),
                    deps=[wgl] + hl + prev_lg))
            e_lg = kb.ins("vector", lambda e: e.tensor_copy(lg, plg[:, 0:E]),
                          deps=mms[-1:] + prev_lgread)
            prev_lg = [e_lg]
            m0 = m_sb[:, t * 2 * E: t * 2 * E + E]
            m1 = m_sb[:, t * 2 * E + E: (t + 1) * 2 * E]
            t_mx1 = kb.ins("vector", lambda e: e.reduce_max(mx[:, 0:1], lg, axis=AX.X),
                           deps=[e_lg] + prev_mx)
            raw0 = kb.ins("vector", lambda e: e.tensor_single_scalar(
                sc0[:], lg, mx[:, 0:1], ALU.is_ge), deps=[t_mx1] + prev_sc0)
            x0 = _exclusive_first(kb, sc0, sc1, m0, prev_m=[raw0] + prev_mdep)
            t_lg2 = kb.ins("vector", lambda e, m0=m0: e.scalar_tensor_tensor(
                lg2, m0, -1e9, lg, ALU.mult, ALU.add), deps=[x0] + prev_lg2)
            t_mx2 = kb.ins("vector", lambda e: e.reduce_max(mx[:, 1:2], lg2, axis=AX.X),
                           deps=[t_lg2])
            raw1 = kb.ins("vector", lambda e: e.tensor_single_scalar(
                sc0[:], lg2, mx[:, 1:2], ALU.is_ge), deps=[t_mx2, x0])
            x1t = _exclusive_first(kb, sc0, sc1, m1, prev_m=[raw1, x0])
            t_d = kb.ins("vector", lambda e: e.tensor_sub(mx[:, 2:3], mx[:, 0:1], mx[:, 1:2]),
                         deps=[t_mx2, t_mx1] + prev_g)
            g0 = g_sb[:, t * 2: t * 2 + 1]
            g1 = g_sb[:, t * 2 + 1: t * 2 + 2]
            t_g0 = kb.ins("scalar", lambda e, g0=g0: e.activation(g0, mx[:, 2:3], AF.Sigmoid),
                          deps=[t_d] + prev_gw)
            t_g1 = kb.ins("scalar", lambda e, g1=g1: e.activation(g1, mx[:, 2:3], AF.Sigmoid,
                                                                  scale=-1.0),
                          deps=[t_d] + prev_gw)
            cnt_toks.append(kb.ins("tensor", lambda e, t=t, m0=m0: e.matmul(
                pc[:], st["c_ones_col"][:], m_sb[:, t * 2 * E:(t + 1) * 2 * E],
                start=(t == 0), stop=(t == NT - 1)),
                deps=[x0, x1t]))
            prev_h = mms + [e_lg]
            prev_lgread = [t_mx1, raw0, t_lg2]
            prev_lg2 = [t_mx2, raw1]
            prev_mx = [t_d, raw0, raw1]
            prev_sc0 = [x0, x1t]
            prev_mdep = [cnt_toks[-1]]
            prev_g = [t_g0, t_g1]
            prev_gw = [t_d]
        ec = kb.ins("vector", lambda e: e.tensor_copy(cnt_sb[:], pc[:]),
                    deps=cnt_toks[-1:])
        dcl = kb.dma("sync", aps["cntloc"][:], cnt_sb[:], deps=[ec], stream="d_rt_c")
        cc = kb.ins("gpsimd", lambda e: e.collective_compute(
            "AllGather", ALU.bypass, [list(range(cfg["NCORES"]))],
            [aps["cntloc"][:].opt()], [aps["cntg"][:].opt()]),
            deps=[dcl])
        gcl = kb.dma("sync", cg_sb[:], aps["cntg"][:], deps=[cc], stream="d_rt_c")
        mp = kb.ins("tensor", lambda e: e.matmul(
            pc[:], st["c_prevmask"][0:cfg["NCORES"], :], cg_sb[:],
            start=True, stop=True), deps=[gcl, ec])
        ep = kb.ins("vector", lambda e: e.tensor_copy(go_sb[0:1, 0:16], pc[:]),
                    deps=[mp])
        ma = kb.ins("tensor", lambda e: e.matmul(
            pc[:], st["c_allmask"][0:cfg["NCORES"], :], cg_sb[:],
            start=True, stop=True), deps=[ep])
        ea = kb.ins("vector", lambda e: e.tensor_copy(go_sb[0:1, 16:32], pc[:]),
                    deps=[ma])
        c0i = kb.ins("vector", lambda e: e.tensor_copy(carry_sb[0:1, 0:E],
                                                       go_sb[0:1, 0:E]), deps=[ep])
        c1i = kb.ins("vector", lambda e: e.tensor_add(
            carry_sb[0:1, E:2 * E], go_sb[0:1, 16:24], go_sb[0:1, 8:16]),
            deps=[ea])
        carry_toks = [c0i, c1i]
        g_done = kb.barrier("scalar")
        prev_pos, prev_keep, prev_w, prev_pw = [], [], [], []
        for t in range(NT):
            kts = []
            for p in range(2):
                m_p = m_sb[:, t * 2 * E + p * E: t * 2 * E + (p + 1) * E]
                car = carry_sb[0:1, p * E:(p + 1) * E]
                mmp = kb.ins("tensor", lambda e, m_p=m_p: e.matmul(
                    ppos[:, 0:E], st["c_triT"][:], m_p, start=True, stop=False),
                    deps=carry_toks + prev_pos)
                mmc = kb.ins("tensor", lambda e, car=car: e.matmul(
                    ppos[:, 0:E], st["c_ones_row"][0:1, :], car,
                    start=False, stop=True), deps=[])
                t_pe = kb.ins("vector", lambda e, m_p=m_p: e.tensor_sub(
                    pos_sb[:], ppos[:, 0:E], m_p), deps=[mmc] + prev_keep)
                t_k = kb.ins("vector", lambda e: e.tensor_single_scalar(
                    pos_sb[:], pos_sb[:], float(CAP), ALU.is_lt), deps=[t_pe])
                t_km = kb.ins("vector", lambda e, p=p, m_p=m_p: e.tensor_mul(
                    keep_sb[:, p * E:(p + 1) * E], pos_sb[:], m_p), deps=[t_k] + prev_w)
                kts.append(t_km)
                cm = kb.ins("tensor", lambda e, m_p=m_p: e.matmul(
                    pc[0:1, 0:E], st["c_ones_col"][:], m_p, start=True, stop=True),
                    deps=[ea if t == 0 else carry_toks[-1], mmc])
                cev = kb.ins("vector", lambda e, car=car: e.tensor_add(
                    car, car, pc[0:1, 0:E]), deps=[cm])
                carry_toks = [cev]
                prev_pos = [t_pe]
            g0 = g_sb[:, t * 2: t * 2 + 1]
            g1 = g_sb[:, t * 2 + 1: t * 2 + 2]
            t_w1 = kb.ins("vector", lambda e, g1=g1: e.tensor_scalar(
                w_tile[:], keep_sb[:, E:2 * E], g1, None, ALU.mult),
                deps=kts + prev_w + [g_done])
            t_w = kb.ins("vector", lambda e, g0=g0: e.scalar_tensor_tensor(
                w_tile[:], keep_sb[:, 0:E], g0, w_tile[:], ALU.mult, ALU.add),
                deps=[t_w1])
            prev_keep = [t_w1, t_w]
            prev_w = [t_w]
            tw = kb.ins("tensor", lambda e: e.transpose(
                pw[:], w_tile[:], st["c_identf"][:]), deps=[t_w] + prev_pw)
            ev8 = kb.ins("vector", lambda e, t=t: e.tensor_copy(
                w8_sb[0:8, t * 128:(t + 1) * 128], pw[0:8, :]), deps=[tw])
            prev_pw = [ev8]
            prev_w = [t_w, tw]
        wd = kb.dma("sync", aps["w8"][:], w8_sb[:], deps=[ev8], stream="d_rt_c")
    return [wd]


def _exclusive_first(kb, raw, scr, out_m, prev_m):
    E = 8
    t1a = kb.ins("vector", lambda e: e.tensor_copy(scr[:, 0:1], raw[:, 0:1]),
                 deps=prev_m)
    t1b = kb.ins("vector", lambda e: e.tensor_add(
        scr[:, 1:E], raw[:, 1:E], raw[:, 0:E - 1]), deps=prev_m)
    t2a = kb.ins("vector", lambda e: e.tensor_copy(out_m[:, 0:2], scr[:, 0:2]),
                 deps=[t1a, t1b] + prev_m)
    t2b = kb.ins("vector", lambda e: e.tensor_add(
        out_m[:, 2:E], scr[:, 2:E], scr[:, 0:E - 2]), deps=[t1a, t1b] + prev_m)
    t3a = kb.ins("vector", lambda e: e.tensor_copy(scr[:, 0:4], out_m[:, 0:4]),
                 deps=[t2a, t2b])
    t3b = kb.ins("vector", lambda e: e.tensor_add(
        scr[:, 4:E], out_m[:, 4:E], out_m[:, 0:E - 4]), deps=[t2a, t2b])
    teq = kb.ins("vector", lambda e: e.tensor_single_scalar(
        out_m[:, :], scr[:, 0:E], 1.0, ALU.is_equal), deps=[t3a, t3b])
    tq = kb.ins("vector", lambda e: e.tensor_mul(out_m[:, :], out_m[:, :], raw[:, :]),
                deps=[teq])
    return tq



def ph_moe(kb, cfg, st, aps):
    D, S, T, E, HF = cfg["D"], cfg["S"], cfg["T"], cfg["E"], cfg["HF"]
    KC = D // 128
    HC = HF // 128
    SL = min(512, S)
    nc = kb.nc
    with contextlib.ExitStack() as es:
        def sbuf(n, sh, dt):
            return es.enter_context(nc.sbuf_tensor(n, sh, dt))
        def psum(n, sh, dt=F32):
            return es.enter_context(nc.psum_tensor(n, sh, dt))
        h_sb = sbuf("mo_h", [128, KC * SL], BF16)
        x_sb = sbuf("mo_x", [128, KC * SL], F32)
        acc = sbuf("mo_acc", [128, KC * SL], F32)
        y1_sb = sbuf("mo_y1", [128, HC * SL], BF16)
        w1_sb = sbuf("mo_w1", [128, 2 * KC * 1024], BF16)
        w2_sb = sbuf("mo_w2", [128, 2 * 4 * D], BF16)
        wrep = sbuf("mo_wrep", [128, SL], F32)
        w8_sb = sbuf("mo_w8", [8, T], F32)
        b1_sb = sbuf("mo_b1", [128, HC], F32)
        b2_sb = sbuf("mo_b2", [128, KC], F32)
        b1r = sbuf("mo_b1r", [32, 128], F32)
        b2r = sbuf("mo_b2r", [8, 128], F32)
        o_sb = sbuf("mo_o", [128, SL], F32)
        modT_sb = sbuf("mo_modT", [128, 6 * KC * cfg["BPC"]], F32)
        psy = [psum("mo_p%d" % i, [128, 512]) for i in range(8)]
        w8l = kb.dma("sync", w8_sb[:], aps["w8"][:], stream="d_mo_s")
        mtok = kb.dma("sync", modT_sb[:], aps["modT0"][:], stream="d_mo_s2")
        prev = {}
        for sl in range(T // SL):
            b = (sl * SL) // S
            hdl, xdl = [], []
            for c in range(KC):
                hdl.append(kb.dma("sync", sb3(h_sb[:, c * SL:(c + 1) * SL]),
                                  fm_ap(aps["h2"], c, sl * SL, SL),
                                  deps=prev.get("h", []), stream="d_mo_h"))
                xdl.append(kb.dma("gpsimd", sb3(x_sb[:, c * SL:(c + 1) * SL]),
                                  fm_ap(aps["x1fm"], c, sl * SL, SL),
                                  deps=prev.get("x", []), stream="d_mo_x"))
            hdl, xdl = [hdl[-1]], [xdl[-1]]
            acc_tok = []
            for e_i in range(E):
                mw = kb.ins("tensor", lambda e, e_i=e_i, sl=sl: e.matmul(
                    psy[7][:, 0:SL], st["c_sel8"][0:8, e_i * 128:(e_i + 1) * 128],
                    w8_sb[0:8, sl * SL:(sl + 1) * SL], start=True, stop=True),
                    deps=[w8l] + prev.get("pw", []))
                ew = kb.ins("vector", lambda e: e.tensor_copy(wrep[:], psy[7][:, 0:SL]),
                            deps=[mw] + prev.get("wrep", []))
                prev["pw"] = [ew]
                bl1 = kb.dma("sync", b1r[0:HC, :],
                             aps["L0_ffn_b1"][e_i, :].rearrange("(a b) -> a b", b=128),
                             deps=prev.get("b1", []), stream="d_mo_b")
                tb1 = kb.ins("tensor", lambda e: e.transpose(
                    psy[6][:, 0:HC], b1r[0:HC, :], st["c_identf"][0:HC, 0:HC]),
                    deps=[bl1] + prev.get("pb", []))
                eb1 = kb.ins("vector", lambda e: e.tensor_copy(
                    b1_sb[:, 0:HC], psy[6][:, 0:HC]), deps=[tb1] + prev.get("b1r", []))
                bl2 = kb.dma("sync", b2r[0:KC, :],
                             aps["L0_ffn_b2"][e_i, :].rearrange("(a b) -> a b", b=128),
                             deps=prev.get("b2", []), stream="d_mo_b2")
                tb2 = kb.ins("tensor", lambda e: e.transpose(
                    psy[6][:, 0:KC], b2r[0:KC, :], st["c_identf"][0:KC, 0:KC]),
                    deps=[bl2, eb1])
                eb2 = kb.ins("vector", lambda e: e.tensor_copy(
                    b2_sb[:, 0:KC], psy[6][:, 0:KC]), deps=[tb2] + prev.get("b2r", []))
                prev["b1"] = [tb1]
                prev["b2"] = [tb2]
                prev["pb"] = [eb2]
                gelu_toks = []
                for hg in range(HF // 1024):
                    wg_idx = hg % 2
                    w1g = w1_sb[:, wg_idx * KC * 1024:(wg_idx + 1) * KC * 1024]
                    wls = [kb.dma(
                        "gpsimd", sb3(w1g, 1024),
                        aps["L0_ffn_w1"][e_i, :, hg * 1024:(hg + 1) * 1024]
                        .rearrange("(kc kp) h -> kp kc h", kp=128),
                        deps=prev.get(("w1", wg_idx), []),
                        stream="d_mo_w1%d" % wg_idx)]
                    for hcl in range(8):
                        hc = hg * 8 + hcl
                        pp = psy[hcl % 2]
                        mms = []
                        for k in range(KC):
                            mms.append(kb.ins("tensor",
                                lambda e, k=k, hcl=hcl, pp=pp, w1g=w1g: e.matmul(
                                    pp[:, 0:SL],
                                    w1g[:, k * 1024 + hcl * 128: k * 1024 + (hcl + 1) * 128],
                                    h_sb[:, k * SL:(k + 1) * SL],
                                    start=(k == 0), stop=(k == KC - 1)),
                                deps=wls + hdl + prev.get(("py", hcl % 2), [])))
                        g = kb.ins("scalar", lambda e, hc=hc, pp=pp: e.activation(
                            y1_sb[:, hc * SL:(hc + 1) * SL], pp[:, 0:SL], AF.Gelu,
                            bias=b1_sb[:, hc:hc + 1]),
                            deps=mms[-1:] + [eb1] + prev.get("y1", []))
                        gelu_toks.append(g)
                        prev[("py", hcl % 2)] = [g]
                    prev[("w1", wg_idx)] = mms
                mm2_toks = {dc: [] for dc in range(KC)}
                for hq in range(HC // 4):
                    par = hq % 2
                    w2big = w2_sb[:, par * 4 * D:(par + 1) * 4 * D]
                    w2l = kb.dma("gpsimd", sb3(w2big, D),
                                 aps["L0_ffn_w2"][e_i, hq * 512:(hq + 1) * 512, :]
                                 .rearrange("(c p) d -> p c d", p=128),
                                 deps=prev.get(("w2", par), []),
                                 stream="d_mo_w2%d" % par)
                    for hcl in range(4):
                        hc = hq * 4 + hcl
                        for dc in range(KC):
                            mm2_toks[dc].append(kb.ins("tensor",
                                lambda e, hc=hc, hcl=hcl, dc=dc, w2big=w2big: e.matmul(
                                    psy[dc][:, 0:SL],
                                    w2big[:, hcl * D + dc * 128: hcl * D + (dc + 1) * 128],
                                    y1_sb[:, hc * SL:(hc + 1) * SL],
                                    start=(hc == 0), stop=(hc == HC - 1)),
                                deps=[w2l, gelu_toks[hc]] +
                                     (prev.get(("py2", dc), []) + [eb2, ew, gelu_toks[-1]]
                                      if hc == 0 else [])))
                    prev[("w2", par)] = [mm2_toks[dc][-1] for dc in range(KC)]
                new_acc = []
                for dc in range(KC):
                    if e_i == 0:
                        a = kb.ins("vector", lambda e, dc=dc: e.scalar_tensor_tensor(
                            acc[:, dc * SL:(dc + 1) * SL], psy[dc][:, 0:SL],
                            b2_sb[:, dc:dc + 1], wrep[:], ALU.add, ALU.mult),
                            deps=mm2_toks[dc][-1:] + [eb2, ew] + prev.get("acc", []))
                    else:
                        a = kb.ins("vector", lambda e, dc=dc: e.scalar_tensor_tensor(
                            o_sb[:], psy[dc][:, 0:SL],
                            b2_sb[:, dc:dc + 1], wrep[:], ALU.add, ALU.mult),
                            deps=mm2_toks[dc][-1:] + [eb2, ew] + prev.get("o", []))
                        a = kb.ins("vector", lambda e, dc=dc: e.tensor_add(
                            acc[:, dc * SL:(dc + 1) * SL],
                            acc[:, dc * SL:(dc + 1) * SL], o_sb[:]),
                            deps=[a] + prev.get("acc", []))
                        prev["o"] = [a]
                    new_acc.append(a)
                    prev[("py2", dc)] = [a]
                prev["acc"] = new_acc
                prev["y1"] = mm2_toks[KC - 1]
                prev["wrep"] = new_acc
                prev["pw"] = new_acc[7:8]
                prev[("py", 0)] = new_acc[0:1] + new_acc[2:3] + new_acc[4:5] + new_acc[6:7]
                prev[("py", 1)] = new_acc[1:2] + new_acc[3:4] + new_acc[5:6] + new_acc[7:8]
                prev["pb"] = new_acc[6:7]
                prev["b1r"] = gelu_toks[-1:]
                prev["b2r"] = new_acc[7:8]
            eps_toks = []
            for c in range(KC):
                ep = kb.ins("vector", lambda e, c=c, b=b: e.scalar_tensor_tensor(
                    x_sb[:, c * SL:(c + 1) * SL], acc[:, c * SL:(c + 1) * SL],
                    modT_sb[:, modcol(cfg, 5, c, b):modcol(cfg, 5, c, b) + 1],
                    x_sb[:, c * SL:(c + 1) * SL], ALU.mult, ALU.add),
                    deps=prev["acc"] + xdl + [mtok] + prev.get("eps", []))
                eps_toks.append(ep)
            odmas = []
            for c in range(KC):
                odmas.append(kb.dma("sync", fm_ap(aps["x2fm"], c, sl * SL, SL),
                                    sb3(x_sb[:, c * SL:(c + 1) * SL]),
                                    deps=[eps_toks[c]], stream="d_mo_o"))
            prev["h"] = [m for dc in range(KC) for m in mm2_toks[dc][-1:]] + gelu_toks[-1:]
            prev["x"] = [odmas[-1]]
            prev["eps"] = [odmas[-1]]
            prev["acc"] = eps_toks
    return odmas



def ph_l1proj(kb, cfg, st, aps):
    D, S, T, NH, DH, DL = cfg["D"], cfg["S"], cfg["T"], cfg["NH"], cfg["DH"], cfg["DLAT"]
    KC = D // 128
    SL = min(512, S)
    nc = kb.nc
    with contextlib.ExitStack() as es:
        def sbuf(n, sh, dt):
            return es.enter_context(nc.sbuf_tensor(n, sh, dt))
        def psum(n, sh, dt=F32):
            return es.enter_context(nc.psum_tensor(n, sh, dt))
        wq_sb = sbuf("lp_wq", [128, KC * D], BF16)
        wdn_sb = sbuf("lp_wdn", [128, KC * DL], BF16)
        wup_sb = sbuf("lp_wup", [128, 2 * 2 * D], BF16)
        bq_sb = sbuf("lp_bq", [1, D], BF16)
        bup_sb = sbuf("lp_bup", [1, 2 * D], BF16)
        bdn_r = sbuf("lp_bdnr", [2, 128], F32)
        bdn_sb = sbuf("lp_bdn", [128, 2], F32)
        ones_sl = sbuf("lp_ones", [1, SL], BF16)
        h_sb = sbuf("lp_h", [128, KC * SL], BF16)
        lat_sb = sbuf("lp_lat", [128, 2 * SL], BF16)
        q_sb = sbuf("lp_q", [128, D], F32)
        kv_sb = sbuf("lp_kv", [128, 2 * D], F32)
        ro_sb = sbuf("lp_ro", [128, D], BF16)
        kro_sb = sbuf("lp_kro", [128, D], BF16)
        v_sb = sbuf("lp_v", [128, NH * DH], BF16)
        cos_sb = sbuf("lp_cos", [128, 512], F32)
        sin_sb = sbuf("lp_sin", [128, 512], F32)
        t1_sb = sbuf("lp_t1", [128, 512], F32)
        t2_sb = sbuf("lp_t2", [128, 512], F32)
        pq = psum("lp_pq", [128, 512])
        pl = psum("lp_pl", [128, 512])
        pk = psum("lp_pk", [128, 512])
        wql = kb.dma("gpsimd", sb3(wq_sb[:], D),
                     aps["L1_attn_Wq"].rearrange("(kc kp) o -> kp kc o", kp=128),
                     stream="d_lp_w")
        wdl = kb.dma("gpsimd", sb3(wdn_sb[:], DL),
                     aps["L1_attn_Wdown"].rearrange("(kc kp) o -> kp kc o", kp=128),
                     stream="d_lp_w")
        wul = kb.dma("gpsimd", sb3(wup_sb[:], 2 * D),
                     aps["L1_attn_Wup"].rearrange("(kc kp) o -> kp kc o", kp=128),
                     stream="d_lp_w")
        bql = kb.dma("gpsimd", bq_sb[:], aps["L1_attn_bq"][:], stream="d_lp_w")
        bul = kb.dma("gpsimd", bup_sb[:], aps["L1_attn_bup"][:], stream="d_lp_w")
        bdl = kb.dma("sync", bdn_r[:], aps["L1_attn_bdown"][0, :]
                     .rearrange("(a b) -> a b", b=128), stream="d_lp_b")
        tbd = kb.ins("tensor", lambda e: e.transpose(
            pq[:, 0:2], bdn_r[:], st["c_identf"][0:2, 0:2]), deps=[bdl])
        ebd = kb.ins("vector", lambda e: e.tensor_copy(bdn_sb[:], pq[:, 0:2]),
                     deps=[tbd])
        onم = kb.ins("vector", lambda e: e.memset(ones_sl[:], 1.0))
        wtoks = [wql, wdl, wul, bql, bul, ebd, onم]
        prev = {}
        for sl in range(T // SL):
            b = (sl * SL) // S
            p0s = sl * SL - b * S
            hdl = []
            for c in range(KC):
                hdl.append(kb.dma("sync", sb3(h_sb[:, c * SL:(c + 1) * SL]),
                                  fm_ap(aps["h3"], c, sl * SL, SL),
                                  deps=prev.get("h", []), stream="d_lp_h"))
            hdl = [hdl[-1]]
            lats = []
            for lc in range(DL // 128):
                mms = []
                for k in range(KC):
                    mms.append(kb.ins("tensor", lambda e, k=k, lc=lc: e.matmul(
                        pl[:, 0:SL], wdn_sb[:, k * DL + lc * 128: k * DL + (lc + 1) * 128],
                        h_sb[:, k * SL:(k + 1) * SL],
                        start=(k == 0), stop=(k == KC - 1)),
                        deps=wtoks + hdl + prev.get("pl", [])))
                ev = kb.ins("vector", lambda e, lc=lc: e.tensor_scalar(
                    lat_sb[:, lc * SL:(lc + 1) * SL], pl[:, 0:SL],
                    bdn_sb[:, lc:lc + 1], None, ALU.add),
                    deps=mms[-1:] + prev.get("lat", []))
                lats.append(ev)
                prev["pl"] = [ev]
            for tt in range(SL // 128):
                cl = kb.dma("sync", cos_sb[:],
                            aps["c_l1cos"][p0s + tt * 128:p0s + (tt + 1) * 128, :],
                            deps=prev.get("cos", []), stream="d_lp_c")
                sl_l = kb.dma("sync", sin_sb[:],
                              aps["c_l1sin"][p0s + tt * 128:p0s + (tt + 1) * 128, :],
                              deps=prev.get("cos", []), stream="d_lp_s")
                qevs = []
                for oh in range(2):
                    mms = []
                    for k in range(KC):
                        mms.append(kb.ins("tensor", lambda e, k=k, oh=oh, tt=tt: e.matmul(
                            pq[:, 0:512],
                            h_sb[:, k * SL + tt * 128: k * SL + (tt + 1) * 128],
                            wq_sb[:, k * D + oh * 512:(k * D + (oh + 1) * 512)],
                            start=(k == 0), stop=False),
                            deps=wtoks + hdl + prev.get("pq", [])))
                    mb = kb.ins("tensor", lambda e, oh=oh: e.matmul(
                        pq[:, 0:512], ones_sl[0:1, 0:128],
                        bq_sb[0:1, oh * 512:(oh + 1) * 512],
                        start=False, stop=True), deps=[])
                    ev = kb.ins("vector", lambda e, oh=oh: e.tensor_copy(
                        q_sb[:, oh * 512:(oh + 1) * 512], pq[:, 0:512]),
                        deps=[mb] + prev.get("qsb", []))
                    qevs.append(ev)
                    prev["pq"] = [ev]
                kvevs = []
                for qh in range(4):
                    mms = []
                    for lc in range(DL // 128):
                        mms.append(kb.ins("tensor", lambda e, lc=lc, qh=qh, tt=tt: e.matmul(
                            pk[:, 0:512],
                            lat_sb[:, lc * SL + tt * 128: lc * SL + (tt + 1) * 128],
                            wup_sb[:, lc * 2 * D + qh * 512: lc * 2 * D + (qh + 1) * 512],
                            start=(lc == 0), stop=False),
                            deps=lats + prev.get("pk", [])))
                    mb = kb.ins("tensor", lambda e, qh=qh: e.matmul(
                        pk[:, 0:512], ones_sl[0:1, 0:128],
                        bup_sb[0:1, qh * 512:(qh + 1) * 512],
                        start=False, stop=True), deps=[])
                    ev = kb.ins("vector", lambda e, qh=qh: e.tensor_copy(
                        kv_sb[:, qh * 512:(qh + 1) * 512], pk[:, 0:512]),
                        deps=[mb] + prev.get("kvsb", []))
                    kvevs.append(ev)
                    prev["pk"] = [ev]
                ct = cos_sb[:]
                st_ = sin_sb[:]
                def q1v(z):
                    return z
                q_t = q_sb
                q1 = q_t[:].rearrange("p (h x) -> p h x", x=64)[:, :, 0:32]
                q2 = q_t[:].rearrange("p (h x) -> p h x", x=64)[:, :, 32:64]
                ro1 = ro_sb[:].rearrange("p (h x) -> p h x", x=64)[:, :, 0:32]
                ro2 = ro_sb[:].rearrange("p (h x) -> p h x", x=64)[:, :, 32:64]
                c3 = sb3(ct, 32)
                s3 = sb3(st_, 32)
                rq = _rope_l1(kb, q1, q2, ro1, ro2, c3, s3, t1_sb, t2_sb,
                              qevs + [cl, sl_l] + prev.get("ro", []), prev)
                kv3 = kv_sb[:].rearrange("p (h x) -> p h x", x=128)
                k1 = kv3[:, :, 0:32]
                k2 = kv3[:, :, 32:64]
                ko1 = kro_sb[:].rearrange("p (h x) -> p h x", x=64)[:, :, 0:32]
                ko2 = kro_sb[:].rearrange("p (h x) -> p h x", x=64)[:, :, 32:64]
                rk = _rope_l1(kb, k1, k2, ko1, ko2, c3, s3, t1_sb, t2_sb,
                              kvevs + [cl, sl_l] + prev.get("kro", []), prev)
                vv = kb.ins("vector", lambda e: e.tensor_copy(
                    sb3(v_sb[:], DH), kv3[:, :, 64:128]),
                    deps=kvevs + prev.get("vsb", []))
                d1 = kb.dma("sync", aps["l1q"][sl * SL + tt * 128: sl * SL + (tt + 1) * 128, :],
                            ro_sb[:], deps=[rq], stream="d_lp_oq")
                d2 = kb.dma("sync", aps["l1k"][sl * SL + tt * 128: sl * SL + (tt + 1) * 128, :],
                            kro_sb[:], deps=[rk], stream="d_lp_ok")
                d3 = kb.dma("sync", aps["l1v"][sl * SL + tt * 128: sl * SL + (tt + 1) * 128, :],
                            v_sb[:], deps=[vv], stream="d_lp_ov")
                prev["qsb"] = [rq]
                prev["kvsb"] = [rk, vv]
                prev["ro"] = [d1]
                prev["kro"] = [d2]
                prev["vsb"] = [d3]
            prev["h"] = mms + qevs
            prev["lat"] = mms
            prev["cos"] = [rq, rk]
    return [d1, d2, d3]


def _rope_l1(kb, x1, x2, o1, o2, c3, s3, t1_sb, t2_sb, deps0, prev):
    t3a = sb3(t1_sb[:], 32)
    t3b = sb3(t2_sb[:], 32)
    a1 = kb.ins("vector", lambda e: e.tensor_mul(t3a, x1, c3),
                deps=deps0 + prev.get("rt1", []))
    a2 = kb.ins("vector", lambda e: e.tensor_mul(t3b, x2, s3),
                deps=deps0 + prev.get("rt2", []))
    a3 = kb.ins("vector", lambda e: e.tensor_sub(o1, t3a, t3b), deps=[a1, a2])
    b1 = kb.ins("vector", lambda e: e.tensor_mul(t3a, x1, s3), deps=[a3])
    b2 = kb.ins("vector", lambda e: e.tensor_mul(t3b, x2, c3), deps=[a3])
    b3 = kb.ins("vector", lambda e: e.tensor_add(o2, t3a, t3b), deps=[b1, b2])
    prev["rt1"] = [b3]
    prev["rt2"] = [b3]
    return b3


def ph_l1attn(kb, cfg, st, aps):
    D, S, T, NH, DH = cfg["D"], cfg["S"], cfg["T"], cfg["NH"], cfg["DH"]
    HSPLIT = 7
    nc = kb.nc
    with contextlib.ExitStack() as es:
        def sbuf(n, sh, dt):
            return es.enter_context(nc.sbuf_tensor(n, sh, dt))
        def psum(n, sh, dt=F32):
            return es.enter_context(nc.psum_tensor(n, sh, dt))
        q_sb = sbuf("la_q", [128, D], BF16)
        k_sb = sbuf("la_k", [128, D], BF16)
        v_sb = sbuf("la_v", [128, D], BF16)
        prod = sbuf("la_prod", [128, 4 * 1024], F32)
        att = sbuf("la_att", [128, NH * NH], F32)
        mx = sbuf("la_mx", [128, 4 * 2], F32)
        li = sbuf("la_l", [128, 4], F32)
        ao = sbuf("la_ao", [128, D], F32)
        oo = sbuf("la_oo", [128, D], BF16)
        ps0 = psum("la_p0", [128, 512])
        ps1 = psum("la_p1", [128, 512])
        prev = {}
        for tile in range(T // 128):
            t0 = tile * 128
            ql = kb.dma("sync", q_sb[:], aps["l1q"][t0:t0 + 128, :],
                        deps=prev.get("q", []), stream="d_la_q")
            kl = kb.dma("sync", k_sb[:], aps["l1k"][t0:t0 + 128, :],
                        deps=prev.get("k", []), stream="d_la_k")
            vl = kb.dma("sync", v_sb[:], aps["l1v"][t0:t0 + 128, :],
                        deps=prev.get("v", []), stream="d_la_v")
            hts = []
            lane_n = {0: 0, 1: 0}
            for h in range(NH):
                lane = 0 if h < HSPLIT else 1
                eng = "vector" if lane == 0 else "gpsimd"
                buf = 2 * lane + (lane_n[lane] % 2)
                lane_n[lane] += 1
                pr = prod[:, buf * 1024:(buf + 1) * 1024]
                mxl = mx[:, buf * 2:(buf + 1) * 2]
                lil = li[:, buf:buf + 1]
                q_bc = q_sb[:].rearrange("p (hh x) -> p hh x", x=DH)[
                    :, h:h + 1, :].broadcast_to([128, NH, DH])
                k_all = k_sb[:].rearrange("p (g x) -> p g x", x=DH)
                tm = kb.ins(eng, lambda e, q_bc=q_bc, k_all=k_all, pr=pr: e.tensor_mul(
                    sb3(pr, DH), q_bc, k_all),
                    deps=[ql, kl] + prev.get(("prod", buf), []))
                tr = kb.ins("vector", lambda e, h=h, pr=pr: e.reduce_sum(
                    att[:, h * NH:(h + 1) * NH], sb3(pr, DH), axis=AX.X),
                    deps=[tm] + prev.get(("att", buf), []))
                t_mx = kb.ins("vector", lambda e, h=h, mxl=mxl: e.reduce_max(
                    mxl[:, 0:1], att[:, h * NH:(h + 1) * NH], axis=AX.X),
                    deps=[tr] + prev.get(("mx", buf), []))
                t_ng = kb.ins("scalar", lambda e, mxl=mxl: e.mul(
                    mxl[:, 1:2], mxl[:, 0:1], -0.125),
                    deps=[t_mx] + prev.get(("mx2", buf), []))
                t_ex = kb.ins("scalar", lambda e, h=h, mxl=mxl, lil=lil: e.activation(
                    att[:, h * NH:(h + 1) * NH], att[:, h * NH:(h + 1) * NH],
                    AF.Exp, scale=0.125, bias=mxl[:, 1:2], accum_out=lil),
                    deps=[t_ng] + prev.get(("li", buf), []))
                t_rc = kb.ins("vector", lambda e, lil=lil: e.reciprocal(lil, lil),
                              deps=[t_ex])
                t_nm = kb.ins("vector", lambda e, h=h, lil=lil: e.tensor_scalar(
                    att[:, h * NH:(h + 1) * NH], att[:, h * NH:(h + 1) * NH],
                    lil, None, ALU.mult), deps=[t_rc])
                a_bc = att[:, h * NH:(h + 1) * NH].rearrange(
                    "p (o g) -> p o g", o=1).broadcast_to([128, DH, NH])
                v_dg = v_sb[:].rearrange("p (g x) -> p x g", x=DH)
                tm2 = kb.ins(eng, lambda e, a_bc=a_bc, v_dg=v_dg, pr=pr: e.tensor_mul(
                    pr.rearrange("p (x g) -> p x g", g=NH),
                    a_bc, v_dg), deps=[t_nm, vl, tm, tr] + prev.get(("prod2", buf), []))
                tr2 = kb.ins("vector", lambda e, h=h, pr=pr: e.reduce_sum(
                    ao[:, h * DH:(h + 1) * DH],
                    pr.rearrange("p (x g) -> p x g", g=NH), axis=AX.X),
                    deps=[tm2] + prev.get(("ao", lane), []))
                hts.append(tr2)
                prev[("prod", buf)] = [tm2, tr2]
                prev[("att", buf)] = [tm2]
                prev[("mx", buf)] = [t_ng]
                prev[("mx2", buf)] = [t_ex]
                prev[("li", buf)] = [t_nm]
                prev[("prod2", buf)] = [tr2]
            evs = []
            for g in range(2):
                tts = []
                for c in range(4):
                    ch = g * 4 + c
                    tts.append(kb.ins("tensor",
                        lambda e, ch=ch, g=g, c=c: e.transpose(
                            (ps0 if g == 0 else ps1)[:, c * 128:(c + 1) * 128],
                            ao[:, ch * 128:(ch + 1) * 128], st["c_identf"][:]),
                        deps=hts + prev.get("ps" + str(g), [])))
                ev = kb.ins("vector", lambda e, g=g: e.tensor_copy(
                    oo[:, g * 512:(g + 1) * 512], (ps0 if g == 0 else ps1)[:]),
                    deps=tts + prev.get("oo", []))
                evs.append(ev)
                prev["ps" + str(g)] = [ev]
                prev["ao"] = prev.get("ao", []) + tts
            od = kb.dma("sync", aps["l1at"][:, t0:t0 + 128].rearrange(
                "(c p) t -> p c t", p=128), sb3(oo[:]), deps=evs, stream="d_la_o")
            prev["q"] = hts[-1:] + [hts[HSPLIT - 1]]
            prev["k"] = prev["q"]
            prev["v"] = prev["q"]
            prev[("ao", 0)] = tts
            prev[("ao", 1)] = tts
            prev["oo"] = [od]
    return [od]


def ph_l1out(kb, cfg, st, aps):
    D = cfg["D"]
    KC = D // 128
    nc = kb.nc
    with contextlib.ExitStack() as es:
        def sbuf(n, sh, dt):
            return es.enter_context(nc.sbuf_tensor(n, sh, dt))
        wo_sb = sbuf("lo_w", [128, KC * D], BF16)
        bo_sb = sbuf("lo_b", [1, D], BF16)
        on_sb = sbuf("lo_on", [1, min(512, cfg["S"])], BF16)
        modT_sb = sbuf("lo_m", [128, 6 * KC * cfg["BPC"]], F32)
        wtok = kb.dma("gpsimd", sb3(wo_sb[:], D),
                      aps["L1_attn_Wout"].rearrange("(kc kp) o -> kp kc o", kp=128),
                      stream="d_lo_w")
        btok = kb.dma("gpsimd", bo_sb[:], aps["L1_attn_bout"][:], stream="d_lo_b")
        onm = kb.ins("vector", lambda e: e.memset(on_sb[:], 1.0))
        mtok = kb.dma("sync", modT_sb[:], aps["modT1"][:], stream="d_lo_m")
        r = emit_proj_epilogue(kb, cfg, st, aps, "l1at", wo_sb, wtok,
                               "x2fm", "x3fm", modT_sb, mtok, 2, "_lo",
                               bias_sb=(bo_sb, on_sb, btok, onm))
    return r



def ph_l1ffn(kb, cfg, st, aps):
    D, S, T = cfg["D"], cfg["S"], cfg["T"]
    KC = D // 128
    SL = min(512, S)
    nc = kb.nc
    with contextlib.ExitStack() as es:
        def sbuf(n, sh, dt):
            return es.enter_context(nc.sbuf_tensor(n, sh, dt))
        def psum(n, sh, dt=F32):
            return es.enter_context(nc.psum_tensor(n, sh, dt))
        kb_sb = sbuf("lf_k", [128, KC * D], BF16)
        k8_sb = sbuf("lf_k8", [128, KC * D], F8)
        h_sb = sbuf("lf_h", [128, KC * SL], BF16)
        h8_sb = sbuf("lf_h8", [128, KC * SL], F8)
        x_sb = sbuf("lf_x", [128, KC * SL], F32)
        y_sb = sbuf("lf_y", [128, SL], BF16)
        sc_sb = sbuf("lf_sc", [1, 8], F32)
        rep_sb = sbuf("lf_rep", [128, 3], F32)
        bia_r = sbuf("lf_br", [8, 128], BF16)
        bia_sb = sbuf("lf_b", [128, 8], F32)
        modT_sb = sbuf("lf_m", [128, 6 * KC * cfg["BPC"]], F32)
        eg_sb = sbuf("lf_eg", [128, 2 * D], F32)
        pp0 = psum("lf_p0", [128, 512])
        pp1 = psum("lf_p1", [128, 512])
        pb8 = psum("lf_pb", [128, 8], BF16)
        mtok = kb.dma("sync", modT_sb[:], aps["modT1"][:], stream="d_lf_m")
        s1 = kb.dma("sync", sc_sb[0:1, 0:1], aps["L1_ffn_input_scale"][:],
                    stream="d_lf_s")
        s2 = kb.dma("sync", sc_sb[0:1, 1:2], aps["L1_ffn_kernel_scale"][:],
                    stream="d_lf_s2")
        t_a = kb.ins("vector", lambda e: e.tensor_add(
            sc_sb[0:1, 2:3], sc_sb[0:1, 0:1], sc_sb[0:1, 0:1]), deps=[s1])
        t_b = kb.ins("vector", lambda e: e.tensor_add(
            sc_sb[0:1, 3:4], sc_sb[0:1, 1:2], sc_sb[0:1, 1:2]), deps=[s2])
        t_ra = kb.ins("vector", lambda e: e.reciprocal(
            sc_sb[0:1, 5:6], sc_sb[0:1, 2:3]), deps=[t_a])
        t_rb = kb.ins("vector", lambda e: e.reciprocal(
            sc_sb[0:1, 6:7], sc_sb[0:1, 3:4]), deps=[t_b])
        t_c = kb.ins("vector", lambda e: e.tensor_mul(
            sc_sb[0:1, 4:5], sc_sb[0:1, 0:1], sc_sb[0:1, 1:2]), deps=[s1, s2])
        t_d = kb.ins("vector", lambda e: e.tensor_scalar(
            sc_sb[0:1, 7:8], sc_sb[0:1, 4:5], 4.0, None, ALU.mult), deps=[t_c])
        mrep = kb.ins("tensor", lambda e: e.matmul(
            pp0[:, 0:3], st["c_ones_row"][0:1, :], sc_sb[0:1, 5:8],
            start=True, stop=True), deps=[t_ra, t_rb, t_d])
        erep = kb.ins("vector", lambda e: e.tensor_copy(rep_sb[:], pp0[:, 0:3]),
                      deps=[mrep])
        kl = kb.dma("sync", sb3(kb_sb[:], D),
                    aps["L1_ffn_kernel"].rearrange("(kc kp) o -> kp kc o", kp=128),
                    stream="d_lf_k")
        kq = kb.ins("vector", lambda e: e.tensor_scalar(
            k8_sb[:], kb_sb[:], rep_sb[:, 1:2], None, ALU.mult),
            deps=[kl, erep])
        bl = kb.dma("sync", bia_r[:], aps["L1_ffn_bias"][0, :]
                    .rearrange("(a b) -> a b", b=128), stream="d_lf_b")
        tb = kb.ins("tensor", lambda e: e.transpose(
            pb8[:], bia_r[:], st["c_ident"][0:8, 0:8]), deps=[bl])
        eb = kb.ins("vector", lambda e: e.tensor_copy(bia_sb[:], pb8[:]),
                    deps=[tb])
        prev = {}
        for sl in range(T // SL):
            b = (sl * SL) // S
            hdl, xdl = [], []
            for c in range(KC):
                hdl.append(kb.dma("sync", sb3(h_sb[:, c * SL:(c + 1) * SL]),
                                  fm_ap(aps["h4"], c, sl * SL, SL),
                                  deps=prev.get("h", []), stream="d_lf_h"))
                xdl.append(kb.dma("gpsimd", sb3(x_sb[:, c * SL:(c + 1) * SL]),
                                  fm_ap(aps["x3fm"], c, sl * SL, SL),
                                  deps=prev.get("x", []), stream="d_lf_x"))
            hdl, xdl = [hdl[-1]], [xdl[-1]]
            hqs = []
            for c in range(KC):
                hqs.append(kb.ins("vector", lambda e, c=c: e.tensor_scalar(
                    h8_sb[:, c * SL:(c + 1) * SL], h_sb[:, c * SL:(c + 1) * SL],
                    rep_sb[:, 0:1], None, ALU.mult),
                    deps=hdl + [erep] + prev.get("h8", [])))
            outs = []
            for och in range(KC):
                pp = pp0 if och % 2 == 0 else pp1
                mms = []
                for k in range(KC):
                    mms.append(kb.ins("tensor", lambda e, k=k, och=och, pp=pp: e.matmul(
                        pp[:, 0:SL], k8_sb[:, k * D + och * 128: k * D + (och + 1) * 128],
                        h8_sb[:, k * SL:(k + 1) * SL],
                        start=(k == 0), stop=(k == KC - 1)),
                        deps=[kq, hqs[k]] + prev.get(("pp", och % 2), [])))
                ya = kb.ins("vector", lambda e, pp=pp: e.tensor_scalar(
                    y_sb[:], pp[:, 0:SL], rep_sb[:, 2:3], None, ALU.mult),
                    deps=mms[-1:] + [erep] + prev.get("y", []))
                yb = kb.ins("vector", lambda e, och=och: e.tensor_scalar(
                    y_sb[:], y_sb[:], bia_sb[:, och:och + 1], None, ALU.add),
                    deps=[ya, eb])
                ep = kb.ins("vector", lambda e, och=och, b=b: e.scalar_tensor_tensor(
                    x_sb[:, och * SL:(och + 1) * SL], y_sb[:],
                    modT_sb[:, modcol(cfg, 5, och, b):modcol(cfg, 5, och, b) + 1],
                    x_sb[:, och * SL:(och + 1) * SL], ALU.mult, ALU.add),
                    deps=[yb, mtok] + xdl)
                outs.append(ep)
                prev[("pp", och % 2)] = [ya]
                prev["y"] = [ep]
            odmas = []
            for tt in range(SL // 128):
                tts = []
                for c in range(KC):
                    pp2 = pp0 if c < 4 else pp1
                    tts.append(kb.ins("tensor", lambda e, c=c, tt=tt, pp2=pp2: e.transpose(
                        pp2[:, (c % 4) * 128:(c % 4 + 1) * 128],
                        x_sb[:, c * SL + tt * 128: c * SL + (tt + 1) * 128],
                        st["c_identf"][:]),
                        deps=[outs[c], outs[-1]] + prev.get("egprev", [])))
                ev = kb.ins("vector", lambda e, tt=tt: e.tensor_copy(
                    eg_sb[:, (tt % 2) * D: (tt % 2) * D + 512], pp0[:]),
                    deps=tts + prev.get(("egs", tt % 2), []))
                ev2 = kb.ins("vector", lambda e, tt=tt: e.tensor_copy(
                    eg_sb[:, (tt % 2) * D + 512:(tt % 2 + 1) * D], pp1[:]),
                    deps=tts + prev.get(("egs", tt % 2), []))
                od = kb.dma("sync", aps["out"][sl * SL + tt * 128: sl * SL + (tt + 1) * 128, :],
                            eg_sb[:, (tt % 2) * D:(tt % 2 + 1) * D],
                            deps=[ev, ev2], stream="d_lf_o%d" % (tt % 2))
                odmas.append(od)
                prev["egprev"] = [ev, ev2]
                prev[("egs", tt % 2)] = [od]
            prev["h"] = hqs
            prev["h8"] = mms
            prev["x"] = tts
            prev[("pp", 0)] = [ev]
            prev[("pp", 1)] = [ev2]
    return odmas


def ph_egress(kb, cfg, st, aps):
    return emit_tm2fm(kb, cfg, st, aps["x4fm"], aps["out"], "_eg", rev=True)



_LAUNCH_SPECS = [
    ([(ph_ingest, {}), (ph_wT, {}), (ph_mod, {"layer": 0}),
      (ph_ln, {"x": "x0fm", "h": "h1", "layer": 0, "which": "A"}),
      (ph_l0qkv, {})],
     [], ["x0fm", "modT0", "qro", "kro", "vtm", "woT"]),
    ([(ph_l0attn, {}), (ph_l0wo, {})],
     ["x0fm", "modT0", "qro", "kro", "vtm", "woT"], ["x1fm"]),
    ([(ph_ln, {"x": "x1fm", "h": "h2", "layer": 0, "which": "B"}),
      (ph_route, {}), (ph_moe, {})],
     ["x1fm", "modT0"], ["x2fm"]),
    ([(ph_mod, {"layer": 1}),
      (ph_ln, {"x": "x2fm", "h": "h3", "layer": 1, "which": "A"}),
      (ph_l1proj, {}), (ph_l1attn, {}), (ph_l1out, {})],
     ["x2fm"], ["x3fm", "modT1"]),
    ([(ph_ln, {"x": "x3fm", "h": "h4", "layer": 1, "which": "B"}),
      (ph_l1ffn, {})],
     ["x3fm", "modT1"], ["out"]),
]

_prog_cache = {}


def _get_prog(cfg_key, li, cfg):
    if (cfg_key, li) not in _prog_cache:
        phases, sc_in, sc_out = _LAUNCH_SPECS[li]
        ext = {n: "in" for n in sc_in}
        ext.update({n: "out" for n in sc_out})
        _prog_cache[(cfg_key, li)] = build_program(cfg, phases, ext)
    return _prog_cache[(cfg_key, li)]


def _shard_inputs(cfg, full):
    NC, BPC, D, S = cfg["NCORES"], cfg["BPC"], cfg["D"], cfg["S"]
    specs = input_specs_per_core(cfg)
    maps = []
    for c in range(NC):
        m = {}
        for name, (shape, dt) in specs.items():
            a = full[name]
            if name == "L0_attn_hidden_states":
                a = np.ascontiguousarray(a[c * BPC:(c + 1) * BPC]).reshape(BPC * S, D)
            elif name in ("L0_c", "L1_c"):
                a = np.ascontiguousarray(a[c * BPC:(c + 1) * BPC])
            else:
                a = np.asarray(a)
            a = a.reshape(shape)
            m[name] = a
        m.update(host_constants(cfg, c))
        maps.append(m)
    return maps


def kernel(**inputs):
    if FUSED:
        return kernel_fused(**inputs)
    return _kernel_phased(**inputs)


_ALL_PHASES = [
    (ph_ingest, {}), (ph_wT, {}), (ph_mod, {"layer": 0}),
    (ph_ln, {"x": "x0fm", "h": "h1", "layer": 0, "which": "A"}),
    (ph_l0qkv, {}),
    (ph_l0attn, {}), (ph_l0wo, {}),
    (ph_ln, {"x": "x1fm", "h": "h2", "layer": 0, "which": "B"}),
    (ph_route, {}), (ph_moe, {}),
    (ph_mod, {"layer": 1}),
    (ph_ln, {"x": "x2fm", "h": "h3", "layer": 1, "which": "A"}),
    (ph_l1proj, {}), (ph_l1attn, {}), (ph_l1out, {}),
    (ph_ln, {"x": "x3fm", "h": "h4", "layer": 1, "which": "B"}),
    (ph_l1ffn, {}),
]

FUSED = True


def kernel_fused(**inputs):
    cfg = make_cfg()
    NC = cfg["NCORES"]
    maps = _shard_inputs(cfg, inputs)
    if ("fused", 0) not in _prog_cache:
        _prog_cache[("fused", 0)] = build_program(cfg, _ALL_PHASES, {"out": "out"})
    nc = _prog_cache[("fused", 0)]
    res = run_bass_kernel_spmd(nc, maps, core_ids=list(range(NC)))
    out = np.concatenate(
        [np.asarray(res.results[c]["out"]).reshape(cfg["BPC"], cfg["S"], cfg["D"])
         for c in range(NC)], axis=0)
    return out.astype(np.float32)


def _kernel_phased(**inputs):
    cfg = make_cfg()
    NC = cfg["NCORES"]
    maps = _shard_inputs(cfg, inputs)
    carry = [dict() for _ in range(NC)]
    for li in range(len(_LAUNCH_SPECS)):
        phases, sc_in, sc_out = _LAUNCH_SPECS[li]
        nc = _get_prog("full", li, cfg)
        in_maps = []
        for c in range(NC):
            m = dict(maps[c])
            for n in sc_in:
                m[n] = carry[c][n]
            in_maps.append(m)
        res = run_bass_kernel_spmd(nc, in_maps, core_ids=list(range(NC)))
        for c in range(NC):
            for n in sc_out:
                carry[c][n] = np.asarray(res.results[c][n])
    out = np.concatenate([carry[c]["out"].reshape(cfg["BPC"], cfg["S"], cfg["D"])
                          for c in range(NC)], axis=0)
    return out.astype(np.float32)
